# version 11
# baseline (speedup 1.0000x reference)
import contextlib
import math
import numpy as np
import concourse.bass as bass
import concourse.mybir as mybir
from concourse.alu_op_type import AluOpType as OP
from concourse.bass_utils import run_bass_kernel_spmd

F32 = mybir.dt.float32
BF16 = mybir.dt.bfloat16
U32 = mybir.dt.uint32
I32 = mybir.dt.int32
AF = mybir.ActivationFunctionType
AX = mybir.AxisListType

NCORES = 8
D = 1024
NB = 4
T = 4096
TL = 2048
NCH = D // 128
TWO_PI = 2.0 * math.pi
HALF_PI = 0.5 * math.pi

ESEM = {"tensor": "ts", "vector": "vs", "scalar": "ss", "gpsimd": "gs"}
SUB4 = 9


class Ctx:

    def __init__(self, nc, block):
        self.nc = nc
        self.block = block
        self.sem = {}
        self.cnt = {}
        for s in ("ts", "vs", "ss", "gs", "dS", "dG", "cc", "st"):
            h = nc.ctx.enter_context(nc.semaphore(s))
            self.sem[s] = h
            self.cnt[s] = 0
        self.ones128 = None
        self.ident_bf = None
        self.cbank = None
        self.cmap = {}
        self.cused = 0

    AUTO_SELF = True

    def emit(self, eng, f, wait=(), dma=False, coll=False, store=False):
        if dma:
            s = "dG" if eng == "gpsimd" else ("st" if store else "dS")
            n = 16
        elif coll:
            s, n = "cc", 1
        else:
            s, n = ESEM[eng], 1
        sems = self.sem
        if self.AUTO_SELF and not dma and not coll:
            wait = tuple(wait) + ((s, self.cnt[s]),)
        wait = tuple((w, v) for (w, v) in wait if v > 0)
        sh = sems[s]

        def body(e):
            for w, v in wait:
                e.wait_ge(sems[w], v)
            f(e).then_inc(sh, n)

        getattr(self.block, eng)(body)
        self.cnt[s] += n
        return self.cnt[s]

    def tnop(self):
        return self.emit("tensor", lambda t: t.nop())

    def constcol(self, val):
        if val in self.cmap:
            return self.cmap[val]
        i = self.cused
        self.cused += 1
        ap = self.cbank[:, i:i + 1]
        self.emit("gpsimd", lambda g: g.memset(ap, float(val)))
        self.cmap[val] = ap
        return ap

    def wait(self, eng, *pairs):
        pairs = tuple((w, v) for (w, v) in pairs if v > 0)
        if not pairs:
            return

        def body(e):
            for w, v in pairs:
                e.wait_ge(self.sem[w], v)

        getattr(self.block, eng)(body)


def ceil_div(a, b):
    return (a + b - 1) // b




def emit_prelude(ctx, ident_bf, ones_f32, onesrow, ident_f32=None):
    ctx.emit("gpsimd", lambda g: g.memset(onesrow[:, :], 1.0))
    ctx.onesrow = onesrow[0:1, 0:128]
    if ident_f32 is not None:
        ctx.emit("gpsimd", lambda g: g.memset(ident_f32[:, :], 1.0))
        ctx.emit("gpsimd", lambda g: g.affine_select(
            ident_f32[:, :], ident_f32[:, :], pattern=[[1, 128]],
            compare_op=OP.is_equal, fill=0.0, base=0, channel_multiplier=-1))
        ctx.ident_f32 = ident_f32[:, :]
    ctx.emit("gpsimd", lambda g: g.memset(ident_bf[:], 1.0))
    ctx.emit("gpsimd", lambda g: g.affine_select(
        ident_bf[:], ident_bf[:], pattern=[[1, 128]], compare_op=OP.is_equal,
        fill=0.0, base=0, channel_multiplier=-1))
    gm = ctx.emit("gpsimd", lambda g: g.memset(ones_f32[:], 1.0))
    if getattr(ctx, "onesrow_bf_t", None) is not None:
        ctx.emit("gpsimd", lambda g: g.memset(ctx.onesrow_bf_t[:, :], 1.0))
        ctx.onesrow_bf = ctx.onesrow_bf_t[0:1, 0:128]
    ctx.ones128 = ones_f32[:, 0:1]
    ctx.ones_row = ones_f32[0:1, 0:1].rearrange("p o -> p o") if False else None
    ctx.ident_bf = ident_bf
    return gm


def bcast_row(ctx, ps_dst, row_ap, wait=()):
    onesr = ctx.onesrow
    return ctx.emit("tensor", lambda t: t.matmul(
        ps_dst, onesr, row_ap, start=True, stop=True), wait=wait)


def emit_ln(ctx, x_sb, xn_sb, Ttok, ps_s1, ps_s2, sqbuf, rows, ps_bc, ready=()):
    ones = ctx.ones128
    ngr = Ttok // 512
    sq_reader = [0, 0]
    mrow = rows[0:1, 0:Ttok]
    vrow = rows[0:1, Ttok:2 * Ttok]
    trow = rows[0:1, 2 * Ttok:3 * Ttok]
    ev = 0
    for g in range(ngr):
        w0 = tuple(ready) + (("vs", ev),)
        for c in range(NCH):
            xs = x_sb[:, c * Ttok + g * 512: c * Ttok + g * 512 + 512]
            ctx.emit("tensor", lambda t, xs=xs, c=c: t.matmul(
                ps_s1[0:1, :], ones, xs,
                start=(c == 0), stop=(c == NCH - 1)),
                wait=w0 if c == 0 else ())
            sl = c % 2
            sq = sqbuf[:, sl * 512:(sl + 1) * 512]
            wv = ctx.emit("scalar", lambda s, xs=xs, sq=sq: s.activation(
                out=sq, in_=xs, func=AF.Square),
                wait=(("ts", sq_reader[sl]),) + (tuple(ready) if (g == 0 and c == 0) else ()))
            sq_reader[sl] = ctx.emit("tensor", lambda t, sq=sq, c=c: t.matmul(
                ps_s2[0:1, :], ones, sq,
                start=(c == 0), stop=(c == NCH - 1)),
                wait=(("ss", wv),))
        td = ctx.cnt["ts"]
        ctx.emit("vector", lambda v, g=g: v.tensor_scalar(
            out=mrow[:, g * 512:(g + 1) * 512], in0=ps_s1[0:1, :],
            scalar1=1.0 / D, scalar2=None, op0=OP.mult), wait=(("ts", td),))
        ev = ctx.emit("vector", lambda v, g=g: v.tensor_scalar(
            out=vrow[:, g * 512:(g + 1) * 512], in0=ps_s2[0:1, :],
            scalar1=1.0 / D, scalar2=None, op0=OP.mult))
    ctx.emit("vector", lambda v: v.scalar_tensor_tensor(
        out=trow, in0=mrow, scalar=-1.0, in1=mrow, op0=OP.mult, op1=OP.mult))
    vv = ctx.emit("vector", lambda v: v.tensor_tensor(
        out=vrow, in0=vrow, in1=trow, op=OP.add))
    lnv = ctx.emit("scalar", lambda s: s.activation(
        out=vrow, in_=vrow, func=AF.Ln, bias=ctx.constcol(1e-5)[0:1, :], scale=1.0),
        wait=(("vs", vv),))
    rv = ctx.emit("scalar", lambda s: s.activation(
        out=vrow, in_=vrow, func=AF.Exp, scale=-0.5), wait=(("ss", lnv),))
    mbc = ps_bc[:, 0:512]
    rbc = ps_bc[:, 512:1024]
    out_v = 0
    bc_rd = 0
    for g in range(ngr):
        mr = mrow[:, g * 512:(g + 1) * 512]
        vr = vrow[:, g * 512:(g + 1) * 512]
        bcast_row(ctx, mbc, mr, wait=(("ss", rv), ("vs", bc_rd)))
        gv2 = bcast_row(ctx, rbc, vr)
        for c in range(NCH):
            sl = c % 2
            tp = sqbuf[:, sl * 512:(sl + 1) * 512]
            xs = x_sb[:, c * Ttok + g * 512: c * Ttok + g * 512 + 512]
            xo = xn_sb[:, c * Ttok + g * 512: c * Ttok + g * 512 + 512]
            w = (("ts", gv2),) if c == 0 else ()
            ctx.emit("vector", lambda v, xs=xs, tp=tp: v.tensor_tensor(
                out=tp, in0=xs, in1=mbc, op=OP.subtract), wait=w)
            out_v = ctx.emit("vector", lambda v, xo=xo, tp=tp: v.tensor_tensor(
                out=xo, in0=tp, in1=rbc, op=OP.mult))
        bc_rd = out_v
    return out_v


def emit_qprep(ctx, qp_in, k_i, qsc, Atil, sinb, ready):
    th, wr, wi, b1, kf = (qp_in[:, i * 32:(i + 1) * 32] for i in range(5))
    thr, gt, lt, s2, t1, t2 = (qsc[:, i * 32:(i + 1) * 32] for i in range(6))
    _ = k_i
    e = ctx.emit
    e("vector", lambda v: v.tensor_scalar(out=kf, in0=th, scalar1=0.0,
                                          scalar2=None, op0=OP.mult), wait=ready)
    for j in range(6):
        thr_j = (2 * j + 1) * math.pi
        e("vector", lambda v, thr_j=thr_j: v.tensor_scalar(
            out=t1, in0=th, scalar1=thr_j, scalar2=None, op0=OP.is_gt))
        e("vector", lambda v: v.tensor_tensor(out=kf, in0=kf, in1=t1, op=OP.add))
        e("vector", lambda v, thr_j=thr_j: v.tensor_scalar(
            out=t1, in0=th, scalar1=-thr_j, scalar2=None, op0=OP.is_lt))
        e("vector", lambda v: v.tensor_tensor(out=kf, in0=kf, in1=t1, op=OP.subtract))
    e("vector", lambda v: v.scalar_tensor_tensor(
        out=thr, in0=kf, scalar=-TWO_PI, in1=th, op0=OP.mult, op1=OP.add))
    e("vector", lambda v: v.tensor_scalar(out=gt, in0=thr, scalar1=HALF_PI,
                                          scalar2=None, op0=OP.is_gt))
    e("vector", lambda v: v.tensor_scalar(out=lt, in0=thr, scalar1=-HALF_PI,
                                          scalar2=None, op0=OP.is_lt))
    e("vector", lambda v: v.tensor_tensor(out=s2, in0=gt, in1=lt, op=OP.subtract))
    vv = e("vector", lambda v: v.scalar_tensor_tensor(
        out=thr, in0=s2, scalar=-math.pi, in1=thr, op0=OP.mult, op1=OP.add))
    e("scalar", lambda s: s.activation(out=gt, in_=thr, func=AF.Sin),
      wait=(("vs", vv),))
    hp = ctx.constcol(HALF_PI)
    sv = e("scalar", lambda s: s.activation(out=lt, in_=thr, func=AF.Sin,
                                            bias=hp, scale=-1.0))
    e("vector", lambda v: v.tensor_tensor(out=t1, in0=lt, in1=wr, op=OP.mult),
      wait=(("ss", sv),))
    e("vector", lambda v: v.tensor_tensor(out=t2, in0=gt, in1=wi, op=OP.mult))
    e("vector", lambda v: v.tensor_tensor(out=t1, in0=t1, in1=t2, op=OP.add))
    e("vector", lambda v: v.tensor_tensor(out=t1, in0=t1, in1=wi, op=OP.mult))
    e("vector", lambda v: v.tensor_scalar(out=Atil[:, :], in0=t1, scalar1=0.1,
                                          scalar2=None, op0=OP.mult))
    out_v = e("vector", lambda v: v.scalar_tensor_tensor(
        out=sinb[:, :], in0=b1, scalar=0.1, in1=thr, op0=OP.mult, op1=OP.add))
    return out_v


def emit_l1ffn(ctx, x_sb, xn_sb, xn_ready, w1_sb, w2_sb, w1_dram, w2_dram,
               Atil, sinb, b1c, b2c, act, ps_h, ps_o, ssc, Ttok=TL):
    TGR = 512
    ngr = Ttok // TGR
    NF = 32
    w1_ready = [0, 0]
    w1_user = [0, 0]
    w2_ready = [0, 0]
    w2_user = [0, 0]

    def w1_load(q, fc, bufi):
        return q.dma_start(
            out=w1_sb[:, bufi * 1024:(bufi + 1) * 1024]
            .rearrange("p (c f) -> p c f", c=8),
            in_=w1_dram[:, fc * 128:(fc + 1) * 128]
            .rearrange("(c p) f -> p c f", p=128))

    def w2_load(q, oc, bufi):
        return q.dma_start(
            out=w2_sb[:, bufi * 4096:(bufi + 1) * 4096]
            .rearrange("p (c o) -> p c o", c=32),
            in_=w2_dram[:, oc * 128:(oc + 1) * 128]
            .rearrange("(c p) o -> p c o", p=128))

    w1_ready[0] = ctx.emit("gpsimd", lambda q: w1_load(q, 0, 0), dma=True)
    last_act_rd = 0
    sv_prev = 0
    vv_prev = 0
    stt_prev = 0
    for g in range(ngr):
        act_done = 0
        for fc in range(NF):
            bufi = fc % 2
            nxt = 1 - bufi
            nfc = fc + 1 if fc + 1 < NF else 0
            if fc + 1 < NF or g + 1 < ngr:
                w1_ready[nxt] = ctx.emit(
                    "gpsimd", lambda q, nfc=nfc, nxt=nxt: w1_load(q, nfc, nxt),
                    wait=(("ts", w1_user[nxt]),), dma=True)
            tv = 0
            for kc in range(NCH):
                tv = ctx.emit("tensor", lambda t, kc=kc, bufi=bufi, g=g: t.matmul(
                    ps_h[:, 0:TGR],
                    w1_sb[:, bufi * 1024 + kc * 128: bufi * 1024 + (kc + 1) * 128],
                    xn_sb[:, kc * Ttok + g * TGR: kc * Ttok + g * TGR + TGR],
                    start=(kc == 0), stop=(kc == NCH - 1)),
                    wait=(("dG", w1_ready[bufi]), ("vs", max(xn_ready, vv_prev)),
                          ("ss", sv_prev)) if kc == 0 else ())
            w1_user[bufi] = tv
            sv_prev = ctx.emit("scalar", lambda s, fc=fc: s.activation(
                out=ssc[:, 0:TGR], in_=ps_h[:, 0:TGR], func=AF.Sin,
                bias=sinb[:, fc:fc + 1], scale=0.1),
                wait=(("ts", tv), ("vs", vv_prev)))
            vv_prev = ctx.emit("vector", lambda v, fc=fc: v.scalar_tensor_tensor(
                out=ssc[:, TGR:2 * TGR], in0=ssc[:, 0:TGR],
                scalar=Atil[:, fc:fc + 1], in1=ps_h[:, 0:TGR],
                op0=OP.mult, op1=OP.add), wait=(("ss", sv_prev),))
            act_done = ctx.emit("vector", lambda v, fc=fc: v.tensor_scalar(
                out=act[:, fc * TGR:(fc + 1) * TGR], in0=ssc[:, TGR:2 * TGR],
                scalar1=b1c[:, fc:fc + 1], scalar2=0.0,
                op0=OP.add, op1=OP.max),
                wait=(("ts", last_act_rd),) if fc == 0 else ())
        for oc in range(NCH):
            bufi = oc % 2
            nxt = 1 - bufi
            if oc == 0:
                w2_ready[0] = ctx.emit("gpsimd", lambda q, g=g: w2_load(q, 0, 0),
                                       wait=(("ts", w2_user[0]),), dma=True)
            if oc + 1 < NCH:
                w2_ready[nxt] = ctx.emit(
                    "gpsimd", lambda q, oc=oc, nxt=nxt: w2_load(q, oc + 1, nxt),
                    wait=(("ts", w2_user[nxt]),), dma=True)
            tv = 0
            for kc in range(NF):
                tv = ctx.emit("tensor", lambda t, kc=kc, oc=oc, bufi=bufi: t.matmul(
                    ps_o[:, 0:TGR],
                    w2_sb[:, bufi * 4096 + kc * 128: bufi * 4096 + kc * 128 + 128],
                    act[:, kc * TGR:(kc + 1) * TGR],
                    start=(kc == 0), stop=(kc == NF - 1)),
                    wait=(("vs", act_done), ("dG", w2_ready[bufi]),
                          ("vs", stt_prev)) if kc == 0 else ())
            w2_user[bufi] = tv
            if oc == NCH - 1:
                last_act_rd = tv
            stt_prev = ctx.emit("vector", lambda v, oc=oc, g=g: v.scalar_tensor_tensor(
                out=x_sb[:, oc * Ttok + g * TGR: oc * Ttok + g * TGR + TGR],
                in0=ps_o[:, 0:TGR], scalar=b2c[:, oc:oc + 1],
                in1=x_sb[:, oc * Ttok + g * TGR: oc * Ttok + g * TGR + TGR],
                op0=OP.add, op1=OP.add), wait=(("ts", tv),))
    return stt_prev




def build_l1ffn_launch():
    nc = bass.Bass("TRN2", target_bir_lowering=False, debug=False,
                   num_devices=NCORES)
    xin = nc.dram_tensor("xin", [D, TL], F32, kind="ExternalInput").ap()
    w1 = nc.dram_tensor("w1", [D, 4096], F32, kind="ExternalInput").ap()
    w2 = nc.dram_tensor("w2", [4096, D], F32, kind="ExternalInput").ap()
    qpv = nc.dram_tensor("qpv", [128, 160], F32, kind="ExternalInput").ap()
    b2v = nc.dram_tensor("b2v", [128, 8], F32, kind="ExternalInput").ap()
    yout = nc.dram_tensor("yout", [D, TL], F32, kind="ExternalOutput").ap()
    xnd = nc.dram_tensor("xnd", [D, TL], BF16, kind="ExternalOutput").ap()
    qpd = nc.dram_tensor("qpd", [128, 128], F32, kind="ExternalOutput").ap()

    es = contextlib.ExitStack()

    def sb(name, shape, dt):
        return es.enter_context(nc.sbuf_tensor(name, shape, dt))

    def ps(name, shape, dt=F32):
        return es.enter_context(nc.psum_tensor(name, shape, dt))

    with es:
        x_sb = sb("x_sb", [128, NCH * TL], F32)
        xn_sb = sb("xn_sb", [128, NCH * TL], BF16)
        w1_sb = sb("w1_sb", [128, 2 * 1024], BF16)
        w2_sb = sb("w2_sb", [128, 2 * 4096], BF16)
        act = sb("act", [128, 32 * 512], BF16)
        sqbuf = sb("sqbuf", [128, 1024], F32)
        rows = sb("rows", [1, 3 * TL], F32)
        ssc = sb("ssc", [128, 1024], F32)
        qp = sb("qp", [128, 160], F32)
        qsc = sb("qsc", [128, 192], F32)
        kI = sb("kI", [128, 32], I32)
        Atil = sb("Atil", [128, 32], F32)
        sinb = sb("sinb", [128, 32], F32)
        b2c = sb("b2c", [128, 8], F32)
        ident = sb("ident", [128, 128], BF16)
        onesb = sb("onesb", [128, 1], F32)
        onesrow = sb("onesrow", [1, 128], F32)
        cbank = sb("cbank", [128, 16], F32)
        ps_s1 = ps("ps_s1", [1, 512])
        ps_s2 = ps("ps_s2", [1, 512])
        ps_bc = ps("ps_bc", [128, 1024])
        ps_h = ps("ps_h", [128, 512])
        ps_o = ps("ps_o", [128, 512])
        block = es.enter_context(nc.Block())
        ctx = Ctx(nc, block)
        ctx.cbank = cbank
        emit_prelude(ctx, ident, onesb, onesrow)
        ctx.constcol(1e-5)
        ctx.constcol(HALF_PI)
        xv = ctx.emit("sync", lambda q: q.dma_start(
            out=x_sb[:, :].rearrange("p (c t) -> p c t", c=NCH),
            in_=xin.rearrange("(c p) t -> p c t", p=128)),
            dma=True)
        qv = ctx.emit("sync", lambda q: q.dma_start(out=qp[:, :], in_=qpv),
                      dma=True)
        bv = ctx.emit("sync", lambda q: q.dma_start(out=b2c[:, :], in_=b2v),
                      dma=True)
        emit_qprep(ctx, qp, kI, qsc, Atil, sinb,
                   (("dS", qv), ("gs", ctx.cnt["gs"])))
        xn_ready = emit_ln(ctx, x_sb, xn_sb, TL, ps_s1, ps_s2, sqbuf, rows,
                           ps_bc, ready=(("dS", xv), ("gs", ctx.cnt["gs"])))
        ctx.wait("vector", ("dS", bv))
        ctx.wait("scalar", ("dS", bv))
        done = emit_l1ffn(ctx, x_sb, xn_sb, xn_ready, w1_sb, w2_sb, w1, w2,
                          Atil, sinb, qp[:, 96:128], b2c, act, ps_h, ps_o, ssc)
        sv = ctx.emit("sync", lambda q: q.dma_start(
            out=yout.rearrange("(c p) t -> p c t", p=128),
            in_=x_sb[:, :].rearrange("p (c t) -> p c t", c=NCH)),
            wait=(("vs", done), ("dS", bv)), dma=True, store=True)
        ctx.emit("sync", lambda q: q.dma_start(
            out=xnd.rearrange("(c p) t -> p c t", p=128),
            in_=xn_sb[:, :].rearrange("p (c t) -> p c t", c=NCH)),
            wait=(("vs", done),), dma=True, store=True)
        ctx.emit("sync", lambda q: q.dma_start(
            out=qpd[:, 0:32], in_=Atil[:, :]), wait=(("vs", done),),
            dma=True, store=True)
        ctx.emit("sync", lambda q: q.dma_start(
            out=qpd[:, 32:64], in_=sinb[:, :]), dma=True, store=True)
        ctx.emit("sync", lambda q: q.dma_start(
            out=qpd[:, 64:96], in_=qp[:, 128:160]), dma=True, store=True)
        ctx.emit("sync", lambda q: q.dma_start(
            out=qpd[:, 96:128], in_=qsc[:, 0:32]), dma=True, store=True)
        ctx.wait("sync", ("st", ctx.cnt["st"]))
    return nc




def _stage_qpv(b1, theta, wr, wi):
    def chunked(v):
        return np.ascontiguousarray(v.reshape(32, 128).T.astype(np.float32))
    out = np.zeros((128, 160), np.float32)
    out[:, 0:32] = chunked(theta)
    out[:, 32:64] = chunked(wr)
    out[:, 64:96] = chunked(wi)
    out[:, 96:128] = chunked(b1)
    return out


_PROGS = {}
DBG = None


def _dbg_snap(xs):
    if DBG is not None:
        out = np.zeros((NB, T, D), np.float32)
        for c in range(NCORES):
            out[c // 2, (c % 2) * TL:(c % 2 + 1) * TL, :] = xs[c].T
        DBG.append(out)


def _prog(name, builder, *a):
    if name not in _PROGS:
        _PROGS[name] = builder(*a)
    return _PROGS[name]


def _run8(nc, inmaps):
    return run_bass_kernel_spmd(nc, inmaps, core_ids=list(range(NCORES)))


def _asc(a):
    return np.ascontiguousarray(a)


def _kernel_unfused(**kw):
    x0 = kw["L0_attn_a_x"]
    cores = list(range(NCORES))
    tsl = [slice((c % 2) * TL, (c % 2 + 1) * TL) for c in cores]
    csl = [slice((c % 2) * 512, (c % 2 + 1) * 512) for c in cores]
    xs = [_asc(x0[c // 2, tsl[c], :].T.astype(np.float32)) for c in cores]

    def ln_pass(xs):
        nc = _prog("ln", build_epi_launch, "ln")
        r = _run8(nc, [{"xin": xs[c]} for c in cores])
        return [_asc(r.results[c]["xnout"]) for c in cores]

    def xn_join(xnh):
        return [_asc(np.concatenate([xnh[(c // 2) * 2], xnh[(c // 2) * 2 + 1]],
                                    axis=1)) for c in cores]

    def o_join(ofm):
        return [_asc(np.concatenate([ofm[(c // 2) * 2][:, tsl[c]],
                                     ofm[(c // 2) * 2 + 1][:, tsl[c]]], axis=0))
                for c in cores]

    def epi_pass(xs, ofull, warr, bpc=None):
        kind = "bias" if bpc is not None else "plain"
        nc = _prog("epi_" + kind, build_epi_launch, kind)
        ins = []
        for c in cores:
            m = {"xin": xs[c], "ofu": ofull[c], "warr": warr}
            if bpc is not None:
                m["bpc"] = bpc
            ins.append(m)
        r = _run8(nc, ins)
        return ([_asc(r.results[c]["xout"]) for c in cores],
                [_asc(r.results[c]["xnout"]) for c in cores])

    def moe_pass(xs, xnh):
        WgT = _asc(kw["L0_ffn_Wg"].T.astype(np.float32))
        nc1 = _prog("moe_route", build_moe_route_launch)
        r1 = _run8(nc1, [{"xnin": xnh[c], "wgt": WgT} for c in cores])
        xn_all = _asc(np.concatenate(
            [r1.results[c]["xntm_o"] for c in cores], 0))
        rout_all = _asc(np.concatenate(
            [r1.results[c]["rout_o"] for c in cores], 0))
        tot_all = _asc(np.concatenate(
            [r1.results[c]["tot_o"].T for c in cores], 0).astype(np.float32))
        nc2 = _prog("moe_expert", build_moe_expert_launch)
        ins2 = []
        for e in range(8):
            tbl = (e * CAP + np.arange(CAP)).astype(np.uint32)
            ins2.append({
                "xn_all": xn_all, "rout_all": rout_all, "tot_all": tot_all,
                "w1e": _asc(kw["L0_ffn_W1"][e].astype(np.float32)),
                "w2e": _asc(kw["L0_ffn_W2"][e].astype(np.float32)),
                "tblidx": _asc(tbl.reshape(CAP // 128, 128).T),
            })
        r2 = _run8(nc2, ins2)
        expo_all = _asc(np.concatenate(
            [r2.results[e]["expo"] for e in range(8)], 0))
        nc3 = _prog("moe_combine", build_moe_combine_launch)
        ins3 = []
        for c in cores:
            onehot = np.zeros((8, 1), np.float32)
            onehot[c] = 1.0
            ins3.append({
                "xin": xs[c], "expo_all": expo_all,
                "rout_my": _asc(rout_all[c * TL:(c + 1) * TL]),
                "tot_all": tot_all, "mycol": onehot,
            })
        r3 = _run8(nc3, ins3)
        return [_asc(r3.results[c]["xout"]) for c in cores]

    def l1ffn_pass(xs):
        nc = _prog("l1ffn", build_l1ffn_launch)
        qpv = _stage_qpv(kw["L1_ffn_b1"].astype(np.float32),
                         kw["L1_ffn_quantum_phases"].astype(np.float32),
                         kw["L1_ffn_quantum_weights_real"].astype(np.float32),
                         kw["L1_ffn_quantum_weights_imag"].astype(np.float32))
        b2v = _asc(kw["L1_ffn_b2"].astype(np.float32).reshape(8, 128).T)
        w1 = _asc(kw["L1_ffn_W1"].astype(np.float32))
        w2 = _asc(kw["L1_ffn_W2"].astype(np.float32))
        r = _run8(nc, [{"xin": xs[c], "w1": w1, "w2": w2, "qpv": qpv,
                        "b2v": b2v} for c in cores])
        return [_asc(r.results[c]["yout"]) for c in cores]

    f32 = np.float32
    xnh = ln_pass(xs)
    xnf = xn_join(xnh)
    ncg = _prog("gla", build_gla_front_launch)
    Wq, Wk, Wv = (kw["L0_attn_a_W_q"].astype(f32),
                  kw["L0_attn_a_W_k"].astype(f32),
                  kw["L0_attn_a_W_v"].astype(f32))
    Wz, bgt = kw["L0_attn_a_W_gate"].astype(f32), kw["L0_attn_a_b_gate"].astype(f32)
    rg = _run8(ncg, [{
        "xnin": xnf[c], "wq": _asc(Wq[:, csl[c]]), "wk": _asc(Wk[:, csl[c]]),
        "wv": _asc(Wv[:, csl[c]]), "wz": _asc(Wz[:, csl[c]]),
        "bg": _asc(bgt[csl[c]].reshape(4, 128).T)} for c in cores])
    ofull = o_join([rg.results[c]["ofm"] for c in cores])
    xs, xnh = epi_pass(xs, ofull, _asc(kw["L0_attn_a_W_o"].astype(f32)))
    _dbg_snap(xs)
    xs = moe_pass(xs, xnh)
    xnh = ln_pass(xs)
    xnf = xn_join(xnh)
    _dbg_snap(xs)
    ncw = _prog("win", build_win_front_launch)
    wq, wk, wv = (kw["L0_attn_b_wq"].astype(f32), kw["L0_attn_b_wk"].astype(f32),
                  kw["L0_attn_b_wv"].astype(f32))
    gq, gk = (kw["L0_attn_b_q_norm_w"].astype(f32),
              kw["L0_attn_b_k_norm_w"].astype(f32))
    rr = 2.0 ** (-8.0 / 8)
    SCW = 1.0 / np.sqrt(128.0)
    insw = []
    for c in cores:
        hg = c % 2
        my, ot = csl[c], slice((1 - hg) * 512, (2 - hg) * 512)
        slpv = np.zeros((1, 16), np.float32)
        for hl in range(4):
            gh = hg * 4 + hl
            slpv[0, hl] = (rr ** gh) / SCW
            slpv[0, 8 + hl] = -(rr ** gh) / SCW
        insw.append({
            "xnin": xnf[c],
            "wq": _asc(np.concatenate([wq[:, my], wq[:, ot]], 1)),
            "wk": _asc(np.concatenate([wk[:, my], wk[:, ot]], 1)),
            "wv": _asc(wv[:, my]),
            "gq": _asc(gq[my].reshape(4, 128).T),
            "gk": _asc(gk[my].reshape(4, 128).T),
            "slp": slpv,
        })
    rw = _run8(ncw, insw)
    ofull = o_join([rw.results[c]["ofm"] for c in cores])
    xs, xnh = epi_pass(xs, ofull, _asc(kw["L0_attn_b_wo"].astype(f32)))
    _dbg_snap(xs)
    xs = moe_pass(xs, xnh)
    xnh = ln_pass(xs)
    xnf = xn_join(xnh)
    _dbg_snap(xs)
    ncd = _prog("delta", build_delta_front_launch)
    dWq, dWk, dWv, dWg = (kw["L1_attn_a_Wq"].astype(f32),
                          kw["L1_attn_a_Wk"].astype(f32),
                          kw["L1_attn_a_Wv"].astype(f32),
                          kw["L1_attn_a_Wg"].astype(f32))
    Wb, Wfd, Wsd = (kw["L1_attn_a_Wb"].astype(f32),
                    kw["L1_attn_a_Wfd"].astype(f32),
                    kw["L1_attn_a_Wsd"].astype(f32))
    bfd, bsd = kw["L1_attn_a_bfd"].astype(f32), kw["L1_attn_a_bsd"].astype(f32)
    rep4 = np.zeros((4, 128), np.float32)
    for p in range(4):
        rep4[p, p * 32:(p + 1) * 32] = 1.0
    insd = []
    for c in cores:
        hg = c % 2
        hs = slice(hg * 8, (hg + 1) * 8)
        wb = np.concatenate([Wb.T[:, hs], Wfd.T[:, hs], Wsd.T[:, hs]], 1)
        brow = np.concatenate([np.zeros(8, np.float32), bfd[hs],
                               bsd[hs]])[None, :]
        insd.append({
            "xnin": xnf[c],
            "wq": _asc(dWq.T[:, csl[c]]), "wk": _asc(dWk.T[:, csl[c]]),
            "wv": _asc(dWv.T[:, csl[c]]), "wg": _asc(dWg.T[:, csl[c]]),
            "wb": _asc(wb), "brow": _asc(brow), "rep4": rep4,
        })
    rd = _run8(ncd, insd)
    ofull = o_join([rd.results[c]["ofm"] for c in cores])
    xs, xnh = epi_pass(xs, ofull, _asc(kw["L1_attn_a_Wo"].astype(f32).T))
    _dbg_snap(xs)
    xs = l1ffn_pass(xs)
    xnh = ln_pass(xs)
    xnf = xn_join(xnh)
    _dbg_snap(xs)
    nca = _prog("attn", build_attn_front_launch)
    aWq, aWk, aWv = (kw["L1_attn_b_Wq"].astype(f32),
                     kw["L1_attn_b_Wk"].astype(f32),
                     kw["L1_attn_b_Wv"].astype(f32))
    sincos = stage_yarn_sincos()
    ra = _run8(nca, [{
        "xnin": xnf[c], "wq": _asc(aWq.T[:, csl[c]]),
        "wk": _asc(aWk.T[:, csl[c]]), "wv": _asc(aWv.T[:, csl[c]]),
        "sincos": sincos} for c in cores])
    aofm = [ra.results[c]["ofm"] for c in cores]
    ofull = []
    for c in cores:
        b = c // 2
        att = np.concatenate([aofm[b * 2], aofm[b * 2 + 1]], 0)
        att = att.reshape(16, 64, T).transpose(0, 2, 1)
        ysc = att.reshape(16, 256, 16, 64).reshape(T, D).T
        ofull.append(_asc(ysc[:, tsl[c]]))
    bpc = _asc(kw["L1_attn_b_bp"].astype(f32).reshape(8, 128).T)
    xs, _ = epi_pass(xs, ofull, _asc(kw["L1_attn_b_Wp"].astype(f32).T), bpc)
    _dbg_snap(xs)
    xs = l1ffn_pass(xs)
    out = np.zeros((NB, T, D), np.float32)
    for c in cores:
        out[c // 2, tsl[c], :] = xs[c].T
    return out



def emit_attn_front(ctx, xn_sb, q_fm, k_fm, v_tm, o_fm, sincos, wq_sb, wk_sb,
                    wv_sb, w_ready, ps, sc, nheads=8, Ttok=T, causal=True):
    nt = Ttok // 128
    NH2 = nheads // 2
    e = ctx.emit
    vdone = 0
    mx_init = False
    tr_rd = 0
    pq_rd = (0, 0, 0)
    for tt in range(nt):
        tv = {}
        for name, wsb, psn in (("q", wq_sb, "pq"), ("k", wk_sb, "pk"),
                               ("v", wv_sb, "pv")):
            for kc in range(NCH):
                tv[name] = e("tensor", lambda t, kc=kc, wsb=wsb, psn=psn, tt=tt: t.matmul(
                    ps[psn][:, :],
                    xn_sb[:, kc * Ttok + tt * 128: kc * Ttok + tt * 128 + 128],
                    wsb[:, kc * 512:(kc + 1) * 512],
                    start=(kc == 0), stop=(kc == NCH - 1)),
                    wait=(w_ready + (("vs", pq_rd[0]), ("vs", pq_rd[1]),
                                     ("vs", pq_rd[2]))) if kc == 0 else ())
        sins = sincos[:, tt * 64: tt * 64 + 32]
        coss = sincos[:, tt * 64 + 32: tt * 64 + 64]

        def b3(ap):
            return bass.AP(tensor=ap.tensor, offset=ap.offset,
                           ap=[list(ap.ap[0])] + [[0, NH2 * 2]] + [list(ap.ap[1])])

        sins_b, coss_b = b3(sins), b3(coss)
        rope_done = {}
        qtm_rd = getattr(ctx, "_qtm_rd", {"q": (0, 0), "k": (0, 0)})
        def _evod(ap2d, off):
            v = ap2d[:, off:off + 1]
            return bass.AP(tensor=v.tensor, offset=v.offset,
                           ap=[list(v.ap[0]), [64, 8], [2, 32]])

        for name, psn, dst in (("q", "pq", sc["qtm"]), ("k", "pk", sc["ktm"])):
            src = ps[psn][:, :]
            ev = _evod(src, 0)
            od = _evod(src, 1)
            dev = _evod(dst[:, :], 0)
            dod = _evod(dst[:, :], 1)
            t1 = sc["rsc"][:, 0:256].rearrange("p (h i) -> p h i", i=32)
            t2 = sc["rsc"][:, 256:512].rearrange("p (h i) -> p h i", i=32)
            e("vector", lambda v, od=od, t1=t1: v.tensor_tensor(
                out=t1, in0=od, in1=sins_b, op=OP.mult),
                wait=(("ts", max(tv[name], qtm_rd[name][0])),
                      ("ss", qtm_rd[name][1])))
            e("vector", lambda v, ev=ev, t2=t2: v.tensor_tensor(
                out=t2, in0=ev, in1=sins_b, op=OP.mult))
            e("vector", lambda v, ev=ev, t1=t1, dev=dev: v.tensor_tensor(
                out=t1, in0=ev, in1=coss_b, op=OP.mult) if False else
                v.tensor_tensor(out=dev, in0=ev, in1=coss_b, op=OP.mult))
            e("vector", lambda v, dev=dev, t1=t1: v.tensor_tensor(
                out=dev, in0=dev, in1=t1, op=OP.subtract))
            e("vector", lambda v, od=od, dod=dod: v.tensor_tensor(
                out=dod, in0=od, in1=coss_b, op=OP.mult))
            rope_done[name] = e("vector", lambda v, dod=dod, t2=t2: v.tensor_tensor(
                out=dod, in0=dod, in1=t2, op=OP.add))
        vdst = v_tm[:, tt * (nheads * 65): tt * (nheads * 65) + nheads * 65]
        vdst3 = vdst.rearrange("p (h c) -> p h c", c=65)[:, :, 0:64]
        vdone = e("vector", lambda v, vdst3=vdst3: v.tensor_copy(
            vdst3, ps["pv"][:, :].rearrange("p (h c) -> p h c", c=64)),
            wait=(("ts", tv["v"]),))
        nred = getattr(ctx, "_nred", 0)
        sqv = e("scalar", lambda s: s.activation(
            out=sc["sq"][:, :], in_=sc["qtm"][:, :], func=AF.Square),
            wait=(("vs", max(rope_done["q"], nred)),))
        qtm_rd["q"] = (qtm_rd["q"][0], sqv)
        nred = e("vector", lambda v: v.tensor_reduce(
            out=sc["n2"][:, 0:8], in_=sc["sq"][:, :].rearrange(
                "p (h c) -> p h c", c=64), axis=AX.X, op=OP.add),
            wait=(("ss", sqv),))
        sqv = e("scalar", lambda s: s.activation(
            out=sc["sq"][:, :], in_=sc["ktm"][:, :], func=AF.Square),
            wait=(("vs", max(rope_done["k"], nred)),))
        qtm_rd["k"] = (qtm_rd["k"][0], sqv)
        nred = e("vector", lambda v: v.tensor_reduce(
            out=sc["n2"][:, 8:16], in_=sc["sq"][:, :].rearrange(
                "p (h c) -> p h c", c=64), axis=AX.X, op=OP.add),
            wait=(("ss", sqv),))
        ctx._nred = nred
        if not mx_init:
            mxv = e("vector", lambda v: v.tensor_copy(sc["mx"][:, :], sc["n2"][:, :]))
            mx_init = True
        else:
            mxv = e("vector", lambda v: v.tensor_tensor(
                out=sc["mx"][:, :], in0=sc["mx"][:, :], in1=sc["n2"][:, :],
                op=OP.max))
        qkdone = {}
        for name, src in (("q", sc["qtm"]), ("k", sc["ktm"])):
            dst_fm = q_fm if name == "q" else k_fm
            tjl = 0
            for j in range(4):
                tj = e("tensor", lambda t, src=src, j=j: t.transpose(
                    ps["pt"][:, j * 128:(j + 1) * 128], src[:, j * 128:(j + 1) * 128],
                    ctx.ident_bf[:, :]),
                    wait=(("vs", rope_done[name]), ("vs", tr_rd)))
                tjl = tj
                tr_rd = e("vector", lambda v, dst_fm=dst_fm, j=j, tt=tt: v.tensor_copy(
                    dst_fm[:, j * Ttok + tt * 128: j * Ttok + tt * 128 + 128],
                    ps["pt"][:, j * 128:(j + 1) * 128]),
                    wait=(("ts", tj),))
            qtm_rd[name] = (tjl, 0)
        ctx._qtm_rd = qtm_rd
        pq_rd = (rope_done["k"], tr_rd, vdone)
    vsnow = ctx.cnt["vs"]
    tv = e("tensor", lambda t: t.transpose(
        ps["pb2"][0:16, 0:128], sc["mx"][:, :], ctx.ident_f32[:, :]),
        wait=(("vs", vsnow),))
    mv = e("vector", lambda v: v.tensor_reduce(
        out=sc["row8"][0:16, 0:1], in_=ps["pb2"][0:16, 0:128],
        axis=AX.X, op=OP.max), wait=(("ts", tv),))
    tv = e("tensor", lambda t: t.transpose(
        ps["pb2"][0:1, 0:16], sc["row8"][0:16, 0:1], ctx.ident_f32[0:16, 0:16]),
        wait=(("vs", mv),))
    e("vector", lambda v: v.tensor_reduce(
        out=sc["row8"][0:1, 2:3], in_=ps["pb2"][0:1, 0:8], axis=AX.X, op=OP.max),
        wait=(("ts", tv),))
    mv = e("vector", lambda v: v.tensor_reduce(
        out=sc["row8"][0:1, 3:4], in_=ps["pb2"][0:1, 8:16], axis=AX.X, op=OP.max))
    e("vector", lambda v: v.tensor_tensor(
        out=sc["row8"][0:1, 4:5], in0=sc["row8"][0:1, 2:3],
        in1=sc["row8"][0:1, 3:4], op=OP.mult))
    sv = e("scalar", lambda s2: s2.activation(
        out=sc["row8"][0:1, 5:6], in_=sc["row8"][0:1, 4:5], func=AF.Sqrt),
        wait=(("vs", ctx.cnt["vs"]),))
    mv = e("vector", lambda v: v.tensor_scalar(
        out=sc["row8"][0:1, 6:7], in0=sc["row8"][0:1, 5:6], scalar1=-1.0,
        scalar2=-8.0, op0=OP.mult, op1=OP.add), wait=(("ss", sv),))
    e("vector", lambda v: v.tensor_scalar(
        out=sc["mrow"][0:1, 0:512], in0=sc["sq"][0:1, 0:512],
        scalar1=0.0, scalar2=sc["row8"][0:1, 6:7], op0=OP.mult, op1=OP.add))
    mrow_done = e("vector", lambda v: v.tensor_copy(
        sc["mrowbf"][0:1, 0:512], sc["mrow"][0:1, 0:512]))
    NT2 = Ttok // 512
    join = (("vs", ctx.cnt["vs"]), ("ss", ctx.cnt["ss"]), ("ts", ctx.cnt["ts"]))
    qk_rd = [0, 0]
    po_rd = 0
    bc_rd = 0
    pr_rd = [0, 0]
    for h in range(nheads):
        hb = (h % 2) * 64
        hc = h // 2
        for g in range(NT2):
            nslots = 4 * g + 4
            pvt = 0
            for j in range(nslots):
                bi = j % 2
                qps = ps["pq"] if bi == 0 else ps["pk"]
                t1 = e("tensor", lambda t, hb=hb, hc=hc, j=j, g=g, qps=qps: t.matmul(
                    qps[:, :],
                    k_fm[hb:hb + 64, hc * Ttok + j * 128: hc * Ttok + j * 128 + 128],
                    q_fm[hb:hb + 64, hc * Ttok + g * 512: hc * Ttok + g * 512 + 512],
                    start=True, stop=False, tile_position=(hb, 0)),
                    wait=(join + (("vs", mrow_done),)) if (h == 0 and g == 0 and j == 0)
                    else (("ss", qk_rd[bi]), ("vs", mrow_done)))
                t1 = e("tensor", lambda t, qps=qps: t.matmul(
                    qps[:, :], sc["onesbf"][0:1, 0:128], sc["mrowbf"][0:1, 0:512],
                    start=False, stop=True))
                prob = sc["prob"][:, bi * 512:(bi + 1) * 512]
                sv = e("scalar", lambda s2, prob=prob, qps=qps: s2.activation(
                    out=prob, in_=qps[:, :], func=AF.Exp, scale=0.125),
                    wait=(("ts", max(t1, pr_rd[bi])),))
                qk_rd[bi] = sv
                pwait = ("ss", sv)
                if j >= 4 * g:
                    gv = e("gpsimd", lambda gq, prob=prob, j=j, g=g: gq.affine_select(
                        prob, prob, pattern=[[1, 512]], compare_op=OP.is_ge,
                        fill=0.0, base=g * 512 - j * 128, channel_multiplier=-1),
                        wait=(("ss", sv),))
                    pwait = ("gs", gv)
                pvt = e("tensor", lambda t, j=j, h=h, prob=prob, nslots=nslots: t.matmul(
                    ps["pv"][0:65, :],
                    v_tm[:, j * (nheads * 65) + h * 65: j * (nheads * 65) + h * 65 + 65],
                    prob, start=(j == 0), stop=(j == nslots - 1)),
                    wait=(pwait,) + ((("vs", po_rd),) if j == 0 else ()))
                pr_rd[bi] = pvt
            rv = e("vector", lambda v: v.reciprocal(
                out=sc["rrow"][0:1, 0:512],
                in_=ps["pv"][64:65, 0:512]), wait=(("ts", pvt),))
            bv = e("tensor", lambda t: t.matmul(
                ps["pb2"][0:64, 0:512], ctx.onesrow[0:1, 0:64], sc["rrow"][0:1, 0:512],
                start=True, stop=True), wait=(("vs", max(rv, bc_rd)),))
            ov = e("vector", lambda v: v.tensor_copy(
                sc["osc"][0:64, 0:512], ps["pv"][0:64, 0:512]))
            po_rd = ov
            bc_rd = e("vector", lambda v, h=h, g=g, hb=hb, hc=hc: v.tensor_tensor(
                out=o_fm[hb:hb + 64, hc * Ttok + g * 512: hc * Ttok + g * 512 + 512],
                in0=sc["osc"][0:64, 0:512], in1=ps["pb2"][0:64, 0:512], op=OP.mult),
                wait=(("ts", bv),))
    return bc_rd


def build_attn_front_launch():
    nc = bass.Bass("TRN2", target_bir_lowering=False, debug=False,
                   num_devices=NCORES)
    xnin = nc.dram_tensor("xnin", [D, T], BF16, kind="ExternalInput").ap()
    wq = nc.dram_tensor("wq", [D, 512], F32, kind="ExternalInput").ap()
    wk = nc.dram_tensor("wk", [D, 512], F32, kind="ExternalInput").ap()
    wv = nc.dram_tensor("wv", [D, 512], F32, kind="ExternalInput").ap()
    sincos_d = nc.dram_tensor("sincos", [128, 32 * 64], F32,
                              kind="ExternalInput").ap()
    ofm = nc.dram_tensor("ofm", [512, T], BF16, kind="ExternalOutput").ap()
    qfm_d = nc.dram_tensor("qfm_d", [512, T], BF16, kind="ExternalOutput").ap()
    kfm_d = nc.dram_tensor("kfm_d", [512, T], BF16, kind="ExternalOutput").ap()
    vtm_d = nc.dram_tensor("vtm_d", [128, 32 * 520], BF16, kind="ExternalOutput").ap()

    import contextlib
    es = contextlib.ExitStack()

    def sb(name, shape, dt):
        return es.enter_context(nc.sbuf_tensor(name, shape, dt))

    def pst(name, shape, dt=F32):
        return es.enter_context(nc.psum_tensor(name, shape, dt))

    with es:
        xn_sb = sb("xn_sb", [128, NCH * T], BF16)
        q_fm = sb("q_fm", [128, 4 * T], BF16)
        k_fm = sb("k_fm", [128, 4 * T], BF16)
        v_tm = sb("v_tm", [128, 32 * 520], BF16)
        sincos = sb("sincos_sb", [128, 2048], F32)
        wq_sb = sb("wq_sb", [128, 8 * 512], BF16)
        wk_sb = sb("wk_sb", [128, 8 * 512], BF16)
        wv_sb = sb("wv_sb", [128, 8 * 512], BF16)
        sc = {
            "qtm": sb("qtm", [128, 512], BF16),
            "ktm": sb("ktm", [128, 512], BF16),
            "sq": sb("sqs", [128, 512], F32),
            "rsc": sb("rsc", [128, 512], F32),
            "n2": sb("n2", [128, 16], F32),
            "mx": sb("mx", [128, 16], F32),
            "row8": sb("row8", [16, 16], F32),
            "mrowbf": sb("mrowbf", [1, 512], BF16),
            "onesbf": sb("onesbf", [1, 128], BF16),
            "prob": sb("prob", [128, 1024], BF16),
            "osc": sb("osc", [128, 512], F32),
        }
        sc["mrow"] = sc["sq"]
        sc["rrow"] = sc["rsc"]
        ident = sb("ident", [128, 128], BF16)
        identf = sb("identf", [128, 128], F32)
        onesb = sb("onesb", [128, 1], F32)
        onesrow = sb("onesrow", [1, 128], F32)
        cbank = sb("cbank", [128, 16], F32)
        ps = {
            "pq": pst("pq", [128, 512]),
            "pk": pst("pk", [128, 512]),
            "pv": pst("pv", [128, 512]),
            "pt": pst("pt", [128, 512], BF16),
            "pb2": pst("pb2", [128, 512]),
        }
        block = es.enter_context(nc.Block())
        ctx = Ctx(nc, block)
        ctx.cbank = cbank
        emit_prelude(ctx, ident, onesb, onesrow, identf)
        ctx.emit("gpsimd", lambda g: g.memset(sc["onesbf"][:, :], 1.0))
        v3 = v_tm[:, :].rearrange("p (t h c) -> p (t h) c", c=65, h=8)[:, :, 64:65]
        gv = ctx.emit("gpsimd", lambda g: g.memset(v3, 1.0))
        xv = ctx.emit("sync", lambda q: q.dma_start(
            out=xn_sb[:, :].rearrange("p (c t) -> p c t", c=NCH),
            in_=xnin.rearrange("(c p) t -> p c t", p=128)), dma=True)
        sv = ctx.emit("sync", lambda q: q.dma_start(out=sincos[:, :], in_=sincos_d),
                      dma=True)
        wvv = 0
        for w_d, w_s in ((wq, wq_sb), (wk, wk_sb), (wv, wv_sb)):
            wvv = ctx.emit("gpsimd", lambda g, w_d=w_d, w_s=w_s: g.dma_start(
                out=w_s[:, :].rearrange("p (c f) -> p c f", c=NCH),
                in_=w_d.rearrange("(c p) f -> p c f", p=128)), dma=True)
        w_ready = (("dS", xv), ("dS", sv), ("dG", wvv), ("gs", ctx.cnt["gs"]))
        o_fm = xn_sb[:, 0:4 * T]
        done = emit_attn_front(ctx, xn_sb, q_fm, k_fm, v_tm, o_fm, sincos,
                               wq_sb, wk_sb, wv_sb, w_ready, ps, sc)
        ctx.emit("sync", lambda q: q.dma_start(
            out=ofm.rearrange("(c p) t -> p c t", p=128),
            in_=o_fm.rearrange("p (c t) -> p c t", c=4)),
            wait=(("vs", done),), dma=True, store=True)
        ctx.emit("sync", lambda q: q.dma_start(
            out=qfm_d.rearrange("(c p) t -> p c t", p=128),
            in_=q_fm[:, :].rearrange("p (c t) -> p c t", c=4)),
            wait=(("vs", done), ("ss", ctx.cnt["ss"])), dma=True, store=True)
        ctx.emit("sync", lambda q: q.dma_start(
            out=kfm_d.rearrange("(c p) t -> p c t", p=128),
            in_=k_fm[:, :].rearrange("p (c t) -> p c t", c=4)), dma=True, store=True)
        ctx.emit("sync", lambda q: q.dma_start(
            out=vtm_d, in_=v_tm[:, :]), dma=True, store=True)
        ctx.wait("sync", ("st", ctx.cnt["st"]))
    return nc


def stage_yarn_sincos():
    i = np.arange(0, 64, 2, dtype=np.float32)
    inv_freq = (1.0 / (10000.0 ** (i / (64.0 * 4.0)))).astype(np.float32)
    pos = np.arange(T, dtype=np.float32)
    ys = (1.0 / np.sqrt(1.0 + (pos / 1024.0) ** 2)).astype(np.float32)
    theta = (pos[:, None] * inv_freq[None, :] * ys[:, None]).astype(np.float32)
    sin = np.sin(theta).astype(np.float32)
    cos = np.cos(theta).astype(np.float32)
    out = np.zeros((128, 32 * 64), np.float32)
    for tt in range(32):
        out[:, tt * 64: tt * 64 + 32] = sin[tt * 128:(tt + 1) * 128]
        out[:, tt * 64 + 32: tt * 64 + 64] = cos[tt * 128:(tt + 1) * 128]
    return out




def emit_gla_front(ctx, xn_sb, wq_sb, wk_sb, wv_sb, wz_sb, bgate, o_fm, S_sb,
                   ps, sc, w_ready, Ttok=T):
    e = ctx.emit
    NG = Ttok // 512
    e("gpsimd", lambda g: g.memset(S_sb[:, :], 0.0))
    gready = ctx.cnt["gs"]
    ev_rd = {"pq": ("vs", 0), "pk": ("vs", 0), "pz": ("ss", 0),
             "pv": ("vs", 0)}
    buf_rd = 0
    s_upd = {}
    o_ev = 0
    pr_ev = 0
    ktm_ev = 0
    pds_rd = 0
    prb_rd = 0
    ktld_rd = 0
    khat_rd = 0
    for g in range(NG):
        evd = {}
        for name, wsb, psn, tm in (("q", wq_sb, "pq", False), ("k", wk_sb, "pk", False),
                                   ("z", wz_sb, "pz", False), ("v", wv_sb, "pv", True)):
            for dc in range(4):
                tv = 0
                for kc in range(NCH):
                    if not tm:
                        tv = e("tensor", lambda t, kc=kc, dc=dc, wsb=wsb, psn=psn, g=g: t.matmul(
                            ps[psn][:, :],
                            wsb[:, kc * 512 + dc * 128: kc * 512 + dc * 128 + 128],
                            xn_sb[:, kc * Ttok + g * 512: kc * Ttok + g * 512 + 512],
                            start=(kc == 0), stop=(kc == NCH - 1)),
                            wait=(w_ready + (ev_rd[psn], ("gs", gready),
                                             ("ts", buf_rd)))
                            if kc == 0 else ())
                    else:
                        tv = e("tensor", lambda t, kc=kc, dc=dc, wsb=wsb, psn=psn, g=g: t.matmul(
                            ps[psn][:, :],
                            xn_sb[:, kc * Ttok + g * 512 + dc * 128:
                                  kc * Ttok + g * 512 + dc * 128 + 128],
                            wsb[:, kc * 512:(kc + 1) * 512],
                            start=(kc == 0), stop=(kc == NCH - 1)),
                            wait=(ev_rd[psn], ("ts", buf_rd)) if kc == 0 else ())
                if name == "q":
                    ev_rd[psn] = ("vs", e("vector", lambda v, dc=dc: v.tensor_copy(
                        sc["qg"][:, dc * 512:(dc + 1) * 512], ps["pq"][:, :]),
                        wait=(("ts", tv),)))
                elif name == "k":
                    ev_rd[psn] = ("vs", e("vector", lambda v, dc=dc: v.tensor_copy(
                        sc["kg"][:, dc * 512:(dc + 1) * 512], ps["pk"][:, :]),
                        wait=(("ts", tv),)))
                elif name == "v":
                    ev_rd[psn] = ("vs", e("vector", lambda v, dc=dc: v.tensor_copy(
                        sc["vg"][:, dc * 512:(dc + 1) * 512], ps["pv"][:, :]),
                        wait=(("ts", tv),)))
                else:
                    sv = e("scalar", lambda s2, dc=dc: s2.activation(
                        out=sc["asb"][:, :], in_=ps["pz"][:, :], func=AF.Sigmoid,
                        bias=bgate[:, dc:dc + 1], scale=1.0),
                        wait=(("ts", tv), ("vs", ctx.cnt["vs"] if g == 0 else 0)))
                    ev_rd[psn] = ("ss", e("scalar", lambda s2, dc=dc: s2.activation(
                        out=sc["lag"][:, dc * 512:(dc + 1) * 512], in_=sc["asb"][:, :],
                        func=AF.Ln), wait=(("ss", sv),)))
        proj_ev = tuple(ev_rd.values())
        for tt in range(4):
            for dc in range(4):
                la = sc["lag"][:, dc * 512 + tt * 128: dc * 512 + tt * 128 + 128]
                scv = e("vector", lambda v, la=la: v.tensor_tensor_scan(
                    out=sc["Bt"][:, :], data0=la, data1=la, initial=0.0,
                    op0=OP.add, op1=OP.bypass),
                    wait=proj_ev + (("ss", ctx.cnt["ss"]),))
                dxv = e("scalar", lambda s2: s2.activation(
                    out=sc["Dex"][:, :], in_=sc["Bt"][:, :], func=AF.Exp),
                    wait=(("vs", scv),))
                dvv = e("scalar", lambda s2: s2.activation(
                    out=sc["Div"][:, :], in_=sc["Bt"][:, :], func=AF.Exp, scale=-1.0))
                qv = e("vector", lambda v, dc=dc, tt=tt: v.tensor_tensor(
                    out=sc["qt32"][:, :],
                    in0=sc["qg"][:, dc * 512 + tt * 128: dc * 512 + tt * 128 + 128],
                    in1=sc["Dex"][:, :], op=OP.mult),
                    wait=(("ss", dvv), ("ts", khat_rd)))
                qbv = e("vector", lambda v: v.tensor_copy(sc["qtb"][:, :], sc["qt32"][:, :]))
                khv = e("vector", lambda v, dc=dc, tt=tt: v.tensor_tensor(
                    out=sc["khat"][:, :],
                    in0=sc["kg"][:, dc * 512 + tt * 128: dc * 512 + tt * 128 + 128],
                    in1=sc["Div"][:, :], op=OP.mult))
                ktv = e("vector", lambda v: v.tensor_scalar(
                    out=sc["ktld"][:, :], in0=sc["khat"][:, :],
                    scalar1=sc["Dex"][:, 127:128], scalar2=None, op0=OP.mult),
                    wait=(("ts", ktld_rd),))
                trv = e("tensor", lambda t: t.transpose(
                    ps["ptr"][:, :], sc["ktld"][:, :], ctx.ident_bf[:, :]),
                    wait=(("vs", ktv), ("vs", ktm_ev)))
                ktld_rd = trv
                ktm_ev = e("vector", lambda v: v.tensor_copy(
                    sc["ktm"][:, :], ps["ptr"][:, :]), wait=(("ts", trv),))
                for h2 in range(2):
                    hb = h2 * 64
                    key = (dc, h2)
                    scv2 = e("tensor", lambda t, hb=hb: t.matmul(
                        ps["pscore"][:, :], sc["khat"][hb:hb + 64, :],
                        sc["qtb"][hb:hb + 64, :], start=True, stop=True,
                        tile_position=(hb, 0)),
                        wait=(("vs", max(qbv, khv)), ("vs", pr_ev)))
                    khat_rd = scv2
                    prv = e("vector", lambda v: v.tensor_copy(
                        sc["prb"][:, :], ps["pscore"][:, :]),
                        wait=(("ts", max(scv2, prb_rd)),))
                    pr_ev = prv
                    mkv = e("gpsimd", lambda gq: gq.affine_select(
                        sc["prb"][:, :], sc["prb"][:, :], pattern=[[1, 128]],
                        compare_op=OP.is_ge, fill=0.0, base=0,
                        channel_multiplier=-1), wait=(("vs", prv),))
                    ohv = e("tensor", lambda t, hb=hb, dc=dc: t.matmul(
                        ps["po"][0:64, :], S_sb[hb:hb + 64, dc * 64:(dc + 1) * 64],
                        sc["qt32"][hb:hb + 64, :], start=True, stop=False,
                        tile_position=(hb, 0)),
                        wait=(("vs", s_upd.get(key, 0)), ("vs", o_ev)))
                    oiv = e("tensor", lambda t, hb=hb, dc=dc, tt=tt, g=g: t.matmul(
                        ps["po"][0:64, :],
                        sc["vg"][:, tt * 512 + dc * 128 + hb: tt * 512 + dc * 128 + hb + 64],
                        sc["prb"][:, :], start=False, stop=True),
                        wait=(("gs", mkv),))
                    prb_rd = oiv
                    dsv = e("tensor", lambda t, hb=hb, dc=dc, tt=tt: t.matmul(
                        ps["pds"][0:64, 0:64], sc["ktm"][:, hb:hb + 64],
                        sc["vg"][:, tt * 512 + dc * 128 + hb: tt * 512 + dc * 128 + hb + 64],
                        start=True, stop=True),
                        wait=(("vs", ktm_ev), ("vs", s_upd.get(key, 0)),
                              ("vs", pds_rd)))
                    s_upd[key] = e("vector", lambda v, hb=hb, dc=dc: v.scalar_tensor_tensor(
                        out=S_sb[hb:hb + 64, dc * 64:(dc + 1) * 64],
                        in0=S_sb[hb:hb + 64, dc * 64:(dc + 1) * 64],
                        scalar=sc["Dex"][hb:hb + 64, 127:128],
                        in1=ps["pds"][0:64, 0:64], op0=OP.mult, op1=OP.add),
                        wait=(("ts", max(dsv, ohv)),))
                    pds_rd = s_upd[key]
                    o_ev = e("vector", lambda v, hb=hb, dc=dc, g=g, tt=tt: v.tensor_copy(
                        o_fm[hb:hb + 64, dc * Ttok + g * 512 + tt * 128:
                             dc * Ttok + g * 512 + tt * 128 + 128],
                        ps["po"][0:64, :]), wait=(("ts", oiv),))
            buf_rd = ctx.cnt["ts"]
    return o_ev




def emit_win_front(ctx, xn_dram, xs_sb, q_fm, k_fm, v_tm, o_fm, wq_sb, wk_sb,
                   wv_sb, gq_row, gk_row, slopes8, ps, sc, w_ready,
                   Ttok=T):
    e = ctx.emit
    nt = Ttok // 128
    NH = 4
    SC = 1.0 / math.sqrt(128.0)
    mx_init = False
    tr_rd = 0
    war = {"pq": 0, "pk": 0, "pv": 0, "pqs": 0, "pks": 0, "sq_rd": 0,
           "qtm": (0, 0), "ktm": (0, 0)}
    wfold = tuple(w_ready)

    xld = [0, 0]
    xs_user = [0, 0]

    def xload(tt, bi):
        return e("sync", lambda q, tt=tt, bi=bi: q.dma_start(
            out=xs_sb[:, bi * 1024:(bi + 1) * 1024]
            .rearrange("p (c t) -> p c t", c=NCH),
            in_=xn_dram[:, tt * 128:(tt + 1) * 128]
            .rearrange("(c p) t -> p c t", p=128)),
            wait=(("ts", xs_user[bi]),), dma=True)
    xld[0] = xload(0, 0)
    for tt in range(nt):
        bi = tt % 2
        if tt + 1 < nt:
            xld[1 - bi] = xload(tt + 1, 1 - bi)
        xs = xs_sb[:, bi * 1024:(bi + 1) * 1024]
        tv = {}
        sq_rd = war["sq_rd"]
        for name, wsb, psn, col in (("q", wq_sb, "pq", 0), ("k", wk_sb, "pk", 1)):
            for hf in (1, 0):
                tvv = 0
                for kc in range(NCH):
                    tvv = e("tensor", lambda t, kc=kc, hf=hf, wsb=wsb, psn=psn, xs=xs: t.matmul(
                        ps[psn][:, 0:512],
                        xs[:, kc * 128:(kc + 1) * 128],
                        wsb[:, kc * 1024 + hf * 512: kc * 1024 + hf * 512 + 512],
                        start=(kc == 0), stop=(kc == NCH - 1)),
                        wait=(wfold + (("dS", xld[bi]), ("vs", war[psn]),
                                       ("ss", war[psn + "s"]))) if kc == 0 else ())
                tvv = ctx.tnop()
                tv[(name, hf)] = tvv
                sv = e("scalar", lambda s2, psn=psn: s2.activation(
                    out=sc["sq"][:, :], in_=ps[psn][:, 0:512], func=AF.Square),
                    wait=(("ts", tvv), ("vs", sq_rd)))
                war[psn + "s"] = sv
                sq_rd = e("vector", lambda v, col=col, hf=hf: v.tensor_reduce(
                    out=sc["n2"][:, 2 * col + hf: 2 * col + hf + 1],
                    in_=sc["sq"][:, :], axis=AX.X, op=OP.add),
                    wait=(("ss", sv),))
            vv = e("vector", lambda v, col=col: v.tensor_tensor(
                out=sc["rstd"][:, col:col + 1], in0=sc["n2"][:, 2 * col:2 * col + 1],
                in1=sc["n2"][:, 2 * col + 1:2 * col + 2], op=OP.add))
            e("scalar", lambda s2, col=col: s2.activation(
                out=sc["rstd"][:, col:col + 1], in_=sc["rstd"][:, col:col + 1],
                func=AF.Ln, bias=ctx.constcol(1e-6), scale=1.0 / D),
                wait=(("vs", vv),))
            e("scalar", lambda s2, col=col: s2.activation(
                out=sc["rstd"][:, col:col + 1], in_=sc["rstd"][:, col:col + 1],
                func=AF.Exp, scale=-0.5))
            rsv = ctx.cnt["ss"]
            dst = sc["qtm"] if name == "q" else sc["ktm"]
            wk = war["qtm"] if name == "q" else war["ktm"]
            nv = e("vector", lambda v, psn=psn, col=col, dst=dst: v.tensor_scalar(
                out=dst[:, :], in0=ps[psn][:, 0:512],
                scalar1=sc["rstd"][:, col:col + 1], scalar2=None, op0=OP.mult),
                wait=(("ss", rsv), ("ts", wk[0]), ("ss", wk[1])))
            war[psn] = nv
            if "rstd_dbg" in sc:
                e("vector", lambda v, col=col, tt=tt: v.tensor_copy(
                    sc["rstd_dbg"][:, tt * 2 + col: tt * 2 + col + 1],
                    sc["rstd"][:, col:col + 1]), wait=(("ss", rsv),))
        war["sq_rd"] = sq_rd
        qv = war["pq"]
        kv = war["pk"]
        tvv = 0
        for kc in range(NCH):
            tvv = e("tensor", lambda t, kc=kc, xs=xs: t.matmul(
                ps["pv"][:, 0:512],
                xs[:, kc * 128:(kc + 1) * 128],
                wv_sb[:, kc * 512:(kc + 1) * 512],
                start=(kc == 0), stop=(kc == NCH - 1)),
                wait=(wfold + (("vs", war["pv"]),)) if kc == 0 else ())
        xs_user[bi] = tvv
        vdst = v_tm[:, tt * (NH * 129): tt * (NH * 129) + NH * 129]
        vdst3 = vdst.rearrange("p (h c) -> p h c", c=129)[:, :, 0:128]
        war["pv"] = e("vector", lambda v, vdst3=vdst3: v.tensor_copy(
            vdst3, ps["pv"][:, 0:512].rearrange("p (h c) -> p h c", c=128)),
            wait=(("ts", tvv),))
        if getattr(ctx, "_win_ablate", 0) & 1:
            continue
        sv = e("scalar", lambda s2: s2.activation(
            out=sc["sq"][:, :], in_=sc["qtm"][:, :], func=AF.Square),
            wait=(("vs", max(qv, war["sq_rd"])),))
        n1 = e("vector", lambda v: v.tensor_reduce(
            out=sc["nb"][:, 0:4], in_=sc["sq"][:, :].rearrange(
                "p (h c) -> p h c", c=128), axis=AX.X, op=OP.add),
            wait=(("ss", sv),))
        sv = e("scalar", lambda s2: s2.activation(
            out=sc["sq"][:, :], in_=sc["ktm"][:, :], func=AF.Square),
            wait=(("vs", max(kv, n1)),))
        n2v = e("vector", lambda v: v.tensor_reduce(
            out=sc["nb"][:, 4:8], in_=sc["sq"][:, :].rearrange(
                "p (h c) -> p h c", c=128), axis=AX.X, op=OP.add),
            wait=(("ss", sv),))
        war["sq_rd"] = n2v
        if not mx_init:
            mxv = e("vector", lambda v: v.tensor_copy(sc["mx"][:, :], sc["nb"][:, 0:8]))
            mx_init = True
        else:
            mxv = e("vector", lambda v: v.tensor_tensor(
                out=sc["mx"][:, :], in0=sc["mx"][:, :], in1=sc["nb"][:, 0:8],
                op=OP.max))
        for name, src, dst_fm, gcol in (("q", sc["qtm"], q_fm, gq_row),
                                        ("k", sc["ktm"], k_fm, gk_row)):
            tjl = 0
            for j in range(4):
                tj = e("tensor", lambda t, src=src, j=j: t.transpose(
                    ps["pt"][:, j * 128:(j + 1) * 128], src[:, j * 128:(j + 1) * 128],
                    ctx.ident_bf[:, :]),
                    wait=(("vs", qv if name == "q" else kv), ("ss", tr_rd)))
                tjl = tj
                tr_rd = e("scalar", lambda s2, dst_fm=dst_fm, j=j, tt=tt, gcol=gcol: s2.activation(
                    out=dst_fm[:, j * Ttok + tt * 128: j * Ttok + tt * 128 + 128],
                    in_=ps["pt"][:, j * 128:(j + 1) * 128], func=AF.Copy,
                    scale=gcol[:, j:j + 1]),
                    wait=(("ts", tj),))
            if name == "q":
                war["qtm"] = (tjl, ctx.cnt["ss"])
            else:
                war["ktm"] = (tjl, ctx.cnt["ss"])
    vsnow = ctx.cnt["vs"]
    tvx = e("tensor", lambda t: t.transpose(
        ps["pb2"][0:8, 0:128], sc["mx"][:, :], ctx.ident_f32[:, :]),
        wait=(("vs", vsnow),))
    mv = e("vector", lambda v: v.tensor_reduce(
        out=sc["row8"][0:8, 0:1], in_=ps["pb2"][0:8, 0:128],
        axis=AX.X, op=OP.max), wait=(("ts", tvx),))
    tvx = e("tensor", lambda t: t.transpose(
        ps["pb2"][0:1, 0:8], sc["row8"][0:8, 0:1], ctx.ident_f32[0:8, 0:8]),
        wait=(("vs", mv),))
    e("vector", lambda v: v.tensor_reduce(
        out=sc["row8"][0:1, 2:3], in_=ps["pb2"][0:1, 0:4], axis=AX.X, op=OP.max),
        wait=(("ts", tvx),))
    e("vector", lambda v: v.tensor_reduce(
        out=sc["row8"][0:1, 3:4], in_=ps["pb2"][0:1, 4:8], axis=AX.X, op=OP.max))
    e("vector", lambda v: v.tensor_tensor(
        out=sc["row8"][0:1, 4:5], in0=sc["row8"][0:1, 2:3],
        in1=sc["row8"][0:1, 3:4], op=OP.mult))
    sv = e("scalar", lambda s2: s2.activation(
        out=sc["row8"][0:1, 5:6], in_=sc["row8"][0:1, 4:5], func=AF.Sqrt),
        wait=(("vs", ctx.cnt["vs"]),))
    mbv = e("vector", lambda v: v.tensor_scalar(
        out=sc["row8"][0:1, 6:7], in0=sc["row8"][0:1, 5:6], scalar1=-SC,
        scalar2=-1.0, op0=OP.mult, op1=OP.add), wait=(("ss", sv),))
    bcv = e("tensor", lambda t: t.matmul(
        ps["pb2"][0:128, 0:1], ctx.onesrow[0:1, 0:128], sc["row8"][0:1, 6:7],
        start=True, stop=True), wait=(("vs", mbv),))
    bco = e("vector", lambda v: v.tensor_copy(sc["bcol"][:, 0:1],
                                              ps["pb2"][0:128, 0:1]),
            wait=(("ts", bcv),))
    mvv = bco
    NT2 = Ttok // 512
    join = (("vs", ctx.cnt["vs"]), ("ss", ctx.cnt["ss"]), ("ts", ctx.cnt["ts"]))
    qk_rd = [0, 0]
    po_rd = 0
    bc_rd = 0
    pr_rd = [0, 0]
    row_rd = 0
    sum_rd = 0
    for h in range(NH):
        sr = e("vector", lambda v, h=h: v.tensor_scalar(
            out=sc["srow"][0:1, 0:512], in0=sc["posT"][0:1, 0:512],
            scalar1=0.0, scalar2=slopes8[0:1, h:h + 1], op0=OP.mult, op1=OP.add),
            wait=(("vs", mvv), ("ts", row_rd)))
        for g in range(NT2):
            trv = e("vector", lambda v, h=h, g=g: v.tensor_scalar(
                out=sc["trow"][0:1, 0:512], in0=sc["posT"][0:1, g * 512:(g + 1) * 512],
                scalar1=slopes8[0:1, 8 + h:9 + h], scalar2=0.0,
                op0=OP.mult, op1=OP.add),
                wait=(("ts", row_rd),))
            slots = [ss_ for ss_ in range(g * 4 - 1, g * 4 + 4) if ss_ >= 0]
            pvt = 0
            sumt = 0
            for j, slot in enumerate(slots):
                bi = j % 2
                qps = ps["pq"] if bi == 0 else ps["pk"]
                e("tensor", lambda t, h=h, slot=slot, g=g, qps=qps: t.matmul(
                    qps[:, 0:512],
                    k_fm[:, h * Ttok + slot * 128: h * Ttok + slot * 128 + 128],
                    q_fm[:, h * Ttok + g * 512: h * Ttok + g * 512 + 512],
                    start=True, stop=False),
                    wait=(join + (("vs", trv),)) if (h == 0 and g == 0 and j == 0)
                    else (("ss", qk_rd[bi]), ("vs", trv)))
                e("tensor", lambda t, slot=slot, qps=qps: t.matmul(
                    qps[:, 0:512], sc["posT"][0:1, slot * 128: slot * 128 + 128],
                    sc["srow"][0:1, 0:512], start=False, stop=False))
                t1 = e("tensor", lambda t, qps=qps: t.matmul(
                    qps[:, 0:512], ctx.onesrow[0:1, 0:128], sc["trow"][0:1, 0:512],
                    start=False, stop=True))
                row_rd = t1
                prob = sc["prob"][:, bi * 512:(bi + 1) * 512]
                svp = e("scalar", lambda s2, prob=prob, qps=qps: s2.activation(
                    out=prob, in_=qps[:, 0:512], func=AF.Exp, scale=SC,
                    bias=sc["bcol"][:, 0:1]),
                    wait=(("ts", max(t1, pr_rd[bi])), ("vs", bco)))
                qk_rd[bi] = svp
                gv = e("gpsimd", lambda gq, prob=prob, slot=slot, g=g: gq.affine_select(
                    prob, prob, pattern=[[1, 512]], compare_op=OP.is_ge,
                    fill=0.0, base=g * 512 - slot * 128, channel_multiplier=-1),
                    wait=(("ss", svp),))
                gv = e("gpsimd", lambda gq, prob=prob, slot=slot, g=g: gq.affine_select(
                    prob, prob, pattern=[[-1, 512]], compare_op=OP.is_ge,
                    fill=0.0, base=16 - (g * 512 - slot * 128),
                    channel_multiplier=1))
                pvt = e("tensor", lambda t, slot=slot, h=h, prob=prob, j=j, n=len(slots): t.matmul(
                    ps["po"][0:128, :],
                    v_tm[:, slot * (NH * 129) + h * 129: slot * (NH * 129) + h * 129 + 128],
                    prob, start=(j == 0), stop=(j == n - 1)),
                    wait=(("gs", gv),) + ((("vs", po_rd),) if j == 0 else ()))
                sumt = e("tensor", lambda t, slot=slot, h=h, prob=prob, j=j, n=len(slots): t.matmul(
                    ps["pr"][0:1, :],
                    v_tm[:, slot * (NH * 129) + h * 129 + 128: slot * (NH * 129) + h * 129 + 129],
                    prob, start=(j == 0), stop=(j == n - 1)),
                    wait=(("vs", sum_rd),) if j == 0 else ())
                pr_rd[bi] = sumt
            rcv = e("vector", lambda v: v.reciprocal(
                out=sc["rrow"][0:1, 0:512], in_=ps["pr"][0:1, 0:512]),
                wait=(("ts", sumt),))
            sum_rd = rcv
            bv = e("tensor", lambda t: t.matmul(
                ps["pb2"][0:128, 0:512], ctx.onesrow[0:1, 0:128],
                sc["rrow"][0:1, 0:512],
                start=True, stop=True), wait=(("vs", max(rcv, bc_rd)),))
            ov = e("vector", lambda v: v.tensor_copy(
                sc["osc"][:, 0:512], ps["po"][0:128, 0:512]),
                wait=(("ts", pvt),))
            po_rd = ov
            bc_rd = e("vector", lambda v, h=h, g=g: v.tensor_tensor(
                out=o_fm[:, h * Ttok + g * 512: h * Ttok + g * 512 + 512],
                in0=sc["osc"][:, 0:512], in1=ps["pb2"][0:128, 0:512], op=OP.mult),
                wait=(("ts", bv),))
    return bc_rd


def build_gla_front_launch():
    nc = bass.Bass("TRN2", target_bir_lowering=False, debug=False,
                   num_devices=NCORES)
    xnin = nc.dram_tensor("xnin", [D, T], BF16, kind="ExternalInput").ap()
    wq = nc.dram_tensor("wq", [D, 512], F32, kind="ExternalInput").ap()
    wk = nc.dram_tensor("wk", [D, 512], F32, kind="ExternalInput").ap()
    wv = nc.dram_tensor("wv", [D, 512], F32, kind="ExternalInput").ap()
    wz = nc.dram_tensor("wz", [D, 512], F32, kind="ExternalInput").ap()
    bg = nc.dram_tensor("bg", [128, 4], F32, kind="ExternalInput").ap()
    ofm = nc.dram_tensor("ofm", [512, T], BF16, kind="ExternalOutput").ap()

    import contextlib
    es = contextlib.ExitStack()

    def sb(name, shape, dt):
        return es.enter_context(nc.sbuf_tensor(name, shape, dt))

    def pst(name, shape, dt=F32):
        return es.enter_context(nc.psum_tensor(name, shape, dt))

    with es:
        xn_sb = sb("xn_sb", [128, NCH * T], BF16)
        o_fm = sb("o_fm", [128, 4 * T], BF16)
        wq_sb = sb("wq_sb", [128, 8 * 512], BF16)
        wk_sb = sb("wk_sb", [128, 8 * 512], BF16)
        wv_sb = sb("wv_sb", [128, 8 * 512], BF16)
        wz_sb = sb("wz_sb", [128, 8 * 512], BF16)
        bgate = sb("bgate", [128, 4], F32)
        S_sb = sb("S_sb", [128, 256], F32)
        sc = {
            "qg": sb("qg", [128, 4 * 512], BF16),
            "kg": sb("kg", [128, 4 * 512], BF16),
            "vg": sb("vg", [128, 4 * 512], BF16),
            "lag": sb("lag", [128, 4 * 512], F32),
            "Bt": sb("Bt", [128, 128], F32),
            "Dex": sb("Dex", [128, 128], F32),
            "Div": sb("Div", [128, 128], F32),
            "qt32": sb("qt32", [128, 128], F32),
            "qtb": sb("qtb", [128, 128], BF16),
            "khat": sb("khat", [128, 128], BF16),
            "ktld": sb("ktld", [128, 128], BF16),
            "ktm": sb("ktm", [128, 128], BF16),
            "prb": sb("prb", [128, 128], BF16),
            "asb": sb("asb", [128, 512], F32),
        }
        ident = sb("ident", [128, 128], BF16)
        onesb = sb("onesb", [128, 1], F32)
        onesrow = sb("onesrow", [1, 128], F32)
        cbank = sb("cbank", [128, 16], F32)
        ps = {
            "pq": pst("pq", [128, 512]),
            "pk": pst("pk", [128, 512]),
            "pv": pst("pv", [128, 512]),
            "pz": pst("pz", [128, 512]),
            "pscore": pst("pscore", [128, 128]),
            "po": pst("po", [64, 128]),
            "pds": pst("pds", [64, 64]),
            "ptr": pst("ptr", [128, 128], BF16),
        }
        block = es.enter_context(nc.Block())
        ctx = Ctx(nc, block)
        ctx.cbank = cbank
        emit_prelude(ctx, ident, onesb, onesrow)
        xv = ctx.emit("sync", lambda q: q.dma_start(
            out=xn_sb[:, :].rearrange("p (c t) -> p c t", c=NCH),
            in_=xnin.rearrange("(c p) t -> p c t", p=128)), dma=True)
        bv = ctx.emit("sync", lambda q: q.dma_start(out=bgate[:, :], in_=bg),
                      dma=True)
        wvv = 0
        for w_d, w_s in ((wq, wq_sb), (wk, wk_sb), (wv, wv_sb), (wz, wz_sb)):
            wvv = ctx.emit("gpsimd", lambda g, w_d=w_d, w_s=w_s: g.dma_start(
                out=w_s[:, :].rearrange("p (c f) -> p c f", c=NCH),
                in_=w_d.rearrange("(c p) f -> p c f", p=128)), dma=True)
        w_ready = (("dS", xv), ("dS", bv), ("dG", wvv))
        done = emit_gla_front(ctx, xn_sb, wq_sb, wk_sb, wv_sb, wz_sb, bgate,
                              o_fm, S_sb, ps, sc, w_ready)
        ctx.emit("sync", lambda q: q.dma_start(
            out=ofm.rearrange("(c p) t -> p c t", p=128),
            in_=o_fm[:, :].rearrange("p (c t) -> p c t", c=4)),
            wait=(("ss", done),), dma=True, store=True)
        ctx.wait("sync", ("st", ctx.cnt["st"]))
    return nc



def build_win_front_launch():
    nc = bass.Bass("TRN2", target_bir_lowering=False, debug=False,
                   num_devices=NCORES)
    xnin = nc.dram_tensor("xnin", [D, T], BF16, kind="ExternalInput").ap()
    wq = nc.dram_tensor("wq", [D, 1024], F32, kind="ExternalInput").ap()
    wk = nc.dram_tensor("wk", [D, 1024], F32, kind="ExternalInput").ap()
    wv = nc.dram_tensor("wv", [D, 512], F32, kind="ExternalInput").ap()
    gq = nc.dram_tensor("gq", [128, 4], F32, kind="ExternalInput").ap()
    gk = nc.dram_tensor("gk", [128, 4], F32, kind="ExternalInput").ap()
    slp = nc.dram_tensor("slp", [1, 16], F32, kind="ExternalInput").ap()
    ofm = nc.dram_tensor("ofm", [512, T], BF16, kind="ExternalOutput").ap()
    qfm_d = nc.dram_tensor("qfm_d", [512, T], BF16, kind="ExternalOutput").ap()
    kfm_d = nc.dram_tensor("kfm_d", [512, T], BF16, kind="ExternalOutput").ap()
    vtm_d = nc.dram_tensor("vtm_d", [128, 32 * 516], BF16, kind="ExternalOutput").ap()
    rstd_d = nc.dram_tensor("rstd_d", [128, 64], F32, kind="ExternalOutput").ap()

    import contextlib
    es = contextlib.ExitStack()

    def sb(name, shape, dt):
        return es.enter_context(nc.sbuf_tensor(name, shape, dt))

    def pst(name, shape, dt=F32):
        return es.enter_context(nc.psum_tensor(name, shape, dt))

    with es:
        xs_sb = sb("xs_sb", [128, 2 * 1024], BF16)
        q_fm = sb("q_fm", [128, 4 * T], BF16)
        k_fm = sb("k_fm", [128, 4 * T], BF16)
        v_tm = sb("v_tm", [128, 32 * 516], BF16)
        big2 = sb("big2", [128, 4 * T], BF16)
        o_fm = big2[:, 0:4 * T]
        wq_sb = big2[:, 0:8 * 1024]
        wk_sb = big2[:, 8 * 1024:16 * 1024]
        wv_sb = sb("wv_sb", [128, 8 * 512], BF16)
        gqr = sb("gqr", [128, 4], F32)
        gkr = sb("gkr", [128, 4], F32)
        slps = sb("slps", [1, 16], F32)
        sc = {
            "qtm": sb("qtm", [128, 512], BF16),
            "ktm": sb("ktm", [128, 512], BF16),
            "sq": sb("sqs", [128, 512], F32),
            "n2": sb("n2", [128, 4], F32),
            "nb": sb("nb", [128, 8], F32),
            "mx": sb("mx", [128, 8], F32),
            "row8": sb("row8", [16, 16], F32),
            "rstd": sb("rstd", [128, 2], F32),
            "prob": sb("prob", [128, 1024], BF16),
            "osc": sb("osc", [128, 512], F32),
            "trow": sb("trow", [1, 512], F32),
            "srow": sb("srow", [1, 512], F32),
            "posT": sb("posT", [1, T], F32),
            "bcol": sb("bcol", [128, 1], F32),
            "rstd_dbg": sb("rstd_dbg", [128, 64], F32),
        }
        sc["rrow"] = sc["trow"]
        ident = sb("ident", [128, 128], BF16)
        identf = sb("identf", [128, 128], F32)
        onesb = sb("onesb", [128, 1], F32)
        onesrow = sb("onesrow", [1, 128], F32)
        cbank = sb("cbank", [128, 16], F32)
        rrow = sb("rrowt", [1, 512], F32)
        sc["rrow"] = rrow
        ps = {
            "pq": pst("pq", [128, 512]),
            "pk": pst("pk", [128, 512]),
            "pv": pst("pv", [128, 512]),
            "pt": pst("pt", [128, 512], BF16),
            "po": pst("po", [128, 512]),
            "pb2": pst("pb2", [128, 512]),
            "pr": pst("pr", [1, 512]),
        }
        block = es.enter_context(nc.Block())
        ctx = Ctx(nc, block)
        ctx.cbank = cbank
        emit_prelude(ctx, ident, onesb, onesrow, identf)
        ctx.emit("gpsimd", lambda g: g.iota(
            sc["posT"][0:1, :], pattern=[[1, T]], base=0, channel_multiplier=0,
            allow_small_or_imprecise_dtypes=True))
        v3 = v_tm[:, :].rearrange("p (t h c) -> p (t h) c", c=129, h=4)[:, :, 128:129]
        ctx.emit("gpsimd", lambda g: g.memset(v3, 1.0))
        gv = ctx.emit("gpsimd", lambda g: g.memset(sc["bcol"][:, :], 0.0))
        dv = 0
        for d_, s_ in ((gq, gqr), (gk, gkr), (slp, slps)):
            dv = ctx.emit("sync", lambda q, d_=d_, s_=s_: q.dma_start(
                out=s_[:, :], in_=d_), dma=True)
        wvv = 0
        for w_d, w_s, n in ((wq, wq_sb, 1024), (wk, wk_sb, 1024), (wv, wv_sb, 512)):
            wvv = ctx.emit("gpsimd", lambda g, w_d=w_d, w_s=w_s: g.dma_start(
                out=w_s[:, :].rearrange("p (c f) -> p c f", c=NCH),
                in_=w_d.rearrange("(c p) f -> p c f", p=128)), dma=True)
        ctx.constcol(1e-6)
        ctx.constcol(1e-6)
        w_ready = (("dS", dv), ("dG", wvv), ("gs", ctx.cnt["gs"]))
        done = emit_win_front(ctx, xnin, xs_sb, q_fm, k_fm, v_tm, o_fm,
                              wq_sb, wk_sb, wv_sb, gqr, gkr, slps, ps, sc,
                              w_ready)
        ctx.emit("sync", lambda q: q.dma_start(
            out=qfm_d.rearrange("(c p) t -> p c t", p=128),
            in_=q_fm[:, :].rearrange("p (c t) -> p c t", c=4)),
            wait=(("vs", done), ("ss", ctx.cnt["ss"])), dma=True, store=True)
        ctx.emit("sync", lambda q: q.dma_start(
            out=kfm_d.rearrange("(c p) t -> p c t", p=128),
            in_=k_fm[:, :].rearrange("p (c t) -> p c t", c=4)), dma=True, store=True)
        ctx.emit("sync", lambda q: q.dma_start(
            out=vtm_d, in_=v_tm[:, :]), dma=True, store=True)
        ctx.emit("sync", lambda q: q.dma_start(
            out=rstd_d, in_=sc["rstd_dbg"][:, :]), dma=True, store=True)
        ctx.emit("sync", lambda q: q.dma_start(
            out=ofm.rearrange("(c p) t -> p c t", p=128),
            in_=o_fm[:, :].rearrange("p (c t) -> p c t", c=4)),
            wait=(("vs", done),), dma=True, store=True)
        ctx.wait("sync", ("st", ctx.cnt["st"]))
    return nc



CAP = 5120
E_EXP = 8


def emit_moe_route(ctx, xn_sb, wgt_sb, o_tm, rout_sb, oh_all, h_fm, c_tm_all,
                   ps, sc, w_ready, Ttok=TL):
    e = ctx.emit
    nt = Ttok // 128
    e("gpsimd", lambda g: g.iota(sc["iota8"][:, :], pattern=[[1, 8]], base=0,
                                 channel_multiplier=0,
                                 allow_small_or_imprecise_dtypes=True))
    gv = ctx.cnt["gs"]
    pl_rd = 0
    pt_rd = 0
    ptx_rd = 0
    for i in range(nt):
        tv = 0
        for kc in range(NCH):
            tv = e("tensor", lambda t, kc=kc, i=i: t.matmul(
                ps["pl"][:, :],
                xn_sb[:, kc * Ttok + i * 128: kc * Ttok + i * 128 + 128],
                wgt_sb[:, kc * 8:(kc + 1) * 8],
                start=(kc == 0), stop=(kc == NCH - 1)),
                wait=(w_ready + (("vs", pl_rd), ("gs", gv))) if kc == 0 else ())
        lt = sc["lt"]
        vv = e("vector", lambda v: v.tensor_copy(lt[:, :], ps["pl"][:, :]),
               wait=(("ts", tv),))
        pl_rd = vv
        r = rout_sb[:, i * 8:(i + 1) * 8]
        e("vector", lambda v: v.tensor_reduce(
            out=sc["col"][:, 0:1], in_=lt[:, :], axis=AX.X, op=OP.max))
        e("vector", lambda v: v.tensor_scalar(
            out=sc["sc8a"][:, :], in0=lt[:, :], scalar1=sc["col"][:, 0:1],
            scalar2=None, op0=OP.is_equal))
        e("vector", lambda v: v.tensor_tensor_scan(
            out=sc["sc8b"][:, :], data0=sc["sc8a"][:, :], data1=sc["sc8a"][:, :],
            initial=0.0, op0=OP.add, op1=OP.bypass))
        e("vector", lambda v: v.tensor_scalar(
            out=sc["sc8b"][:, :], in0=sc["sc8b"][:, :], scalar1=1.0,
            scalar2=None, op0=OP.is_equal))
        oh1 = oh_all[:, i * 16: i * 16 + 8]
        e("vector", lambda v, oh1=oh1: v.tensor_tensor(
            out=oh1, in0=sc["sc8a"][:, :], in1=sc["sc8b"][:, :], op=OP.mult))
        e("vector", lambda v, oh1=oh1: v.tensor_tensor(
            out=sc["sc8a"][:, :], in0=oh1, in1=sc["iota8"][:, :], op=OP.mult))
        e("vector", lambda v, r=r: v.tensor_reduce(
            out=r[:, 0:1], in_=sc["sc8a"][:, :], axis=AX.X, op=OP.add))
        e("vector", lambda v, oh1=oh1: v.scalar_tensor_tensor(
            out=sc["sc8a"][:, :], in0=oh1, scalar=-1e30, in1=lt[:, :],
            op0=OP.mult, op1=OP.add))
        e("vector", lambda v: v.tensor_reduce(
            out=sc["col"][:, 1:2], in_=sc["sc8a"][:, :], axis=AX.X, op=OP.max))
        e("vector", lambda v: v.tensor_scalar(
            out=sc["sc8b"][:, :], in0=sc["sc8a"][:, :], scalar1=sc["col"][:, 1:2],
            scalar2=None, op0=OP.is_equal))
        e("vector", lambda v: v.tensor_tensor_scan(
            out=sc["sc8a"][:, :], data0=sc["sc8b"][:, :], data1=sc["sc8b"][:, :],
            initial=0.0, op0=OP.add, op1=OP.bypass))
        e("vector", lambda v: v.tensor_scalar(
            out=sc["sc8a"][:, :], in0=sc["sc8a"][:, :], scalar1=1.0,
            scalar2=None, op0=OP.is_equal))
        oh2 = oh_all[:, i * 16 + 8: i * 16 + 16]
        e("vector", lambda v, oh2=oh2: v.tensor_tensor(
            out=oh2, in0=sc["sc8b"][:, :], in1=sc["sc8a"][:, :], op=OP.mult))
        e("vector", lambda v, oh2=oh2: v.tensor_tensor(
            out=sc["sc8b"][:, :], in0=oh2, in1=sc["iota8"][:, :], op=OP.mult))
        e("vector", lambda v, r=r: v.tensor_reduce(
            out=r[:, 1:2], in_=sc["sc8b"][:, :], axis=AX.X, op=OP.add))
        e("vector", lambda v: v.tensor_tensor(
            out=sc["col"][:, 2:3], in0=sc["col"][:, 1:2], in1=sc["col"][:, 0:1],
            op=OP.subtract))
        svv = e("scalar", lambda s2: s2.activation(
            out=sc["col"][:, 3:4], in_=sc["col"][:, 2:3], func=AF.Exp),
            wait=(("vs", ctx.cnt["vs"]),))
        e("vector", lambda v: v.tensor_scalar(
            out=sc["col"][:, 4:5], in0=sc["col"][:, 3:4], scalar1=1.0,
            scalar2=None, op0=OP.add), wait=(("ss", svv),))
        e("vector", lambda v, r=r: v.reciprocal(out=r[:, 2:3], in_=sc["col"][:, 4:5]))
        e("vector", lambda v, r=r: v.tensor_tensor(
            out=r[:, 3:4], in0=sc["col"][:, 3:4], in1=r[:, 2:3], op=OP.mult))
        hv = e("vector", lambda v, oh1=oh1, oh2=oh2: v.tensor_tensor(
            out=sc["sc8a"][:, :], in0=oh1, in1=oh2, op=OP.add))
        tvh = e("tensor", lambda t, i=i: t.transpose(
            ps["pt8"][0:8, :], sc["sc8a"][:, :], ctx.ident_f32[:, :]),
            wait=(("vs", hv), ("ss", pt_rd)))
        pt_rd = e("scalar", lambda s2, i=i: s2.copy(
            out=h_fm[0:8, i * 128:(i + 1) * 128], in_=ps["pt8"][0:8, :]),
            wait=(("ts", tvh),))
        for ch in range(NCH):
            tvx = e("tensor", lambda t, ch=ch, i=i: t.transpose(
                ps["ptx"][:, 0:128],
                xn_sb[:, ch * Ttok + i * 128: ch * Ttok + i * 128 + 128],
                ctx.ident_bf[:, :]),
                wait=(("ss", ptx_rd),) + (tuple(w_ready) if (i == 0 and ch == 0) else ()))
            ptx_rd = e("scalar", lambda s2, ch=ch, i=i: s2.copy(
                out=o_tm[:, i * 1024 + ch * 128: i * 1024 + ch * 128 + 128],
                in_=ps["ptx"][:, 0:128]), wait=(("ts", tvx),))
    scv = e("vector", lambda v: v.tensor_tensor_scan(
        out=h_fm[0:8, Ttok:2 * Ttok], data0=h_fm[0:8, 0:Ttok],
        data1=h_fm[0:8, 0:Ttok], initial=0.0, op0=OP.add, op1=OP.bypass),
        wait=(("ss", pt_rd),))
    ev = e("vector", lambda v: v.tensor_tensor(
        out=h_fm[0:8, Ttok:2 * Ttok], in0=h_fm[0:8, Ttok:2 * Ttok],
        in1=h_fm[0:8, 0:Ttok], op=OP.subtract))
    cbf = ev
    ptb_rd = 0
    out_v = 0
    for i in range(nt):
        tvb = e("tensor", lambda t, i=i: t.transpose(
            ps["ptb"][:, 0:8], h_fm[0:8, Ttok + i * 128: Ttok + (i + 1) * 128],
            ctx.ident_f32[0:8, 0:8]),
            wait=(("vs", cbf), ("vs", ptb_rd)))
        ptb_rd = e("vector", lambda v, i=i: v.tensor_copy(
            c_tm_all[:, i * 8:(i + 1) * 8], ps["ptb"][:, 0:8]),
            wait=(("ts", tvb),))
        r = rout_sb[:, i * 8:(i + 1) * 8]
        e("vector", lambda v, i=i: v.tensor_tensor(
            out=sc["sc8a"][:, :], in0=oh_all[:, i * 16:i * 16 + 8],
            in1=c_tm_all[:, i * 8:(i + 1) * 8], op=OP.mult))
        e("vector", lambda v, r=r: v.tensor_reduce(
            out=r[:, 4:5], in_=sc["sc8a"][:, :], axis=AX.X, op=OP.add))
        e("vector", lambda v, i=i: v.tensor_tensor(
            out=sc["sc8a"][:, :], in0=oh_all[:, i * 16 + 8:i * 16 + 16],
            in1=c_tm_all[:, i * 8:(i + 1) * 8], op=OP.mult))
        out_v = e("vector", lambda v, r=r: v.tensor_reduce(
            out=r[:, 5:6], in_=sc["sc8a"][:, :], axis=AX.X, op=OP.add))
    return out_v, ev


def emit_moe_slots(ctx, rt_src, i, coreseg_bc, sc, keepw_dst=None):
    e = ctx.emit
    for r_ in range(2):
        e("vector", lambda v, r_=r_: v.tensor_scalar(
            out=sc["m8a"][:, :], in0=sc["iota8"][:, :],
            scalar1=rt_src[:, r_:r_ + 1], scalar2=None, op0=OP.is_equal))
        e("vector", lambda v: v.tensor_tensor(
            out=sc["m8b"][:, :], in0=sc["m8a"][:, :], in1=coreseg_bc, op=OP.mult))
        e("vector", lambda v: v.tensor_reduce(
            out=sc["mcol"][:, 0:1], in_=sc["m8b"][:, :], axis=AX.X, op=OP.add))
        e("vector", lambda v, r_=r_: v.tensor_tensor(
            out=sc["mcol"][:, 1:2], in0=sc["mcol"][:, 0:1],
            in1=rt_src[:, 4 + r_:5 + r_], op=OP.add))
        e("vector", lambda v: v.tensor_scalar(
            out=sc["mcol"][:, 2:3], in0=sc["mcol"][:, 1:2], scalar1=float(CAP),
            scalar2=None, op0=OP.is_lt))
        e("vector", lambda v, r_=r_: v.scalar_tensor_tensor(
            out=sc["mcol"][:, 3:4], in0=rt_src[:, r_:r_ + 1], scalar=float(CAP),
            in1=sc["mcol"][:, 1:2], op0=OP.mult, op1=OP.add))
        e("vector", lambda v: v.tensor_scalar(
            out=sc["mcol"][:, 4:5], in0=sc["mcol"][:, 1:2], scalar1=float(CAP - 1),
            scalar2=None, op0=OP.min))
        e("vector", lambda v, r_=r_: v.scalar_tensor_tensor(
            out=sc["mcol"][:, 4:5], in0=rt_src[:, r_:r_ + 1], scalar=float(CAP),
            in1=sc["mcol"][:, 4:5], op0=OP.mult, op1=OP.add))
        e("vector", lambda v: v.tensor_scalar(
            out=sc["mcol"][:, 5:6], in0=sc["mcol"][:, 2:3], scalar1=-1e9,
            scalar2=1e9, op0=OP.mult, op1=OP.add))
        e("vector", lambda v: v.tensor_tensor(
            out=sc["mcol"][:, 5:6], in0=sc["mcol"][:, 5:6],
            in1=sc["mcol"][:, 3:4], op=OP.add))
        vv = e("vector", lambda v, r_=r_: v.tensor_copy(
            sc["slotu"][:, r_:r_ + 1], sc["mcol"][:, 5:6]))
        if keepw_dst is not None:
            e("vector", lambda v, r_=r_: v.tensor_copy(
                sc["slotc"][:, r_:r_ + 1], sc["mcol"][:, 4:5]))
            vv = e("vector", lambda v, r_=r_: v.tensor_tensor(
                out=keepw_dst[:, r_:r_ + 1], in0=rt_src[:, 2 + r_:3 + r_],
                in1=sc["mcol"][:, 2:3], op=OP.mult))
    return vv


def emit_moe_offsets(ctx, tot_sb, offs_cm, ps, sc):
    e = ctx.emit
    tv = e("tensor", lambda t: t.transpose(
        ps["pt8"][0:8, 0:8], tot_sb[0:8, 0:8], ctx.ident_f32[0:8, 0:8]),
        wait=(("dS", ctx.cnt["dS"]), ("vs", ctx.cnt["vs"])))
    ev = e("vector", lambda v: v.tensor_copy(sc["te8"][0:8, 0:8], ps["pt8"][0:8, 0:8]),
           wait=(("ts", tv),))
    ev = e("vector", lambda v: v.tensor_tensor_scan(
        out=sc["te8"][0:8, 8:16], data0=sc["te8"][0:8, 0:8],
        data1=sc["te8"][0:8, 0:8], initial=0.0, op0=OP.add, op1=OP.bypass))
    ev = e("vector", lambda v: v.tensor_tensor(
        out=sc["te8"][0:8, 8:16], in0=sc["te8"][0:8, 8:16],
        in1=sc["te8"][0:8, 0:8], op=OP.subtract))
    tv = e("tensor", lambda t: t.transpose(
        ps["pt8"][0:8, 0:8], sc["te8"][0:8, 8:16], ctx.ident_f32[0:8, 0:8]),
        wait=(("vs", ev),))
    ev = e("vector", lambda v: v.tensor_copy(offs_cm[0:8, 0:8], ps["pt8"][0:8, 0:8]),
           wait=(("ts", tv),))
    return ev


def emit_moe_expert(ctx, xn_all_d, table_d, expo_d, rt_sb, tot_sb, offs_cm,
                    tbi_sb, w1_sb, w2_sb, ps, sc, w_ready):
    e = ctx.emit
    e("gpsimd", lambda g: g.iota(sc["fill"][:, :], pattern=[[0, 1280]], base=-1,
                                 channel_multiplier=0,
                                 allow_small_or_imprecise_dtypes=True))
    fv = ctx.emit("gpsimd", lambda g: g.dma_start(
        out=table_d.ap().rearrange("(p a) f -> p a f", a=E_EXP * CAP // 128),
        in_=sc["fill"][:, :].rearrange("p (a f) -> p a f", f=4)),
        wait=(("gs", ctx.cnt["gs"]),), dma=True)
    ofv = emit_moe_offsets(ctx, tot_sb, offs_cm, ps, sc)
    slot_war = 0
    seg_bc = 0
    scat = fv
    for i in range(128):
        c = i // 16
        if i % 16 == 0:
            tvx = e("tensor", lambda t, c=c: t.matmul(
                ps["pt8"][0:1, 0:8], ctx.ident_f32[0:8, c:c + 1],
                offs_cm[0:8, 0:8], start=True, stop=True),
                wait=(("vs", ofv if i == 0 else ctx.cnt["vs"]),))
            rv = e("vector", lambda v: v.tensor_copy(sc["orow"][0:1, 0:8],
                                                     ps["pt8"][0:1, 0:8]),
                   wait=(("ts", tvx),))
            seg_bc = e("tensor", lambda t: t.matmul(
                ps["pseg"][:, 0:8], ctx.onesrow[0:1, 0:128], sc["orow"][0:1, 0:8],
                start=True, stop=True), wait=(("vs", rv),))
        ctx.wait("vector", ("ts", seg_bc), ("dG", slot_war))
        vv = emit_moe_slots(ctx, rt_sb[:, i * 8:(i + 1) * 8], i,
                            ps["pseg"][:, 0:8], sc)
        gv = e("gpsimd", lambda g, i=i: g.iota(
            sc["tokid"][:, :], pattern=[[0, 4]], base=i * 128,
            channel_multiplier=1, allow_small_or_imprecise_dtypes=True),
            wait=(("dG", scat),))
        for r_ in range(2):
            scat = ctx.emit("gpsimd", lambda g, r_=r_: g.indirect_dma_start(
                out=table_d.ap(), out_offset=bass.IndirectOffsetOnAxis(
                    ap=sc["slotu"][:, r_:r_ + 1], axis=0),
                in_=sc["tokid"][:, :], in_offset=None),
                wait=(("vs", vv), ("gs", gv)), dma=True)
        slot_war = scat
    tbl_ld = ctx.emit("sync", lambda q: q.dma_start(
        out=tbi_sb[:, :], in_=ctx.tblidx_d), dma=True)
    NG = CAP // 512
    act = sc["act"]
    xg_war = 0
    expo_war = 0
    for g in range(NG):
        gvz = 0
        xrd = ctx._xg_rd if hasattr(ctx, "_xg_rd") else 0
        for j4 in range(4):
            j = g * 4 + j4
            gvz = e("gpsimd", lambda g_, j4=j4: g_.memset(
                sc["xb"][:, j4 * 1024:(j4 + 1) * 1024], 0.0),
                wait=(("dG", scat), ("dS", tbl_ld), ("ts", xg_war),
                      ("st", expo_war)))
            tb = ctx.emit("gpsimd", lambda g_, j=j, j4=j4: g_.indirect_dma_start(
                out=sc["tbl4"][:, :], out_offset=None, in_=table_d.ap(),
                in_offset=bass.IndirectOffsetOnAxis(ap=tbi_sb[:, j:j + 1], axis=0)),
                wait=(("gs", gvz), ("dG", ctx._xgather if hasattr(ctx, "_xgather") else 0)),
                dma=True)
            gvz = ctx.emit("gpsimd", lambda g_, j4=j4: g_.indirect_dma_start(
                out=sc["xb"][:, j4 * 1024:(j4 + 1) * 1024], out_offset=None,
                in_=xn_all_d,
                in_offset=bass.IndirectOffsetOnAxis(ap=sc["tbl4"][:, 0:1], axis=0)),
                wait=(("dG", tb),), dma=True)
            ctx._xgather = gvz
        trv = 0
        ev = 0
        for j4 in range(4):
            for ch in range(NCH):
                tvx = e("tensor", lambda t, j4=j4, ch=ch: t.transpose(
                    ps["ptx"][:, 0:128],
                    sc["xb"][:, j4 * 1024 + ch * 128: j4 * 1024 + ch * 128 + 128],
                    ctx.ident_bf[:, :]),
                    wait=(("dG", gvz), ("vs", ev)))
                trv = tvx
                ev = e("vector", lambda v, j4=j4, ch=ch: v.tensor_copy(
                    sc["xg"][:, ch * 512 + j4 * 128: ch * 512 + j4 * 128 + 128],
                    ps["ptx"][:, 0:128]), wait=(("ts", tvx),))
        xg_rdy = ev
        for fc in range(16):
            tv1 = 0
            for kc in range(NCH):
                tv1 = e("tensor", lambda t, kc=kc, fc=fc: t.matmul(
                    ps["ph"][:, :],
                    w1_sb[:, kc * 4096 + fc * 128: kc * 4096 + fc * 128 + 128],
                    sc["xg"][:, kc * 512:(kc + 1) * 512],
                    start=(kc == 0), stop=(kc == NCH - 1)),
                    wait=(w_ready + (("vs", xg_rdy), ("ss", ctx.cnt["ss"]),
                                     ("vs", ctx.cnt["vs"]))) if kc == 0 else ())
            sv = e("scalar", lambda s2: s2.activation(
                out=sc["gel"][:, :], in_=ps["ph"][:, :], func=AF.Gelu),
                wait=(("ts", tv1),))
            tv2 = 0
            for kc in range(NCH):
                tv2 = e("tensor", lambda t, kc=kc, fc=fc: t.matmul(
                    ps["ph2"][:, :],
                    w1_sb[:, kc * 4096 + (fc + 16) * 128: kc * 4096 + (fc + 16) * 128 + 128],
                    sc["xg"][:, kc * 512:(kc + 1) * 512],
                    start=(kc == 0), stop=(kc == NCH - 1)),
                    wait=(("ss", sv),) if kc == 0 else ())
            e("vector", lambda v, fc=fc: v.tensor_tensor(
                out=act[:, fc * 512:(fc + 1) * 512], in0=sc["gel"][:, :],
                in1=ps["ph2"][:, :], op=OP.mult), wait=(("ts", tv2),))
        actv = ctx.cnt["vs"]
        for oc in range(NCH):
            tv3 = 0
            for kc in range(16):
                tv3 = e("tensor", lambda t, kc=kc, oc=oc: t.matmul(
                    ps["ph"][:, :],
                    w2_sb[:, kc * 1024 + oc * 128: kc * 1024 + oc * 128 + 128],
                    act[:, kc * 512:(kc + 1) * 512],
                    start=(kc == 0), stop=(kc == 15)),
                    wait=(("vs", actv), ("ss", ctx.cnt["ss"]),
                          ("vs", ctx.cnt["vs"])) if kc == 0 else ())
            e("vector", lambda v, oc=oc: v.tensor_copy(
                sc["of"][:, oc * 512:(oc + 1) * 512], ps["ph"][:, :]),
                wait=(("ts", tv3),))
        ofv2 = ctx.cnt["vs"]
        ev2 = 0
        for j4 in range(4):
            for ch in range(NCH):
                tvx = e("tensor", lambda t, j4=j4, ch=ch: t.transpose(
                    ps["ptx"][:, 0:128],
                    sc["of"][:, ch * 512 + j4 * 128: ch * 512 + j4 * 128 + 128],
                    ctx.ident_bf[:, :]),
                    wait=(("vs", ofv2), ("vs", ev2)))
                ev2 = e("vector", lambda v, j4=j4, ch=ch: v.tensor_copy(
                    sc["xb"][:, j4 * 1024 + ch * 128: j4 * 1024 + ch * 128 + 128],
                    ps["ptx"][:, 0:128]), wait=(("ts", tvx),))
            expo_war = ctx.emit("sync", lambda q, g=g, j4=j4: q.dma_start(
                out=expo_d[g * 512 + j4 * 128: g * 512 + (j4 + 1) * 128, :],
                in_=sc["xb"][:, j4 * 1024:(j4 + 1) * 1024]),
                wait=(("vs", ev2),), dma=True, store=True)
        xg_war = ctx.cnt["ts"]
        ctx._xg_rd = xg_war
    return expo_war


def emit_moe_combine(ctx, x_sb, expo_all_d, rt_my, tot_sb, offs_cm, mycol_sb,
                     ps, sc, ready, Ttok=TL):
    e = ctx.emit
    ofv = emit_moe_offsets(ctx, tot_sb, offs_cm, ps, sc)
    tvx = e("tensor", lambda t: t.matmul(
        ps["pt8"][0:1, 0:8], mycol_sb[0:8, 0:1], offs_cm[0:8, 0:8],
        start=True, stop=True), wait=(("vs", ofv),) + tuple(ready))
    rv = e("vector", lambda v: v.tensor_copy(sc["orow"][0:1, 0:8],
                                             ps["pt8"][0:1, 0:8]),
           wait=(("ts", tvx),))
    seg_bc = e("tensor", lambda t: t.matmul(
        ps["pseg"][:, 0:8], ctx.onesrow[0:1, 0:128], sc["orow"][0:1, 0:8],
        start=True, stop=True), wait=(("vs", rv),))
    ctx.wait("vector", ("ts", seg_bc))
    g1 = 0
    ev2 = 0
    out_v = 0
    for i in range(Ttok // 128):
        vv = emit_moe_slots(ctx, rt_my[:, i * 8:(i + 1) * 8], i,
                            ps["pseg"][:, 0:8], sc, keepw_dst=sc["kw"])
        for r_ in range(2):
            g1 = ctx.emit("gpsimd", lambda g_, r_=r_: g_.indirect_dma_start(
                out=sc["rowg"][:, r_ * 1024:(r_ + 1) * 1024], out_offset=None,
                in_=expo_all_d,
                in_offset=bass.IndirectOffsetOnAxis(ap=sc["slotc"][:, r_:r_ + 1],
                                                    axis=0)),
                wait=(("vs", vv), ("vs", ctx.cnt["vs"])), dma=True)
        yv = e("vector", lambda v: v.tensor_scalar(
            out=sc["ytm"][:, 0:1024], in0=sc["rowg"][:, 0:1024],
            scalar1=sc["kw"][:, 0:1], scalar2=None, op0=OP.mult),
            wait=(("dG", g1),))
        yv = e("vector", lambda v: v.scalar_tensor_tensor(
            out=sc["ytm"][:, 0:1024], in0=sc["rowg"][:, 1024:2048],
            scalar=sc["kw"][:, 1:2], in1=sc["ytm"][:, 0:1024],
            op0=OP.mult, op1=OP.add))
        yb = e("vector", lambda v: v.tensor_copy(sc["ybf"][:, :], sc["ytm"][:, 0:1024]))
        for ch in range(NCH):
            tvx = e("tensor", lambda t, ch=ch: t.transpose(
                ps["ptx"][:, 0:128], sc["ybf"][:, ch * 128:(ch + 1) * 128],
                ctx.ident_bf[:, :]), wait=(("vs", yb), ("vs", ev2)))
            ev2 = e("vector", lambda v, ch=ch, i=i: v.tensor_tensor(
                out=x_sb[:, ch * Ttok + i * 128: ch * Ttok + i * 128 + 128],
                in0=x_sb[:, ch * Ttok + i * 128: ch * Ttok + i * 128 + 128],
                in1=ps["ptx"][:, 0:128], op=OP.add), wait=(("ts", tvx),))
        out_v = ev2
    return out_v


def build_moe_route_launch():
    nc = bass.Bass("TRN2", target_bir_lowering=False, debug=False,
                   num_devices=NCORES)
    xnin = nc.dram_tensor("xnin", [D, TL], BF16, kind="ExternalInput").ap()
    wgt = nc.dram_tensor("wgt", [D, 8], F32, kind="ExternalInput").ap()
    xntm_o = nc.dram_tensor("xntm_o", [TL, D], BF16, kind="ExternalOutput").ap()
    rout_o = nc.dram_tensor("rout_o", [TL, 8], F32, kind="ExternalOutput").ap()
    tot_o = nc.dram_tensor("tot_o", [8, 1], F32, kind="ExternalOutput").ap()
    import contextlib
    es = contextlib.ExitStack()

    def sb(name, shape, dt):
        return es.enter_context(nc.sbuf_tensor(name, shape, dt))

    def pst(name, shape, dt=F32):
        return es.enter_context(nc.psum_tensor(name, shape, dt))

    with es:
        xn_sb = sb("xn_sb", [128, NCH * TL], BF16)
        o_tm = sb("o_tm", [128, 16 * 1024], BF16)
        wgt_sb = sb("wgt_sb", [128, 8 * 8], BF16)
        rout_sb = sb("rout_sb", [128, 16 * 8], F32)
        oh_all = sb("oh_all", [128, 16 * 16], F32)
        h_fm = sb("h_fm", [8, 2 * TL], F32)
        c_tm_all = sb("c_tm_all", [128, 16 * 8], F32)
        sc = {
            "lt": sb("lt", [128, 8], F32),
            "sc8a": sb("sc8a", [128, 8], F32),
            "sc8b": sb("sc8b", [128, 8], F32),
            "col": sb("colx", [128, 8], F32),
            "iota8": sb("iota8", [128, 8], F32),
        }
        tot_sb = sb("tot_sb", [8, 1], F32)
        ident = sb("ident", [128, 128], BF16)
        identf = sb("identf", [128, 128], F32)
        onesb = sb("onesb", [128, 1], F32)
        onesrow = sb("onesrow", [1, 128], F32)
        cbank = sb("cbank", [128, 16], F32)
        ps = {
            "pl": pst("pl", [128, 8]),
            "pt8": pst("pt8", [8, 128]),
            "ptb": pst("ptb", [128, 8]),
            "ptx": pst("ptx", [128, 128], BF16),
        }
        block = es.enter_context(nc.Block())
        ctx = Ctx(nc, block)
        ctx.cbank = cbank
        emit_prelude(ctx, ident, onesb, onesrow, identf)
        xv = ctx.emit("sync", lambda q: q.dma_start(
            out=xn_sb[:, :].rearrange("p (c t) -> p c t", c=NCH),
            in_=xnin.rearrange("(c p) t -> p c t", p=128)), dma=True)
        wv = ctx.emit("gpsimd", lambda g: g.dma_start(
            out=wgt_sb[:, :].rearrange("p (c f) -> p c f", c=NCH),
            in_=wgt.rearrange("(c p) f -> p c f", p=128)), dma=True)
        w_ready = (("dS", xv), ("dG", wv), ("gs", ctx.cnt["gs"]))
        rv, hv = emit_moe_route(ctx, xn_sb, wgt_sb, o_tm, rout_sb, oh_all,
                                h_fm, c_tm_all, ps, sc, w_ready)
        tv = ctx.emit("vector", lambda v: v.tensor_tensor(
            out=tot_sb[0:8, 0:1], in0=h_fm[0:8, 2 * TL - 1:2 * TL],
            in1=h_fm[0:8, TL - 1:TL], op=OP.add), wait=(("vs", hv),))
        ctx.emit("sync", lambda q: q.dma_start(
            out=xntm_o.rearrange("(a p) d -> p a d", p=128),
            in_=o_tm[:, :].rearrange("p (a d) -> p a d", d=D)),
            wait=(("ss", ctx.cnt["ss"]),), dma=True, store=True)
        ctx.emit("sync", lambda q: q.dma_start(
            out=rout_o.rearrange("(a p) e -> p a e", p=128),
            in_=rout_sb[:, :].rearrange("p (a e) -> p a e", e=8)),
            wait=(("vs", rv),), dma=True, store=True)
        ctx.emit("sync", lambda q: q.dma_start(out=tot_o, in_=tot_sb[0:8, 0:1]),
                 wait=(("vs", tv),), dma=True, store=True)
        ctx.wait("sync", ("st", ctx.cnt["st"]))
    return nc


def build_moe_expert_launch():
    nc = bass.Bass("TRN2", target_bir_lowering=False, debug=False,
                   num_devices=NCORES)
    xn_all = nc.dram_tensor("xn_all", [8 * TL, D], BF16, kind="ExternalInput").ap()
    rout_all = nc.dram_tensor("rout_all", [8 * TL, 8], F32, kind="ExternalInput").ap()
    tot_all = nc.dram_tensor("tot_all", [8, 8], F32, kind="ExternalInput").ap()
    w1e = nc.dram_tensor("w1e", [D, 4096], F32, kind="ExternalInput").ap()
    w2e = nc.dram_tensor("w2e", [2048, D], F32, kind="ExternalInput").ap()
    tblidx = nc.dram_tensor("tblidx", [128, CAP // 128], U32, kind="ExternalInput").ap()
    expo = nc.dram_tensor("expo", [CAP, D], BF16, kind="ExternalOutput").ap()
    table = nc.dram_tensor("table", [8 * CAP, 4], U32)
    import contextlib
    es = contextlib.ExitStack()

    def sb(name, shape, dt):
        return es.enter_context(nc.sbuf_tensor(name, shape, dt))

    def pst(name, shape, dt=F32):
        return es.enter_context(nc.psum_tensor(name, shape, dt))

    with es:
        rt_sb = sb("rt_sb", [128, 128 * 8], F32)
        w1_sb = sb("w1_sb", [128, 8 * 4096], BF16)
        w2_sb = sb("w2_sb", [128, 16 * 1024], BF16)
        tot_sb = sb("tot_sb", [8, 8], F32)
        offs_cm = sb("offs_cm", [8, 8], F32)
        tbi_sb = sb("tbi_sb", [128, 40], U32)
        sc = {
            "fill": sb("fill", [128, 1280], U32),
            "xb": sb("xb", [128, 4096], BF16),
            "xg": sb("xg", [128, 4096], BF16),
            "act": sb("act", [128, 16 * 512], BF16),
            "of": sb("of", [128, 4096], BF16),
            "gel": sb("gel", [128, 512], F32),
            "tbl4": sb("tbl4", [128, 4], U32),
            "tokid": sb("tokid", [128, 4], U32),
            "slotu": sb("slotu", [128, 2], U32),
            "m8a": sb("m8a", [128, 8], F32),
            "m8b": sb("m8b", [128, 8], F32),
            "mcol": sb("mcol", [128, 8], F32),
            "iota8": sb("iota8", [128, 8], F32),
            "orow": sb("orowx", [1, 8], F32),
            "te8": sb("te8", [8, 16], F32),
        }
        ident = sb("ident", [128, 128], BF16)
        identf = sb("identf", [128, 128], F32)
        onesb = sb("onesb", [128, 1], F32)
        onesrow = sb("onesrow", [1, 128], F32)
        cbank = sb("cbank", [128, 16], F32)
        ps = {
            "ph": pst("ph", [128, 512]),
            "ph2": pst("ph2", [128, 512]),
            "ptx": pst("ptx", [128, 128], BF16),
            "pt8": pst("pt8", [8, 128]),
            "pseg": pst("pseg", [128, 8]),
        }
        block = es.enter_context(nc.Block())
        ctx = Ctx(nc, block)
        ctx.cbank = cbank
        ctx.tblidx_d = tblidx
        emit_prelude(ctx, ident, onesb, onesrow, identf)
        ctx.emit("gpsimd", lambda g: g.iota(
            sc["iota8"][:, :], pattern=[[1, 8]], base=0, channel_multiplier=0,
            allow_small_or_imprecise_dtypes=True))
        rv = ctx.emit("sync", lambda q: q.dma_start(
            out=rt_sb[:, :].rearrange("p (a e) -> p a e", e=8),
            in_=rout_all.rearrange("(a p) e -> p a e", p=128)), dma=True)
        tv = ctx.emit("sync", lambda q: q.dma_start(out=tot_sb[0:8, 0:8],
                                                    in_=tot_all), dma=True)
        w1v = ctx.emit("gpsimd", lambda g: g.dma_start(
            out=w1_sb[:, :].rearrange("p (c f) -> p c f", c=NCH),
            in_=w1e.rearrange("(c p) f -> p c f", p=128)), dma=True)
        w2v = ctx.emit("gpsimd", lambda g: g.dma_start(
            out=w2_sb[:, :].rearrange("p (c f) -> p c f", c=16),
            in_=w2e.rearrange("(c p) f -> p c f", p=128)), dma=True)
        ctx.wait("vector", ("dS", tv), ("dS", rv))
        ctx.wait("tensor", ("dS", tv), ("dS", rv))
        w_ready = (("dG", w2v),)
        done = emit_moe_expert(ctx, xn_all, table, expo, rt_sb, tot_sb,
                               offs_cm, tbi_sb, w1_sb, w2_sb, ps, sc, w_ready)
        ctx.wait("sync", ("st", ctx.cnt["st"]))
    return nc


def build_moe_combine_launch():
    nc = bass.Bass("TRN2", target_bir_lowering=False, debug=False,
                   num_devices=NCORES)
    xin = nc.dram_tensor("xin", [D, TL], F32, kind="ExternalInput").ap()
    expo_all = nc.dram_tensor("expo_all", [8 * CAP, D], BF16,
                              kind="ExternalInput").ap()
    rout_my = nc.dram_tensor("rout_my", [TL, 8], F32, kind="ExternalInput").ap()
    tot_all = nc.dram_tensor("tot_all", [8, 8], F32, kind="ExternalInput").ap()
    mycol = nc.dram_tensor("mycol", [8, 1], F32, kind="ExternalInput").ap()
    xout = nc.dram_tensor("xout", [D, TL], F32, kind="ExternalOutput").ap()
    import contextlib
    es = contextlib.ExitStack()

    def sb(name, shape, dt):
        return es.enter_context(nc.sbuf_tensor(name, shape, dt))

    def pst(name, shape, dt=F32):
        return es.enter_context(nc.psum_tensor(name, shape, dt))

    with es:
        x_sb = sb("x_sb", [128, NCH * TL], F32)
        rt_sb = sb("rt_sb", [128, 16 * 8], F32)
        tot_sb = sb("tot_sb", [8, 8], F32)
        offs_cm = sb("offs_cm", [8, 8], F32)
        mycol_sb = sb("mycol_sb", [8, 1], F32)
        sc = {
            "rowg": sb("rowg", [128, 2048], BF16),
            "ytm": sb("ytm", [128, 1024], F32),
            "ybf": sb("ybf", [128, 1024], BF16),
            "kw": sb("kw", [128, 2], F32),
            "slotu": sb("slotu", [128, 2], U32),
            "slotc": sb("slotc", [128, 2], U32),
            "m8a": sb("m8a", [128, 8], F32),
            "m8b": sb("m8b", [128, 8], F32),
            "mcol": sb("mcol", [128, 8], F32),
            "iota8": sb("iota8", [128, 8], F32),
            "orow": sb("orowx", [1, 8], F32),
            "te8": sb("te8", [8, 16], F32),
        }
        ident = sb("ident", [128, 128], BF16)
        identf = sb("identf", [128, 128], F32)
        onesb = sb("onesb", [128, 1], F32)
        onesrow = sb("onesrow", [1, 128], F32)
        cbank = sb("cbank", [128, 16], F32)
        ps = {
            "ptx": pst("ptx", [128, 128], BF16),
            "pt8": pst("pt8", [8, 128]),
            "pseg": pst("pseg", [128, 8]),
        }
        block = es.enter_context(nc.Block())
        ctx = Ctx(nc, block)
        ctx.cbank = cbank
        emit_prelude(ctx, ident, onesb, onesrow, identf)
        ctx.emit("gpsimd", lambda g: g.iota(
            sc["iota8"][:, :], pattern=[[1, 8]], base=0, channel_multiplier=0,
            allow_small_or_imprecise_dtypes=True))
        xv = ctx.emit("sync", lambda q: q.dma_start(
            out=x_sb[:, :].rearrange("p (c t) -> p c t", c=NCH),
            in_=xin.rearrange("(c p) t -> p c t", p=128)), dma=True)
        rv = ctx.emit("sync", lambda q: q.dma_start(
            out=rt_sb[:, :].rearrange("p (a e) -> p a e", e=8),
            in_=rout_my.rearrange("(a p) e -> p a e", p=128)), dma=True)
        tv = ctx.emit("sync", lambda q: q.dma_start(out=tot_sb[0:8, 0:8],
                                                    in_=tot_all), dma=True)
        mv = ctx.emit("sync", lambda q: q.dma_start(out=mycol_sb[0:8, 0:1],
                                                    in_=mycol), dma=True)
        ready = (("dS", mv), ("gs", ctx.cnt["gs"]))
        done = emit_moe_combine(ctx, x_sb, expo_all, rt_sb, tot_sb, offs_cm,
                                mycol_sb, ps, sc, ready)
        ctx.emit("sync", lambda q: q.dma_start(
            out=xout.rearrange("(c p) t -> p c t", p=128),
            in_=x_sb[:, :].rearrange("p (c t) -> p c t", c=NCH)),
            wait=(("vs", done),), dma=True, store=True)
        ctx.wait("sync", ("st", ctx.cnt["st"]))
    return nc



EPSM = 0.05


def emit_delta_front(ctx, xn_dram, xs_sb, wq_sb, wk_sb, wv_sb, wg_sb, wb_sb,
                     wf_sb, ws_sb, bfs_sb, rep4_sb, o_fm,
                     fs_dram, g_dram, ab_dram, hv_dram, w_dram,
                     ps, sc, w_ready, Ttok=T):
    e = ctx.emit
    nt = Ttok // 128
    NSB = Ttok // 32
    xld = [0, 0]
    xs_user = [0, 0]

    def xload(tt, bi):
        return e("sync", lambda q, tt=tt, bi=bi: q.dma_start(
            out=xs_sb[:, bi * 1024:(bi + 1) * 1024]
            .rearrange("p (c t) -> p c t", c=NCH),
            in_=xn_dram[:, tt * 128:(tt + 1) * 128]
            .rearrange("(c p) t -> p c t", p=128)),
            wait=(("ts", xs_user[bi]),), dma=True)
    xld[0] = xload(0, 0)
    war = {"pq": 0, "pk": 0, "pv": 0, "pb": 0}
    fs_st = 0
    for tt in range(nt):
        bi = tt % 2
        if tt + 1 < nt:
            xld[1 - bi] = xload(tt + 1, 1 - bi)
        xs = xs_sb[:, bi * 1024:(bi + 1) * 1024]
        tvk = tvv = tvb = 0
        for kc in range(NCH):
            tvk = e("tensor", lambda t, kc=kc, xs=xs: t.matmul(
                ps["pk"][:, 0:512], xs[:, kc * 128:(kc + 1) * 128],
                wk_sb[:, kc * 512:(kc + 1) * 512],
                start=(kc == 0), stop=(kc == NCH - 1)),
                wait=(w_ready + (("dS", xld[bi]), ("ss", war["pk"]))) if kc == 0 else ())
        kv = e("scalar", lambda s2: s2.activation(
            out=sc["ktm"][:, :], in_=ps["pk"][:, 0:512], func=AF.Silu),
            wait=(("ts", tvk), ("vs", war.get("u_rd", 0))))
        war["pk"] = kv
        for kc in range(NCH):
            tvv = e("tensor", lambda t, kc=kc, xs=xs: t.matmul(
                ps["pv"][:, 0:512], xs[:, kc * 128:(kc + 1) * 128],
                wv_sb[:, kc * 512:(kc + 1) * 512],
                start=(kc == 0), stop=(kc == NCH - 1)),
                wait=(("ss", war["pv"]),) if kc == 0 else ())
        vv = e("scalar", lambda s2: s2.activation(
            out=sc["vtm"][:, :], in_=ps["pv"][:, 0:512], func=AF.Silu),
            wait=(("ts", tvv), ("vs", war.get("u_rd", 0))))
        war["pv"] = vv
        for kc in range(NCH):
            tvb = e("tensor", lambda t, kc=kc, xs=xs: t.matmul(
                ps["pb"][:, 0:24], xs[:, kc * 128:(kc + 1) * 128],
                wb_sb[:, kc * 24:(kc + 1) * 24],
                start=(kc == 0), stop=False),
                wait=(("ss", war["pb"]),) if kc == 0 else ())
        tvb = e("tensor", lambda t: t.matmul(
            ps["pb"][:, 0:24], sc["onesbf"][0:1, 0:128], ctx.brow_sb[0:1, 0:24],
            start=False, stop=True))
        bv = e("scalar", lambda s2: s2.activation(
            out=sc["bfs"][:, 0:24], in_=ps["pb"][:, 0:24], func=AF.Sigmoid),
            wait=(("ts", tvb),))
        war["pb"] = bv
        uv = e("vector", lambda v: v.tensor_tensor(
            out=sc["utm"][:, :], in0=sc["ktm"][:, :], in1=sc["vtm"][:, :],
            op=OP.mult), wait=(("ss", vv),))
        b0 = sc["bfs"][:, 0:1]
        bbc = bass.AP(tensor=b0.tensor, offset=b0.offset,
                      ap=[list(b0.ap[0]), [1, 8], [0, 64]])
        u3 = ctx.u_tm[:, tt * 512:(tt + 1) * 512].rearrange(
            "p (h d) -> p h d", d=64)
        uv = e("vector", lambda v, u3=u3, bbc=bbc: v.tensor_tensor(
            out=u3, in0=sc["utm"][:, :].rearrange("p (h d) -> p h d", d=64),
            in1=bbc, op=OP.mult), wait=(("ss", bv),))
        war["u_rd"] = uv
        for which, cst in (("f", 8), ("s", 16)):
            tvf = e("tensor", lambda t, cst=cst: t.transpose(
                ps["pt8"][0:8, :], sc["bfs"][:, cst:cst + 8], ctx.ident_f32[:, :]),
                wait=(("ss", bv), ("st", fs_st)))
            ev = e("vector", lambda v: v.tensor_copy(sc["fsrow"][0:8, :],
                                                     ps["pt8"][0:8, :]),
                   wait=(("ts", tvf),))
            fs_st = ctx.emit("sync", lambda q, cst=cst, tt=tt: q.dma_start(
                out=fs_dram[(cst - 8):(cst - 8) + 8, tt * 128:(tt + 1) * 128],
                in_=sc["fsrow"][0:8, :]), wait=(("vs", ev),), dma=True, store=True)
        for name, wsb, fn in (("q", wq_sb, AF.Silu), ("g", wg_sb, AF.Sigmoid)):
            for dc in range(4):
                tvq = 0
                for kc in range(NCH):
                    tvq = e("tensor", lambda t, kc=kc, dc=dc, wsb=wsb, xs=xs: t.matmul(
                        ps["pq"][:, 0:128],
                        wsb[:, kc * 512 + dc * 128: kc * 512 + dc * 128 + 128],
                        xs[:, kc * 128:(kc + 1) * 128],
                        start=(kc == 0), stop=(kc == NCH - 1)),
                        wait=(("ss", war["pq"]),) if kc == 0 else ())
                dst = ctx.q_fm if name == "q" else ctx.g_fm
                war["pq"] = e("scalar", lambda s2, dst=dst, dc=dc, tt=tt, fn=fn: s2.activation(
                    out=dst[:, dc * Ttok + tt * 128: dc * Ttok + tt * 128 + 128],
                    in_=ps["pq"][:, 0:128], func=fn), wait=(("ts", tvq),))
        xs_user[bi] = ctx.cnt["ts"]
    return fs_st


def emit_delta_scan(ctx, fs_dram, g_dram, ab_dram, hv_dram, w_dram, rep4_sb,
                    o_fm, ps, sc, Ttok=T):
    e = ctx.emit
    NSB = Ttok // 32
    fpl = ctx.emit("sync", lambda q: q.dma_start(
        out=sc["Fpl"][:, :].rearrange("p (h j) -> p h j", j=32),
        in_=bass.AP(tensor=fs_dram.ap().tensor, offset=0,
                    ap=[[32, 128], [Ttok, 8], [1, 32]])),
        wait=(("st", ctx.cnt["st"]),), dma=True)
    fpl = ctx.emit("sync", lambda q: q.dma_start(
        out=sc["Spl"][:, :].rearrange("p (h j) -> p h j", j=32),
        in_=bass.AP(tensor=fs_dram.ap().tensor, offset=8 * Ttok,
                    ap=[[32, 128], [Ttok, 8], [1, 32]])), dma=True)
    gz = e("gpsimd", lambda g: g.memset(sc["H1"][:, :], 0.0))
    gz = e("gpsimd", lambda g: g.memset(sc["H2"][:, :], 0.0))
    gz = e("gpsimd", lambda g: g.memset(sc["Hv"][:, :], 0.0))
    vz = e("vector", lambda v: v.tensor_scalar(
        out=sc["Hv"][:, :].rearrange("p (a h c) -> p a h c", a=2, c=2)[:, :, :, 0:1],
        in0=sc["Hv"][:, :].rearrange("p (a h c) -> p a h c", a=2, c=2)[:, 0:2, :, 0:1],
        scalar1=0.0, scalar2=1.0, op0=OP.mult, op1=OP.add),
        wait=(("gs", gz), ("dS", fpl)))
    vz = e("vector", lambda v: v.tensor_scalar(
        out=sc["Hv"][:, 16:32].rearrange("p (h c) -> p h c", c=2)[:, :, 0:1],
        in0=sc["Hv"][:, 16:32].rearrange("p (h c) -> p h c", c=2)[:, :, 0:1],
        scalar1=0.0, scalar2=0.0, op0=OP.mult, op1=OP.add))
    vz = e("vector", lambda v: v.tensor_scalar(
        out=sc["Hv"][:, 16:32].rearrange("p (h c) -> p h c", c=2)[:, :, 1:2],
        in0=sc["Hv"][:, 16:32].rearrange("p (h c) -> p h c", c=2)[:, :, 1:2],
        scalar1=0.0, scalar2=1.0, op0=OP.mult, op1=OP.add))

    def fsl(j):
        v = sc["Fpl"][:, j:j + 1]
        return bass.AP(tensor=v.tensor, offset=v.offset,
                       ap=[list(v.ap[0]), [32, 8], [0, 32]])

    def ssl(j):
        v = sc["Spl"][:, j:j + 1]
        return bass.AP(tensor=v.tensor, offset=v.offset,
                       ap=[list(v.ap[0]), [32, 8], [0, 32]])

    def fs8(j, src):
        v = src[:, j:j + 1]
        return bass.AP(tensor=v.tensor, offset=v.offset,
                       ap=[list(v.ap[0]), [32, 8], [0, 2]])

    for j in range(32):
        t1 = e("vector", lambda v, j=j: v.tensor_tensor(
            out=sc["T1"][:, :], in0=sc["H1"][:, :], in1=fsl(j), op=OP.mult))
        t2 = e("vector", lambda v, j=j: v.tensor_tensor(
            out=sc["T2"][:, :], in0=sc["H2"][:, :], in1=ssl(j), op=OP.mult))
        gdst = bass.AP(tensor=sc["Gall"][:, 0:1].tensor,
                       offset=sc["Gall"][:, 0:1].offset + j,
                       ap=[list(sc["Gall"][:, 0:1].ap[0]), [34 * 32, 8], [34, 32]])
        e("vector", lambda v, gdst=gdst: v.tensor_tensor(
            out=gdst, in0=sc["T1"][:, :], in1=sc["T2"][:, :], op=OP.add))
        e("vector", lambda v: v.scalar_tensor_tensor(
            out=sc["H1"][:, :], in0=sc["T2"][:, :], scalar=EPSM, in1=sc["T1"][:, :],
            op0=OP.mult, op1=OP.add))
        e("vector", lambda v: v.scalar_tensor_tensor(
            out=sc["H2"][:, :], in0=sc["T1"][:, :], scalar=EPSM, in1=sc["T2"][:, :],
            op0=OP.mult, op1=OP.add))
        hv = sc["Hv"]
        for a_i in range(2):
            hvs = hv[:, a_i * 16:(a_i + 1) * 16]
            ev_ = hvs.rearrange("p (h c) -> p h c", c=2)[:, :, 0:1]
            od_ = hvs.rearrange("p (h c) -> p h c", c=2)[:, :, 1:2]
            e("vector", lambda v, ev_=ev_, j=j: v.tensor_tensor(
                out=sc["tva"][:, 0:8].rearrange("p (h o) -> p h o", o=1),
                in0=ev_, in1=fs8(j, sc["Fpl"])[:, :, 0:1], op=OP.mult))
            e("vector", lambda v, od_=od_, j=j: v.tensor_tensor(
                out=sc["tva"][:, 8:16].rearrange("p (h o) -> p h o", o=1),
                in0=od_, in1=fs8(j, sc["Spl"])[:, :, 0:1], op=OP.mult))
            adst = bass.AP(tensor=sc["abll"][:, 0:1].tensor,
                           offset=sc["abll"][:, 0:1].offset + a_i * 8 * 32 + j,
                           ap=[list(sc["abll"][:, 0:1].ap[0]), [32, 8], [1, 1]])
            e("vector", lambda v, adst=adst: v.tensor_tensor(
                out=adst, in0=sc["tva"][:, 0:8].rearrange("p (h o) -> p h o", o=1),
                in1=sc["tva"][:, 8:16].rearrange("p (h o) -> p h o", o=1), op=OP.add))
            e("vector", lambda v, ev_=ev_: v.scalar_tensor_tensor(
                out=ev_, in0=sc["tva"][:, 8:16].rearrange("p (h o) -> p h o", o=1),
                scalar=EPSM, in1=sc["tva"][:, 0:8].rearrange("p (h o) -> p h o", o=1),
                op0=OP.mult, op1=OP.add))
            e("vector", lambda v, od_=od_: v.scalar_tensor_tensor(
                out=od_, in0=sc["tva"][:, 0:8].rearrange("p (h o) -> p h o", o=1),
                scalar=EPSM, in1=sc["tva"][:, 8:16].rearrange("p (h o) -> p h o", o=1),
                op0=OP.mult, op1=OP.add))
        for Hx in ("H1", "H2"):
            cdst = bass.AP(tensor=sc[Hx][:, 0:1].tensor,
                           offset=sc[Hx][:, 0:1].offset + j,
                           ap=[list(sc[Hx][:, 0:1].ap[0]), [32, 8], [1, 1]])
            e("vector", lambda v, cdst=cdst: v.tensor_scalar(
                out=cdst, in0=cdst, scalar1=0.0, scalar2=1.0 + EPSM,
                op0=OP.mult, op1=OP.add))
    for Hx, so in (("H1", 32), ("H2", 33)):
        gdst = bass.AP(tensor=sc["Gall"][:, 0:1].tensor,
                       offset=sc["Gall"][:, 0:1].offset + so,
                       ap=[list(sc["Gall"][:, 0:1].ap[0]), [34 * 32, 8], [34, 32]])
        e("vector", lambda v, gdst=gdst, Hx=Hx: v.tensor_copy(
            gdst, sc[Hx][:, :].rearrange("p (h s) -> p h s", s=32)))
    gb_done = ctx.cnt["vs"]
    return gb_done


def emit_delta_scan2(ctx, g_dram, ab_dram, hv_dram, w_dram, rep4_sb, o_fm,
                     ps, sc, gb_done, Ttok=T):
    e = ctx.emit
    NSB = Ttok // 32
    NG4 = NSB // 4
    gst = ctx.emit("sync", lambda q: q.dma_start(
        out=g_dram.ap(), in_=sc["Gall"][:, :]), wait=(("vs", gb_done),),
        dma=True, store=True)
    ast = ctx.emit("sync", lambda q: q.dma_start(
        out=ab_dram.ap(), in_=sc["abll"][:, :]), wait=(("vs", gb_done),),
        dma=True, store=True)
    tvh = e("tensor", lambda t: t.transpose(
        ps["pth"][0:32, :], sc["Hv"][:, :], ctx.ident_f32[:, :]),
        wait=(("vs", gb_done),))
    hvv = e("vector", lambda v: v.tensor_copy(sc["hvt"][0:32, :], ps["pth"][0:32, :]),
            wait=(("ts", tvh),))
    hst = ctx.emit("sync", lambda q: q.dma_start(
        out=hv_dram.ap(), in_=sc["hvt"][0:32, :]), wait=(("vs", hvv),),
        dma=True, store=True)
    gzz = e("gpsimd", lambda g: g.memset(sc["Grb"][:, :], 0.0),
            wait=(("vs", ctx.cnt["vs"]), ("ss", ctx.cnt["ss"])))
    grd = 0
    dwev = 0
    oev = 0
    for g4 in range(NG4):
        for sb4 in range(4):
            grd = ctx.emit("sync", lambda q, g4=g4, sb4=sb4: q.dma_start(
                out=bass.AP(
                    tensor=sc["Grb"][sb4 * 32:(sb4 + 1) * 32, 0:1].tensor,
                    offset=sc["Grb"][sb4 * 32:(sb4 + 1) * 32, 0:1].offset
                    + sb4 * 34,
                    ap=[list(sc["Grb"][sb4 * 32:(sb4 + 1) * 32, 0:1].ap[0]),
                        [136, 8], [1, 34]]),
                in_=bass.AP(tensor=g_dram.ap().tensor,
                            offset=(g4 * 4 + sb4) * 8704,
                            ap=[[34, 32], [1088, 8], [1, 34]]),
                ),
                wait=(("st", gst), ("ts", ctx.cnt["ts"]))
                + ((("gs", gzz),) if g4 == 0 else ()), dma=True)
        for h in range(8):
            tvm = e("tensor", lambda t, h=h, g4=g4: t.matmul(
                ps["pdo"][0:64, 0:136],
                ctx.u_tm[:, g4 * 512 + h * 64: g4 * 512 + h * 64 + 64],
                sc["Grb"][:, h * 136:(h + 1) * 136],
                start=True, stop=True),
                wait=(("dS", grd), ("ss", oev), ("vs", dwev)))
            oev = e("scalar", lambda s2, h=h, g4=g4: s2.copy(
                out=o_fm[(h % 2) * 64:(h % 2) * 64 + 64,
                         (h // 2) * Ttok + g4 * 128: (h // 2) * Ttok + g4 * 128 + 128],
                in_=bass.AP(tensor=ps["pdo"][0:64, 0:1].tensor,
                            offset=ps["pdo"][0:64, 0:1].offset,
                            ap=[list(ps["pdo"][0:64, 0:1].ap[0]), [34, 4], [1, 32]])),
                wait=(("ts", tvm),))
            for c_ in range(2):
                dwev = e("vector", lambda v, h=h, g4=g4, c_=c_: v.tensor_copy(
                    bass.AP(tensor=sc["dsA"][0:64, 0:1].tensor,
                            offset=sc["dsA"][0:64, 0:1].offset
                            + c_ * 1536 + h * 192 + 64 + g4 * 4,
                            ap=[list(sc["dsA"][0:64, 0:1].ap[0]), [1, 4]]),
                    bass.AP(tensor=ps["pdo"][0:64, 0:1].tensor,
                            offset=ps["pdo"][0:64, 0:1].offset + 32 + c_,
                            ap=[list(ps["pdo"][0:64, 0:1].ap[0]), [34, 4]])),
                    wait=(("ts", tvm),))
    return dwev, oev, hst, ast


def emit_delta_scan3(ctx, ab_dram, hv_dram, w_dram, rep4_sb, o_fm, qg_fm,
                     ps, sc, dwev, oev, hst, ast, Ttok=T, sub=5):
    e = ctx.emit
    NSB = Ttok // 32
    NG4 = NSB // 4
    mld = 0
    for mi, (a_i, c_) in enumerate(((0, 0), (1, 0), (0, 1), (1, 1))):
        mld = ctx.emit("gpsimd", lambda q, mi=mi, a_i=a_i, c_=c_: q.dma_start(
            out=bass.AP(tensor=sc["mr"][mi][0:1, 0:1].tensor,
                        offset=sc["mr"][mi][0:1, 0:1].offset + 64,
                        ap=[list(sc["mr"][mi][0:1, 0:1].ap[0]), [192, 8], [1, 128]]),
            in_=bass.AP(tensor=hv_dram.ap().tensor,
                        offset=(a_i * 16 + c_) * 128,
                        ap=[[1, 1], [2 * 128, 8], [1, 128]])),
            wait=(("st", hst),), dma=True)

    def padfix(which, lvl):
        gv = 0
        for mi in range(4):
            padv = 1.0 if mi in (0, 3) else 0.0
            gv = e("gpsimd", lambda g, mi=mi, padv=padv, which=which: g.memset(
                bass.AP(tensor=sc[which][mi][0:1, 0:1].tensor,
                        offset=sc[which][mi][0:1, 0:1].offset,
                        ap=[list(sc[which][mi][0:1, 0:1].ap[0]), [192, 8], [1, 64]]),
                padv), wait=(("vs", ctx.cnt["vs"]),))
        return gv

    if sub < 2:
        return ctx.cnt["vs"]
    gz = e("gpsimd", lambda g: g.memset(sc["dsB"][0:64, :], 0.0))
    gz = padfix("mr", -1)
    gz0 = e("gpsimd", lambda g: g.memset(
        bass.AP(tensor=sc["dsA"][0:64, 0:1].tensor,
                offset=sc["dsA"][0:64, 0:1].offset,
                ap=[list(sc["dsA"][0:64, 0:1].ap[0]), [1536, 2], [192, 8],
                    [1, 64]]), 0.0))
    cur_d, nxt_d = "dsA", "dsB"
    cur_m, nxt_m = "mr", "mr2"
    lvl_w = (("vs", dwev), ("dG", mld), ("gs", ctx.cnt["gs"]))

    def shr(ap2d, sh):
        return bass.AP(tensor=ap2d.tensor, offset=ap2d.offset - sh,
                       ap=[list(d) for d in ap2d.ap])

    for k in range(7):
        sh = 1 << k
        mv = 0
        for (o_, a_, bsh, c2_, dsh) in ((0, 0, 0, 1, 1), (1, 0, 0, 1, 1),
                                        (2, 2, 0, 3, 1), (3, 2, 0, 3, 1)):
            pass
        W = 1536 - sh
        for o_, (x1, y1, x2, y2) in enumerate(((0, 0, 1, 2), (0, 1, 1, 3),
                                               (2, 0, 3, 2), (2, 1, 3, 3))):
            for q3 in range(3):
                c0 = q3 * 512
                lo = max(sh, c0) - c0
                if lo >= 512:
                    continue
                e("vector", lambda v, x1=x1, y1=y1, sh=sh, c0=c0, lo=lo: v.tensor_tensor(
                    out=sc["mtf"][0:1, c0 + lo:c0 + 512],
                    in0=sc[cur_m][x1][0:1, c0 + lo:c0 + 512],
                    in1=sc[cur_m][y1][0:1, c0 + lo - sh:c0 + 512 - sh],
                    op=OP.mult),
                    wait=lvl_w if (k == 0 and o_ == 0 and q3 == 0) else ())
                e("vector", lambda v, x2=x2, y2=y2, sh=sh, c0=c0, lo=lo: v.tensor_tensor(
                    out=ps["pB"][0:1, lo:512],
                    in0=sc[cur_m][x2][0:1, c0 + lo:c0 + 512],
                    in1=sc[cur_m][y2][0:1, c0 + lo - sh:c0 + 512 - sh],
                    op=OP.mult))
                mv = e("vector", lambda v, o_=o_, c0=c0, lo=lo: v.tensor_tensor(
                    out=sc[nxt_m][o_][0:1, c0 + lo:c0 + 512],
                    in0=sc["mtf"][0:1, c0 + lo:c0 + 512],
                    in1=ps["pB"][0:1, lo:512], op=OP.add))
        dv = 0
        for ci in range(2):
            for q3 in range(3):
                cs = q3 * 512
                t1 = e("tensor", lambda t, ci=ci, cs=cs: t.matmul(
                    ps["pA"][:, 0:512], ctx.onesrow_bf[0:1, 0:128],
                    sc[cur_m][0 + ci * 2][0:1, cs:cs + 512],
                    start=True, stop=True),
                    wait=(("vs", max(mv, dv)), ("vs", ctx.cnt["vs"])))
                t2 = e("tensor", lambda t, ci=ci, cs=cs: t.matmul(
                    ps["pB"][:, 0:512], ctx.onesrow_bf[0:1, 0:128],
                    sc[cur_m][1 + ci * 2][0:1, cs:cs + 512],
                    start=True, stop=True))
                lo = sh if q3 == 0 else 0
                e("vector", lambda v, ci=ci, cs=cs, lo=lo, sh=sh: v.tensor_tensor(
                    out=sc["dt"][0:64, lo:512],
                    in0=sc[cur_d][0:64, cs + lo - sh: cs + 512 - sh],
                    in1=ps["pA"][0:64, lo:512], op=OP.mult),
                    wait=(("ts", t2),))
                e("vector", lambda v, ci=ci, cs=cs, lo=lo, sh=sh: v.tensor_tensor(
                    out=sc["dt2"][0:64, lo:512],
                    in0=sc[cur_d][0:64, 1536 + cs + lo - sh: 1536 + cs + 512 - sh],
                    in1=ps["pB"][0:64, lo:512], op=OP.mult))
                e("vector", lambda v, ci=ci, cs=cs, lo=lo: v.tensor_tensor(
                    out=sc["dt"][0:64, lo:512], in0=sc["dt"][0:64, lo:512],
                    in1=sc["dt2"][0:64, lo:512], op=OP.add))
                dv = e("vector", lambda v, ci=ci, cs=cs, lo=lo: v.tensor_tensor(
                    out=sc[nxt_d][0:64, ci * 1536 + cs + lo:ci * 1536 + cs + 512],
                    in0=sc["dt"][0:64, lo:512],
                    in1=sc[cur_d][0:64, ci * 1536 + cs + lo:ci * 1536 + cs + 512],
                    op=OP.add))
        e("gpsimd", lambda g: g.memset(
            bass.AP(tensor=sc[nxt_d][0:64, 0:1].tensor,
                    offset=sc[nxt_d][0:64, 0:1].offset,
                    ap=[list(sc[nxt_d][0:64, 0:1].ap[0]), [1536, 2], [192, 8],
                        [1, 64]]), 0.0),
            wait=(("vs", dv),))
        padfix(nxt_m, k)
        cur_d, nxt_d = nxt_d, cur_d
        cur_m, nxt_m = nxt_m, cur_m
        lvl_w = (("gs", ctx.cnt["gs"]),)
    if sub < 3:
        return ctx.cnt["vs"]
    dcp = 0
    wst = 0
    for h in range(8):
        tvw = 0
        for c_ in range(2):
            tvw = e("tensor", lambda t, h=h, c_=c_: t.transpose(
                ps["pg"][:, c_ * 64:(c_ + 1) * 64],
                sc[cur_d][0:64, c_ * 1536 + h * 192 + 64:
                          c_ * 1536 + h * 192 + 64 + 128],
                ctx.ident_f32[0:64, 0:64]),
                wait=((("gs", ctx.cnt["gs"]),) if h == 0 else ())
                + ((("vs", dcp),) if c_ == 0 else ()))
        dcp = e("vector", lambda v: v.tensor_copy(sc["wcp"][:, 0:128],
                                                  ps["pg"][:, 0:128]),
                wait=(("ts", tvw), ("st", wst)))
        wst = ctx.emit("sync", lambda q, h=h: q.dma_start(
            out=w_dram.ap()[h * 128:(h + 1) * 128, :], in_=sc["wcp"][:, 0:128]),
            wait=(("vs", dcp),), dma=True, store=True)
    mtmp = sc["ostm"]
    e("gpsimd", lambda g: g.iota(mtmp[:, 0:32], pattern=[[1, 32]], base=0,
                                 channel_multiplier=0,
                                 allow_small_or_imprecise_dtypes=True),
      wait=(("vs", ctx.cnt["vs"]), ("ss", ctx.cnt["ss"]), ("ts", ctx.cnt["ts"])))
    e("gpsimd", lambda g: g.iota(mtmp[:, 32:33], pattern=[[1, 1]], base=0,
                                 channel_multiplier=1,
                                 allow_small_or_imprecise_dtypes=True))
    ctx.wait("vector", ("gs", ctx.cnt["gs"]))
    for i, thr in enumerate((32.0, 64.0, 96.0)):
        e("vector", lambda v, i=i, thr=thr: v.tensor_scalar(
            out=mtmp[:, 33 + i:34 + i], in0=mtmp[:, 32:33], scalar1=thr,
            scalar2=None, op0=OP.is_ge))
    e("vector", lambda v: v.tensor_tensor(
        out=mtmp[:, 36:37], in0=mtmp[:, 33:34], in1=mtmp[:, 34:35], op=OP.add))
    e("vector", lambda v: v.tensor_tensor(
        out=mtmp[:, 36:37], in0=mtmp[:, 36:37], in1=mtmp[:, 35:36], op=OP.add))
    e("vector", lambda v: v.scalar_tensor_tensor(
        out=mtmp[:, 36:37], in0=mtmp[:, 36:37], scalar=-32.0,
        in1=mtmp[:, 32:33], op0=OP.mult, op1=OP.add))
    e("vector", lambda v: v.tensor_scalar(
        out=sc["msk32"][:, :], in0=mtmp[:, 0:32], scalar1=mtmp[:, 36:37],
        scalar2=None, op0=OP.is_equal))
    if sub < 4:
        return ctx.cnt["vs"]
    abrd = 0
    wrd_c = 0
    war_pab = 0
    for g4 in range(NG4):
        abrd = ctx.emit("sync", lambda q, g4=g4: q.dma_start(
            out=sc["abg"][0:4, :], in_=ab_dram.ap()[g4 * 4:(g4 + 1) * 4, :]),
            wait=(("st", ast), ("ts", war_pab))
            + ((("vs", ctx.cnt["vs"]),) if g4 == 0 else ()), dma=True)
        if g4 > 0:
            wv_ = ctx.emit("sync", lambda q, g4=g4: q.dma_start(
                out=sc["wrd"][0:4, 0:1024].rearrange("p (h x) -> p h x", x=128),
                in_=bass.AP(tensor=w_dram.ap().tensor,
                            offset=(g4 * 4 - 1) * 128,
                            ap=[[128, 4], [128 * 128, 8], [1, 128]])),
                wait=(("st", wst), ("ts", ctx.cnt["ts"])), dma=True)
            wwait = ("dS", wv_)
        else:
            gv_ = e("gpsimd", lambda g: g.memset(sc["wrd"][0:4, 0:1024], 0.0),
                    wait=(("ts", ctx.cnt["ts"]),))
            wv_ = ctx.emit("sync", lambda q: q.dma_start(
                out=sc["wrd"][1:4, 0:1024].rearrange("p (h x) -> p h x", x=128),
                in_=bass.AP(tensor=w_dram.ap().tensor, offset=0,
                            ap=[[128, 3], [128 * 128, 8], [1, 128]])),
                wait=(("st", wst), ("gs", gv_)), dma=True)
            wwait = ("dS", wv_)
        if SUB4 < 3:
            wrd_c = e("vector", lambda v: v.tensor_copy(
                sc["abp"][:, 0:16], sc["abp"][:, 0:16]),
                wait=(("ts", war_pab), ("dS", wv_ if isinstance(wwait, tuple)
                                        and False else wwait[1])))
            continue
        for half in range(2):
            e("tensor", lambda t, half=half: t.matmul(
                (ps["pA"] if half == 0 else ps["pB"])[:, 0:512],
                sc["rep4f"][0:4, 0:128],
                sc["wrd"][0:4, half * 512:(half + 1) * 512],
                start=True, stop=True),
                wait=(wwait, ("vs", ctx.cnt["vs"])))
        if SUB4 < 2:
            wrd_c = e("vector", lambda v: v.tensor_copy(
                sc["abp"][:, 0:16], sc["abg"][0:4, 0:16][0:1, :]
                if False else sc["abp"][:, 0:16]), wait=(("dS", abrd),))
            continue
        war_pab = e("tensor", lambda t: t.matmul(
            ps["pab"][:, 0:512], sc["rep4f"][0:4, 0:128], sc["abg"][0:4, 0:512],
            start=True, stop=True),
            wait=(("dS", abrd), ("vs", wrd_c)))
        e("vector", lambda v: v.tensor_tensor(
            out=sc["abll"][:, :].rearrange("p (c j) -> p c j", j=32),
            in0=ps["pab"][:, 0:512].rearrange("p (c j) -> p c j", j=32),
            in1=bass.AP(tensor=sc["msk32"][:, 0:1].tensor,
                        offset=sc["msk32"][:, 0:1].offset,
                        ap=[list(sc["msk32"][:, 0:1].ap[0]), [0, 16], [1, 32]]),
            op=OP.mult), wait=(("ts", war_pab),))
        abred = e("vector", lambda v: v.tensor_reduce(
            out=sc["abp"][:, 0:16],
            in_=sc["abll"][:, :].rearrange("p (c j) -> p c j", j=32),
            axis=AX.X, op=OP.add))
        if SUB4 < 4:
            wrd_c = abred
            continue
        tvb = ctx.cnt["ts"]
        wA = ps["pA"][:, 0:512].rearrange("p (h c d) -> p h c d", c=2, d=64)
        wB = ps["pB"][:, 0:512].rearrange("p (h c d) -> p h c d", c=2, d=64)
        ot = 0
        for half, wpl in ((0, wA), (1, wB)):
            e("vector", lambda v, wpl=wpl, half=half: v.tensor_tensor(
                out=sc["ostm"][:, half * 256:(half + 1) * 256]
                .rearrange("p (h d) -> p h d", d=64),
                in0=wpl[:, :, 0, :],
                in1=bass.AP(tensor=sc["abp"][:, half * 4:half * 4 + 1].tensor,
                            offset=sc["abp"][:, half * 4:half * 4 + 1].offset,
                            ap=[list(sc["abp"][:, 0:1].ap[0]), [1, 4], [0, 64]]),
                op=OP.mult), wait=(("ts", tvb), ("vs", abred)))
            e("vector", lambda v, wpl=wpl, half=half: v.tensor_tensor(
                out=sc["ostm2"][:, half * 256:(half + 1) * 256]
                .rearrange("p (h d) -> p h d", d=64),
                in0=wpl[:, :, 1, :],
                in1=bass.AP(tensor=sc["abp"][:, 8 + half * 4:8 + half * 4 + 1].tensor,
                            offset=sc["abp"][:, 8 + half * 4:8 + half * 4 + 1].offset,
                            ap=[list(sc["abp"][:, 0:1].ap[0]), [1, 4], [0, 64]]),
                op=OP.mult))
        ot = e("vector", lambda v: v.tensor_tensor(
            out=sc["ostm"][:, :], in0=sc["ostm"][:, :], in1=sc["ostm2"][:, :],
            op=OP.add))
        ot = e("vector", lambda v: v.tensor_copy(sc["ostb"][:, :], sc["ostm"][:, :]))
        wrd_c = ot
        if SUB4 < 5:
            continue
        for ch in range(4):
            tvx = e("tensor", lambda t, ch=ch: t.transpose(
                ps["ptx"][:, 0:128], sc["ostb"][:, ch * 128:(ch + 1) * 128],
                ctx.ident_bf[:, :]),
                wait=(("vs", ot),) if ch == 0 else (("vs", ctx.cnt["vs"]),))
            e("vector", lambda v, ch=ch, g4=g4: v.tensor_tensor(
                out=o_fm[:, ch * Ttok + g4 * 128: ch * Ttok + g4 * 128 + 128],
                in0=o_fm[:, ch * Ttok + g4 * 128: ch * Ttok + g4 * 128 + 128],
                in1=ps["ptx"][:, 0:128], op=OP.add),
                wait=(("ts", tvx), ("ss", oev)))
    if sub < 5:
        return ctx.cnt["vs"]
    fin = 0
    for ch in range(4):
        for gg in range(Ttok // 512):
            fin = e("vector", lambda v, ch=ch, gg=gg: v.tensor_tensor(
                out=o_fm[:, ch * Ttok + gg * 512: ch * Ttok + gg * 512 + 512],
                in0=o_fm[:, ch * Ttok + gg * 512: ch * Ttok + gg * 512 + 512],
                in1=qg_fm[:, ch * Ttok + gg * 512: ch * Ttok + gg * 512 + 512],
                op=OP.mult))
    return fin


def build_delta_front_launch(stage=8):
    nc = bass.Bass("TRN2", target_bir_lowering=False, debug=False,
                   num_devices=NCORES)
    xnin = nc.dram_tensor("xnin", [D, T], BF16, kind="ExternalInput").ap()
    wq = nc.dram_tensor("wq", [D, 512], F32, kind="ExternalInput").ap()
    wk = nc.dram_tensor("wk", [D, 512], F32, kind="ExternalInput").ap()
    wv = nc.dram_tensor("wv", [D, 512], F32, kind="ExternalInput").ap()
    wg = nc.dram_tensor("wg", [D, 512], F32, kind="ExternalInput").ap()
    wb = nc.dram_tensor("wb", [D, 24], F32, kind="ExternalInput").ap()
    brow = nc.dram_tensor("brow", [1, 24], F32, kind="ExternalInput").ap()
    rep4 = nc.dram_tensor("rep4", [4, 128], F32, kind="ExternalInput").ap()
    ofm = nc.dram_tensor("ofm", [512, T], BF16, kind="ExternalOutput").ap()
    fs_dram = nc.dram_tensor("fs_dram", [16, T], F32)
    g_dram = nc.dram_tensor("g_dram", [128, 8704], BF16)
    ab_dram = nc.dram_tensor("ab_dram", [128, 512], F32)
    hv_dram = nc.dram_tensor("hv_dram", [32, 128], F32)
    w_dram = nc.dram_tensor("w_dram", [8 * 128, 128], F32)
    import contextlib
    es = contextlib.ExitStack()

    def sb(name, shape, dt):
        return es.enter_context(nc.sbuf_tensor(name, shape, dt))

    def pst(name, shape, dt=F32):
        return es.enter_context(nc.psum_tensor(name, shape, dt))

    with es:
        xs_sb = sb("xs_sb", [128, 2 * 1024], BF16)
        q_fm = sb("q_fm", [128, 4 * T], BF16)
        g_fm = sb("g_fm", [128, 4 * T], BF16)
        u_tm = sb("u_tm", [128, 32 * 512], BF16)
        wun = sb("wun", [128, 4 * 4096 + 8 * 24], BF16)
        wq_sb = wun[:, 0:4096]
        wk_sb = wun[:, 4096:8192]
        wv_sb = wun[:, 8192:12288]
        wg_sb = wun[:, 12288:16384]
        wb_sb = wun[:, 16384:16384 + 192]
        Gall = wun[:, 0:8704]
        poolf = sb("poolf", [128, 1024], F32)
        dtx = sb("dtx", [128, 512], F32)
        dt2x = sb("dt2x", [128, 512], F32)
        scr3 = sb("scr3", [128, 1536], BF16)
        sc = {
            "ktm": scr3[:, 0:512],
            "vtm": scr3[:, 512:1024],
            "utm": scr3[:, 1024:1536],
            "bfs": sb("bfs", [128, 24], F32),
            "fsrow": sb("fsrow", [8, 128], F32),
            "Fpl": poolf[:, 512:768],
            "Spl": poolf[:, 768:1024],
            "H1": poolf[:, 0:256],
            "H2": poolf[:, 256:512],
            "T1": dtx[:, 0:256],
            "T2": dt2x[:, 0:256],
            "Hv": sb("Hv", [128, 32], F32),
            "tva": sb("tva", [128, 16], F32),
            "abll": sb("abll", [128, 512], F32),
            "dsA": sb("dsA", [64, 3072], F32),
            "dsB": sb("dsB", [64, 3072], F32),
            "dt": dtx,
            "dt2": dt2x,
            "mtf": sb("mtf", [1, 1536], F32),
            "abp": sb("abp", [128, 16], F32),
            "wrd": sb("wrd", [4, 1024], F32),
            "ostm": poolf[:, 0:512],
            "ostm2": poolf[:, 512:1024],
            "ostb": sb("ostb", [128, 512], BF16),
        }
        sc["Grb"] = scr3[:, 0:1088]
        sc.update({
            "wcp": sb("wcp", [128, 128], F32),
            "hvt": sb("hvt", [32, 128], F32),
            "rep4f": sb("rep4f", [4, 128], F32),
        })
        sc["msk32"] = sc["Hv"]
        sc["abg"] = sc["dsA"][0:4, 0:512]
        rep4_sb = sc["rep4f"]
        sc["mr"] = [sb(f"mr{i}", [1, 1536], BF16) for i in range(4)]
        sc["mr2"] = [sb(f"ms{i}", [1, 1536], BF16) for i in range(4)]
        sc["Gall"] = Gall

        brow_sb = sb("brow_sb", [1, 24], BF16)
        ident = sb("ident", [128, 128], BF16)
        identf = sb("identf", [128, 128], F32)
        onesb = sb("onesb", [128, 1], F32)
        onesrow = sc["fsrow"][0:1, 0:128]
        onesrowb = sb("onesrowb", [1, 128], BF16)
        cbank = sb("cbank", [128, 8], F32)
        sc["onesbf"] = onesrowb
        ps = {
            "pk": pst("pk", [128, 512]),
            "pv": pst("pv", [128, 512]),
            "pb": pst("pb", [128, 24]),
            "pq": pst("pq", [128, 128]),
            "pdo": pst("pdo", [64, 136]),
            "ptx": pst("ptx", [128, 128], BF16),
            "pg": pst("pgx", [128, 128]),
        }
        ps["pab"] = pst("pab", [128, 512])
        ps["pA"] = ps["pk"]
        ps["pB"] = ps["pv"]
        ps["pt8"] = ps["pg"][0:8, :]
        ps["pth"] = ps["pg"][0:32, :]
        block = es.enter_context(nc.Block())
        ctx = Ctx(nc, block)
        ctx.cbank = cbank
        ctx.onesrow_bf_t = onesrowb
        ctx.u_tm = u_tm[:, :]
        ctx.q_fm = q_fm[:, :]
        ctx.g_fm = g_fm[:, :]
        ctx.brow_sb = brow_sb
        emit_prelude(ctx, ident, onesb, onesrow, identf)
        wld = 0
        for d_, s_ in ((wq, wq_sb), (wk, wk_sb), (wv, wv_sb), (wg, wg_sb)):
            wld = ctx.emit("gpsimd", lambda g, d_=d_, s_=s_: g.dma_start(
                out=s_.rearrange("p (c f) -> p c f", c=NCH),
                in_=d_.rearrange("(c p) f -> p c f", p=128)), dma=True)
        wld = ctx.emit("gpsimd", lambda g: g.dma_start(
            out=wb_sb.rearrange("p (c f) -> p c f", c=NCH),
            in_=wb.rearrange("(c p) f -> p c f", p=128)), dma=True)
        wld = ctx.emit("gpsimd", lambda g: g.dma_start(
            out=brow_sb[:, :], in_=brow), dma=True)
        wld = ctx.emit("gpsimd", lambda g: g.dma_start(
            out=sc["rep4f"][0:4, :], in_=rep4), dma=True)
        w_ready = (("dG", wld), ("gs", ctx.cnt["gs"]))
        fs_st = emit_delta_front(ctx, xnin, xs_sb, wq_sb, wk_sb, wv_sb, wg_sb,
                                 wb_sb, None, None, None, rep4_sb, None,
                                 fs_dram.ap(), None, None, None, None,
                                 ps, sc, w_ready)
        qgv = 0
        for ch in range(4):
            for gg in range(T // 512):
                qgv = ctx.emit("vector", lambda v, ch=ch, gg=gg: v.scalar_tensor_tensor(
                    out=g_fm[:, ch * T + gg * 512: ch * T + gg * 512 + 512],
                    in0=q_fm[:, ch * T + gg * 512: ch * T + gg * 512 + 512],
                    scalar=0.5, in1=g_fm[:, ch * T + gg * 512: ch * T + gg * 512 + 512],
                    op0=OP.mult, op1=OP.mult),
                    wait=(("ss", ctx.cnt["ss"]),) if (ch == 0 and gg == 0) else ())
        o_fm = q_fm[:, :]
        zf = ctx.emit("gpsimd", lambda g: g.memset(o_fm, 0.0),
                      wait=(("vs", qgv),))
        ctx.wait("vector", ("st", fs_st), ("gs", zf))
        fin = ("gs", zf)
        if stage >= 2:
            gb = emit_delta_scan(ctx, fs_dram, g_dram, ab_dram, hv_dram,
                                 w_dram, rep4_sb, o_fm, ps, sc)
            fin = ("vs", gb)
        if stage >= 3:
            dwev, oev, hst, ast = emit_delta_scan2(
                ctx, g_dram, ab_dram, hv_dram, w_dram, rep4_sb, o_fm, ps, sc, gb)
            fin = ("ss", oev)
        if stage >= 4:
            f4 = emit_delta_scan3(ctx, ab_dram, hv_dram, w_dram, rep4_sb, o_fm,
                                  g_fm, ps, sc, dwev, oev, hst, ast,
                                  sub=stage - 3)
            fin = ("vs", f4)
        ctx.emit("sync", lambda q: q.dma_start(
            out=ofm.rearrange("(c p) t -> p c t", p=128),
            in_=o_fm.rearrange("p (c t) -> p c t", c=4)),
            wait=(fin, ("vs", ctx.cnt["vs"]), ("ss", ctx.cnt["ss"])),
            dma=True, store=True)
        ctx.wait("sync", ("st", ctx.cnt["st"]))
    return nc


def build_epi_launch(kind="plain"):
    nc = bass.Bass("TRN2", target_bir_lowering=False, debug=False,
                   num_devices=NCORES)
    xin = nc.dram_tensor("xin", [D, TL], F32, kind="ExternalInput").ap()
    if kind != "ln":
        ofu = nc.dram_tensor("ofu", [D, TL], BF16, kind="ExternalInput").ap()
        warr = nc.dram_tensor("warr", [D, D], F32, kind="ExternalInput").ap()
        xout = nc.dram_tensor("xout", [D, TL], F32, kind="ExternalOutput").ap()
    if kind == "bias":
        bpc = nc.dram_tensor("bpc", [128, 8], F32, kind="ExternalInput").ap()
    xnout = nc.dram_tensor("xnout", [D, TL], BF16, kind="ExternalOutput").ap()
    es = contextlib.ExitStack()

    def sb(name, shape, dt):
        return es.enter_context(nc.sbuf_tensor(name, shape, dt))

    def pst(name, shape, dt=F32):
        return es.enter_context(nc.psum_tensor(name, shape, dt))

    with es:
        x_sb = sb("x_sb", [128, NCH * TL], F32)
        xn_sb = sb("xn_sb", [128, NCH * TL], BF16)
        if kind != "ln":
            o_sb = sb("o_sb", [128, NCH * TL], BF16)
            w_sb = sb("w_sb", [128, NCH * 1024], BF16)
        if kind == "bias":
            bcc = sb("bcc", [128, 8], F32)
        sqbuf = sb("sqbuf", [128, 1024], F32)
        rows = sb("rows", [1, 3 * TL], F32)
        ident = sb("ident", [128, 128], BF16)
        onesb = sb("onesb", [128, 1], F32)
        onesrow = sb("onesrow", [1, 128], F32)
        cbank = sb("cbank", [128, 8], F32)
        ps_s1 = pst("ps_s1", [1, 512])
        ps_s2 = pst("ps_s2", [1, 512])
        ps_bc = pst("ps_bc", [128, 1024])
        pm = [pst("pm0", [128, 512]), pst("pm1", [128, 512])]
        block = es.enter_context(nc.Block())
        ctx = Ctx(nc, block)
        ctx.cbank = cbank
        emit_prelude(ctx, ident, onesb, onesrow)
        ctx.constcol(1e-5)
        xv = ctx.emit("sync", lambda q: q.dma_start(
            out=x_sb[:, :].rearrange("p (c t) -> p c t", c=NCH),
            in_=xin.rearrange("(c p) t -> p c t", p=128)), dma=True)
        ev = 0
        if kind != "ln":
            ov = ctx.emit("sync", lambda q: q.dma_start(
                out=o_sb[:, :].rearrange("p (c t) -> p c t", c=NCH),
                in_=ofu.rearrange("(c p) t -> p c t", p=128)), dma=True)
            wv = ctx.emit("gpsimd", lambda g: g.dma_start(
                out=w_sb[:, :].rearrange("p (c f) -> p c f", c=NCH),
                in_=warr.rearrange("(c p) f -> p c f", p=128)), dma=True)
            if kind == "bias":
                bv = ctx.emit("sync", lambda q: q.dma_start(
                    out=bcc[:, :], in_=bpc), dma=True)
            rd = [0, 0]
            for tg in range(TL // 512):
                for jc in range(NCH):
                    bi = (tg * NCH + jc) % 2
                    tv = 0
                    for kc in range(NCH):
                        tv = ctx.emit("tensor", lambda t, tg=tg, jc=jc, kc=kc, bi=bi: t.matmul(
                            pm[bi][:, 0:512],
                            w_sb[:, kc * 1024 + jc * 128: kc * 1024 + jc * 128 + 128],
                            o_sb[:, kc * TL + tg * 512: kc * TL + tg * 512 + 512],
                            start=(kc == 0), stop=(kc == NCH - 1)),
                            wait=(("dG", wv), ("dS", ov), ("vs", rd[bi]))
                            if kc == 0 else ())
                    if kind == "bias":
                        bb = bcc[:, jc:jc + 1]
                        bbc = bass.AP(tensor=bb.tensor, offset=bb.offset,
                                      ap=[list(bb.ap[0]), [0, 512]])
                        ctx.emit("vector", lambda v, tg=tg, jc=jc, bi=bi, bbc=bbc: v.tensor_tensor(
                            out=x_sb[:, jc * TL + tg * 512: jc * TL + tg * 512 + 512],
                            in0=x_sb[:, jc * TL + tg * 512: jc * TL + tg * 512 + 512],
                            in1=bbc, op=OP.add),
                            wait=(("dS", max(xv, bv)),) if (tg == 0 and jc == 0) else ())
                    ev = ctx.emit("vector", lambda v, tg=tg, jc=jc, bi=bi: v.tensor_tensor(
                        out=x_sb[:, jc * TL + tg * 512: jc * TL + tg * 512 + 512],
                        in0=x_sb[:, jc * TL + tg * 512: jc * TL + tg * 512 + 512],
                        in1=pm[bi][:, 0:512], op=OP.add),
                        wait=(("ts", tv),) + ((("dS", xv),) if (tg == 0 and jc == 0 and kind != "bias") else ()))
                    rd[bi] = ev
            lnready = (("vs", ev), ("gs", ctx.cnt["gs"]))
        else:
            lnready = (("dS", xv), ("gs", ctx.cnt["gs"]))
        xn_ready = emit_ln(ctx, x_sb, xn_sb, TL, ps_s1, ps_s2, sqbuf, rows,
                           ps_bc, ready=lnready)
        if kind != "ln":
            ctx.emit("sync", lambda q: q.dma_start(
                out=xout.rearrange("(c p) t -> p c t", p=128),
                in_=x_sb[:, :].rearrange("p (c t) -> p c t", c=NCH)),
                wait=(("vs", ev),), dma=True, store=True)
        ctx.emit("sync", lambda q: q.dma_start(
            out=xnout.rearrange("(c p) t -> p c t", p=128),
            in_=xn_sb[:, :].rearrange("p (c t) -> p c t", c=NCH)),
            wait=(("vs", ctx.cnt["vs"]),), dma=True, store=True)
        ctx.wait("sync", ("st", ctx.cnt["st"]))
    return nc


from concourse.aot_env import allow_ici_all_to_all

PAIRS = [[0, 1], [2, 3], [4, 5], [6, 7]]
ALL8 = [list(range(NCORES))]


def _barrier(ctx):
    snap = tuple((k, ctx.cnt[k]) for k in
                 ("ts", "vs", "ss", "gs", "dS", "dG", "cc", "st"))
    for eng in ("tensor", "vector", "scalar", "gpsimd", "sync"):
        ctx.wait(eng, *snap)


def build_mega():
    nc = bass.Bass("TRN2", target_bir_lowering=False, debug=False,
                   num_devices=NCORES)

    def EI(n, s, d):
        return nc.dram_tensor(n, s, d, kind="ExternalInput").ap()

    xin = EI("xin", [D, TL], F32)
    gla_w = {n: EI("gla_" + n, [D, 512], F32)
             for n in ("wq", "wk", "wv", "wz")}
    gla_bg = EI("gla_bg", [128, 4], F32)
    gla_wo = EI("gla_wo", [D, D], F32)
    win_wq = EI("win_wq", [D, 1024], F32)
    win_wk = EI("win_wk", [D, 1024], F32)
    win_wv = EI("win_wv", [D, 512], F32)
    win_gq = EI("win_gq", [128, 4], F32)
    win_gk = EI("win_gk", [128, 4], F32)
    win_slp = EI("win_slp", [1, 16], F32)
    win_wo = EI("win_wo", [D, D], F32)
    moe_wgt = EI("moe_wgt", [D, 8], F32)
    w1e = EI("w1e", [D, 4096], F32)
    w2e = EI("w2e", [2048, D], F32)
    tblidx = EI("tblidx", [128, CAP // 128], U32)
    mycol = EI("mycol", [8, 1], F32)
    dl_w = {n: EI("dl_" + n, [D, 512], F32)
            for n in ("wq", "wk", "wv", "wg")}
    dl_wb = EI("dl_wb", [D, 24], F32)
    dl_brow = EI("dl_brow", [1, 24], F32)
    dl_rep4 = EI("dl_rep4", [4, 128], F32)
    dl_wo = EI("dl_wo", [D, D], F32)
    at_wq = EI("at_wq", [D, 512], F32)
    at_wk = EI("at_wk", [D, 512], F32)
    at_wv = EI("at_wv", [D, 512], F32)
    at_sincos = EI("at_sincos", [128, 32 * 64], F32)
    at_wp = EI("at_wp", [D, D], F32)
    at_bp = EI("at_bp", [128, 8], F32)
    f_w1 = EI("f_w1", [D, 4096], F32)
    f_w2 = EI("f_w2", [4096, D], F32)
    f_qpv = EI("f_qpv", [128, 160], F32)
    f_b2v = EI("f_b2v", [128, 8], F32)
    xout = nc.dram_tensor("xout", [D, TL], F32, kind="ExternalOutput").ap()
    x_d = nc.dram_tensor("x_d", [D, TL], F32)
    xnmy_d = nc.dram_tensor("xnmy_d", [D, TL], BF16)
    xnp_d = nc.dram_tensor("xnp_d", [2 * D, TL], BF16)
    xnf_d = nc.dram_tensor("xnf_d", [D, T], BF16)
    oa_in_d = nc.dram_tensor("oa_in_d", [512, T], BF16)
    og_d = nc.dram_tensor("og_d", [2 * D, TL], BF16)
    oidx = EI("oidx", [128, 8], U32)
    od2_d = nc.dram_tensor("od2_d", [8 * T, 64], BF16)
    xntm_my_d = nc.dram_tensor("xntm_my_d", [TL, D], BF16)
    xn_all_d = nc.dram_tensor("xn_all_d", [8 * TL, D], BF16)
    rout_my_d = nc.dram_tensor("rout_my_d", [TL, 8], F32)
    rout_all_d = nc.dram_tensor("rout_all_d", [8 * TL, 8], F32)
    totmy_d = nc.dram_tensor("totmy_d", [8, 1], F32)
    totall_d = nc.dram_tensor("totall_d", [64, 1], F32)
    expo_d = nc.dram_tensor("expo_d", [CAP, D], BF16)
    expo_all_d = nc.dram_tensor("expo_all_d", [8 * CAP, D], BF16)
    table_d = nc.dram_tensor("table_d", [8 * CAP, 4], U32)
    fs_dram = nc.dram_tensor("fs_dram", [16, T], F32)
    g_dram = nc.dram_tensor("g_dram", [128, 8704], BF16)
    ab_dram = nc.dram_tensor("ab_dram", [128, 512], F32)
    hv_dram = nc.dram_tensor("hv_dram", [32, 128], F32)
    w_dram = nc.dram_tensor("w_dram", [8 * 128, 128], F32)

    top = contextlib.ExitStack()
    with top:
        ident = top.enter_context(nc.sbuf_tensor("P_ident", [128, 128], BF16))
        identf = top.enter_context(nc.sbuf_tensor("P_identf", [128, 128], F32))
        onesb = top.enter_context(nc.sbuf_tensor("P_onesb", [128, 1], F32))
        onesrow = top.enter_context(nc.sbuf_tensor("P_onesrow", [1, 128], F32))
        onesrowb = top.enter_context(nc.sbuf_tensor("P_onesrowb", [1, 128], BF16))
        cbank = top.enter_context(nc.sbuf_tensor("P_cbank", [128, 4], F32))
        block = top.enter_context(nc.Block())
        ctx = Ctx(nc, block)
        ctx.cbank = cbank
        ctx.onesrow_bf_t = onesrowb
        emit_prelude(ctx, ident, onesb, onesrow, identf)
        ctx.constcol(1e-5)
        ctx.constcol(HALF_PI)
        ctx.constcol(1e-6)
        _barrier(ctx)

        def sget(es, pfx):
            def sb(name, shape, dt):
                return es.enter_context(nc.sbuf_tensor(pfx + name, shape, dt))

            def pst(name, shape, dt=F32):
                return es.enter_context(nc.psum_tensor(pfx + name, shape, dt))
            return sb, pst

        def sec_ln(pfx, x_src):
            es = contextlib.ExitStack()
            with es:
                sb, pst = sget(es, pfx)
                x_sb = sb("x", [128, NCH * TL], F32)
                xn_sb = sb("xn", [128, NCH * TL], BF16)
                sqbuf = sb("sq", [128, 1024], F32)
                rows = sb("rows", [1, 3 * TL], F32)
                ps_s1 = pst("s1", [1, 512])
                ps_s2 = pst("s2", [1, 512])
                ps_bc = pst("bc", [128, 1024])
                xv = ctx.emit("sync", lambda q: q.dma_start(
                    out=x_sb[:, :].rearrange("p (c t) -> p c t", c=NCH),
                    in_=x_src.rearrange("(c p) t -> p c t", p=128)), dma=True)
                done = emit_ln(ctx, x_sb, xn_sb, TL, ps_s1, ps_s2, sqbuf, rows,
                               ps_bc, ready=(("dS", xv), ("gs", ctx.cnt["gs"])))
                ctx.emit("sync", lambda q: q.dma_start(
                    out=xnmy_d.ap().rearrange("(c p) t -> p c t", p=128),
                    in_=xn_sb[:, :].rearrange("p (c t) -> p c t", c=NCH)),
                    wait=(("vs", done),), dma=True, store=True)
                ctx.wait("sync", ("st", ctx.cnt["st"]))
            _barrier(ctx)

        def xn_exchange():
            ctx.emit("gpsimd", lambda g: g.collective_compute(
                "AllGather", mybir.AluOpType.bypass, replica_groups=PAIRS,
                ins=[xnmy_d.ap().opt()], outs=[xnp_d.ap().opt()]), coll=True)
            cc = ctx.cnt["cc"]
            for s_ in range(2):
                ctx.emit("sync", lambda q, s_=s_: q.dma_start(
                    out=bass.AP(tensor=xnf_d.ap().tensor, offset=s_ * TL,
                                ap=[[T, D], [1, TL]]),
                    in_=xnp_d.ap()[s_ * D:(s_ + 1) * D, :]),
                    wait=(("cc", cc),), dma=True)
            ctx.wait("sync", ("dS", ctx.cnt["dS"]))
            _barrier(ctx)

        def store_o_a2a(o_fm):
            ctx.emit("sync", lambda q: q.dma_start(
                out=oa_in_d.ap().rearrange("(c p) t -> p c t", p=128),
                in_=o_fm.rearrange("p (c t) -> p c t", c=4)),
                dma=True, store=True)
            ctx.wait("sync", ("st", ctx.cnt["st"]))
            _barrier(ctx)
            ctx.emit("gpsimd", lambda g: g.collective_compute(
                "AllGather", mybir.AluOpType.bypass, replica_groups=PAIRS,
                ins=[oa_in_d.ap().opt()], outs=[og_d.ap().opt()]),
                coll=True)
            _barrier(ctx)

        def sec_gla(pfx):
            es = contextlib.ExitStack()
            with es:
                sb, pst = sget(es, pfx)
                xn_sb = sb("xn", [128, NCH * T], BF16)
                o_fm = sb("ofm", [128, 4 * T], BF16)
                wq_sb = sb("wq", [128, 8 * 512], BF16)
                wk_sb = sb("wk", [128, 8 * 512], BF16)
                wv_sb = sb("wv", [128, 8 * 512], BF16)
                wz_sb = sb("wz", [128, 8 * 512], BF16)
                bgate = sb("bg", [128, 4], F32)
                S_sb = sb("S", [128, 256], F32)
                sc = {
                    "qg": sb("qg", [128, 4 * 512], BF16),
                    "kg": sb("kg", [128, 4 * 512], BF16),
                    "vg": sb("vg", [128, 4 * 512], BF16),
                    "lag": sb("lag", [128, 4 * 512], F32),
                    "Bt": sb("Bt", [128, 128], F32),
                    "Dex": sb("Dex", [128, 128], F32),
                    "Div": sb("Div", [128, 128], F32),
                    "qt32": sb("qt32", [128, 128], F32),
                    "qtb": sb("qtb", [128, 128], BF16),
                    "khat": sb("khat", [128, 128], BF16),
                    "ktld": sb("ktld", [128, 128], BF16),
                    "ktm": sb("ktm", [128, 128], BF16),
                    "prb": sb("prb", [128, 128], BF16),
                    "asb": sb("asb", [128, 512], F32),
                }
                ps = {
                    "pq": pst("pq", [128, 512]),
                    "pk": pst("pk", [128, 512]),
                    "pv": pst("pv", [128, 512]),
                    "pz": pst("pz", [128, 512]),
                    "pscore": pst("psc", [128, 128]),
                    "po": pst("po", [64, 128]),
                    "pds": pst("pds", [64, 64]),
                    "ptr": pst("ptr", [128, 128], BF16),
                }
                xv = ctx.emit("sync", lambda q: q.dma_start(
                    out=xn_sb[:, :].rearrange("p (c t) -> p c t", c=NCH),
                    in_=xnf_d.ap().rearrange("(c p) t -> p c t", p=128)),
                    dma=True)
                bv = ctx.emit("sync", lambda q: q.dma_start(
                    out=bgate[:, :], in_=gla_bg), dma=True)
                wvv = 0
                for w_d, w_s in ((gla_w["wq"], wq_sb), (gla_w["wk"], wk_sb),
                                 (gla_w["wv"], wv_sb), (gla_w["wz"], wz_sb)):
                    wvv = ctx.emit("gpsimd", lambda g, w_d=w_d, w_s=w_s: g.dma_start(
                        out=w_s[:, :].rearrange("p (c f) -> p c f", c=NCH),
                        in_=w_d.rearrange("(c p) f -> p c f", p=128)), dma=True)
                w_ready = (("dS", xv), ("dS", bv), ("dG", wvv))
                done = emit_gla_front(ctx, xn_sb, wq_sb, wk_sb, wv_sb, wz_sb,
                                      bgate, o_fm, S_sb, ps, sc, w_ready)
                ctx.wait("sync", ("vs", done))
                store_o_a2a(o_fm)

        def sec_epi(pfx, x_src, warr_ap, x_dst, bias_ap=None, store_xn=True,
                    otm=False):
            es = contextlib.ExitStack()
            with es:
                sb, pst = sget(es, pfx)
                x_sb = sb("x", [128, NCH * TL], F32)
                xn_sb = sb("xn", [128, NCH * TL], BF16)
                o_sb = sb("o", [128, NCH * TL], BF16)
                w_sb = sb("w", [128, NCH * 1024], BF16)
                if bias_ap is not None:
                    bcc = sb("bcc", [128, 8], F32)
                sqbuf = sb("sq", [128, 1024], F32)
                rows = sb("rows", [1, 3 * TL], F32)
                ps_s1 = pst("s1", [1, 512])
                ps_s2 = pst("s2", [1, 512])
                ps_bc = pst("bc", [128, 1024])
                pm = [pst("pm0", [128, 512]), pst("pm1", [128, 512])]
                xv = ctx.emit("sync", lambda q: q.dma_start(
                    out=x_sb[:, :].rearrange("p (c t) -> p c t", c=NCH),
                    in_=x_src.rearrange("(c p) t -> p c t", p=128)), dma=True)
                if not otm:
                    oix = sb("oix", [128, 8], U32)
                    iv = ctx.emit("sync", lambda q: q.dma_start(
                        out=oix[:, :], in_=oidx), dma=True)
                    gv = 0
                    for chg in range(NCH):
                        gv = ctx.emit("gpsimd", lambda g, chg=chg: g.indirect_dma_start(
                            out=o_sb[:, chg * TL:(chg + 1) * TL], out_offset=None,
                            in_=og_d.ap(),
                            in_offset=bass.IndirectOffsetOnAxis(
                                ap=oix[:, chg:chg + 1], axis=0)),
                            wait=(("dS", iv),), dma=True)
                    ctx.wait("tensor", ("dG", gv))
                    ov = ctx.cnt["dS"]
                else:
                    otm_sb = sb("otm", [128, 1024], BF16)
                    pstx = pst("ptx", [128, 128], BF16)
                    evs = 0
                    trd = 0
                    for i in range(16):
                        ld = ctx.emit("sync", lambda q, i=i: q.dma_start(
                            out=otm_sb[:, :],
                            in_=bass.AP(tensor=od2_d.ap().tensor,
                                        offset=(i // 2) * (T * 64)
                                        + (i % 2) * 128 * 1024,
                                        ap=[[1024, 128], [1, 1024]])),
                            wait=(("ts", trd),), dma=True)
                        for ch in range(NCH):
                            tv = ctx.emit("tensor", lambda t, ch=ch: t.transpose(
                                pstx[:, :], otm_sb[:, ch * 128:(ch + 1) * 128],
                                ident[:, :]),
                                wait=(("dS", ld), ("ss", evs)))
                            trd = tv
                            evs = ctx.emit("scalar", lambda s2, ch=ch, i=i: s2.copy(
                                out=o_sb[:, ch * TL + i * 128: ch * TL + i * 128 + 128],
                                in_=pstx[:, :]), wait=(("ts", tv),))
                    ov = ctx.cnt["dS"]
                    ctx.wait("tensor", ("ss", evs))
                wv = ctx.emit("gpsimd", lambda g: g.dma_start(
                    out=w_sb[:, :].rearrange("p (c f) -> p c f", c=NCH),
                    in_=warr_ap.rearrange("(c p) f -> p c f", p=128)), dma=True)
                bv = 0
                if bias_ap is not None:
                    bv = ctx.emit("sync", lambda q: q.dma_start(
                        out=bcc[:, :], in_=bias_ap), dma=True)
                rd = [0, 0]
                ev = 0
                for tg in range(TL // 512):
                    for jc in range(NCH):
                        bi = (tg * NCH + jc) % 2
                        tv = 0
                        for kc in range(NCH):
                            tv = ctx.emit("tensor", lambda t, tg=tg, jc=jc, kc=kc, bi=bi: t.matmul(
                                pm[bi][:, 0:512],
                                w_sb[:, kc * 1024 + jc * 128: kc * 1024 + jc * 128 + 128],
                                o_sb[:, kc * TL + tg * 512: kc * TL + tg * 512 + 512],
                                start=(kc == 0), stop=(kc == NCH - 1)),
                                wait=(("dG", wv), ("dS", ov), ("vs", rd[bi]))
                                if kc == 0 else ())
                        if bias_ap is not None:
                            bb = bcc[:, jc:jc + 1]
                            bbc = bass.AP(tensor=bb.tensor, offset=bb.offset,
                                          ap=[list(bb.ap[0]), [0, 512]])
                            ctx.emit("vector", lambda v, tg=tg, jc=jc, bbc=bbc: v.tensor_tensor(
                                out=x_sb[:, jc * TL + tg * 512: jc * TL + tg * 512 + 512],
                                in0=x_sb[:, jc * TL + tg * 512: jc * TL + tg * 512 + 512],
                                in1=bbc, op=OP.add),
                                wait=(("dS", max(xv, bv)),) if (tg == 0 and jc == 0) else ())
                        ev = ctx.emit("vector", lambda v, tg=tg, jc=jc, bi=bi: v.tensor_tensor(
                            out=x_sb[:, jc * TL + tg * 512: jc * TL + tg * 512 + 512],
                            in0=x_sb[:, jc * TL + tg * 512: jc * TL + tg * 512 + 512],
                            in1=pm[bi][:, 0:512], op=OP.add),
                            wait=(("ts", tv),) + ((("dS", xv),) if (tg == 0 and jc == 0 and bias_ap is None) else ()))
                        rd[bi] = ev
                xn_ready = emit_ln(ctx, x_sb, xn_sb, TL, ps_s1, ps_s2, sqbuf,
                                   rows, ps_bc,
                                   ready=(("vs", ev), ("gs", ctx.cnt["gs"])))
                ctx.emit("sync", lambda q: q.dma_start(
                    out=x_dst.rearrange("(c p) t -> p c t", p=128),
                    in_=x_sb[:, :].rearrange("p (c t) -> p c t", c=NCH)),
                    wait=(("vs", ev),), dma=True, store=True)
                if store_xn:
                    ctx.emit("sync", lambda q: q.dma_start(
                        out=xnmy_d.ap().rearrange("(c p) t -> p c t", p=128),
                        in_=xn_sb[:, :].rearrange("p (c t) -> p c t", c=NCH)),
                        wait=(("vs", ctx.cnt["vs"]),), dma=True, store=True)
                ctx.wait("sync", ("st", ctx.cnt["st"]))
            _barrier(ctx)

        def sec_win(pfx):
            es = contextlib.ExitStack()
            with es:
                sb, pst = sget(es, pfx)
                xs_sb = sb("xs", [128, 2 * 1024], BF16)
                q_fm = sb("qfm", [128, 4 * T], BF16)
                k_fm = sb("kfm", [128, 4 * T], BF16)
                v_tm = sb("vtm", [128, 32 * 516], BF16)
                big2 = sb("big2", [128, 4 * T], BF16)
                o_fm = big2[:, 0:4 * T]
                wq_sb = big2[:, 0:8 * 1024]
                wk_sb = big2[:, 8 * 1024:16 * 1024]
                wv_sb = sb("wv", [128, 8 * 512], BF16)
                gqr = sb("gqr", [128, 4], F32)
                gkr = sb("gkr", [128, 4], F32)
                slps = sb("slps", [1, 16], F32)
                sc = {
                    "qtm": sb("qtm", [128, 512], BF16),
                    "ktm": sb("ktm", [128, 512], BF16),
                    "sq": sb("sqs", [128, 512], F32),
                    "n2": sb("n2", [128, 4], F32),
                    "nb": sb("nb", [128, 8], F32),
                    "mx": sb("mx", [128, 8], F32),
                    "row8": sb("row8", [16, 16], F32),
                    "rstd": sb("rstd", [128, 2], F32),
                    "prob": sb("prob", [128, 1024], BF16),
                    "osc": sb("osc", [128, 512], F32),
                    "trow": sb("trow", [1, 512], F32),
                    "srow": sb("srow", [1, 512], F32),
                    "posT": sb("posT", [1, T], F32),
                    "bcol": sb("bcol", [128, 1], F32),
                    "rstd_dbg": sb("rstd_dbg", [128, 64], F32),
                }
                rrow = sb("rrowt", [1, 512], F32)
                sc["rrow"] = rrow
                ps = {
                    "pq": pst("pq", [128, 512]),
                    "pk": pst("pk", [128, 512]),
                    "pv": pst("pv", [128, 512]),
                    "pt": pst("pt", [128, 512], BF16),
                    "po": pst("po", [128, 512]),
                    "pb2": pst("pb2", [128, 512]),
                    "pr": pst("pr", [1, 512]),
                }
                ctx.emit("gpsimd", lambda g: g.iota(
                    sc["posT"][0:1, :], pattern=[[1, T]], base=0,
                    channel_multiplier=0,
                    allow_small_or_imprecise_dtypes=True))
                v3 = v_tm[:, :].rearrange("p (t h c) -> p (t h) c", c=129,
                                          h=4)[:, :, 128:129]
                ctx.emit("gpsimd", lambda g: g.memset(v3, 1.0))
                ctx.emit("gpsimd", lambda g: g.memset(sc["bcol"][:, :], 0.0))
                dv = 0
                for d_, s_ in ((win_gq, gqr), (win_gk, gkr), (win_slp, slps)):
                    dv = ctx.emit("sync", lambda q, d_=d_, s_=s_: q.dma_start(
                        out=s_[:, :], in_=d_), dma=True)
                wvv = 0
                for w_d, w_s in ((win_wq, wq_sb), (win_wk, wk_sb),
                                 (win_wv, wv_sb)):
                    wvv = ctx.emit("gpsimd", lambda g, w_d=w_d, w_s=w_s: g.dma_start(
                        out=w_s[:, :].rearrange("p (c f) -> p c f", c=NCH),
                        in_=w_d.rearrange("(c p) f -> p c f", p=128)), dma=True)
                w_ready = (("dS", dv), ("dG", wvv), ("gs", ctx.cnt["gs"]))
                done = emit_win_front(ctx, xnf_d.ap(), xs_sb, q_fm, k_fm, v_tm,
                                      o_fm, wq_sb, wk_sb, wv_sb, gqr, gkr,
                                      slps, ps, sc, w_ready)
                ctx.wait("sync", ("vs", done))
                store_o_a2a(o_fm)

        def sec_delta(pfx):
            es = contextlib.ExitStack()
            with es:
                sb, pst = sget(es, pfx)
                xs_sb = sb("xs", [128, 2 * 1024], BF16)
                q_fm = sb("qfm", [128, 4 * T], BF16)
                g_fm = sb("gfm", [128, 4 * T], BF16)
                u_tm = sb("utm", [128, 32 * 512], BF16)
                wun = sb("wun", [128, 4 * 4096 + 8 * 24], BF16)
                wq_sb = wun[:, 0:4096]
                wk_sb = wun[:, 4096:8192]
                wv_sb = wun[:, 8192:12288]
                wg_sb = wun[:, 12288:16384]
                wb_sb = wun[:, 16384:16384 + 192]
                Gall = wun[:, 0:8704]
                poolf = sb("poolf", [128, 1024], F32)
                dtx = sb("dtx", [128, 512], F32)
                dt2x = sb("dt2x", [128, 512], F32)
                scr3 = sb("scr3", [128, 1536], BF16)
                sc = {
                    "ktm": scr3[:, 0:512],
                    "vtm": scr3[:, 512:1024],
                    "utm": scr3[:, 1024:1536],
                    "bfs": sb("bfs", [128, 24], F32),
                    "Fpl": poolf[:, 512:768],
                    "Spl": poolf[:, 768:1024],
                    "H1": poolf[:, 0:256],
                    "H2": poolf[:, 256:512],
                    "T1": dtx[:, 0:256],
                    "T2": dt2x[:, 0:256],
                    "Hv": sb("Hv", [128, 32], F32),
                    "tva": sb("tva", [128, 16], F32),
                    "abll": sb("abll", [128, 512], F32),
                    "dsA": sb("dsA", [64, 3072], F32),
                    "dsB": sb("dsB", [64, 3072], F32),
                    "dt": dtx,
                    "dt2": dt2x,
                    "mtf": sb("mtf", [1, 1536], F32),
                    "abp": sb("abp", [128, 16], F32),
                    "wrd": sb("wrd", [4, 1024], F32),
                    "ostm": poolf[:, 0:512],
                    "ostm2": poolf[:, 512:1024],
                    "ostb": sb("ostb", [128, 512], BF16),
                }
                sc["Grb"] = scr3[:, 0:1088]
                fsh = sb("fsh", [32, 128], F32)
                sc["fsrow"] = fsh
                sc.update({
                    "wcp": sb("wcp", [128, 128], F32),
                    "hvt": fsh,
                    "rep4f": sb("rep4f", [4, 128], F32),
                })
                sc["msk32"] = sc["Hv"]
                sc["abg"] = sc["dsA"][0:4, 0:512]
                rep4_sb = sc["rep4f"]
                sc["mr"] = [sb(f"mr{i}", [1, 1536], BF16) for i in range(4)]
                sc["mr2"] = [sb(f"ms{i}", [1, 1536], BF16) for i in range(4)]
                sc["Gall"] = Gall
                brow_sb = sb("brow", [1, 24], BF16)
                sc["onesbf"] = onesrowb
                ps = {
                    "pk": pst("pk", [128, 512]),
                    "pv": pst("pv", [128, 512]),
                    "pb": pst("pb", [128, 24]),
                    "pq": pst("pq", [128, 128]),
                    "pdo": pst("pdo", [64, 136]),
                    "ptx": pst("ptx", [128, 128], BF16),
                    "pg": pst("pgx", [128, 128]),
                }
                ps["pab"] = pst("pab", [128, 512])
                ps["pA"] = ps["pk"]
                ps["pB"] = ps["pv"]
                ps["pt8"] = ps["pg"][0:8, :]
                ps["pth"] = ps["pg"][0:32, :]
                ctx.u_tm = u_tm[:, :]
                ctx.q_fm = q_fm[:, :]
                ctx.g_fm = g_fm[:, :]
                ctx.brow_sb = brow_sb
                wld = 0
                for d_, s_ in ((dl_w["wq"], wq_sb), (dl_w["wk"], wk_sb),
                               (dl_w["wv"], wv_sb), (dl_w["wg"], wg_sb)):
                    wld = ctx.emit("gpsimd", lambda g, d_=d_, s_=s_: g.dma_start(
                        out=s_.rearrange("p (c f) -> p c f", c=NCH),
                        in_=d_.rearrange("(c p) f -> p c f", p=128)), dma=True)
                wld = ctx.emit("gpsimd", lambda g: g.dma_start(
                    out=wb_sb.rearrange("p (c f) -> p c f", c=NCH),
                    in_=dl_wb.rearrange("(c p) f -> p c f", p=128)), dma=True)
                wld = ctx.emit("gpsimd", lambda g: g.dma_start(
                    out=brow_sb[:, :], in_=dl_brow), dma=True)
                wld = ctx.emit("gpsimd", lambda g: g.dma_start(
                    out=sc["rep4f"][0:4, :], in_=dl_rep4), dma=True)
                w_ready = (("dG", wld), ("gs", ctx.cnt["gs"]))
                fs_st = emit_delta_front(ctx, xnf_d.ap(), xs_sb, wq_sb, wk_sb,
                                         wv_sb, wg_sb, wb_sb, None, None, None,
                                         rep4_sb, None, fs_dram.ap(), None,
                                         None, None, None, ps, sc, w_ready)
                qgv = 0
                for ch in range(4):
                    for gg in range(T // 512):
                        qgv = ctx.emit("vector", lambda v, ch=ch, gg=gg: v.scalar_tensor_tensor(
                            out=g_fm[:, ch * T + gg * 512: ch * T + gg * 512 + 512],
                            in0=q_fm[:, ch * T + gg * 512: ch * T + gg * 512 + 512],
                            scalar=0.5,
                            in1=g_fm[:, ch * T + gg * 512: ch * T + gg * 512 + 512],
                            op0=OP.mult, op1=OP.mult),
                            wait=(("ss", ctx.cnt["ss"]),) if (ch == 0 and gg == 0) else ())
                o_fm = q_fm[:, :]
                zf = ctx.emit("gpsimd", lambda g: g.memset(o_fm, 0.0),
                              wait=(("vs", qgv),))
                ctx.wait("vector", ("st", fs_st), ("gs", zf))
                gb = emit_delta_scan(ctx, fs_dram, g_dram, ab_dram, hv_dram,
                                     w_dram, rep4_sb, o_fm, ps, sc)
                dwev, oev, hst, ast = emit_delta_scan2(
                    ctx, g_dram, ab_dram, hv_dram, w_dram, rep4_sb, o_fm, ps,
                    sc, gb)
                f4 = emit_delta_scan3(ctx, ab_dram, hv_dram, w_dram, rep4_sb,
                                      o_fm, g_fm, ps, sc, dwev, oev, hst, ast,
                                      sub=5)
                ctx.wait("sync", ("vs", f4), ("vs", ctx.cnt["vs"]),
                         ("ss", ctx.cnt["ss"]))
                store_o_a2a(o_fm)

        def sec_moe(pfx):
            es = contextlib.ExitStack()
            with es:
                sb, pst = sget(es, pfx + "r")
                xn_sb = sb("xn", [128, NCH * TL], BF16)
                o_tm = sb("otm", [128, 16 * 1024], BF16)
                wgt_sb = sb("wgt", [128, 8 * 8], BF16)
                rout_sb = sb("rout", [128, 16 * 8], F32)
                oh_all = sb("oh", [128, 16 * 16], F32)
                h_fm = sb("hfm", [8, 2 * TL], F32)
                c_tm_all = sb("ctm", [128, 16 * 8], F32)
                sc = {
                    "lt": sb("lt", [128, 8], F32),
                    "sc8a": sb("sc8a", [128, 8], F32),
                    "sc8b": sb("sc8b", [128, 8], F32),
                    "col": sb("colx", [128, 8], F32),
                    "iota8": sb("iota8", [128, 8], F32),
                }
                tot_sb = sb("tot", [8, 1], F32)
                ps = {
                    "pl": pst("pl", [128, 8]),
                    "pt8": pst("pt8", [8, 128]),
                    "ptb": pst("ptb", [128, 8]),
                    "ptx": pst("ptx", [128, 128], BF16),
                }
                xv = ctx.emit("sync", lambda q: q.dma_start(
                    out=xn_sb[:, :].rearrange("p (c t) -> p c t", c=NCH),
                    in_=xnmy_d.ap().rearrange("(c p) t -> p c t", p=128)),
                    dma=True)
                wv = ctx.emit("gpsimd", lambda g: g.dma_start(
                    out=wgt_sb[:, :].rearrange("p (c f) -> p c f", c=NCH),
                    in_=moe_wgt.rearrange("(c p) f -> p c f", p=128)), dma=True)
                w_ready = (("dS", xv), ("dG", wv), ("gs", ctx.cnt["gs"]))
                rv, hv = emit_moe_route(ctx, xn_sb, wgt_sb, o_tm, rout_sb,
                                        oh_all, h_fm, c_tm_all, ps, sc, w_ready)
                tv = ctx.emit("vector", lambda v: v.tensor_tensor(
                    out=tot_sb[0:8, 0:1], in0=h_fm[0:8, 2 * TL - 1:2 * TL],
                    in1=h_fm[0:8, TL - 1:TL], op=OP.add), wait=(("vs", hv),))
                ctx.emit("sync", lambda q: q.dma_start(
                    out=xntm_my_d.ap().rearrange("(a p) d -> p a d", p=128),
                    in_=o_tm[:, :].rearrange("p (a d) -> p a d", d=D)),
                    wait=(("ss", ctx.cnt["ss"]),), dma=True, store=True)
                ctx.emit("sync", lambda q: q.dma_start(
                    out=rout_my_d.ap().rearrange("(a p) e -> p a e", p=128),
                    in_=rout_sb[:, :].rearrange("p (a e) -> p a e", e=8)),
                    wait=(("vs", rv),), dma=True, store=True)
                ctx.emit("sync", lambda q: q.dma_start(
                    out=totmy_d.ap(), in_=tot_sb[0:8, 0:1]),
                    wait=(("vs", tv),), dma=True, store=True)
                ctx.wait("sync", ("st", ctx.cnt["st"]))
            _barrier(ctx)
            for src, dst in ((xntm_my_d, xn_all_d), (rout_my_d, rout_all_d),
                             (totmy_d, totall_d)):
                ctx.emit("gpsimd", lambda g, src=src, dst=dst: g.collective_compute(
                    "AllGather", mybir.AluOpType.bypass, replica_groups=ALL8,
                    ins=[src.ap().opt()], outs=[dst.ap().opt()]), coll=True)
            _barrier(ctx)
            es = contextlib.ExitStack()
            with es:
                sb, pst = sget(es, pfx + "e")
                rt_sb = sb("rt", [128, 128 * 8], F32)
                w1_sb = sb("w1", [128, 8 * 4096], BF16)
                w2_sb = sb("w2", [128, 16 * 1024], BF16)
                tot_sb = sb("tot", [8, 8], F32)
                offs_cm = sb("offs", [8, 8], F32)
                tbi_sb = sb("tbi", [128, 40], U32)
                sc = {
                    "fill": sb("fill", [128, 1280], U32),
                    "tokall": sb("tokall", [128, 512], U32),
                    "xb": sb("xb", [128, 4096], BF16),
                    "xg": sb("xg", [128, 4096], BF16),
                    "act": sb("act", [128, 16 * 512], BF16),
                    "of": sb("of", [128, 4096], BF16),
                    "gel": sb("gel", [128, 512], F32),
                    "tbl4": sb("tbl4", [128, 4], U32),
                    "tokid": sb("tokid", [128, 4], U32),
                    "slotu": sb("slotu", [128, 2], U32),
                    "m8a": sb("m8a", [128, 8], F32),
                    "m8b": sb("m8b", [128, 8], F32),
                    "mcol": sb("mcol", [128, 8], F32),
                    "iota8": sb("iota8", [128, 8], F32),
                    "orow": sb("orowx", [1, 8], F32),
                    "te8": sb("te8", [8, 16], F32),
                }
                ps = {
                    "ph": pst("ph", [128, 512]),
                    "ph2": pst("ph2", [128, 512]),
                    "ptx": pst("ptx", [128, 128], BF16),
                    "pt8": pst("pt8", [8, 128]),
                    "pseg": pst("pseg", [128, 8]),
                }
                ctx.tblidx_d = tblidx
                ctx.emit("gpsimd", lambda g: g.iota(
                    sc["iota8"][:, :], pattern=[[1, 8]], base=0,
                    channel_multiplier=0,
                    allow_small_or_imprecise_dtypes=True))
                rv = ctx.emit("sync", lambda q: q.dma_start(
                    out=rt_sb[:, :].rearrange("p (a e) -> p a e", e=8),
                    in_=rout_all_d.ap().rearrange("(a p) e -> p a e", p=128)),
                    dma=True)
                tv = ctx.emit("sync", lambda q: q.dma_start(
                    out=tot_sb[0:8, 0:8],
                    in_=totall_d.ap().rearrange("(c e) o -> c (e o)", e=8)),
                    dma=True)
                w1v = ctx.emit("gpsimd", lambda g: g.dma_start(
                    out=w1_sb[:, :].rearrange("p (c f) -> p c f", c=NCH),
                    in_=w1e.rearrange("(c p) f -> p c f", p=128)), dma=True)
                w2v = ctx.emit("gpsimd", lambda g: g.dma_start(
                    out=w2_sb[:, :].rearrange("p (c f) -> p c f", c=16),
                    in_=w2e.rearrange("(c p) f -> p c f", p=128)), dma=True)
                ctx.wait("vector", ("dS", tv), ("dS", rv))
                ctx.wait("tensor", ("dS", tv), ("dS", rv))
                w_ready = (("dG", w2v),)
                emit_moe_expert(ctx, xn_all_d.ap(), table_d, expo_d.ap(),
                                rt_sb, tot_sb, offs_cm, tbi_sb, w1_sb, w2_sb,
                                ps, sc, w_ready)
                ctx.wait("sync", ("st", ctx.cnt["st"]))
            _barrier(ctx)
            ctx.emit("gpsimd", lambda g: g.collective_compute(
                "AllGather", mybir.AluOpType.bypass, replica_groups=ALL8,
                ins=[expo_d.ap().opt()], outs=[expo_all_d.ap().opt()]),
                coll=True)
            _barrier(ctx)
            es = contextlib.ExitStack()
            with es:
                sb, pst = sget(es, pfx + "c")
                x_sb = sb("x", [128, NCH * TL], F32)
                rt_sb = sb("rt", [128, 16 * 8], F32)
                tot_sb = sb("tot", [8, 8], F32)
                offs_cm = sb("offs", [8, 8], F32)
                mycol_sb = sb("mycol", [8, 1], F32)
                sc = {
                    "rowg": sb("rowg", [128, 2048], BF16),
                    "ytm": sb("ytm", [128, 1024], F32),
                    "ybf": sb("ybf", [128, 1024], BF16),
                    "kw": sb("kw", [128, 2], F32),
                    "slotu": sb("slotu", [128, 2], U32),
                    "slotc": sb("slotc", [128, 2], U32),
                    "m8a": sb("m8a", [128, 8], F32),
                    "m8b": sb("m8b", [128, 8], F32),
                    "mcol": sb("mcol", [128, 8], F32),
                    "iota8": sb("iota8", [128, 8], F32),
                    "orow": sb("orowx", [1, 8], F32),
                    "te8": sb("te8", [8, 16], F32),
                }
                ps = {
                    "ptx": pst("ptx", [128, 128], BF16),
                    "pt8": pst("pt8", [8, 128]),
                    "pseg": pst("pseg", [128, 8]),
                }
                ctx.emit("gpsimd", lambda g: g.iota(
                    sc["iota8"][:, :], pattern=[[1, 8]], base=0,
                    channel_multiplier=0,
                    allow_small_or_imprecise_dtypes=True))
                xv = ctx.emit("sync", lambda q: q.dma_start(
                    out=x_sb[:, :].rearrange("p (c t) -> p c t", c=NCH),
                    in_=x_d.ap().rearrange("(c p) t -> p c t", p=128)),
                    dma=True)
                rv = ctx.emit("sync", lambda q: q.dma_start(
                    out=rt_sb[:, :].rearrange("p (a e) -> p a e", e=8),
                    in_=rout_my_d.ap().rearrange("(a p) e -> p a e", p=128)),
                    dma=True)
                tv = ctx.emit("sync", lambda q: q.dma_start(
                    out=tot_sb[0:8, 0:8],
                    in_=totall_d.ap().rearrange("(c e) o -> c (e o)", e=8)),
                    dma=True)
                mv = ctx.emit("sync", lambda q: q.dma_start(
                    out=mycol_sb[0:8, 0:1], in_=mycol), dma=True)
                ctx.wait("vector", ("dS", tv), ("dS", rv), ("dS", xv))
                ctx.wait("tensor", ("dS", tv), ("dS", rv))
                ready = (("dS", mv), ("gs", ctx.cnt["gs"]))
                done = emit_moe_combine(ctx, x_sb, expo_all_d.ap(), rt_sb,
                                        tot_sb, offs_cm, mycol_sb, ps, sc,
                                        ready)
                ctx.emit("sync", lambda q: q.dma_start(
                    out=x_d.ap().rearrange("(c p) t -> p c t", p=128),
                    in_=x_sb[:, :].rearrange("p (c t) -> p c t", c=NCH)),
                    wait=(("vs", done),), dma=True, store=True)
                ctx.wait("sync", ("st", ctx.cnt["st"]))
            _barrier(ctx)

        def sec_attn(pfx):
            es = contextlib.ExitStack()
            with es:
                sb, pst = sget(es, pfx)
                xn_sb = sb("xn", [128, NCH * T], BF16)
                q_fm = sb("qfm", [128, 4 * T], BF16)
                k_fm = sb("kfm", [128, 4 * T], BF16)
                v_tm = sb("vtm", [128, 32 * 520], BF16)
                sincos = sb("sincos", [128, 2048], F32)
                wq_sb = sb("wq", [128, 8 * 512], BF16)
                wk_sb = sb("wk", [128, 8 * 512], BF16)
                wv_sb = sb("wv", [128, 8 * 512], BF16)
                sc = {
                    "qtm": sb("qtm", [128, 512], BF16),
                    "ktm": sb("ktm", [128, 512], BF16),
                    "sq": sb("sqs", [128, 512], F32),
                    "rsc": sb("rsc", [128, 512], F32),
                    "n2": sb("n2", [128, 16], F32),
                    "mx": sb("mx", [128, 16], F32),
                    "row8": sb("row8", [16, 16], F32),
                    "mrowbf": sb("mrowbf", [1, 512], BF16),
                    "onesbf": sb("onesbf", [1, 128], BF16),
                    "prob": sb("prob", [128, 1024], BF16),
                    "osc": sb("osc", [128, 512], F32),
                }
                sc["mrow"] = sc["sq"]
                sc["rrow"] = sc["rsc"]
                stg = sb("stg", [128, 128], BF16)
                ps = {
                    "pq": pst("pq", [128, 512]),
                    "pk": pst("pk", [128, 512]),
                    "pv": pst("pv", [128, 512]),
                    "pt": pst("pt", [128, 512], BF16),
                    "pb2": pst("pb2", [128, 512]),
                }
                ctx.emit("gpsimd", lambda g: g.memset(sc["onesbf"][:, :], 1.0))
                v3 = v_tm[:, :].rearrange("p (t h c) -> p (t h) c", c=65,
                                          h=8)[:, :, 64:65]
                ctx.emit("gpsimd", lambda g: g.memset(v3, 1.0))
                xv = ctx.emit("sync", lambda q: q.dma_start(
                    out=xn_sb[:, :].rearrange("p (c t) -> p c t", c=NCH),
                    in_=xnf_d.ap().rearrange("(c p) t -> p c t", p=128)),
                    dma=True)
                sv = ctx.emit("sync", lambda q: q.dma_start(
                    out=sincos[:, :], in_=at_sincos), dma=True)
                wvv = 0
                for w_d, w_s in ((at_wq, wq_sb), (at_wk, wk_sb), (at_wv, wv_sb)):
                    wvv = ctx.emit("gpsimd", lambda g, w_d=w_d, w_s=w_s: g.dma_start(
                        out=w_s[:, :].rearrange("p (c f) -> p c f", c=NCH),
                        in_=w_d.rearrange("(c p) f -> p c f", p=128)), dma=True)
                w_ready = (("dS", xv), ("dS", sv), ("dG", wvv),
                           ("gs", ctx.cnt["gs"]))
                o_fm = xn_sb[:, 0:4 * T]
                done = emit_attn_front(ctx, xn_sb, q_fm, k_fm, v_tm, o_fm,
                                       sincos, wq_sb, wk_sb, wv_sb, w_ready,
                                       ps, sc)
                stv = 0
                evs = 0
                for hc in range(4):
                    for tt in range(32):
                        k = tt % 4
                        tv = ctx.emit("tensor", lambda t, hc=hc, tt=tt, k=k: t.transpose(
                            ps["pt"][:, k * 128:(k + 1) * 128],
                            o_fm[:, hc * T + tt * 128: hc * T + tt * 128 + 128],
                            ident[:, :]),
                            wait=(("vs", done), ("st", stv)))
                        evs = ctx.emit("scalar", lambda s2, k=k: s2.copy(
                            out=stg[:, :], in_=ps["pt"][:, k * 128:(k + 1) * 128]),
                            wait=(("ts", tv), ("st", stv)))
                        for hh in range(2):
                            h = hc * 2 + hh
                            stv = ctx.emit("sync", lambda q, h=h, tt=tt, hh=hh: q.dma_start(
                                out=bass.AP(tensor=od2_d.ap().tensor,
                                            offset=h * (T * 64) + tt * 128 * 64,
                                            ap=[[64, 128], [1, 64]]),
                                in_=stg[:, hh * 64:(hh + 1) * 64]),
                                wait=(("ss", evs),), dma=True, store=True)
                ctx.wait("sync", ("st", ctx.cnt["st"]))
            _barrier(ctx)

        def sec_l1ffn(pfx, x_src, y_dst):
            es = contextlib.ExitStack()
            with es:
                sb, pst = sget(es, pfx)
                x_sb = sb("x", [128, NCH * TL], F32)
                xn_sb = sb("xn", [128, NCH * TL], BF16)
                w1_sb = sb("w1", [128, 2 * 1024], BF16)
                w2_sb = sb("w2", [128, 2 * 4096], BF16)
                act = sb("act", [128, 32 * 512], BF16)
                sqbuf = sb("sq", [128, 1024], F32)
                rows = sb("rows", [1, 3 * TL], F32)
                ssc = sb("ssc", [128, 1024], F32)
                qp = sb("qp", [128, 160], F32)
                qsc = sb("qsc", [128, 192], F32)
                kI = sb("kI", [128, 32], I32)
                Atil = sb("Atil", [128, 32], F32)
                sinb = sb("sinb", [128, 32], F32)
                b2c = sb("b2c", [128, 8], F32)
                ps_s1 = pst("s1", [1, 512])
                ps_s2 = pst("s2", [1, 512])
                ps_bc = pst("bc", [128, 1024])
                ps_h = pst("ph", [128, 512])
                ps_o = pst("po", [128, 512])
                xv = ctx.emit("sync", lambda q: q.dma_start(
                    out=x_sb[:, :].rearrange("p (c t) -> p c t", c=NCH),
                    in_=x_src.rearrange("(c p) t -> p c t", p=128)), dma=True)
                qv = ctx.emit("sync", lambda q: q.dma_start(
                    out=qp[:, :], in_=f_qpv), dma=True)
                bv = ctx.emit("sync", lambda q: q.dma_start(
                    out=b2c[:, :], in_=f_b2v), dma=True)
                emit_qprep(ctx, qp, kI, qsc, Atil, sinb,
                           (("dS", qv), ("gs", ctx.cnt["gs"])))
                xn_ready = emit_ln(ctx, x_sb, xn_sb, TL, ps_s1, ps_s2, sqbuf,
                                   rows, ps_bc,
                                   ready=(("dS", xv), ("gs", ctx.cnt["gs"])))
                ctx.wait("vector", ("dS", bv))
                ctx.wait("scalar", ("dS", bv))
                done = emit_l1ffn(ctx, x_sb, xn_sb, xn_ready, w1_sb, w2_sb,
                                  f_w1, f_w2, Atil, sinb, qp[:, 96:128], b2c,
                                  act, ps_h, ps_o, ssc)
                ctx.emit("sync", lambda q: q.dma_start(
                    out=y_dst.rearrange("(c p) t -> p c t", p=128),
                    in_=x_sb[:, :].rearrange("p (c t) -> p c t", c=NCH)),
                    wait=(("vs", done), ("dS", bv)), dma=True, store=True)
                ctx.wait("sync", ("st", ctx.cnt["st"]))
            _barrier(ctx)

        sec_ln("a_", xin)
        xn_exchange()
        sec_gla("g_")
        sec_epi("e1_", xin, gla_wo, x_d.ap(), store_xn=True)
        sec_moe("m1")
        sec_ln("b_", x_d.ap())
        xn_exchange()
        sec_win("w_")
        sec_epi("e2_", x_d.ap(), win_wo, x_d.ap(), store_xn=True)
        sec_moe("m2")
        sec_ln("c_", x_d.ap())
        xn_exchange()
        sec_delta("d_")
        sec_epi("e3_", x_d.ap(), dl_wo, x_d.ap(), store_xn=False)
        sec_l1ffn("f1_", x_d.ap(), x_d.ap())
        sec_ln("n4_", x_d.ap())
        xn_exchange()
        sec_attn("t_")
        sec_epi("e4_", x_d.ap(), at_wp, x_d.ap(), bias_ap=at_bp,
                store_xn=False, otm=True)
        sec_l1ffn("f2_", x_d.ap(), xout)
        ctx.wait("sync", ("st", ctx.cnt["st"]))
    return nc


def kernel(**kw):
    f32 = np.float32
    cores = list(range(NCORES))
    tsl = [slice((c % 2) * TL, (c % 2 + 1) * TL) for c in cores]
    csl = [slice((c % 2) * 512, (c % 2 + 1) * 512) for c in cores]
    x0 = kw["L0_attn_a_x"]

    Wq, Wk, Wv = (kw["L0_attn_a_W_q"].astype(f32), kw["L0_attn_a_W_k"].astype(f32),
                  kw["L0_attn_a_W_v"].astype(f32))
    Wz, bgt = kw["L0_attn_a_W_gate"].astype(f32), kw["L0_attn_a_b_gate"].astype(f32)
    wwq, wwk, wwv = (kw["L0_attn_b_wq"].astype(f32), kw["L0_attn_b_wk"].astype(f32),
                     kw["L0_attn_b_wv"].astype(f32))
    gq, gk = (kw["L0_attn_b_q_norm_w"].astype(f32),
              kw["L0_attn_b_k_norm_w"].astype(f32))
    WgT = _asc(kw["L0_ffn_Wg"].T.astype(f32))
    dWq, dWk, dWv, dWg = (kw["L1_attn_a_Wq"].astype(f32), kw["L1_attn_a_Wk"].astype(f32),
                          kw["L1_attn_a_Wv"].astype(f32), kw["L1_attn_a_Wg"].astype(f32))
    Wb, Wfd, Wsd = (kw["L1_attn_a_Wb"].astype(f32), kw["L1_attn_a_Wfd"].astype(f32),
                    kw["L1_attn_a_Wsd"].astype(f32))
    bfd, bsd = kw["L1_attn_a_bfd"].astype(f32), kw["L1_attn_a_bsd"].astype(f32)
    aWq, aWk, aWv = (kw["L1_attn_b_Wq"].astype(f32), kw["L1_attn_b_Wk"].astype(f32),
                     kw["L1_attn_b_Wv"].astype(f32))
    sincos = stage_yarn_sincos()
    qpv = _stage_qpv(kw["L1_ffn_b1"].astype(f32),
                     kw["L1_ffn_quantum_phases"].astype(f32),
                     kw["L1_ffn_quantum_weights_real"].astype(f32),
                     kw["L1_ffn_quantum_weights_imag"].astype(f32))
    b2v = _asc(kw["L1_ffn_b2"].astype(f32).reshape(8, 128).T)
    fw1 = _asc(kw["L1_ffn_W1"].astype(f32))
    fw2 = _asc(kw["L1_ffn_W2"].astype(f32))
    rep4 = np.zeros((4, 128), f32)
    for p in range(4):
        rep4[p, p * 32:(p + 1) * 32] = 1.0
    rr = 2.0 ** (-8.0 / 8)
    SCW = 1.0 / np.sqrt(128.0)
    bpc = _asc(kw["L1_attn_b_bp"].astype(f32).reshape(8, 128).T)

    f_all = np.arange(D)
    orow_of_f = np.where(f_all < 512, 2 * f_all, 1024 + 2 * (f_all - 512))

    in_maps = []
    for c in cores:
        hg = c % 2
        oidx_c = _asc((orow_of_f + hg).astype(np.uint32)
                      .reshape(8, 128).T)
        my, ot = csl[c], slice((1 - hg) * 512, (2 - hg) * 512)
        slpv = np.zeros((1, 16), f32)
        for hl in range(4):
            gh = hg * 4 + hl
            slpv[0, hl] = (rr ** gh) / SCW
            slpv[0, 8 + hl] = -(rr ** gh) / SCW
        hs = slice(hg * 8, (hg + 1) * 8)
        wb = np.concatenate([Wb.T[:, hs], Wfd.T[:, hs], Wsd.T[:, hs]], 1)
        brow = np.concatenate([np.zeros(8, f32), bfd[hs], bsd[hs]])[None, :]
        onehot = np.zeros((8, 1), f32)
        onehot[c] = 1.0
        tbl = (c * CAP + np.arange(CAP)).astype(np.uint32)
        in_maps.append({
            "xin": _asc(x0[c // 2, tsl[c], :].T.astype(f32)),
            "gla_wq": _asc(Wq[:, csl[c]]), "gla_wk": _asc(Wk[:, csl[c]]),
            "gla_wv": _asc(Wv[:, csl[c]]), "gla_wz": _asc(Wz[:, csl[c]]),
            "gla_bg": _asc(bgt[csl[c]].reshape(4, 128).T),
            "gla_wo": _asc(kw["L0_attn_a_W_o"].astype(f32)),
            "win_wq": _asc(np.concatenate([wwq[:, my], wwq[:, ot]], 1)),
            "win_wk": _asc(np.concatenate([wwk[:, my], wwk[:, ot]], 1)),
            "win_wv": _asc(wwv[:, my]),
            "win_gq": _asc(gq[my].reshape(4, 128).T),
            "win_gk": _asc(gk[my].reshape(4, 128).T),
            "win_slp": slpv,
            "win_wo": _asc(kw["L0_attn_b_wo"].astype(f32)),
            "moe_wgt": WgT,
            "w1e": _asc(kw["L0_ffn_W1"][c].astype(f32)),
            "w2e": _asc(kw["L0_ffn_W2"][c].astype(f32)),
            "tblidx": _asc(tbl.reshape(CAP // 128, 128).T),
            "mycol": onehot,
            "dl_wq": _asc(dWq.T[:, csl[c]]), "dl_wk": _asc(dWk.T[:, csl[c]]),
            "dl_wv": _asc(dWv.T[:, csl[c]]), "dl_wg": _asc(dWg.T[:, csl[c]]),
            "dl_wb": _asc(wb), "dl_brow": _asc(brow), "dl_rep4": rep4,
            "dl_wo": _asc(kw["L1_attn_a_Wo"].astype(f32).T),
            "at_wq": _asc(aWq.T[:, csl[c]]), "at_wk": _asc(aWk.T[:, csl[c]]),
            "at_wv": _asc(aWv.T[:, csl[c]]),
            "at_sincos": sincos,
            "at_wp": _asc(kw["L1_attn_b_Wp"].astype(f32).T),
            "at_bp": bpc,
            "f_w1": fw1, "f_w2": fw2, "f_qpv": qpv, "f_b2v": b2v,
            "oidx": oidx_c,
        })
    nc = _prog("mega", build_mega)
    res = run_bass_kernel_spmd(nc, in_maps, core_ids=cores)
    out = np.zeros((NB, T, D), np.float32)
    for c in cores:
        out[c // 2, tsl[c], :] = res.results[c]["xout"].T
    return out


# version 15
# speedup vs baseline: 1.1670x; 1.1670x over previous
import contextlib
import math
import numpy as np
import concourse.bass as bass
import concourse.mybir as mybir
from concourse.alu_op_type import AluOpType as OP
from concourse.bass_utils import run_bass_kernel_spmd

F32 = mybir.dt.float32
BF16 = mybir.dt.bfloat16
U32 = mybir.dt.uint32
I32 = mybir.dt.int32
AF = mybir.ActivationFunctionType
AX = mybir.AxisListType

NCORES = 8
D = 1024
NB = 4
T = 4096
TL = 2048
NCH = D // 128
TWO_PI = 2.0 * math.pi
HALF_PI = 0.5 * math.pi

ESEM = {"tensor": "ts", "vector": "vs", "scalar": "ss", "gpsimd": "gs"}
SUB4 = 9


class Ctx:

    def __init__(self, nc, block):
        self.nc = nc
        self.block = block
        self.sem = {}
        self.cnt = {}
        for s in ("ts", "vs", "ss", "gs", "dS", "dG", "cc", "st"):
            h = nc.ctx.enter_context(nc.semaphore(s))
            self.sem[s] = h
            self.cnt[s] = 0
        self.ones128 = None
        self.ident_bf = None
        self.cbank = None
        self.cmap = {}
        self.cused = 0

    AUTO_SELF = True

    def emit(self, eng, f, wait=(), dma=False, coll=False, store=False,
             chain=False):
        if dma:
            s = "dG" if eng == "gpsimd" else ("st" if store else "dS")
            n = 16
        elif coll:
            s, n = "cc", 1
        else:
            s, n = ESEM[eng], 1
        sems = self.sem
        if self.AUTO_SELF and not dma and not coll and not chain:
            wait = tuple(wait) + ((s, self.cnt[s]),)
        wait = tuple((w, v) for (w, v) in wait if v > 0)
        sh = sems[s]

        def body(e):
            for w, v in wait:
                e.wait_ge(sems[w], v)
            f(e).then_inc(sh, n)

        getattr(self.block, eng)(body)
        self.cnt[s] += n
        return self.cnt[s]

    def tnop(self):
        return self.emit("tensor", lambda t: t.nop())

    def constcol(self, val):
        if val in self.cmap:
            return self.cmap[val]
        i = self.cused
        self.cused += 1
        ap = self.cbank[:, i:i + 1]
        self.emit("gpsimd", lambda g: g.memset(ap, float(val)))
        self.cmap[val] = ap
        return ap

    def wait(self, eng, *pairs):
        pairs = tuple((w, v) for (w, v) in pairs if v > 0)
        if not pairs:
            return

        def body(e):
            for w, v in pairs:
                e.wait_ge(self.sem[w], v)

        getattr(self.block, eng)(body)


def ceil_div(a, b):
    return (a + b - 1) // b




def emit_prelude(ctx, ident_bf, ones_f32, onesrow, ident_f32=None):
    ctx.emit("gpsimd", lambda g: g.memset(onesrow[:, :], 1.0))
    ctx.onesrow = onesrow[0:1, 0:128]
    if ident_f32 is not None:
        ctx.emit("gpsimd", lambda g: g.memset(ident_f32[:, :], 1.0))
        ctx.emit("gpsimd", lambda g: g.affine_select(
            ident_f32[:, :], ident_f32[:, :], pattern=[[1, 128]],
            compare_op=OP.is_equal, fill=0.0, base=0, channel_multiplier=-1))
        ctx.ident_f32 = ident_f32[:, :]
    ctx.emit("gpsimd", lambda g: g.memset(ident_bf[:], 1.0))
    ctx.emit("gpsimd", lambda g: g.affine_select(
        ident_bf[:], ident_bf[:], pattern=[[1, 128]], compare_op=OP.is_equal,
        fill=0.0, base=0, channel_multiplier=-1))
    gm = ctx.emit("gpsimd", lambda g: g.memset(ones_f32[:], 1.0))
    if getattr(ctx, "onesrow_bf_t", None) is not None:
        ctx.emit("gpsimd", lambda g: g.memset(ctx.onesrow_bf_t[:, :], 1.0))
        ctx.onesrow_bf = ctx.onesrow_bf_t[0:1, 0:128]
    ctx.ones128 = ones_f32[:, 0:1]
    ctx.ones_row = ones_f32[0:1, 0:1].rearrange("p o -> p o") if False else None
    ctx.ident_bf = ident_bf
    return gm


def bcast_row(ctx, ps_dst, row_ap, wait=()):
    onesr = ctx.onesrow
    return ctx.emit("tensor", lambda t: t.matmul(
        ps_dst, onesr, row_ap, start=True, stop=True), wait=wait)


def emit_ln(ctx, x_sb, xn_sb, Ttok, ps_s1, ps_s2, sqbuf, rows, ps_bc, ready=()):
    ones = ctx.ones128
    ngr = Ttok // 512
    sq_reader = [0, 0]
    mrow = rows[0:1, 0:Ttok]
    vrow = rows[0:1, Ttok:2 * Ttok]
    trow = rows[0:1, 2 * Ttok:3 * Ttok]
    ev = 0
    for g in range(ngr):
        w0 = tuple(ready) + (("vs", ev),)
        for c in range(NCH):
            xs = x_sb[:, c * Ttok + g * 512: c * Ttok + g * 512 + 512]
            ctx.emit("tensor", lambda t, xs=xs, c=c: t.matmul(
                ps_s1[0:1, :], ones, xs,
                start=(c == 0), stop=(c == NCH - 1)),
                wait=w0 if c == 0 else ())
            sl = c % 2
            sq = sqbuf[:, sl * 512:(sl + 1) * 512]
            wv = ctx.emit("scalar", lambda s, xs=xs, sq=sq: s.activation(
                out=sq, in_=xs, func=AF.Square),
                wait=(("ts", sq_reader[sl]),) + (tuple(ready) if (g == 0 and c == 0) else ()))
            sq_reader[sl] = ctx.emit("tensor", lambda t, sq=sq, c=c: t.matmul(
                ps_s2[0:1, :], ones, sq,
                start=(c == 0), stop=(c == NCH - 1)),
                wait=(("ss", wv),))
        td = ctx.cnt["ts"]
        ctx.emit("vector", lambda v, g=g: v.tensor_scalar(
            out=mrow[:, g * 512:(g + 1) * 512], in0=ps_s1[0:1, :],
            scalar1=1.0 / D, scalar2=None, op0=OP.mult), wait=(("ts", td),))
        ev = ctx.emit("vector", lambda v, g=g: v.tensor_scalar(
            out=vrow[:, g * 512:(g + 1) * 512], in0=ps_s2[0:1, :],
            scalar1=1.0 / D, scalar2=None, op0=OP.mult))
    ctx.emit("vector", lambda v: v.scalar_tensor_tensor(
        out=trow, in0=mrow, scalar=-1.0, in1=mrow, op0=OP.mult, op1=OP.mult))
    vv = ctx.emit("vector", lambda v: v.tensor_tensor(
        out=vrow, in0=vrow, in1=trow, op=OP.add))
    lnv = ctx.emit("scalar", lambda s: s.activation(
        out=vrow, in_=vrow, func=AF.Ln, bias=ctx.constcol(1e-5)[0:1, :], scale=1.0),
        wait=(("vs", vv),))
    rv = ctx.emit("scalar", lambda s: s.activation(
        out=vrow, in_=vrow, func=AF.Exp, scale=-0.5), wait=(("ss", lnv),))
    mbc = ps_bc[:, 0:512]
    rbc = ps_bc[:, 512:1024]
    out_v = 0
    bc_rd = 0
    for g in range(ngr):
        mr = mrow[:, g * 512:(g + 1) * 512]
        vr = vrow[:, g * 512:(g + 1) * 512]
        bcast_row(ctx, mbc, mr, wait=(("ss", rv), ("vs", bc_rd)))
        gv2 = bcast_row(ctx, rbc, vr)
        for c in range(NCH):
            sl = c % 2
            tp = sqbuf[:, sl * 512:(sl + 1) * 512]
            xs = x_sb[:, c * Ttok + g * 512: c * Ttok + g * 512 + 512]
            xo = xn_sb[:, c * Ttok + g * 512: c * Ttok + g * 512 + 512]
            w = (("ts", gv2),) if c == 0 else ()
            ctx.emit("vector", lambda v, xs=xs, tp=tp: v.tensor_tensor(
                out=tp, in0=xs, in1=mbc, op=OP.subtract), wait=w)
            out_v = ctx.emit("vector", lambda v, xo=xo, tp=tp: v.tensor_tensor(
                out=xo, in0=tp, in1=rbc, op=OP.mult))
        bc_rd = out_v
    return out_v


def emit_qprep(ctx, qp_in, k_i, qsc, Atil, sinb, ready):
    th, wr, wi, b1, kf = (qp_in[:, i * 32:(i + 1) * 32] for i in range(5))
    thr, gt, lt, s2, t1, t2 = (qsc[:, i * 32:(i + 1) * 32] for i in range(6))
    _ = k_i
    e = ctx.emit
    e("vector", lambda v: v.tensor_scalar(out=kf, in0=th, scalar1=0.0,
                                          scalar2=None, op0=OP.mult), wait=ready)
    for j in range(6):
        thr_j = (2 * j + 1) * math.pi
        e("vector", lambda v, thr_j=thr_j: v.tensor_scalar(
            out=t1, in0=th, scalar1=thr_j, scalar2=None, op0=OP.is_gt))
        e("vector", lambda v: v.tensor_tensor(out=kf, in0=kf, in1=t1, op=OP.add))
        e("vector", lambda v, thr_j=thr_j: v.tensor_scalar(
            out=t1, in0=th, scalar1=-thr_j, scalar2=None, op0=OP.is_lt))
        e("vector", lambda v: v.tensor_tensor(out=kf, in0=kf, in1=t1, op=OP.subtract))
    e("vector", lambda v: v.scalar_tensor_tensor(
        out=thr, in0=kf, scalar=-TWO_PI, in1=th, op0=OP.mult, op1=OP.add))
    e("vector", lambda v: v.tensor_scalar(out=gt, in0=thr, scalar1=HALF_PI,
                                          scalar2=None, op0=OP.is_gt))
    e("vector", lambda v: v.tensor_scalar(out=lt, in0=thr, scalar1=-HALF_PI,
                                          scalar2=None, op0=OP.is_lt))
    e("vector", lambda v: v.tensor_tensor(out=s2, in0=gt, in1=lt, op=OP.subtract))
    vv = e("vector", lambda v: v.scalar_tensor_tensor(
        out=thr, in0=s2, scalar=-math.pi, in1=thr, op0=OP.mult, op1=OP.add))
    e("scalar", lambda s: s.activation(out=gt, in_=thr, func=AF.Sin),
      wait=(("vs", vv),))
    hp = ctx.constcol(HALF_PI)
    sv = e("scalar", lambda s: s.activation(out=lt, in_=thr, func=AF.Sin,
                                            bias=hp, scale=-1.0))
    e("vector", lambda v: v.tensor_tensor(out=t1, in0=lt, in1=wr, op=OP.mult),
      wait=(("ss", sv),))
    e("vector", lambda v: v.tensor_tensor(out=t2, in0=gt, in1=wi, op=OP.mult))
    e("vector", lambda v: v.tensor_tensor(out=t1, in0=t1, in1=t2, op=OP.add))
    e("vector", lambda v: v.tensor_tensor(out=t1, in0=t1, in1=wi, op=OP.mult))
    e("vector", lambda v: v.tensor_scalar(out=Atil[:, :], in0=t1, scalar1=0.1,
                                          scalar2=None, op0=OP.mult))
    out_v = e("vector", lambda v: v.scalar_tensor_tensor(
        out=sinb[:, :], in0=b1, scalar=0.1, in1=thr, op0=OP.mult, op1=OP.add))
    return out_v


def emit_l1ffn(ctx, x_sb, xn_sb, xn_ready, w1_sb, w2_sb, w1_dram, w2_dram,
               Atil, sinb, b1c, b2c, act, ps_h, ps_o, ssc, Ttok=TL):
    TGR = 512
    ngr = Ttok // TGR
    NF = 32
    w1_ready = [0, 0]
    w1_user = [0, 0]
    w2_ready = [0, 0]
    w2_user = [0, 0]

    def w1_load(q, fc, bufi):
        return q.dma_start(
            out=w1_sb[:, bufi * 1024:(bufi + 1) * 1024]
            .rearrange("p (c f) -> p c f", c=8),
            in_=w1_dram[:, fc * 128:(fc + 1) * 128]
            .rearrange("(c p) f -> p c f", p=128))

    def w2_load(q, oc, bufi):
        return q.dma_start(
            out=w2_sb[:, bufi * 4096:(bufi + 1) * 4096]
            .rearrange("p (c o) -> p c o", c=32),
            in_=w2_dram[:, oc * 128:(oc + 1) * 128]
            .rearrange("(c p) o -> p c o", p=128))

    w1_ready[0] = ctx.emit("gpsimd", lambda q: w1_load(q, 0, 0), dma=True)
    last_act_rd = 0
    sv_prev = 0
    vv_prev = 0
    stt_prev = 0
    for g in range(ngr):
        act_done = 0
        for fc in range(NF):
            bufi = fc % 2
            nxt = 1 - bufi
            nfc = fc + 1 if fc + 1 < NF else 0
            if fc + 1 < NF or g + 1 < ngr:
                w1_ready[nxt] = ctx.emit(
                    "gpsimd", lambda q, nfc=nfc, nxt=nxt: w1_load(q, nfc, nxt),
                    wait=(("ts", w1_user[nxt]),), dma=True)
            tv = 0
            for kc in range(NCH):
                tv = ctx.emit("tensor", lambda t, kc=kc, bufi=bufi, g=g: t.matmul(
                    ps_h[:, 0:TGR],
                    w1_sb[:, bufi * 1024 + kc * 128: bufi * 1024 + (kc + 1) * 128],
                    xn_sb[:, kc * Ttok + g * TGR: kc * Ttok + g * TGR + TGR],
                    start=(kc == 0), stop=(kc == NCH - 1)),
                    wait=(("dG", w1_ready[bufi]), ("vs", max(xn_ready, vv_prev)),
                          ("ss", sv_prev)) if kc == 0 else (), chain=(kc != 0))
            w1_user[bufi] = tv
            sv_prev = ctx.emit("scalar", lambda s, fc=fc: s.activation(
                out=ssc[:, 0:TGR], in_=ps_h[:, 0:TGR], func=AF.Sin,
                bias=sinb[:, fc:fc + 1], scale=0.1),
                wait=(("ts", tv), ("vs", vv_prev)))
            vv_prev = ctx.emit("vector", lambda v, fc=fc: v.scalar_tensor_tensor(
                out=ssc[:, TGR:2 * TGR], in0=ssc[:, 0:TGR],
                scalar=Atil[:, fc:fc + 1], in1=ps_h[:, 0:TGR],
                op0=OP.mult, op1=OP.add), wait=(("ss", sv_prev),))
            act_done = ctx.emit("vector", lambda v, fc=fc: v.tensor_scalar(
                out=act[:, fc * TGR:(fc + 1) * TGR], in0=ssc[:, TGR:2 * TGR],
                scalar1=b1c[:, fc:fc + 1], scalar2=0.0,
                op0=OP.add, op1=OP.max),
                wait=(("ts", last_act_rd),) if fc == 0 else ())
        for oc in range(NCH):
            bufi = oc % 2
            nxt = 1 - bufi
            if oc == 0:
                w2_ready[0] = ctx.emit("gpsimd", lambda q, g=g: w2_load(q, 0, 0),
                                       wait=(("ts", w2_user[0]),), dma=True)
            if oc + 1 < NCH:
                w2_ready[nxt] = ctx.emit(
                    "gpsimd", lambda q, oc=oc, nxt=nxt: w2_load(q, oc + 1, nxt),
                    wait=(("ts", w2_user[nxt]),), dma=True)
            tv = 0
            for kc in range(NF):
                tv = ctx.emit("tensor", lambda t, kc=kc, oc=oc, bufi=bufi: t.matmul(
                    ps_o[:, 0:TGR],
                    w2_sb[:, bufi * 4096 + kc * 128: bufi * 4096 + kc * 128 + 128],
                    act[:, kc * TGR:(kc + 1) * TGR],
                    start=(kc == 0), stop=(kc == NF - 1)),
                    wait=(("vs", act_done), ("dG", w2_ready[bufi]),
                          ("vs", stt_prev)) if kc == 0 else (), chain=(kc != 0))
            w2_user[bufi] = tv
            if oc == NCH - 1:
                last_act_rd = tv
            stt_prev = ctx.emit("vector", lambda v, oc=oc, g=g: v.scalar_tensor_tensor(
                out=x_sb[:, oc * Ttok + g * TGR: oc * Ttok + g * TGR + TGR],
                in0=ps_o[:, 0:TGR], scalar=b2c[:, oc:oc + 1],
                in1=x_sb[:, oc * Ttok + g * TGR: oc * Ttok + g * TGR + TGR],
                op0=OP.add, op1=OP.add), wait=(("ts", tv),))
    return stt_prev




def build_l1ffn_launch():
    nc = bass.Bass("TRN2", target_bir_lowering=False, debug=False,
                   num_devices=NCORES)
    xin = nc.dram_tensor("xin", [D, TL], F32, kind="ExternalInput").ap()
    w1 = nc.dram_tensor("w1", [D, 4096], F32, kind="ExternalInput").ap()
    w2 = nc.dram_tensor("w2", [4096, D], F32, kind="ExternalInput").ap()
    qpv = nc.dram_tensor("qpv", [128, 160], F32, kind="ExternalInput").ap()
    b2v = nc.dram_tensor("b2v", [128, 8], F32, kind="ExternalInput").ap()
    yout = nc.dram_tensor("yout", [D, TL], F32, kind="ExternalOutput").ap()
    xnd = nc.dram_tensor("xnd", [D, TL], BF16, kind="ExternalOutput").ap()
    qpd = nc.dram_tensor("qpd", [128, 128], F32, kind="ExternalOutput").ap()

    es = contextlib.ExitStack()

    def sb(name, shape, dt):
        return es.enter_context(nc.sbuf_tensor(name, shape, dt))

    def ps(name, shape, dt=F32):
        return es.enter_context(nc.psum_tensor(name, shape, dt))

    with es:
        x_sb = sb("x_sb", [128, NCH * TL], F32)
        xn_sb = sb("xn_sb", [128, NCH * TL], BF16)
        w1_sb = sb("w1_sb", [128, 2 * 1024], BF16)
        w2_sb = sb("w2_sb", [128, 2 * 4096], BF16)
        act = sb("act", [128, 32 * 512], BF16)
        sqbuf = sb("sqbuf", [128, 1024], F32)
        rows = sb("rows", [1, 3 * TL], F32)
        ssc = sb("ssc", [128, 1024], F32)
        qp = sb("qp", [128, 160], F32)
        qsc = sb("qsc", [128, 192], F32)
        kI = sb("kI", [128, 32], I32)
        Atil = sb("Atil", [128, 32], F32)
        sinb = sb("sinb", [128, 32], F32)
        b2c = sb("b2c", [128, 8], F32)
        ident = sb("ident", [128, 128], BF16)
        onesb = sb("onesb", [128, 1], F32)
        onesrow = sb("onesrow", [1, 128], F32)
        cbank = sb("cbank", [128, 16], F32)
        ps_s1 = ps("ps_s1", [1, 512])
        ps_s2 = ps("ps_s2", [1, 512])
        ps_bc = ps("ps_bc", [128, 1024])
        ps_h = ps("ps_h", [128, 512])
        ps_o = ps("ps_o", [128, 512])
        block = es.enter_context(nc.Block())
        ctx = Ctx(nc, block)
        ctx.cbank = cbank
        emit_prelude(ctx, ident, onesb, onesrow)
        ctx.constcol(1e-5)
        ctx.constcol(HALF_PI)
        xv = ctx.emit("sync", lambda q: q.dma_start(
            out=x_sb[:, :].rearrange("p (c t) -> p c t", c=NCH),
            in_=xin.rearrange("(c p) t -> p c t", p=128)),
            dma=True)
        qv = ctx.emit("sync", lambda q: q.dma_start(out=qp[:, :], in_=qpv),
                      dma=True)
        bv = ctx.emit("sync", lambda q: q.dma_start(out=b2c[:, :], in_=b2v),
                      dma=True)
        emit_qprep(ctx, qp, kI, qsc, Atil, sinb,
                   (("dS", qv), ("gs", ctx.cnt["gs"])))
        xn_ready = emit_ln(ctx, x_sb, xn_sb, TL, ps_s1, ps_s2, sqbuf, rows,
                           ps_bc, ready=(("dS", xv), ("gs", ctx.cnt["gs"])))
        ctx.wait("vector", ("dS", bv))
        ctx.wait("scalar", ("dS", bv))
        done = emit_l1ffn(ctx, x_sb, xn_sb, xn_ready, w1_sb, w2_sb, w1, w2,
                          Atil, sinb, qp[:, 96:128], b2c, act, ps_h, ps_o, ssc)
        sv = ctx.emit("sync", lambda q: q.dma_start(
            out=yout.rearrange("(c p) t -> p c t", p=128),
            in_=x_sb[:, :].rearrange("p (c t) -> p c t", c=NCH)),
            wait=(("vs", done), ("dS", bv)), dma=True, store=True)
        ctx.emit("sync", lambda q: q.dma_start(
            out=xnd.rearrange("(c p) t -> p c t", p=128),
            in_=xn_sb[:, :].rearrange("p (c t) -> p c t", c=NCH)),
            wait=(("vs", done),), dma=True, store=True)
        ctx.emit("sync", lambda q: q.dma_start(
            out=qpd[:, 0:32], in_=Atil[:, :]), wait=(("vs", done),),
            dma=True, store=True)
        ctx.emit("sync", lambda q: q.dma_start(
            out=qpd[:, 32:64], in_=sinb[:, :]), dma=True, store=True)
        ctx.emit("sync", lambda q: q.dma_start(
            out=qpd[:, 64:96], in_=qp[:, 128:160]), dma=True, store=True)
        ctx.emit("sync", lambda q: q.dma_start(
            out=qpd[:, 96:128], in_=qsc[:, 0:32]), dma=True, store=True)
        ctx.wait("sync", ("st", ctx.cnt["st"]))
    return nc




def _stage_qpv(b1, theta, wr, wi):
    def chunked(v):
        return np.ascontiguousarray(v.reshape(32, 128).T.astype(np.float32))
    out = np.zeros((128, 160), np.float32)
    out[:, 0:32] = chunked(theta)
    out[:, 32:64] = chunked(wr)
    out[:, 64:96] = chunked(wi)
    out[:, 96:128] = chunked(b1)
    return out


_PROGS = {}
DBG = None


def _dbg_snap(xs):
    if DBG is not None:
        out = np.zeros((NB, T, D), np.float32)
        for c in range(NCORES):
            out[c // 2, (c % 2) * TL:(c % 2 + 1) * TL, :] = xs[c].T
        DBG.append(out)


def _prog(name, builder, *a):
    if name not in _PROGS:
        _PROGS[name] = builder(*a)
    return _PROGS[name]


def _run8(nc, inmaps):
    return run_bass_kernel_spmd(nc, inmaps, core_ids=list(range(NCORES)))


def _asc(a):
    return np.ascontiguousarray(a)


def _kernel_unfused(**kw):
    x0 = kw["L0_attn_a_x"]
    cores = list(range(NCORES))
    tsl = [slice((c % 2) * TL, (c % 2 + 1) * TL) for c in cores]
    csl = [slice((c % 2) * 512, (c % 2 + 1) * 512) for c in cores]
    xs = [_asc(x0[c // 2, tsl[c], :].T.astype(np.float32)) for c in cores]

    def ln_pass(xs):
        nc = _prog("ln", build_epi_launch, "ln")
        r = _run8(nc, [{"xin": xs[c]} for c in cores])
        return [_asc(r.results[c]["xnout"]) for c in cores]

    def xn_join(xnh):
        return [_asc(np.concatenate([xnh[(c // 2) * 2], xnh[(c // 2) * 2 + 1]],
                                    axis=1)) for c in cores]

    def o_join(ofm):
        return [_asc(np.concatenate([ofm[(c // 2) * 2][:, tsl[c]],
                                     ofm[(c // 2) * 2 + 1][:, tsl[c]]], axis=0))
                for c in cores]

    def epi_pass(xs, ofull, warr, bpc=None):
        kind = "bias" if bpc is not None else "plain"
        nc = _prog("epi_" + kind, build_epi_launch, kind)
        ins = []
        for c in cores:
            m = {"xin": xs[c], "ofu": ofull[c], "warr": warr}
            if bpc is not None:
                m["bpc"] = bpc
            ins.append(m)
        r = _run8(nc, ins)
        return ([_asc(r.results[c]["xout"]) for c in cores],
                [_asc(r.results[c]["xnout"]) for c in cores])

    def moe_pass(xs, xnh):
        WgT = _asc(kw["L0_ffn_Wg"].T.astype(np.float32))
        nc1 = _prog("moe_route", build_moe_route_launch)
        r1 = _run8(nc1, [{"xnin": xnh[c], "wgt": WgT} for c in cores])
        xn_all = _asc(np.concatenate(
            [r1.results[c]["xntm_o"] for c in cores], 0))
        rout_all = _asc(np.concatenate(
            [r1.results[c]["rout_o"] for c in cores], 0))
        tot_all = _asc(np.concatenate(
            [r1.results[c]["tot_o"].T for c in cores], 0).astype(np.float32))
        nc2 = _prog("moe_expert", build_moe_expert_launch)
        ins2 = []
        for e in range(8):
            tbl = (e * CAP + np.arange(CAP)).astype(np.uint32)
            ins2.append({
                "xn_all": xn_all, "rout_all": rout_all, "tot_all": tot_all,
                "w1e": _asc(kw["L0_ffn_W1"][e].astype(np.float32)),
                "w2e": _asc(kw["L0_ffn_W2"][e].astype(np.float32)),
                "tblidx": _asc(tbl.reshape(CAP // 128, 128).T),
            })
        r2 = _run8(nc2, ins2)
        expo_all = _asc(np.concatenate(
            [r2.results[e]["expo"] for e in range(8)], 0))
        nc3 = _prog("moe_combine", build_moe_combine_launch)
        ins3 = []
        for c in cores:
            onehot = np.zeros((8, 1), np.float32)
            onehot[c] = 1.0
            ins3.append({
                "xin": xs[c], "expo_all": expo_all,
                "rout_my": _asc(rout_all[c * TL:(c + 1) * TL]),
                "tot_all": tot_all, "mycol": onehot,
            })
        r3 = _run8(nc3, ins3)
        return [_asc(r3.results[c]["xout"]) for c in cores]

    def l1ffn_pass(xs):
        nc = _prog("l1ffn", build_l1ffn_launch)
        qpv = _stage_qpv(kw["L1_ffn_b1"].astype(np.float32),
                         kw["L1_ffn_quantum_phases"].astype(np.float32),
                         kw["L1_ffn_quantum_weights_real"].astype(np.float32),
                         kw["L1_ffn_quantum_weights_imag"].astype(np.float32))
        b2v = _asc(kw["L1_ffn_b2"].astype(np.float32).reshape(8, 128).T)
        w1 = _asc(kw["L1_ffn_W1"].astype(np.float32))
        w2 = _asc(kw["L1_ffn_W2"].astype(np.float32))
        r = _run8(nc, [{"xin": xs[c], "w1": w1, "w2": w2, "qpv": qpv,
                        "b2v": b2v} for c in cores])
        return [_asc(r.results[c]["yout"]) for c in cores]

    f32 = np.float32
    xnh = ln_pass(xs)
    xnf = xn_join(xnh)
    ncg = _prog("gla", build_gla_front_launch)
    Wq, Wk, Wv = (kw["L0_attn_a_W_q"].astype(f32),
                  kw["L0_attn_a_W_k"].astype(f32),
                  kw["L0_attn_a_W_v"].astype(f32))
    Wz, bgt = kw["L0_attn_a_W_gate"].astype(f32), kw["L0_attn_a_b_gate"].astype(f32)
    rg = _run8(ncg, [{
        "xnin": xnf[c], "wq": _asc(Wq[:, csl[c]]), "wk": _asc(Wk[:, csl[c]]),
        "wv": _asc(Wv[:, csl[c]]), "wz": _asc(Wz[:, csl[c]]),
        "bg": _asc(bgt[csl[c]].reshape(4, 128).T)} for c in cores])
    ofull = o_join([rg.results[c]["ofm"] for c in cores])
    xs, xnh = epi_pass(xs, ofull, _asc(kw["L0_attn_a_W_o"].astype(f32)))
    _dbg_snap(xs)
    xs = moe_pass(xs, xnh)
    xnh = ln_pass(xs)
    xnf = xn_join(xnh)
    _dbg_snap(xs)
    ncw = _prog("win", build_win_front_launch)
    wq, wk, wv = (kw["L0_attn_b_wq"].astype(f32), kw["L0_attn_b_wk"].astype(f32),
                  kw["L0_attn_b_wv"].astype(f32))
    gq, gk = (kw["L0_attn_b_q_norm_w"].astype(f32),
              kw["L0_attn_b_k_norm_w"].astype(f32))
    rr = 2.0 ** (-8.0 / 8)
    SCW = 1.0 / np.sqrt(128.0)
    insw = []
    for c in cores:
        hg = c % 2
        my, ot = csl[c], slice((1 - hg) * 512, (2 - hg) * 512)
        slpv = np.zeros((1, 16), np.float32)
        for hl in range(4):
            gh = hg * 4 + hl
            slpv[0, hl] = (rr ** gh) / SCW
            slpv[0, 8 + hl] = -(rr ** gh) / SCW
        insw.append({
            "xnin": xnf[c],
            "wq": _asc(np.concatenate([wq[:, my], wq[:, ot]], 1)),
            "wk": _asc(np.concatenate([wk[:, my], wk[:, ot]], 1)),
            "wv": _asc(wv[:, my]),
            "gq": _asc(gq[my].reshape(4, 128).T),
            "gk": _asc(gk[my].reshape(4, 128).T),
            "slp": slpv,
        })
    rw = _run8(ncw, insw)
    ofull = o_join([rw.results[c]["ofm"] for c in cores])
    xs, xnh = epi_pass(xs, ofull, _asc(kw["L0_attn_b_wo"].astype(f32)))
    _dbg_snap(xs)
    xs = moe_pass(xs, xnh)
    xnh = ln_pass(xs)
    xnf = xn_join(xnh)
    _dbg_snap(xs)
    ncd = _prog("delta", build_delta_front_launch)
    dWq, dWk, dWv, dWg = (kw["L1_attn_a_Wq"].astype(f32),
                          kw["L1_attn_a_Wk"].astype(f32),
                          kw["L1_attn_a_Wv"].astype(f32),
                          kw["L1_attn_a_Wg"].astype(f32))
    Wb, Wfd, Wsd = (kw["L1_attn_a_Wb"].astype(f32),
                    kw["L1_attn_a_Wfd"].astype(f32),
                    kw["L1_attn_a_Wsd"].astype(f32))
    bfd, bsd = kw["L1_attn_a_bfd"].astype(f32), kw["L1_attn_a_bsd"].astype(f32)
    rep4 = np.zeros((4, 128), np.float32)
    for p in range(4):
        rep4[p, p * 32:(p + 1) * 32] = 1.0
    insd = []
    for c in cores:
        hg = c % 2
        hs = slice(hg * 8, (hg + 1) * 8)
        wb = np.concatenate([Wb.T[:, hs], Wfd.T[:, hs], Wsd.T[:, hs]], 1)
        brow = np.concatenate([np.zeros(8, np.float32), bfd[hs],
                               bsd[hs]])[None, :]
        insd.append({
            "xnin": xnf[c],
            "wq": _asc(dWq.T[:, csl[c]]), "wk": _asc(dWk.T[:, csl[c]]),
            "wv": _asc(dWv.T[:, csl[c]]), "wg": _asc(dWg.T[:, csl[c]]),
            "wb": _asc(wb), "brow": _asc(brow), "rep4": rep4,
        })
    rd = _run8(ncd, insd)
    ofull = o_join([rd.results[c]["ofm"] for c in cores])
    xs, xnh = epi_pass(xs, ofull, _asc(kw["L1_attn_a_Wo"].astype(f32).T))
    _dbg_snap(xs)
    xs = l1ffn_pass(xs)
    xnh = ln_pass(xs)
    xnf = xn_join(xnh)
    _dbg_snap(xs)
    nca = _prog("attn", build_attn_front_launch)
    aWq, aWk, aWv = (kw["L1_attn_b_Wq"].astype(f32),
                     kw["L1_attn_b_Wk"].astype(f32),
                     kw["L1_attn_b_Wv"].astype(f32))
    sincos = stage_yarn_sincos()
    ra = _run8(nca, [{
        "xnin": xnf[c], "wq": _asc(aWq.T[:, csl[c]]),
        "wk": _asc(aWk.T[:, csl[c]]), "wv": _asc(aWv.T[:, csl[c]]),
        "sincos": sincos} for c in cores])
    aofm = [ra.results[c]["ofm"] for c in cores]
    ofull = []
    for c in cores:
        b = c // 2
        att = np.concatenate([aofm[b * 2], aofm[b * 2 + 1]], 0)
        att = att.reshape(16, 64, T).transpose(0, 2, 1)
        ysc = att.reshape(16, 256, 16, 64).reshape(T, D).T
        ofull.append(_asc(ysc[:, tsl[c]]))
    bpc = _asc(kw["L1_attn_b_bp"].astype(f32).reshape(8, 128).T)
    xs, _ = epi_pass(xs, ofull, _asc(kw["L1_attn_b_Wp"].astype(f32).T), bpc)
    _dbg_snap(xs)
    xs = l1ffn_pass(xs)
    out = np.zeros((NB, T, D), np.float32)
    for c in cores:
        out[c // 2, tsl[c], :] = xs[c].T
    return out



def emit_attn_front(ctx, xn_sb, q_fm, k_fm, v_tm, o_fm, sincos, wq_sb, wk_sb,
                    wv_sb, w_ready, ps, sc, nheads=8, Ttok=T, causal=True):
    nt = Ttok // 128
    NH2 = nheads // 2
    e = ctx.emit
    vdone = 0
    mx_init = False
    tr_rd = 0
    pq_rd = (0, 0, 0)
    for tt in range(nt):
        tv = {}
        for name, wsb, psn in (("q", wq_sb, "pq"), ("k", wk_sb, "pk"),
                               ("v", wv_sb, "pv")):
            for kc in range(NCH):
                tv[name] = e("tensor", lambda t, kc=kc, wsb=wsb, psn=psn, tt=tt: t.matmul(
                    ps[psn][:, :],
                    xn_sb[:, kc * Ttok + tt * 128: kc * Ttok + tt * 128 + 128],
                    wsb[:, kc * 512:(kc + 1) * 512],
                    start=(kc == 0), stop=(kc == NCH - 1)),
                    wait=(w_ready + (("vs", pq_rd[0]), ("vs", pq_rd[1]),
                                     ("vs", pq_rd[2]))) if kc == 0 else (), chain=(kc != 0))
        sins = sincos[:, tt * 64: tt * 64 + 32]
        coss = sincos[:, tt * 64 + 32: tt * 64 + 64]

        def b3(ap):
            return bass.AP(tensor=ap.tensor, offset=ap.offset,
                           ap=[list(ap.ap[0])] + [[0, NH2 * 2]] + [list(ap.ap[1])])

        sins_b, coss_b = b3(sins), b3(coss)
        rope_done = {}
        qtm_rd = getattr(ctx, "_qtm_rd", {"q": (0, 0), "k": (0, 0)})
        def _evod(ap2d, off):
            v = ap2d[:, off:off + 1]
            return bass.AP(tensor=v.tensor, offset=v.offset,
                           ap=[list(v.ap[0]), [64, 8], [2, 32]])

        for name, psn, dst in (("q", "pq", sc["qtm"]), ("k", "pk", sc["ktm"])):
            src = ps[psn][:, :]
            ev = _evod(src, 0)
            od = _evod(src, 1)
            dev = _evod(dst[:, :], 0)
            dod = _evod(dst[:, :], 1)
            t1 = sc["rsc"][:, 0:256].rearrange("p (h i) -> p h i", i=32)
            t2 = sc["rsc"][:, 256:512].rearrange("p (h i) -> p h i", i=32)
            e("vector", lambda v, od=od, t1=t1: v.tensor_tensor(
                out=t1, in0=od, in1=sins_b, op=OP.mult),
                wait=(("ts", max(tv[name], qtm_rd[name][0])),
                      ("ss", qtm_rd[name][1])))
            e("vector", lambda v, ev=ev, t2=t2: v.tensor_tensor(
                out=t2, in0=ev, in1=sins_b, op=OP.mult))
            e("vector", lambda v, ev=ev, t1=t1, dev=dev: v.tensor_tensor(
                out=t1, in0=ev, in1=coss_b, op=OP.mult) if False else
                v.tensor_tensor(out=dev, in0=ev, in1=coss_b, op=OP.mult))
            e("vector", lambda v, dev=dev, t1=t1: v.tensor_tensor(
                out=dev, in0=dev, in1=t1, op=OP.subtract))
            e("vector", lambda v, od=od, dod=dod: v.tensor_tensor(
                out=dod, in0=od, in1=coss_b, op=OP.mult))
            rope_done[name] = e("vector", lambda v, dod=dod, t2=t2: v.tensor_tensor(
                out=dod, in0=dod, in1=t2, op=OP.add))
        vdst = v_tm[:, tt * (nheads * 65): tt * (nheads * 65) + nheads * 65]
        vdst3 = vdst.rearrange("p (h c) -> p h c", c=65)[:, :, 0:64]
        vdone = e("vector", lambda v, vdst3=vdst3: v.tensor_copy(
            vdst3, ps["pv"][:, :].rearrange("p (h c) -> p h c", c=64)),
            wait=(("ts", tv["v"]),))
        nred = getattr(ctx, "_nred", 0)
        sqv = e("scalar", lambda s: s.activation(
            out=sc["sq"][:, :], in_=sc["qtm"][:, :], func=AF.Square),
            wait=(("vs", max(rope_done["q"], nred)),))
        qtm_rd["q"] = (qtm_rd["q"][0], sqv)
        nred = e("vector", lambda v: v.tensor_reduce(
            out=sc["n2"][:, 0:8], in_=sc["sq"][:, :].rearrange(
                "p (h c) -> p h c", c=64), axis=AX.X, op=OP.add),
            wait=(("ss", sqv),))
        sqv = e("scalar", lambda s: s.activation(
            out=sc["sq"][:, :], in_=sc["ktm"][:, :], func=AF.Square),
            wait=(("vs", max(rope_done["k"], nred)),))
        qtm_rd["k"] = (qtm_rd["k"][0], sqv)
        nred = e("vector", lambda v: v.tensor_reduce(
            out=sc["n2"][:, 8:16], in_=sc["sq"][:, :].rearrange(
                "p (h c) -> p h c", c=64), axis=AX.X, op=OP.add),
            wait=(("ss", sqv),))
        ctx._nred = nred
        if not mx_init:
            mxv = e("vector", lambda v: v.tensor_copy(sc["mx"][:, :], sc["n2"][:, :]))
            mx_init = True
        else:
            mxv = e("vector", lambda v: v.tensor_tensor(
                out=sc["mx"][:, :], in0=sc["mx"][:, :], in1=sc["n2"][:, :],
                op=OP.max))
        qkdone = {}
        for name, src in (("q", sc["qtm"]), ("k", sc["ktm"])):
            dst_fm = q_fm if name == "q" else k_fm
            tjl = 0
            for j in range(4):
                tj = e("tensor", lambda t, src=src, j=j: t.transpose(
                    ps["pt"][:, j * 128:(j + 1) * 128], src[:, j * 128:(j + 1) * 128],
                    ctx.ident_bf[:, :]),
                    wait=(("vs", rope_done[name]), ("vs", tr_rd)))
                tjl = tj
                tr_rd = e("vector", lambda v, dst_fm=dst_fm, j=j, tt=tt: v.tensor_copy(
                    dst_fm[:, j * Ttok + tt * 128: j * Ttok + tt * 128 + 128],
                    ps["pt"][:, j * 128:(j + 1) * 128]),
                    wait=(("ts", tj),))
            qtm_rd[name] = (tjl, 0)
        ctx._qtm_rd = qtm_rd
        pq_rd = (rope_done["k"], tr_rd, vdone)
    vsnow = ctx.cnt["vs"]
    tv = e("tensor", lambda t: t.transpose(
        ps["pb2"][0:16, 0:128], sc["mx"][:, :], ctx.ident_f32[:, :]),
        wait=(("vs", vsnow),))
    mv = e("vector", lambda v: v.tensor_reduce(
        out=sc["row8"][0:16, 0:1], in_=ps["pb2"][0:16, 0:128],
        axis=AX.X, op=OP.max), wait=(("ts", tv),))
    tv = e("tensor", lambda t: t.transpose(
        ps["pb2"][0:1, 0:16], sc["row8"][0:16, 0:1], ctx.ident_f32[0:16, 0:16]),
        wait=(("vs", mv),))
    e("vector", lambda v: v.tensor_reduce(
        out=sc["row8"][0:1, 2:3], in_=ps["pb2"][0:1, 0:8], axis=AX.X, op=OP.max),
        wait=(("ts", tv),))
    mv = e("vector", lambda v: v.tensor_reduce(
        out=sc["row8"][0:1, 3:4], in_=ps["pb2"][0:1, 8:16], axis=AX.X, op=OP.max))
    e("vector", lambda v: v.tensor_tensor(
        out=sc["row8"][0:1, 4:5], in0=sc["row8"][0:1, 2:3],
        in1=sc["row8"][0:1, 3:4], op=OP.mult))
    sv = e("scalar", lambda s2: s2.activation(
        out=sc["row8"][0:1, 5:6], in_=sc["row8"][0:1, 4:5], func=AF.Sqrt),
        wait=(("vs", ctx.cnt["vs"]),))
    mv = e("vector", lambda v: v.tensor_scalar(
        out=sc["row8"][0:1, 6:7], in0=sc["row8"][0:1, 5:6], scalar1=-1.0,
        scalar2=-8.0, op0=OP.mult, op1=OP.add), wait=(("ss", sv),))
    e("vector", lambda v: v.tensor_scalar(
        out=sc["mrow"][0:1, 0:512], in0=sc["sq"][0:1, 0:512],
        scalar1=0.0, scalar2=sc["row8"][0:1, 6:7], op0=OP.mult, op1=OP.add))
    mrow_done = e("vector", lambda v: v.tensor_copy(
        sc["mrowbf"][0:1, 0:512], sc["mrow"][0:1, 0:512]))
    NT2 = Ttok // 512
    join = (("vs", ctx.cnt["vs"]), ("ss", ctx.cnt["ss"]), ("ts", ctx.cnt["ts"]))
    qk_rd = [0, 0]
    po_rd = 0
    bc_rd = 0
    pr_rd = [0, 0]
    for h in range(nheads):
        hb = (h % 2) * 64
        hc = h // 2
        for g in range(NT2):
            nslots = 4 * g + 4
            pvt = 0
            for j in range(nslots):
                bi = j % 2
                qps = ps["pq"] if bi == 0 else ps["pk"]
                t1 = e("tensor", lambda t, hb=hb, hc=hc, j=j, g=g, qps=qps: t.matmul(
                    qps[:, :],
                    k_fm[hb:hb + 64, hc * Ttok + j * 128: hc * Ttok + j * 128 + 128],
                    q_fm[hb:hb + 64, hc * Ttok + g * 512: hc * Ttok + g * 512 + 512],
                    start=True, stop=False, tile_position=(hb, 0)),
                    wait=(join + (("vs", mrow_done),)) if (h == 0 and g == 0 and j == 0)
                    else (("ss", qk_rd[bi]), ("vs", mrow_done)))
                t1 = e("tensor", lambda t, qps=qps: t.matmul(
                    qps[:, :], sc["onesbf"][0:1, 0:128], sc["mrowbf"][0:1, 0:512],
                    start=False, stop=True))
                prob = sc["prob"][:, bi * 512:(bi + 1) * 512]
                sv = e("scalar", lambda s2, prob=prob, qps=qps: s2.activation(
                    out=prob, in_=qps[:, :], func=AF.Exp, scale=0.125),
                    wait=(("ts", max(t1, pr_rd[bi])),))
                qk_rd[bi] = sv
                pwait = ("ss", sv)
                if j >= 4 * g:
                    gv = e("gpsimd", lambda gq, prob=prob, j=j, g=g: gq.affine_select(
                        prob, prob, pattern=[[1, 512]], compare_op=OP.is_ge,
                        fill=0.0, base=g * 512 - j * 128, channel_multiplier=-1),
                        wait=(("ss", sv),))
                    pwait = ("gs", gv)
                pvt = e("tensor", lambda t, j=j, h=h, prob=prob, nslots=nslots: t.matmul(
                    ps["pv"][0:65, :],
                    v_tm[:, j * (nheads * 65) + h * 65: j * (nheads * 65) + h * 65 + 65],
                    prob, start=(j == 0), stop=(j == nslots - 1)),
                    wait=(pwait,) + ((("vs", po_rd),) if j == 0 else ()))
                pr_rd[bi] = pvt
            rv = e("vector", lambda v: v.reciprocal(
                out=sc["rrow"][0:1, 0:512],
                in_=ps["pv"][64:65, 0:512]), wait=(("ts", pvt),))
            bv = e("tensor", lambda t: t.matmul(
                ps["pb2"][0:64, 0:512], ctx.onesrow[0:1, 0:64], sc["rrow"][0:1, 0:512],
                start=True, stop=True), wait=(("vs", max(rv, bc_rd)),))
            ov = e("vector", lambda v: v.tensor_copy(
                sc["osc"][0:64, 0:512], ps["pv"][0:64, 0:512]))
            po_rd = ov
            bc_rd = e("vector", lambda v, h=h, g=g, hb=hb, hc=hc: v.tensor_tensor(
                out=o_fm[hb:hb + 64, hc * Ttok + g * 512: hc * Ttok + g * 512 + 512],
                in0=sc["osc"][0:64, 0:512], in1=ps["pb2"][0:64, 0:512], op=OP.mult),
                wait=(("ts", bv),))
    return bc_rd


def build_attn_front_launch():
    nc = bass.Bass("TRN2", target_bir_lowering=False, debug=False,
                   num_devices=NCORES)
    xnin = nc.dram_tensor("xnin", [D, T], BF16, kind="ExternalInput").ap()
    wq = nc.dram_tensor("wq", [D, 512], F32, kind="ExternalInput").ap()
    wk = nc.dram_tensor("wk", [D, 512], F32, kind="ExternalInput").ap()
    wv = nc.dram_tensor("wv", [D, 512], F32, kind="ExternalInput").ap()
    sincos_d = nc.dram_tensor("sincos", [128, 32 * 64], F32,
                              kind="ExternalInput").ap()
    ofm = nc.dram_tensor("ofm", [512, T], BF16, kind="ExternalOutput").ap()
    qfm_d = nc.dram_tensor("qfm_d", [512, T], BF16, kind="ExternalOutput").ap()
    kfm_d = nc.dram_tensor("kfm_d", [512, T], BF16, kind="ExternalOutput").ap()
    vtm_d = nc.dram_tensor("vtm_d", [128, 32 * 520], BF16, kind="ExternalOutput").ap()

    import contextlib
    es = contextlib.ExitStack()

    def sb(name, shape, dt):
        return es.enter_context(nc.sbuf_tensor(name, shape, dt))

    def pst(name, shape, dt=F32):
        return es.enter_context(nc.psum_tensor(name, shape, dt))

    with es:
        xn_sb = sb("xn_sb", [128, NCH * T], BF16)
        q_fm = sb("q_fm", [128, 4 * T], BF16)
        k_fm = sb("k_fm", [128, 4 * T], BF16)
        v_tm = sb("v_tm", [128, 32 * 520], BF16)
        sincos = sb("sincos_sb", [128, 2048], F32)
        wq_sb = sb("wq_sb", [128, 8 * 512], BF16)
        wk_sb = sb("wk_sb", [128, 8 * 512], BF16)
        wv_sb = sb("wv_sb", [128, 8 * 512], BF16)
        sc = {
            "qtm": sb("qtm", [128, 512], BF16),
            "ktm": sb("ktm", [128, 512], BF16),
            "sq": sb("sqs", [128, 512], F32),
            "rsc": sb("rsc", [128, 512], F32),
            "n2": sb("n2", [128, 16], F32),
            "mx": sb("mx", [128, 16], F32),
            "row8": sb("row8", [16, 16], F32),
            "mrowbf": sb("mrowbf", [1, 512], BF16),
            "onesbf": sb("onesbf", [1, 128], BF16),
            "prob": sb("prob", [128, 1024], BF16),
            "osc": sb("osc", [128, 512], F32),
        }
        sc["mrow"] = sc["sq"]
        sc["rrow"] = sc["rsc"]
        ident = sb("ident", [128, 128], BF16)
        identf = sb("identf", [128, 128], F32)
        onesb = sb("onesb", [128, 1], F32)
        onesrow = sb("onesrow", [1, 128], F32)
        cbank = sb("cbank", [128, 16], F32)
        ps = {
            "pq": pst("pq", [128, 512]),
            "pk": pst("pk", [128, 512]),
            "pv": pst("pv", [128, 512]),
            "pt": pst("pt", [128, 512], BF16),
            "pb2": pst("pb2", [128, 512]),
        }
        block = es.enter_context(nc.Block())
        ctx = Ctx(nc, block)
        ctx.cbank = cbank
        emit_prelude(ctx, ident, onesb, onesrow, identf)
        ctx.emit("gpsimd", lambda g: g.memset(sc["onesbf"][:, :], 1.0))
        v3 = v_tm[:, :].rearrange("p (t h c) -> p (t h) c", c=65, h=8)[:, :, 64:65]
        gv = ctx.emit("gpsimd", lambda g: g.memset(v3, 1.0))
        xv = ctx.emit("sync", lambda q: q.dma_start(
            out=xn_sb[:, :].rearrange("p (c t) -> p c t", c=NCH),
            in_=xnin.rearrange("(c p) t -> p c t", p=128)), dma=True)
        sv = ctx.emit("sync", lambda q: q.dma_start(out=sincos[:, :], in_=sincos_d),
                      dma=True)
        wvv = 0
        for w_d, w_s in ((wq, wq_sb), (wk, wk_sb), (wv, wv_sb)):
            wvv = ctx.emit("gpsimd", lambda g, w_d=w_d, w_s=w_s: g.dma_start(
                out=w_s[:, :].rearrange("p (c f) -> p c f", c=NCH),
                in_=w_d.rearrange("(c p) f -> p c f", p=128)), dma=True)
        w_ready = (("dS", xv), ("dS", sv), ("dG", wvv), ("gs", ctx.cnt["gs"]))
        o_fm = xn_sb[:, 0:4 * T]
        done = emit_attn_front(ctx, xn_sb, q_fm, k_fm, v_tm, o_fm, sincos,
                               wq_sb, wk_sb, wv_sb, w_ready, ps, sc)
        ctx.emit("sync", lambda q: q.dma_start(
            out=ofm.rearrange("(c p) t -> p c t", p=128),
            in_=o_fm.rearrange("p (c t) -> p c t", c=4)),
            wait=(("vs", done),), dma=True, store=True)
        ctx.emit("sync", lambda q: q.dma_start(
            out=qfm_d.rearrange("(c p) t -> p c t", p=128),
            in_=q_fm[:, :].rearrange("p (c t) -> p c t", c=4)),
            wait=(("vs", done), ("ss", ctx.cnt["ss"])), dma=True, store=True)
        ctx.emit("sync", lambda q: q.dma_start(
            out=kfm_d.rearrange("(c p) t -> p c t", p=128),
            in_=k_fm[:, :].rearrange("p (c t) -> p c t", c=4)), dma=True, store=True)
        ctx.emit("sync", lambda q: q.dma_start(
            out=vtm_d, in_=v_tm[:, :]), dma=True, store=True)
        ctx.wait("sync", ("st", ctx.cnt["st"]))
    return nc


def stage_yarn_sincos():
    i = np.arange(0, 64, 2, dtype=np.float32)
    inv_freq = (1.0 / (10000.0 ** (i / (64.0 * 4.0)))).astype(np.float32)
    pos = np.arange(T, dtype=np.float32)
    ys = (1.0 / np.sqrt(1.0 + (pos / 1024.0) ** 2)).astype(np.float32)
    theta = (pos[:, None] * inv_freq[None, :] * ys[:, None]).astype(np.float32)
    sin = np.sin(theta).astype(np.float32)
    cos = np.cos(theta).astype(np.float32)
    out = np.zeros((128, 32 * 64), np.float32)
    for tt in range(32):
        out[:, tt * 64: tt * 64 + 32] = sin[tt * 128:(tt + 1) * 128]
        out[:, tt * 64 + 32: tt * 64 + 64] = cos[tt * 128:(tt + 1) * 128]
    return out




def emit_gla_front(ctx, xn_sb, wq_sb, wk_sb, wv_sb, wz_sb, bgate, o_fm, S_sb,
                   ps, sc, w_ready, Ttok=T):
    e = ctx.emit
    NG = Ttok // 512
    e("gpsimd", lambda g: g.memset(S_sb[:, :], 0.0))
    gready = ctx.cnt["gs"]
    ev_rd = {"pq": ("vs", 0), "pk": ("vs", 0), "pz": ("ss", 0),
             "pv": ("vs", 0)}
    buf_rd = 0
    s_upd = {}
    o_ev = 0
    pr_ev = 0
    ktm_ev = 0
    pds_rd = 0
    prb_rd = 0
    ktld_rd = 0
    khat_rd = 0
    for g in range(NG):
        evd = {}
        for name, wsb, psn, tm in (("q", wq_sb, "pq", False), ("k", wk_sb, "pk", False),
                                   ("z", wz_sb, "pz", False), ("v", wv_sb, "pv", True)):
            for dc in range(4):
                tv = 0
                for kc in range(NCH):
                    if not tm:
                        tv = e("tensor", lambda t, kc=kc, dc=dc, wsb=wsb, psn=psn, g=g: t.matmul(
                            ps[psn][:, :],
                            wsb[:, kc * 512 + dc * 128: kc * 512 + dc * 128 + 128],
                            xn_sb[:, kc * Ttok + g * 512: kc * Ttok + g * 512 + 512],
                            start=(kc == 0), stop=(kc == NCH - 1)),
                            wait=(w_ready + (ev_rd[psn], ("gs", gready),
                                             ("ts", buf_rd)))
                            if kc == 0 else (), chain=(kc != 0))
                    else:
                        tv = e("tensor", lambda t, kc=kc, dc=dc, wsb=wsb, psn=psn, g=g: t.matmul(
                            ps[psn][:, :],
                            xn_sb[:, kc * Ttok + g * 512 + dc * 128:
                                  kc * Ttok + g * 512 + dc * 128 + 128],
                            wsb[:, kc * 512:(kc + 1) * 512],
                            start=(kc == 0), stop=(kc == NCH - 1)),
                            wait=(ev_rd[psn], ("ts", buf_rd)) if kc == 0 else (), chain=(kc != 0))
                if name == "q":
                    ev_rd[psn] = ("vs", e("vector", lambda v, dc=dc: v.tensor_copy(
                        sc["qg"][:, dc * 512:(dc + 1) * 512], ps["pq"][:, :]),
                        wait=(("ts", tv),)))
                elif name == "k":
                    ev_rd[psn] = ("vs", e("vector", lambda v, dc=dc: v.tensor_copy(
                        sc["kg"][:, dc * 512:(dc + 1) * 512], ps["pk"][:, :]),
                        wait=(("ts", tv),)))
                elif name == "v":
                    ev_rd[psn] = ("vs", e("vector", lambda v, dc=dc: v.tensor_copy(
                        sc["vg"][:, dc * 512:(dc + 1) * 512], ps["pv"][:, :]),
                        wait=(("ts", tv),)))
                else:
                    sv = e("scalar", lambda s2, dc=dc: s2.activation(
                        out=sc["asb"][:, :], in_=ps["pz"][:, :], func=AF.Sigmoid,
                        bias=bgate[:, dc:dc + 1], scale=1.0),
                        wait=(("ts", tv), ("vs", ctx.cnt["vs"] if g == 0 else 0)))
                    ev_rd[psn] = ("ss", e("scalar", lambda s2, dc=dc: s2.activation(
                        out=sc["lag"][:, dc * 512:(dc + 1) * 512], in_=sc["asb"][:, :],
                        func=AF.Ln), wait=(("ss", sv),)))
        proj_ev = tuple(ev_rd.values())
        for tt in range(4):
            for dc in range(4):
                la = sc["lag"][:, dc * 512 + tt * 128: dc * 512 + tt * 128 + 128]
                scv = e("vector", lambda v, la=la: v.tensor_tensor_scan(
                    out=sc["Bt"][:, :], data0=la, data1=la, initial=0.0,
                    op0=OP.add, op1=OP.bypass),
                    wait=proj_ev + (("ss", ctx.cnt["ss"]),))
                dxv = e("scalar", lambda s2: s2.activation(
                    out=sc["Dex"][:, :], in_=sc["Bt"][:, :], func=AF.Exp),
                    wait=(("vs", scv),))
                dvv = e("scalar", lambda s2: s2.activation(
                    out=sc["Div"][:, :], in_=sc["Bt"][:, :], func=AF.Exp, scale=-1.0))
                qv = e("vector", lambda v, dc=dc, tt=tt: v.tensor_tensor(
                    out=sc["qt32"][:, :],
                    in0=sc["qg"][:, dc * 512 + tt * 128: dc * 512 + tt * 128 + 128],
                    in1=sc["Dex"][:, :], op=OP.mult),
                    wait=(("ss", dvv), ("ts", khat_rd)))
                qbv = e("vector", lambda v: v.tensor_copy(sc["qtb"][:, :], sc["qt32"][:, :]))
                khv = e("vector", lambda v, dc=dc, tt=tt: v.tensor_tensor(
                    out=sc["khat"][:, :],
                    in0=sc["kg"][:, dc * 512 + tt * 128: dc * 512 + tt * 128 + 128],
                    in1=sc["Div"][:, :], op=OP.mult))
                ktv = e("vector", lambda v: v.tensor_scalar(
                    out=sc["ktld"][:, :], in0=sc["khat"][:, :],
                    scalar1=sc["Dex"][:, 127:128], scalar2=None, op0=OP.mult),
                    wait=(("ts", ktld_rd),))
                trv = e("tensor", lambda t: t.transpose(
                    ps["ptr"][:, :], sc["ktld"][:, :], ctx.ident_bf[:, :]),
                    wait=(("vs", ktv), ("vs", ktm_ev)))
                ktld_rd = trv
                ktm_ev = e("vector", lambda v: v.tensor_copy(
                    sc["ktm"][:, :], ps["ptr"][:, :]), wait=(("ts", trv),))
                for h2 in range(2):
                    hb = h2 * 64
                    key = (dc, h2)
                    scv2 = e("tensor", lambda t, hb=hb: t.matmul(
                        ps["pscore"][:, :], sc["khat"][hb:hb + 64, :],
                        sc["qtb"][hb:hb + 64, :], start=True, stop=True,
                        tile_position=(hb, 0)),
                        wait=(("vs", max(qbv, khv)), ("vs", pr_ev)))
                    khat_rd = scv2
                    prv = e("vector", lambda v: v.tensor_copy(
                        sc["prb"][:, :], ps["pscore"][:, :]),
                        wait=(("ts", max(scv2, prb_rd)),))
                    pr_ev = prv
                    mkv = e("gpsimd", lambda gq: gq.affine_select(
                        sc["prb"][:, :], sc["prb"][:, :], pattern=[[1, 128]],
                        compare_op=OP.is_ge, fill=0.0, base=0,
                        channel_multiplier=-1), wait=(("vs", prv),))
                    ohv = e("tensor", lambda t, hb=hb, dc=dc: t.matmul(
                        ps["po"][0:64, :], S_sb[hb:hb + 64, dc * 64:(dc + 1) * 64],
                        sc["qt32"][hb:hb + 64, :], start=True, stop=False,
                        tile_position=(hb, 0)),
                        wait=(("vs", s_upd.get(key, 0)), ("vs", o_ev)))
                    oiv = e("tensor", lambda t, hb=hb, dc=dc, tt=tt, g=g: t.matmul(
                        ps["po"][0:64, :],
                        sc["vg"][:, tt * 512 + dc * 128 + hb: tt * 512 + dc * 128 + hb + 64],
                        sc["prb"][:, :], start=False, stop=True),
                        wait=(("gs", mkv),))
                    prb_rd = oiv
                    dsv = e("tensor", lambda t, hb=hb, dc=dc, tt=tt: t.matmul(
                        ps["pds"][0:64, 0:64], sc["ktm"][:, hb:hb + 64],
                        sc["vg"][:, tt * 512 + dc * 128 + hb: tt * 512 + dc * 128 + hb + 64],
                        start=True, stop=True),
                        wait=(("vs", ktm_ev), ("vs", s_upd.get(key, 0)),
                              ("vs", pds_rd)))
                    s_upd[key] = e("vector", lambda v, hb=hb, dc=dc: v.scalar_tensor_tensor(
                        out=S_sb[hb:hb + 64, dc * 64:(dc + 1) * 64],
                        in0=S_sb[hb:hb + 64, dc * 64:(dc + 1) * 64],
                        scalar=sc["Dex"][hb:hb + 64, 127:128],
                        in1=ps["pds"][0:64, 0:64], op0=OP.mult, op1=OP.add),
                        wait=(("ts", max(dsv, ohv)),))
                    pds_rd = s_upd[key]
                    o_ev = e("vector", lambda v, hb=hb, dc=dc, g=g, tt=tt: v.tensor_copy(
                        o_fm[hb:hb + 64, dc * Ttok + g * 512 + tt * 128:
                             dc * Ttok + g * 512 + tt * 128 + 128],
                        ps["po"][0:64, :]), wait=(("ts", oiv),))
            buf_rd = ctx.cnt["ts"]
    return o_ev




def emit_win_front(ctx, xn_dram, xs_sb, q_fm, k_fm, v_tm, o_fm, wq_sb, wk_sb,
                   wv_sb, gq_row, gk_row, slopes8, ps, sc, w_ready,
                   Ttok=T):
    e = ctx.emit
    nt = Ttok // 128
    NH = 4
    SC = 1.0 / math.sqrt(128.0)
    mx_init = False
    tr_rd = 0
    war = {"pq": 0, "pk": 0, "pv": 0, "pqs": 0, "pks": 0, "sq_rd": 0,
           "qtm": (0, 0), "ktm": (0, 0)}
    wfold = tuple(w_ready)

    xld = [0, 0]
    xs_user = [0, 0]

    def xload(tt, bi):
        return e("sync", lambda q, tt=tt, bi=bi: q.dma_start(
            out=xs_sb[:, bi * 1024:(bi + 1) * 1024]
            .rearrange("p (c t) -> p c t", c=NCH),
            in_=xn_dram[:, tt * 128:(tt + 1) * 128]
            .rearrange("(c p) t -> p c t", p=128)),
            wait=(("ts", xs_user[bi]),), dma=True)
    xld[0] = xload(0, 0)
    for tt in range(nt):
        bi = tt % 2
        if tt + 1 < nt:
            xld[1 - bi] = xload(tt + 1, 1 - bi)
        xs = xs_sb[:, bi * 1024:(bi + 1) * 1024]
        tv = {}
        sq_rd = war["sq_rd"]
        for name, wsb, psn, col in (("q", wq_sb, "pq", 0), ("k", wk_sb, "pk", 1)):
            for hf in (1, 0):
                tvv = 0
                for kc in range(NCH):
                    tvv = e("tensor", lambda t, kc=kc, hf=hf, wsb=wsb, psn=psn, xs=xs: t.matmul(
                        ps[psn][:, 0:512],
                        xs[:, kc * 128:(kc + 1) * 128],
                        wsb[:, kc * 1024 + hf * 512: kc * 1024 + hf * 512 + 512],
                        start=(kc == 0), stop=(kc == NCH - 1)),
                        wait=(wfold + (("dS", xld[bi]), ("vs", war[psn]),
                                       ("ss", war[psn + "s"]))) if kc == 0 else (), chain=(kc != 0))
                tvv = ctx.tnop()
                tv[(name, hf)] = tvv
                sv = e("scalar", lambda s2, psn=psn: s2.activation(
                    out=sc["sq"][:, :], in_=ps[psn][:, 0:512], func=AF.Square),
                    wait=(("ts", tvv), ("vs", sq_rd)))
                war[psn + "s"] = sv
                sq_rd = e("vector", lambda v, col=col, hf=hf: v.tensor_reduce(
                    out=sc["n2"][:, 2 * col + hf: 2 * col + hf + 1],
                    in_=sc["sq"][:, :], axis=AX.X, op=OP.add),
                    wait=(("ss", sv),))
            vv = e("vector", lambda v, col=col: v.tensor_tensor(
                out=sc["rstd"][:, col:col + 1], in0=sc["n2"][:, 2 * col:2 * col + 1],
                in1=sc["n2"][:, 2 * col + 1:2 * col + 2], op=OP.add))
            e("scalar", lambda s2, col=col: s2.activation(
                out=sc["rstd"][:, col:col + 1], in_=sc["rstd"][:, col:col + 1],
                func=AF.Ln, bias=ctx.constcol(1e-6), scale=1.0 / D),
                wait=(("vs", vv),))
            e("scalar", lambda s2, col=col: s2.activation(
                out=sc["rstd"][:, col:col + 1], in_=sc["rstd"][:, col:col + 1],
                func=AF.Exp, scale=-0.5))
            rsv = ctx.cnt["ss"]
            dst = sc["qtm"] if name == "q" else sc["ktm"]
            wk = war["qtm"] if name == "q" else war["ktm"]
            nv = e("vector", lambda v, psn=psn, col=col, dst=dst: v.tensor_scalar(
                out=dst[:, :], in0=ps[psn][:, 0:512],
                scalar1=sc["rstd"][:, col:col + 1], scalar2=None, op0=OP.mult),
                wait=(("ss", rsv), ("ts", wk[0]), ("ss", wk[1])))
            war[psn] = nv
            if "rstd_dbg" in sc:
                e("vector", lambda v, col=col, tt=tt: v.tensor_copy(
                    sc["rstd_dbg"][:, tt * 2 + col: tt * 2 + col + 1],
                    sc["rstd"][:, col:col + 1]), wait=(("ss", rsv),))
        war["sq_rd"] = sq_rd
        qv = war["pq"]
        kv = war["pk"]
        tvv = 0
        for kc in range(NCH):
            tvv = e("tensor", lambda t, kc=kc, xs=xs: t.matmul(
                ps["pv"][:, 0:512],
                xs[:, kc * 128:(kc + 1) * 128],
                wv_sb[:, kc * 512:(kc + 1) * 512],
                start=(kc == 0), stop=(kc == NCH - 1)),
                wait=(wfold + (("vs", war["pv"]),)) if kc == 0 else (), chain=(kc != 0))
        xs_user[bi] = tvv
        vdst = v_tm[:, tt * (NH * 129): tt * (NH * 129) + NH * 129]
        vdst3 = vdst.rearrange("p (h c) -> p h c", c=129)[:, :, 0:128]
        war["pv"] = e("vector", lambda v, vdst3=vdst3: v.tensor_copy(
            vdst3, ps["pv"][:, 0:512].rearrange("p (h c) -> p h c", c=128)),
            wait=(("ts", tvv),))
        if getattr(ctx, "_win_ablate", 0) & 1:
            continue
        sv = e("scalar", lambda s2: s2.activation(
            out=sc["sq"][:, :], in_=sc["qtm"][:, :], func=AF.Square),
            wait=(("vs", max(qv, war["sq_rd"])),))
        n1 = e("vector", lambda v: v.tensor_reduce(
            out=sc["nb"][:, 0:4], in_=sc["sq"][:, :].rearrange(
                "p (h c) -> p h c", c=128), axis=AX.X, op=OP.add),
            wait=(("ss", sv),))
        sv = e("scalar", lambda s2: s2.activation(
            out=sc["sq"][:, :], in_=sc["ktm"][:, :], func=AF.Square),
            wait=(("vs", max(kv, n1)),))
        n2v = e("vector", lambda v: v.tensor_reduce(
            out=sc["nb"][:, 4:8], in_=sc["sq"][:, :].rearrange(
                "p (h c) -> p h c", c=128), axis=AX.X, op=OP.add),
            wait=(("ss", sv),))
        war["sq_rd"] = n2v
        if not mx_init:
            mxv = e("vector", lambda v: v.tensor_copy(sc["mx"][:, :], sc["nb"][:, 0:8]))
            mx_init = True
        else:
            mxv = e("vector", lambda v: v.tensor_tensor(
                out=sc["mx"][:, :], in0=sc["mx"][:, :], in1=sc["nb"][:, 0:8],
                op=OP.max))
        for name, src, dst_fm, gcol in (("q", sc["qtm"], q_fm, gq_row),
                                        ("k", sc["ktm"], k_fm, gk_row)):
            tjl = 0
            for j in range(4):
                tj = e("tensor", lambda t, src=src, j=j: t.transpose(
                    ps["pt"][:, j * 128:(j + 1) * 128], src[:, j * 128:(j + 1) * 128],
                    ctx.ident_bf[:, :]),
                    wait=(("vs", qv if name == "q" else kv), ("ss", tr_rd)))
                tjl = tj
                tr_rd = e("scalar", lambda s2, dst_fm=dst_fm, j=j, tt=tt, gcol=gcol: s2.activation(
                    out=dst_fm[:, j * Ttok + tt * 128: j * Ttok + tt * 128 + 128],
                    in_=ps["pt"][:, j * 128:(j + 1) * 128], func=AF.Copy,
                    scale=gcol[:, j:j + 1]),
                    wait=(("ts", tj),))
            if name == "q":
                war["qtm"] = (tjl, ctx.cnt["ss"])
            else:
                war["ktm"] = (tjl, ctx.cnt["ss"])
    vsnow = ctx.cnt["vs"]
    tvx = e("tensor", lambda t: t.transpose(
        ps["pb2"][0:8, 0:128], sc["mx"][:, :], ctx.ident_f32[:, :]),
        wait=(("vs", vsnow),))
    mv = e("vector", lambda v: v.tensor_reduce(
        out=sc["row8"][0:8, 0:1], in_=ps["pb2"][0:8, 0:128],
        axis=AX.X, op=OP.max), wait=(("ts", tvx),))
    tvx = e("tensor", lambda t: t.transpose(
        ps["pb2"][0:1, 0:8], sc["row8"][0:8, 0:1], ctx.ident_f32[0:8, 0:8]),
        wait=(("vs", mv),))
    e("vector", lambda v: v.tensor_reduce(
        out=sc["row8"][0:1, 2:3], in_=ps["pb2"][0:1, 0:4], axis=AX.X, op=OP.max),
        wait=(("ts", tvx),))
    e("vector", lambda v: v.tensor_reduce(
        out=sc["row8"][0:1, 3:4], in_=ps["pb2"][0:1, 4:8], axis=AX.X, op=OP.max))
    e("vector", lambda v: v.tensor_tensor(
        out=sc["row8"][0:1, 4:5], in0=sc["row8"][0:1, 2:3],
        in1=sc["row8"][0:1, 3:4], op=OP.mult))
    sv = e("scalar", lambda s2: s2.activation(
        out=sc["row8"][0:1, 5:6], in_=sc["row8"][0:1, 4:5], func=AF.Sqrt),
        wait=(("vs", ctx.cnt["vs"]),))
    mbv = e("vector", lambda v: v.tensor_scalar(
        out=sc["row8"][0:1, 6:7], in0=sc["row8"][0:1, 5:6], scalar1=-SC,
        scalar2=-1.0, op0=OP.mult, op1=OP.add), wait=(("ss", sv),))
    bcv = e("tensor", lambda t: t.matmul(
        ps["pb2"][0:128, 0:1], ctx.onesrow[0:1, 0:128], sc["row8"][0:1, 6:7],
        start=True, stop=True), wait=(("vs", mbv),))
    bco = e("vector", lambda v: v.tensor_copy(sc["bcol"][:, 0:1],
                                              ps["pb2"][0:128, 0:1]),
            wait=(("ts", bcv),))
    mvv = bco
    NT2 = Ttok // 512
    join = (("vs", ctx.cnt["vs"]), ("ss", ctx.cnt["ss"]), ("ts", ctx.cnt["ts"]))
    qk_rd = [0, 0]
    po_rd = 0
    bc_rd = 0
    pr_rd = [0, 0]
    row_rd = 0
    sum_rd = 0
    for h in range(NH):
        sr = e("vector", lambda v, h=h: v.tensor_scalar(
            out=sc["srow"][0:1, 0:512], in0=sc["posT"][0:1, 0:512],
            scalar1=0.0, scalar2=slopes8[0:1, h:h + 1], op0=OP.mult, op1=OP.add),
            wait=(("vs", mvv), ("ts", row_rd)))
        for g in range(NT2):
            trv = e("vector", lambda v, h=h, g=g: v.tensor_scalar(
                out=sc["trow"][0:1, 0:512], in0=sc["posT"][0:1, g * 512:(g + 1) * 512],
                scalar1=slopes8[0:1, 8 + h:9 + h], scalar2=0.0,
                op0=OP.mult, op1=OP.add),
                wait=(("ts", row_rd),))
            slots = [ss_ for ss_ in range(g * 4 - 1, g * 4 + 4) if ss_ >= 0]
            pvt = 0
            sumt = 0
            for j, slot in enumerate(slots):
                bi = j % 2
                qps = ps["pq"] if bi == 0 else ps["pk"]
                e("tensor", lambda t, h=h, slot=slot, g=g, qps=qps: t.matmul(
                    qps[:, 0:512],
                    k_fm[:, h * Ttok + slot * 128: h * Ttok + slot * 128 + 128],
                    q_fm[:, h * Ttok + g * 512: h * Ttok + g * 512 + 512],
                    start=True, stop=False),
                    wait=(join + (("vs", trv),)) if (h == 0 and g == 0 and j == 0)
                    else (("ss", qk_rd[bi]), ("vs", trv)))
                e("tensor", lambda t, slot=slot, qps=qps: t.matmul(
                    qps[:, 0:512], sc["posT"][0:1, slot * 128: slot * 128 + 128],
                    sc["srow"][0:1, 0:512], start=False, stop=False))
                t1 = e("tensor", lambda t, qps=qps: t.matmul(
                    qps[:, 0:512], ctx.onesrow[0:1, 0:128], sc["trow"][0:1, 0:512],
                    start=False, stop=True))
                row_rd = t1
                prob = sc["prob"][:, bi * 512:(bi + 1) * 512]
                svp = e("scalar", lambda s2, prob=prob, qps=qps: s2.activation(
                    out=prob, in_=qps[:, 0:512], func=AF.Exp, scale=SC,
                    bias=sc["bcol"][:, 0:1]),
                    wait=(("ts", max(t1, pr_rd[bi])), ("vs", bco)))
                qk_rd[bi] = svp
                gv = e("gpsimd", lambda gq, prob=prob, slot=slot, g=g: gq.affine_select(
                    prob, prob, pattern=[[1, 512]], compare_op=OP.is_ge,
                    fill=0.0, base=g * 512 - slot * 128, channel_multiplier=-1),
                    wait=(("ss", svp),))
                gv = e("gpsimd", lambda gq, prob=prob, slot=slot, g=g: gq.affine_select(
                    prob, prob, pattern=[[-1, 512]], compare_op=OP.is_ge,
                    fill=0.0, base=16 - (g * 512 - slot * 128),
                    channel_multiplier=1))
                pvt = e("tensor", lambda t, slot=slot, h=h, prob=prob, j=j, n=len(slots): t.matmul(
                    ps["po"][0:128, :],
                    v_tm[:, slot * (NH * 129) + h * 129: slot * (NH * 129) + h * 129 + 128],
                    prob, start=(j == 0), stop=(j == n - 1)),
                    wait=(("gs", gv),) + ((("vs", po_rd),) if j == 0 else ()))
                sumt = e("tensor", lambda t, slot=slot, h=h, prob=prob, j=j, n=len(slots): t.matmul(
                    ps["pr"][0:1, :],
                    v_tm[:, slot * (NH * 129) + h * 129 + 128: slot * (NH * 129) + h * 129 + 129],
                    prob, start=(j == 0), stop=(j == n - 1)),
                    wait=(("vs", sum_rd),) if j == 0 else ())
                pr_rd[bi] = sumt
            rcv = e("vector", lambda v: v.reciprocal(
                out=sc["rrow"][0:1, 0:512], in_=ps["pr"][0:1, 0:512]),
                wait=(("ts", sumt),))
            sum_rd = rcv
            bv = e("tensor", lambda t: t.matmul(
                ps["pb2"][0:128, 0:512], ctx.onesrow[0:1, 0:128],
                sc["rrow"][0:1, 0:512],
                start=True, stop=True), wait=(("vs", max(rcv, bc_rd)),))
            ov = e("vector", lambda v: v.tensor_copy(
                sc["osc"][:, 0:512], ps["po"][0:128, 0:512]),
                wait=(("ts", pvt),))
            po_rd = ov
            bc_rd = e("vector", lambda v, h=h, g=g: v.tensor_tensor(
                out=o_fm[:, h * Ttok + g * 512: h * Ttok + g * 512 + 512],
                in0=sc["osc"][:, 0:512], in1=ps["pb2"][0:128, 0:512], op=OP.mult),
                wait=(("ts", bv),))
    return bc_rd


def build_gla_front_launch():
    nc = bass.Bass("TRN2", target_bir_lowering=False, debug=False,
                   num_devices=NCORES)
    xnin = nc.dram_tensor("xnin", [D, T], BF16, kind="ExternalInput").ap()
    wq = nc.dram_tensor("wq", [D, 512], F32, kind="ExternalInput").ap()
    wk = nc.dram_tensor("wk", [D, 512], F32, kind="ExternalInput").ap()
    wv = nc.dram_tensor("wv", [D, 512], F32, kind="ExternalInput").ap()
    wz = nc.dram_tensor("wz", [D, 512], F32, kind="ExternalInput").ap()
    bg = nc.dram_tensor("bg", [128, 4], F32, kind="ExternalInput").ap()
    ofm = nc.dram_tensor("ofm", [512, T], BF16, kind="ExternalOutput").ap()

    import contextlib
    es = contextlib.ExitStack()

    def sb(name, shape, dt):
        return es.enter_context(nc.sbuf_tensor(name, shape, dt))

    def pst(name, shape, dt=F32):
        return es.enter_context(nc.psum_tensor(name, shape, dt))

    with es:
        xn_sb = sb("xn_sb", [128, NCH * T], BF16)
        o_fm = sb("o_fm", [128, 4 * T], BF16)
        wq_sb = sb("wq_sb", [128, 8 * 512], BF16)
        wk_sb = sb("wk_sb", [128, 8 * 512], BF16)
        wv_sb = sb("wv_sb", [128, 8 * 512], BF16)
        wz_sb = sb("wz_sb", [128, 8 * 512], BF16)
        bgate = sb("bgate", [128, 4], F32)
        S_sb = sb("S_sb", [128, 256], F32)
        sc = {
            "qg": sb("qg", [128, 4 * 512], BF16),
            "kg": sb("kg", [128, 4 * 512], BF16),
            "vg": sb("vg", [128, 4 * 512], BF16),
            "lag": sb("lag", [128, 4 * 512], F32),
            "Bt": sb("Bt", [128, 128], F32),
            "Dex": sb("Dex", [128, 128], F32),
            "Div": sb("Div", [128, 128], F32),
            "qt32": sb("qt32", [128, 128], F32),
            "qtb": sb("qtb", [128, 128], BF16),
            "khat": sb("khat", [128, 128], BF16),
            "ktld": sb("ktld", [128, 128], BF16),
            "ktm": sb("ktm", [128, 128], BF16),
            "prb": sb("prb", [128, 128], BF16),
            "asb": sb("asb", [128, 512], F32),
        }
        ident = sb("ident", [128, 128], BF16)
        onesb = sb("onesb", [128, 1], F32)
        onesrow = sb("onesrow", [1, 128], F32)
        cbank = sb("cbank", [128, 16], F32)
        ps = {
            "pq": pst("pq", [128, 512]),
            "pk": pst("pk", [128, 512]),
            "pv": pst("pv", [128, 512]),
            "pz": pst("pz", [128, 512]),
            "pscore": pst("pscore", [128, 128]),
            "po": pst("po", [64, 128]),
            "pds": pst("pds", [64, 64]),
            "ptr": pst("ptr", [128, 128], BF16),
        }
        block = es.enter_context(nc.Block())
        ctx = Ctx(nc, block)
        ctx.cbank = cbank
        emit_prelude(ctx, ident, onesb, onesrow)
        xv = ctx.emit("sync", lambda q: q.dma_start(
            out=xn_sb[:, :].rearrange("p (c t) -> p c t", c=NCH),
            in_=xnin.rearrange("(c p) t -> p c t", p=128)), dma=True)
        bv = ctx.emit("sync", lambda q: q.dma_start(out=bgate[:, :], in_=bg),
                      dma=True)
        wvv = 0
        for w_d, w_s in ((wq, wq_sb), (wk, wk_sb), (wv, wv_sb), (wz, wz_sb)):
            wvv = ctx.emit("gpsimd", lambda g, w_d=w_d, w_s=w_s: g.dma_start(
                out=w_s[:, :].rearrange("p (c f) -> p c f", c=NCH),
                in_=w_d.rearrange("(c p) f -> p c f", p=128)), dma=True)
        w_ready = (("dS", xv), ("dS", bv), ("dG", wvv))
        done = emit_gla_front(ctx, xn_sb, wq_sb, wk_sb, wv_sb, wz_sb, bgate,
                              o_fm, S_sb, ps, sc, w_ready)
        ctx.emit("sync", lambda q: q.dma_start(
            out=ofm.rearrange("(c p) t -> p c t", p=128),
            in_=o_fm[:, :].rearrange("p (c t) -> p c t", c=4)),
            wait=(("ss", done),), dma=True, store=True)
        ctx.wait("sync", ("st", ctx.cnt["st"]))
    return nc



def build_win_front_launch():
    nc = bass.Bass("TRN2", target_bir_lowering=False, debug=False,
                   num_devices=NCORES)
    xnin = nc.dram_tensor("xnin", [D, T], BF16, kind="ExternalInput").ap()
    wq = nc.dram_tensor("wq", [D, 1024], F32, kind="ExternalInput").ap()
    wk = nc.dram_tensor("wk", [D, 1024], F32, kind="ExternalInput").ap()
    wv = nc.dram_tensor("wv", [D, 512], F32, kind="ExternalInput").ap()
    gq = nc.dram_tensor("gq", [128, 4], F32, kind="ExternalInput").ap()
    gk = nc.dram_tensor("gk", [128, 4], F32, kind="ExternalInput").ap()
    slp = nc.dram_tensor("slp", [1, 16], F32, kind="ExternalInput").ap()
    ofm = nc.dram_tensor("ofm", [512, T], BF16, kind="ExternalOutput").ap()
    qfm_d = nc.dram_tensor("qfm_d", [512, T], BF16, kind="ExternalOutput").ap()
    kfm_d = nc.dram_tensor("kfm_d", [512, T], BF16, kind="ExternalOutput").ap()
    vtm_d = nc.dram_tensor("vtm_d", [128, 32 * 516], BF16, kind="ExternalOutput").ap()
    rstd_d = nc.dram_tensor("rstd_d", [128, 64], F32, kind="ExternalOutput").ap()

    import contextlib
    es = contextlib.ExitStack()

    def sb(name, shape, dt):
        return es.enter_context(nc.sbuf_tensor(name, shape, dt))

    def pst(name, shape, dt=F32):
        return es.enter_context(nc.psum_tensor(name, shape, dt))

    with es:
        xs_sb = sb("xs_sb", [128, 2 * 1024], BF16)
        q_fm = sb("q_fm", [128, 4 * T], BF16)
        k_fm = sb("k_fm", [128, 4 * T], BF16)
        v_tm = sb("v_tm", [128, 32 * 516], BF16)
        big2 = sb("big2", [128, 4 * T], BF16)
        o_fm = big2[:, 0:4 * T]
        wq_sb = big2[:, 0:8 * 1024]
        wk_sb = big2[:, 8 * 1024:16 * 1024]
        wv_sb = sb("wv_sb", [128, 8 * 512], BF16)
        gqr = sb("gqr", [128, 4], F32)
        gkr = sb("gkr", [128, 4], F32)
        slps = sb("slps", [1, 16], F32)
        sc = {
            "qtm": sb("qtm", [128, 512], BF16),
            "ktm": sb("ktm", [128, 512], BF16),
            "sq": sb("sqs", [128, 512], F32),
            "n2": sb("n2", [128, 4], F32),
            "nb": sb("nb", [128, 8], F32),
            "mx": sb("mx", [128, 8], F32),
            "row8": sb("row8", [16, 16], F32),
            "rstd": sb("rstd", [128, 2], F32),
            "prob": sb("prob", [128, 1024], BF16),
            "osc": sb("osc", [128, 512], F32),
            "trow": sb("trow", [1, 512], F32),
            "srow": sb("srow", [1, 512], F32),
            "posT": sb("posT", [1, T], F32),
            "bcol": sb("bcol", [128, 1], F32),
            "rstd_dbg": sb("rstd_dbg", [128, 64], F32),
        }
        sc["rrow"] = sc["trow"]
        ident = sb("ident", [128, 128], BF16)
        identf = sb("identf", [128, 128], F32)
        onesb = sb("onesb", [128, 1], F32)
        onesrow = sb("onesrow", [1, 128], F32)
        cbank = sb("cbank", [128, 16], F32)
        rrow = sb("rrowt", [1, 512], F32)
        sc["rrow"] = rrow
        ps = {
            "pq": pst("pq", [128, 512]),
            "pk": pst("pk", [128, 512]),
            "pv": pst("pv", [128, 512]),
            "pt": pst("pt", [128, 512], BF16),
            "po": pst("po", [128, 512]),
            "pb2": pst("pb2", [128, 512]),
            "pr": pst("pr", [1, 512]),
        }
        block = es.enter_context(nc.Block())
        ctx = Ctx(nc, block)
        ctx.cbank = cbank
        emit_prelude(ctx, ident, onesb, onesrow, identf)
        ctx.emit("gpsimd", lambda g: g.iota(
            sc["posT"][0:1, :], pattern=[[1, T]], base=0, channel_multiplier=0,
            allow_small_or_imprecise_dtypes=True))
        v3 = v_tm[:, :].rearrange("p (t h c) -> p (t h) c", c=129, h=4)[:, :, 128:129]
        ctx.emit("gpsimd", lambda g: g.memset(v3, 1.0))
        gv = ctx.emit("gpsimd", lambda g: g.memset(sc["bcol"][:, :], 0.0))
        dv = 0
        for d_, s_ in ((gq, gqr), (gk, gkr), (slp, slps)):
            dv = ctx.emit("sync", lambda q, d_=d_, s_=s_: q.dma_start(
                out=s_[:, :], in_=d_), dma=True)
        wvv = 0
        for w_d, w_s, n in ((wq, wq_sb, 1024), (wk, wk_sb, 1024), (wv, wv_sb, 512)):
            wvv = ctx.emit("gpsimd", lambda g, w_d=w_d, w_s=w_s: g.dma_start(
                out=w_s[:, :].rearrange("p (c f) -> p c f", c=NCH),
                in_=w_d.rearrange("(c p) f -> p c f", p=128)), dma=True)
        ctx.constcol(1e-6)
        ctx.constcol(1e-6)
        w_ready = (("dS", dv), ("dG", wvv), ("gs", ctx.cnt["gs"]))
        done = emit_win_front(ctx, xnin, xs_sb, q_fm, k_fm, v_tm, o_fm,
                              wq_sb, wk_sb, wv_sb, gqr, gkr, slps, ps, sc,
                              w_ready)
        ctx.emit("sync", lambda q: q.dma_start(
            out=qfm_d.rearrange("(c p) t -> p c t", p=128),
            in_=q_fm[:, :].rearrange("p (c t) -> p c t", c=4)),
            wait=(("vs", done), ("ss", ctx.cnt["ss"])), dma=True, store=True)
        ctx.emit("sync", lambda q: q.dma_start(
            out=kfm_d.rearrange("(c p) t -> p c t", p=128),
            in_=k_fm[:, :].rearrange("p (c t) -> p c t", c=4)), dma=True, store=True)
        ctx.emit("sync", lambda q: q.dma_start(
            out=vtm_d, in_=v_tm[:, :]), dma=True, store=True)
        ctx.emit("sync", lambda q: q.dma_start(
            out=rstd_d, in_=sc["rstd_dbg"][:, :]), dma=True, store=True)
        ctx.emit("sync", lambda q: q.dma_start(
            out=ofm.rearrange("(c p) t -> p c t", p=128),
            in_=o_fm[:, :].rearrange("p (c t) -> p c t", c=4)),
            wait=(("vs", done),), dma=True, store=True)
        ctx.wait("sync", ("st", ctx.cnt["st"]))
    return nc



CAP = 5120
E_EXP = 8


def emit_moe_route(ctx, xn_sb, wgt_sb, o_tm, rout_sb, oh_all, h_fm, c_tm_all,
                   ps, sc, w_ready, Ttok=TL):
    e = ctx.emit
    nt = Ttok // 128
    e("gpsimd", lambda g: g.iota(sc["iota8"][:, :], pattern=[[1, 8]], base=0,
                                 channel_multiplier=0,
                                 allow_small_or_imprecise_dtypes=True))
    gv = ctx.cnt["gs"]
    pl_rd = 0
    pt_rd = 0
    ptx_rd = 0
    for i in range(nt):
        tv = 0
        for kc in range(NCH):
            tv = e("tensor", lambda t, kc=kc, i=i: t.matmul(
                ps["pl"][:, :],
                xn_sb[:, kc * Ttok + i * 128: kc * Ttok + i * 128 + 128],
                wgt_sb[:, kc * 8:(kc + 1) * 8],
                start=(kc == 0), stop=(kc == NCH - 1)),
                wait=(w_ready + (("vs", pl_rd), ("gs", gv))) if kc == 0 else (), chain=(kc != 0))
        lt = sc["lt"]
        vv = e("vector", lambda v: v.tensor_copy(lt[:, :], ps["pl"][:, :]),
               wait=(("ts", tv),))
        pl_rd = vv
        r = rout_sb[:, i * 8:(i + 1) * 8]
        e("vector", lambda v: v.tensor_reduce(
            out=sc["col"][:, 0:1], in_=lt[:, :], axis=AX.X, op=OP.max))
        e("vector", lambda v: v.tensor_scalar(
            out=sc["sc8a"][:, :], in0=lt[:, :], scalar1=sc["col"][:, 0:1],
            scalar2=None, op0=OP.is_equal))
        e("vector", lambda v: v.tensor_tensor_scan(
            out=sc["sc8b"][:, :], data0=sc["sc8a"][:, :], data1=sc["sc8a"][:, :],
            initial=0.0, op0=OP.add, op1=OP.bypass))
        e("vector", lambda v: v.tensor_scalar(
            out=sc["sc8b"][:, :], in0=sc["sc8b"][:, :], scalar1=1.0,
            scalar2=None, op0=OP.is_equal))
        oh1 = oh_all[:, i * 16: i * 16 + 8]
        e("vector", lambda v, oh1=oh1: v.tensor_tensor(
            out=oh1, in0=sc["sc8a"][:, :], in1=sc["sc8b"][:, :], op=OP.mult))
        e("vector", lambda v, oh1=oh1: v.tensor_tensor(
            out=sc["sc8a"][:, :], in0=oh1, in1=sc["iota8"][:, :], op=OP.mult))
        e("vector", lambda v, r=r: v.tensor_reduce(
            out=r[:, 0:1], in_=sc["sc8a"][:, :], axis=AX.X, op=OP.add))
        e("vector", lambda v, oh1=oh1: v.scalar_tensor_tensor(
            out=sc["sc8a"][:, :], in0=oh1, scalar=-1e30, in1=lt[:, :],
            op0=OP.mult, op1=OP.add))
        e("vector", lambda v: v.tensor_reduce(
            out=sc["col"][:, 1:2], in_=sc["sc8a"][:, :], axis=AX.X, op=OP.max))
        e("vector", lambda v: v.tensor_scalar(
            out=sc["sc8b"][:, :], in0=sc["sc8a"][:, :], scalar1=sc["col"][:, 1:2],
            scalar2=None, op0=OP.is_equal))
        e("vector", lambda v: v.tensor_tensor_scan(
            out=sc["sc8a"][:, :], data0=sc["sc8b"][:, :], data1=sc["sc8b"][:, :],
            initial=0.0, op0=OP.add, op1=OP.bypass))
        e("vector", lambda v: v.tensor_scalar(
            out=sc["sc8a"][:, :], in0=sc["sc8a"][:, :], scalar1=1.0,
            scalar2=None, op0=OP.is_equal))
        oh2 = oh_all[:, i * 16 + 8: i * 16 + 16]
        e("vector", lambda v, oh2=oh2: v.tensor_tensor(
            out=oh2, in0=sc["sc8b"][:, :], in1=sc["sc8a"][:, :], op=OP.mult))
        e("vector", lambda v, oh2=oh2: v.tensor_tensor(
            out=sc["sc8b"][:, :], in0=oh2, in1=sc["iota8"][:, :], op=OP.mult))
        e("vector", lambda v, r=r: v.tensor_reduce(
            out=r[:, 1:2], in_=sc["sc8b"][:, :], axis=AX.X, op=OP.add))
        e("vector", lambda v: v.tensor_tensor(
            out=sc["col"][:, 2:3], in0=sc["col"][:, 1:2], in1=sc["col"][:, 0:1],
            op=OP.subtract))
        svv = e("scalar", lambda s2: s2.activation(
            out=sc["col"][:, 3:4], in_=sc["col"][:, 2:3], func=AF.Exp),
            wait=(("vs", ctx.cnt["vs"]),))
        e("vector", lambda v: v.tensor_scalar(
            out=sc["col"][:, 4:5], in0=sc["col"][:, 3:4], scalar1=1.0,
            scalar2=None, op0=OP.add), wait=(("ss", svv),))
        e("vector", lambda v, r=r: v.reciprocal(out=r[:, 2:3], in_=sc["col"][:, 4:5]))
        e("vector", lambda v, r=r: v.tensor_tensor(
            out=r[:, 3:4], in0=sc["col"][:, 3:4], in1=r[:, 2:3], op=OP.mult))
        hv = e("vector", lambda v, oh1=oh1, oh2=oh2: v.tensor_tensor(
            out=sc["sc8a"][:, :], in0=oh1, in1=oh2, op=OP.add))
        tvh = e("tensor", lambda t, i=i: t.transpose(
            ps["pt8"][0:8, :], sc["sc8a"][:, :], ctx.ident_f32[:, :]),
            wait=(("vs", hv), ("ss", pt_rd)))
        pt_rd = e("scalar", lambda s2, i=i: s2.copy(
            out=h_fm[0:8, i * 128:(i + 1) * 128], in_=ps["pt8"][0:8, :]),
            wait=(("ts", tvh),))
        for ch in range(NCH):
            tvx = e("tensor", lambda t, ch=ch, i=i: t.transpose(
                ps["ptx"][:, 0:128],
                xn_sb[:, ch * Ttok + i * 128: ch * Ttok + i * 128 + 128],
                ctx.ident_bf[:, :]),
                wait=(("ss", ptx_rd),) + (tuple(w_ready) if (i == 0 and ch == 0) else ()))
            ptx_rd = e("scalar", lambda s2, ch=ch, i=i: s2.copy(
                out=o_tm[:, i * 1024 + ch * 128: i * 1024 + ch * 128 + 128],
                in_=ps["ptx"][:, 0:128]), wait=(("ts", tvx),))
    scv = e("vector", lambda v: v.tensor_tensor_scan(
        out=h_fm[0:8, Ttok:2 * Ttok], data0=h_fm[0:8, 0:Ttok],
        data1=h_fm[0:8, 0:Ttok], initial=0.0, op0=OP.add, op1=OP.bypass),
        wait=(("ss", pt_rd),))
    ev = e("vector", lambda v: v.tensor_tensor(
        out=h_fm[0:8, Ttok:2 * Ttok], in0=h_fm[0:8, Ttok:2 * Ttok],
        in1=h_fm[0:8, 0:Ttok], op=OP.subtract))
    cbf = ev
    ptb_rd = 0
    out_v = 0
    for i in range(nt):
        tvb = e("tensor", lambda t, i=i: t.transpose(
            ps["ptb"][:, 0:8], h_fm[0:8, Ttok + i * 128: Ttok + (i + 1) * 128],
            ctx.ident_f32[0:8, 0:8]),
            wait=(("vs", cbf), ("vs", ptb_rd)))
        ptb_rd = e("vector", lambda v, i=i: v.tensor_copy(
            c_tm_all[:, i * 8:(i + 1) * 8], ps["ptb"][:, 0:8]),
            wait=(("ts", tvb),))
        r = rout_sb[:, i * 8:(i + 1) * 8]
        e("vector", lambda v, i=i: v.tensor_tensor(
            out=sc["sc8a"][:, :], in0=oh_all[:, i * 16:i * 16 + 8],
            in1=c_tm_all[:, i * 8:(i + 1) * 8], op=OP.mult))
        e("vector", lambda v, r=r: v.tensor_reduce(
            out=r[:, 4:5], in_=sc["sc8a"][:, :], axis=AX.X, op=OP.add))
        e("vector", lambda v, i=i: v.tensor_tensor(
            out=sc["sc8a"][:, :], in0=oh_all[:, i * 16 + 8:i * 16 + 16],
            in1=c_tm_all[:, i * 8:(i + 1) * 8], op=OP.mult))
        out_v = e("vector", lambda v, r=r: v.tensor_reduce(
            out=r[:, 5:6], in_=sc["sc8a"][:, :], axis=AX.X, op=OP.add))
    return out_v, ev


def emit_moe_slots(ctx, rt_src, i, coreseg_bc, sc, keepw_dst=None):
    e = ctx.emit
    for r_ in range(2):
        e("vector", lambda v, r_=r_: v.tensor_scalar(
            out=sc["m8a"][:, :], in0=sc["iota8"][:, :],
            scalar1=rt_src[:, r_:r_ + 1], scalar2=None, op0=OP.is_equal))
        e("vector", lambda v: v.tensor_tensor(
            out=sc["m8b"][:, :], in0=sc["m8a"][:, :], in1=coreseg_bc, op=OP.mult))
        e("vector", lambda v: v.tensor_reduce(
            out=sc["mcol"][:, 0:1], in_=sc["m8b"][:, :], axis=AX.X, op=OP.add))
        e("vector", lambda v, r_=r_: v.tensor_tensor(
            out=sc["mcol"][:, 1:2], in0=sc["mcol"][:, 0:1],
            in1=rt_src[:, 4 + r_:5 + r_], op=OP.add))
        e("vector", lambda v: v.tensor_scalar(
            out=sc["mcol"][:, 2:3], in0=sc["mcol"][:, 1:2], scalar1=float(CAP),
            scalar2=None, op0=OP.is_lt))
        e("vector", lambda v, r_=r_: v.scalar_tensor_tensor(
            out=sc["mcol"][:, 3:4], in0=rt_src[:, r_:r_ + 1], scalar=float(CAP),
            in1=sc["mcol"][:, 1:2], op0=OP.mult, op1=OP.add))
        e("vector", lambda v: v.tensor_scalar(
            out=sc["mcol"][:, 4:5], in0=sc["mcol"][:, 1:2], scalar1=float(CAP - 1),
            scalar2=None, op0=OP.min))
        e("vector", lambda v, r_=r_: v.scalar_tensor_tensor(
            out=sc["mcol"][:, 4:5], in0=rt_src[:, r_:r_ + 1], scalar=float(CAP),
            in1=sc["mcol"][:, 4:5], op0=OP.mult, op1=OP.add))
        e("vector", lambda v: v.tensor_scalar(
            out=sc["mcol"][:, 5:6], in0=sc["mcol"][:, 2:3], scalar1=-1e9,
            scalar2=1e9, op0=OP.mult, op1=OP.add))
        e("vector", lambda v: v.tensor_tensor(
            out=sc["mcol"][:, 5:6], in0=sc["mcol"][:, 5:6],
            in1=sc["mcol"][:, 3:4], op=OP.add))
        vv = e("vector", lambda v, r_=r_: v.tensor_copy(
            sc["slotu"][:, r_:r_ + 1], sc["mcol"][:, 5:6]))
        if keepw_dst is not None:
            e("vector", lambda v, r_=r_: v.tensor_copy(
                sc["slotc"][:, r_:r_ + 1], sc["mcol"][:, 4:5]))
            vv = e("vector", lambda v, r_=r_: v.tensor_tensor(
                out=keepw_dst[:, r_:r_ + 1], in0=rt_src[:, 2 + r_:3 + r_],
                in1=sc["mcol"][:, 2:3], op=OP.mult))
    return vv


def emit_moe_offsets(ctx, tot_sb, offs_cm, ps, sc):
    e = ctx.emit
    tv = e("tensor", lambda t: t.transpose(
        ps["pt8"][0:8, 0:8], tot_sb[0:8, 0:8], ctx.ident_f32[0:8, 0:8]),
        wait=(("dS", ctx.cnt["dS"]), ("vs", ctx.cnt["vs"])))
    ev = e("vector", lambda v: v.tensor_copy(sc["te8"][0:8, 0:8], ps["pt8"][0:8, 0:8]),
           wait=(("ts", tv),))
    ev = e("vector", lambda v: v.tensor_tensor_scan(
        out=sc["te8"][0:8, 8:16], data0=sc["te8"][0:8, 0:8],
        data1=sc["te8"][0:8, 0:8], initial=0.0, op0=OP.add, op1=OP.bypass))
    ev = e("vector", lambda v: v.tensor_tensor(
        out=sc["te8"][0:8, 8:16], in0=sc["te8"][0:8, 8:16],
        in1=sc["te8"][0:8, 0:8], op=OP.subtract))
    tv = e("tensor", lambda t: t.transpose(
        ps["pt8"][0:8, 0:8], sc["te8"][0:8, 8:16], ctx.ident_f32[0:8, 0:8]),
        wait=(("vs", ev),))
    ev = e("vector", lambda v: v.tensor_copy(offs_cm[0:8, 0:8], ps["pt8"][0:8, 0:8]),
           wait=(("ts", tv),))
    return ev


def emit_moe_expert(ctx, xn_all_d, table_d, expo_d, rt_sb, tot_sb, offs_cm,
                    tbi_sb, w1_sb, w2_sb, ps, sc, w_ready):
    e = ctx.emit
    e("gpsimd", lambda g: g.iota(sc["fill"][:, :], pattern=[[0, 1280]], base=-1,
                                 channel_multiplier=0,
                                 allow_small_or_imprecise_dtypes=True))
    fv = ctx.emit("gpsimd", lambda g: g.dma_start(
        out=table_d.ap().rearrange("(p a) f -> p a f", a=E_EXP * CAP // 128),
        in_=sc["fill"][:, :].rearrange("p (a f) -> p a f", f=4)),
        wait=(("gs", ctx.cnt["gs"]),), dma=True)
    ofv = emit_moe_offsets(ctx, tot_sb, offs_cm, ps, sc)
    slot_war = 0
    seg_bc = 0
    scat = fv
    for i in range(128):
        c = i // 16
        if i % 16 == 0:
            tvx = e("tensor", lambda t, c=c: t.matmul(
                ps["pt8"][0:1, 0:8], ctx.ident_f32[0:8, c:c + 1],
                offs_cm[0:8, 0:8], start=True, stop=True),
                wait=(("vs", ofv if i == 0 else ctx.cnt["vs"]),))
            rv = e("vector", lambda v: v.tensor_copy(sc["orow"][0:1, 0:8],
                                                     ps["pt8"][0:1, 0:8]),
                   wait=(("ts", tvx),))
            seg_bc = e("tensor", lambda t: t.matmul(
                ps["pseg"][:, 0:8], ctx.onesrow[0:1, 0:128], sc["orow"][0:1, 0:8],
                start=True, stop=True), wait=(("vs", rv),))
        ctx.wait("vector", ("ts", seg_bc), ("dG", slot_war))
        vv = emit_moe_slots(ctx, rt_sb[:, i * 8:(i + 1) * 8], i,
                            ps["pseg"][:, 0:8], sc)
        gv = e("gpsimd", lambda g, i=i: g.iota(
            sc["tokid"][:, :], pattern=[[0, 4]], base=i * 128,
            channel_multiplier=1, allow_small_or_imprecise_dtypes=True),
            wait=(("dG", scat),))
        for r_ in range(2):
            scat = ctx.emit("gpsimd", lambda g, r_=r_: g.indirect_dma_start(
                out=table_d.ap(), out_offset=bass.IndirectOffsetOnAxis(
                    ap=sc["slotu"][:, r_:r_ + 1], axis=0),
                in_=sc["tokid"][:, :], in_offset=None),
                wait=(("vs", vv), ("gs", gv)), dma=True)
        slot_war = scat
    tbl_ld = ctx.emit("sync", lambda q: q.dma_start(
        out=tbi_sb[:, :], in_=ctx.tblidx_d), dma=True)
    NG = CAP // 512
    act = sc["act"]
    xg_war = 0
    expo_war = 0
    for g in range(NG):
        gvz = 0
        xrd = ctx._xg_rd if hasattr(ctx, "_xg_rd") else 0
        for j4 in range(4):
            j = g * 4 + j4
            gvz = e("gpsimd", lambda g_, j4=j4: g_.memset(
                sc["xb"][:, j4 * 1024:(j4 + 1) * 1024], 0.0),
                wait=(("dG", scat), ("dS", tbl_ld), ("ts", xg_war),
                      ("st", expo_war)))
            tb = ctx.emit("gpsimd", lambda g_, j=j, j4=j4: g_.indirect_dma_start(
                out=sc["tbl4"][:, :], out_offset=None, in_=table_d.ap(),
                in_offset=bass.IndirectOffsetOnAxis(ap=tbi_sb[:, j:j + 1], axis=0)),
                wait=(("gs", gvz), ("dG", ctx._xgather if hasattr(ctx, "_xgather") else 0)),
                dma=True)
            gvz = ctx.emit("gpsimd", lambda g_, j4=j4: g_.indirect_dma_start(
                out=sc["xb"][:, j4 * 1024:(j4 + 1) * 1024], out_offset=None,
                in_=xn_all_d,
                in_offset=bass.IndirectOffsetOnAxis(ap=sc["tbl4"][:, 0:1], axis=0)),
                wait=(("dG", tb),), dma=True)
            ctx._xgather = gvz
        trv = 0
        ev = 0
        for j4 in range(4):
            for ch in range(NCH):
                tvx = e("tensor", lambda t, j4=j4, ch=ch: t.transpose(
                    ps["ptx"][:, 0:128],
                    sc["xb"][:, j4 * 1024 + ch * 128: j4 * 1024 + ch * 128 + 128],
                    ctx.ident_bf[:, :]),
                    wait=(("dG", gvz), ("vs", ev)))
                trv = tvx
                ev = e("vector", lambda v, j4=j4, ch=ch: v.tensor_copy(
                    sc["xg"][:, ch * 512 + j4 * 128: ch * 512 + j4 * 128 + 128],
                    ps["ptx"][:, 0:128]), wait=(("ts", tvx),))
        xg_rdy = ev
        for fc in range(16):
            tv1 = 0
            for kc in range(NCH):
                tv1 = e("tensor", lambda t, kc=kc, fc=fc: t.matmul(
                    ps["ph"][:, :],
                    w1_sb[:, kc * 4096 + fc * 128: kc * 4096 + fc * 128 + 128],
                    sc["xg"][:, kc * 512:(kc + 1) * 512],
                    start=(kc == 0), stop=(kc == NCH - 1)),
                    wait=(w_ready + (("vs", xg_rdy), ("ss", ctx.cnt["ss"]),
                                     ("vs", ctx.cnt["vs"]))) if kc == 0 else (), chain=(kc != 0))
            sv = e("scalar", lambda s2: s2.activation(
                out=sc["gel"][:, :], in_=ps["ph"][:, :], func=AF.Gelu),
                wait=(("ts", tv1),))
            tv2 = 0
            for kc in range(NCH):
                tv2 = e("tensor", lambda t, kc=kc, fc=fc: t.matmul(
                    ps["ph2"][:, :],
                    w1_sb[:, kc * 4096 + (fc + 16) * 128: kc * 4096 + (fc + 16) * 128 + 128],
                    sc["xg"][:, kc * 512:(kc + 1) * 512],
                    start=(kc == 0), stop=(kc == NCH - 1)),
                    wait=(("ss", sv), ("vs", ctx.cnt["vs"]))
                    if kc == 0 else (), chain=(kc != 0))
            e("vector", lambda v, fc=fc: v.tensor_tensor(
                out=act[:, fc * 512:(fc + 1) * 512], in0=sc["gel"][:, :],
                in1=ps["ph2"][:, :], op=OP.mult), wait=(("ts", tv2),))
        actv = ctx.cnt["vs"]
        for oc in range(NCH):
            tv3 = 0
            for kc in range(16):
                tv3 = e("tensor", lambda t, kc=kc, oc=oc: t.matmul(
                    ps["ph"][:, :],
                    w2_sb[:, kc * 1024 + oc * 128: kc * 1024 + oc * 128 + 128],
                    act[:, kc * 512:(kc + 1) * 512],
                    start=(kc == 0), stop=(kc == 15)),
                    wait=(("vs", actv), ("ss", ctx.cnt["ss"]),
                          ("vs", ctx.cnt["vs"])) if kc == 0 else (), chain=(kc != 0))
            e("vector", lambda v, oc=oc: v.tensor_copy(
                sc["of"][:, oc * 512:(oc + 1) * 512], ps["ph"][:, :]),
                wait=(("ts", tv3),))
        ofv2 = ctx.cnt["vs"]
        ev2 = 0
        for j4 in range(4):
            for ch in range(NCH):
                tvx = e("tensor", lambda t, j4=j4, ch=ch: t.transpose(
                    ps["ptx"][:, 0:128],
                    sc["of"][:, ch * 512 + j4 * 128: ch * 512 + j4 * 128 + 128],
                    ctx.ident_bf[:, :]),
                    wait=(("vs", ofv2), ("vs", ev2)))
                ev2 = e("vector", lambda v, j4=j4, ch=ch: v.tensor_copy(
                    sc["xb"][:, j4 * 1024 + ch * 128: j4 * 1024 + ch * 128 + 128],
                    ps["ptx"][:, 0:128]), wait=(("ts", tvx),))
            expo_war = ctx.emit("sync", lambda q, g=g, j4=j4: q.dma_start(
                out=expo_d[g * 512 + j4 * 128: g * 512 + (j4 + 1) * 128, :],
                in_=sc["xb"][:, j4 * 1024:(j4 + 1) * 1024]),
                wait=(("vs", ev2),), dma=True, store=True)
        xg_war = ctx.cnt["ts"]
        ctx._xg_rd = xg_war
    return expo_war


def emit_moe_combine(ctx, x_sb, expo_all_d, rt_my, tot_sb, offs_cm, mycol_sb,
                     ps, sc, ready, Ttok=TL):
    e = ctx.emit
    ofv = emit_moe_offsets(ctx, tot_sb, offs_cm, ps, sc)
    tvx = e("tensor", lambda t: t.matmul(
        ps["pt8"][0:1, 0:8], mycol_sb[0:8, 0:1], offs_cm[0:8, 0:8],
        start=True, stop=True), wait=(("vs", ofv),) + tuple(ready))
    rv = e("vector", lambda v: v.tensor_copy(sc["orow"][0:1, 0:8],
                                             ps["pt8"][0:1, 0:8]),
           wait=(("ts", tvx),))
    seg_bc = e("tensor", lambda t: t.matmul(
        ps["pseg"][:, 0:8], ctx.onesrow[0:1, 0:128], sc["orow"][0:1, 0:8],
        start=True, stop=True), wait=(("vs", rv),))
    ctx.wait("vector", ("ts", seg_bc))
    g1 = 0
    ev2 = 0
    out_v = 0
    for i in range(Ttok // 128):
        vv = emit_moe_slots(ctx, rt_my[:, i * 8:(i + 1) * 8], i,
                            ps["pseg"][:, 0:8], sc, keepw_dst=sc["kw"])
        for r_ in range(2):
            g1 = ctx.emit("gpsimd", lambda g_, r_=r_: g_.indirect_dma_start(
                out=sc["rowg"][:, r_ * 1024:(r_ + 1) * 1024], out_offset=None,
                in_=expo_all_d,
                in_offset=bass.IndirectOffsetOnAxis(ap=sc["slotc"][:, r_:r_ + 1],
                                                    axis=0)),
                wait=(("vs", vv), ("vs", ctx.cnt["vs"])), dma=True)
        yv = e("vector", lambda v: v.tensor_scalar(
            out=sc["ytm"][:, 0:1024], in0=sc["rowg"][:, 0:1024],
            scalar1=sc["kw"][:, 0:1], scalar2=None, op0=OP.mult),
            wait=(("dG", g1),))
        yv = e("vector", lambda v: v.scalar_tensor_tensor(
            out=sc["ytm"][:, 0:1024], in0=sc["rowg"][:, 1024:2048],
            scalar=sc["kw"][:, 1:2], in1=sc["ytm"][:, 0:1024],
            op0=OP.mult, op1=OP.add))
        yb = e("vector", lambda v: v.tensor_copy(sc["ybf"][:, :], sc["ytm"][:, 0:1024]))
        for ch in range(NCH):
            tvx = e("tensor", lambda t, ch=ch: t.transpose(
                ps["ptx"][:, 0:128], sc["ybf"][:, ch * 128:(ch + 1) * 128],
                ctx.ident_bf[:, :]), wait=(("vs", yb), ("vs", ev2)))
            ev2 = e("vector", lambda v, ch=ch, i=i: v.tensor_tensor(
                out=x_sb[:, ch * Ttok + i * 128: ch * Ttok + i * 128 + 128],
                in0=x_sb[:, ch * Ttok + i * 128: ch * Ttok + i * 128 + 128],
                in1=ps["ptx"][:, 0:128], op=OP.add), wait=(("ts", tvx),))
        out_v = ev2
    return out_v


def build_moe_route_launch():
    nc = bass.Bass("TRN2", target_bir_lowering=False, debug=False,
                   num_devices=NCORES)
    xnin = nc.dram_tensor("xnin", [D, TL], BF16, kind="ExternalInput").ap()
    wgt = nc.dram_tensor("wgt", [D, 8], F32, kind="ExternalInput").ap()
    xntm_o = nc.dram_tensor("xntm_o", [TL, D], BF16, kind="ExternalOutput").ap()
    rout_o = nc.dram_tensor("rout_o", [TL, 8], F32, kind="ExternalOutput").ap()
    tot_o = nc.dram_tensor("tot_o", [8, 1], F32, kind="ExternalOutput").ap()
    import contextlib
    es = contextlib.ExitStack()

    def sb(name, shape, dt):
        return es.enter_context(nc.sbuf_tensor(name, shape, dt))

    def pst(name, shape, dt=F32):
        return es.enter_context(nc.psum_tensor(name, shape, dt))

    with es:
        xn_sb = sb("xn_sb", [128, NCH * TL], BF16)
        o_tm = sb("o_tm", [128, 16 * 1024], BF16)
        wgt_sb = sb("wgt_sb", [128, 8 * 8], BF16)
        rout_sb = sb("rout_sb", [128, 16 * 8], F32)
        oh_all = sb("oh_all", [128, 16 * 16], F32)
        h_fm = sb("h_fm", [8, 2 * TL], F32)
        c_tm_all = sb("c_tm_all", [128, 16 * 8], F32)
        sc = {
            "lt": sb("lt", [128, 8], F32),
            "sc8a": sb("sc8a", [128, 8], F32),
            "sc8b": sb("sc8b", [128, 8], F32),
            "col": sb("colx", [128, 8], F32),
            "iota8": sb("iota8", [128, 8], F32),
        }
        tot_sb = sb("tot_sb", [8, 1], F32)
        ident = sb("ident", [128, 128], BF16)
        identf = sb("identf", [128, 128], F32)
        onesb = sb("onesb", [128, 1], F32)
        onesrow = sb("onesrow", [1, 128], F32)
        cbank = sb("cbank", [128, 16], F32)
        ps = {
            "pl": pst("pl", [128, 8]),
            "pt8": pst("pt8", [8, 128]),
            "ptb": pst("ptb", [128, 8]),
            "ptx": pst("ptx", [128, 128], BF16),
        }
        block = es.enter_context(nc.Block())
        ctx = Ctx(nc, block)
        ctx.cbank = cbank
        emit_prelude(ctx, ident, onesb, onesrow, identf)
        xv = ctx.emit("sync", lambda q: q.dma_start(
            out=xn_sb[:, :].rearrange("p (c t) -> p c t", c=NCH),
            in_=xnin.rearrange("(c p) t -> p c t", p=128)), dma=True)
        wv = ctx.emit("gpsimd", lambda g: g.dma_start(
            out=wgt_sb[:, :].rearrange("p (c f) -> p c f", c=NCH),
            in_=wgt.rearrange("(c p) f -> p c f", p=128)), dma=True)
        w_ready = (("dS", xv), ("dG", wv), ("gs", ctx.cnt["gs"]))
        rv, hv = emit_moe_route(ctx, xn_sb, wgt_sb, o_tm, rout_sb, oh_all,
                                h_fm, c_tm_all, ps, sc, w_ready)
        tv = ctx.emit("vector", lambda v: v.tensor_tensor(
            out=tot_sb[0:8, 0:1], in0=h_fm[0:8, 2 * TL - 1:2 * TL],
            in1=h_fm[0:8, TL - 1:TL], op=OP.add), wait=(("vs", hv),))
        ctx.emit("sync", lambda q: q.dma_start(
            out=xntm_o.rearrange("(a p) d -> p a d", p=128),
            in_=o_tm[:, :].rearrange("p (a d) -> p a d", d=D)),
            wait=(("ss", ctx.cnt["ss"]),), dma=True, store=True)
        ctx.emit("sync", lambda q: q.dma_start(
            out=rout_o.rearrange("(a p) e -> p a e", p=128),
            in_=rout_sb[:, :].rearrange("p (a e) -> p a e", e=8)),
            wait=(("vs", rv),), dma=True, store=True)
        ctx.emit("sync", lambda q: q.dma_start(out=tot_o, in_=tot_sb[0:8, 0:1]),
                 wait=(("vs", tv),), dma=True, store=True)
        ctx.wait("sync", ("st", ctx.cnt["st"]))
    return nc


def build_moe_expert_launch():
    nc = bass.Bass("TRN2", target_bir_lowering=False, debug=False,
                   num_devices=NCORES)
    xn_all = nc.dram_tensor("xn_all", [8 * TL, D], BF16, kind="ExternalInput").ap()
    rout_all = nc.dram_tensor("rout_all", [8 * TL, 8], F32, kind="ExternalInput").ap()
    tot_all = nc.dram_tensor("tot_all", [8, 8], F32, kind="ExternalInput").ap()
    w1e = nc.dram_tensor("w1e", [D, 4096], F32, kind="ExternalInput").ap()
    w2e = nc.dram_tensor("w2e", [2048, D], F32, kind="ExternalInput").ap()
    tblidx = nc.dram_tensor("tblidx", [128, CAP // 128], U32, kind="ExternalInput").ap()
    expo = nc.dram_tensor("expo", [CAP, D], BF16, kind="ExternalOutput").ap()
    table = nc.dram_tensor("table", [8 * CAP, 4], U32)
    import contextlib
    es = contextlib.ExitStack()

    def sb(name, shape, dt):
        return es.enter_context(nc.sbuf_tensor(name, shape, dt))

    def pst(name, shape, dt=F32):
        return es.enter_context(nc.psum_tensor(name, shape, dt))

    with es:
        rt_sb = sb("rt_sb", [128, 128 * 8], F32)
        w1_sb = sb("w1_sb", [128, 8 * 4096], BF16)
        w2_sb = sb("w2_sb", [128, 16 * 1024], BF16)
        tot_sb = sb("tot_sb", [8, 8], F32)
        offs_cm = sb("offs_cm", [8, 8], F32)
        tbi_sb = sb("tbi_sb", [128, 40], U32)
        sc = {
            "fill": sb("fill", [128, 1280], U32),
            "xb": sb("xb", [128, 4096], BF16),
            "xg": sb("xg", [128, 4096], BF16),
            "act": sb("act", [128, 16 * 512], BF16),
            "of": sb("of", [128, 4096], BF16),
            "gel": sb("gel", [128, 512], F32),
            "tbl4": sb("tbl4", [128, 4], U32),
            "tokid": sb("tokid", [128, 4], U32),
            "slotu": sb("slotu", [128, 2], U32),
            "m8a": sb("m8a", [128, 8], F32),
            "m8b": sb("m8b", [128, 8], F32),
            "mcol": sb("mcol", [128, 8], F32),
            "iota8": sb("iota8", [128, 8], F32),
            "orow": sb("orowx", [1, 8], F32),
            "te8": sb("te8", [8, 16], F32),
        }
        ident = sb("ident", [128, 128], BF16)
        identf = sb("identf", [128, 128], F32)
        onesb = sb("onesb", [128, 1], F32)
        onesrow = sb("onesrow", [1, 128], F32)
        cbank = sb("cbank", [128, 16], F32)
        ps = {
            "ph": pst("ph", [128, 512]),
            "ph2": pst("ph2", [128, 512]),
            "ptx": pst("ptx", [128, 128], BF16),
            "pt8": pst("pt8", [8, 128]),
            "pseg": pst("pseg", [128, 8]),
        }
        block = es.enter_context(nc.Block())
        ctx = Ctx(nc, block)
        ctx.cbank = cbank
        ctx.tblidx_d = tblidx
        emit_prelude(ctx, ident, onesb, onesrow, identf)
        ctx.emit("gpsimd", lambda g: g.iota(
            sc["iota8"][:, :], pattern=[[1, 8]], base=0, channel_multiplier=0,
            allow_small_or_imprecise_dtypes=True))
        rv = ctx.emit("sync", lambda q: q.dma_start(
            out=rt_sb[:, :].rearrange("p (a e) -> p a e", e=8),
            in_=rout_all.rearrange("(a p) e -> p a e", p=128)), dma=True)
        tv = ctx.emit("sync", lambda q: q.dma_start(out=tot_sb[0:8, 0:8],
                                                    in_=tot_all), dma=True)
        w1v = ctx.emit("gpsimd", lambda g: g.dma_start(
            out=w1_sb[:, :].rearrange("p (c f) -> p c f", c=NCH),
            in_=w1e.rearrange("(c p) f -> p c f", p=128)), dma=True)
        w2v = ctx.emit("gpsimd", lambda g: g.dma_start(
            out=w2_sb[:, :].rearrange("p (c f) -> p c f", c=16),
            in_=w2e.rearrange("(c p) f -> p c f", p=128)), dma=True)
        ctx.wait("vector", ("dS", tv), ("dS", rv))
        ctx.wait("tensor", ("dS", tv), ("dS", rv))
        w_ready = (("dG", w2v),)
        done = emit_moe_expert(ctx, xn_all, table, expo, rt_sb, tot_sb,
                               offs_cm, tbi_sb, w1_sb, w2_sb, ps, sc, w_ready)
        ctx.wait("sync", ("st", ctx.cnt["st"]))
    return nc


def build_moe_combine_launch():
    nc = bass.Bass("TRN2", target_bir_lowering=False, debug=False,
                   num_devices=NCORES)
    xin = nc.dram_tensor("xin", [D, TL], F32, kind="ExternalInput").ap()
    expo_all = nc.dram_tensor("expo_all", [8 * CAP, D], BF16,
                              kind="ExternalInput").ap()
    rout_my = nc.dram_tensor("rout_my", [TL, 8], F32, kind="ExternalInput").ap()
    tot_all = nc.dram_tensor("tot_all", [8, 8], F32, kind="ExternalInput").ap()
    mycol = nc.dram_tensor("mycol", [8, 1], F32, kind="ExternalInput").ap()
    xout = nc.dram_tensor("xout", [D, TL], F32, kind="ExternalOutput").ap()
    import contextlib
    es = contextlib.ExitStack()

    def sb(name, shape, dt):
        return es.enter_context(nc.sbuf_tensor(name, shape, dt))

    def pst(name, shape, dt=F32):
        return es.enter_context(nc.psum_tensor(name, shape, dt))

    with es:
        x_sb = sb("x_sb", [128, NCH * TL], F32)
        rt_sb = sb("rt_sb", [128, 16 * 8], F32)
        tot_sb = sb("tot_sb", [8, 8], F32)
        offs_cm = sb("offs_cm", [8, 8], F32)
        mycol_sb = sb("mycol_sb", [8, 1], F32)
        sc = {
            "rowg": sb("rowg", [128, 2048], BF16),
            "ytm": sb("ytm", [128, 1024], F32),
            "ybf": sb("ybf", [128, 1024], BF16),
            "kw": sb("kw", [128, 2], F32),
            "slotu": sb("slotu", [128, 2], U32),
            "slotc": sb("slotc", [128, 2], U32),
            "m8a": sb("m8a", [128, 8], F32),
            "m8b": sb("m8b", [128, 8], F32),
            "mcol": sb("mcol", [128, 8], F32),
            "iota8": sb("iota8", [128, 8], F32),
            "orow": sb("orowx", [1, 8], F32),
            "te8": sb("te8", [8, 16], F32),
        }
        ident = sb("ident", [128, 128], BF16)
        identf = sb("identf", [128, 128], F32)
        onesb = sb("onesb", [128, 1], F32)
        onesrow = sb("onesrow", [1, 128], F32)
        cbank = sb("cbank", [128, 16], F32)
        ps = {
            "ptx": pst("ptx", [128, 128], BF16),
            "pt8": pst("pt8", [8, 128]),
            "pseg": pst("pseg", [128, 8]),
        }
        block = es.enter_context(nc.Block())
        ctx = Ctx(nc, block)
        ctx.cbank = cbank
        emit_prelude(ctx, ident, onesb, onesrow, identf)
        ctx.emit("gpsimd", lambda g: g.iota(
            sc["iota8"][:, :], pattern=[[1, 8]], base=0, channel_multiplier=0,
            allow_small_or_imprecise_dtypes=True))
        xv = ctx.emit("sync", lambda q: q.dma_start(
            out=x_sb[:, :].rearrange("p (c t) -> p c t", c=NCH),
            in_=xin.rearrange("(c p) t -> p c t", p=128)), dma=True)
        rv = ctx.emit("sync", lambda q: q.dma_start(
            out=rt_sb[:, :].rearrange("p (a e) -> p a e", e=8),
            in_=rout_my.rearrange("(a p) e -> p a e", p=128)), dma=True)
        tv = ctx.emit("sync", lambda q: q.dma_start(out=tot_sb[0:8, 0:8],
                                                    in_=tot_all), dma=True)
        mv = ctx.emit("sync", lambda q: q.dma_start(out=mycol_sb[0:8, 0:1],
                                                    in_=mycol), dma=True)
        ready = (("dS", mv), ("gs", ctx.cnt["gs"]))
        done = emit_moe_combine(ctx, x_sb, expo_all, rt_sb, tot_sb, offs_cm,
                                mycol_sb, ps, sc, ready)
        ctx.emit("sync", lambda q: q.dma_start(
            out=xout.rearrange("(c p) t -> p c t", p=128),
            in_=x_sb[:, :].rearrange("p (c t) -> p c t", c=NCH)),
            wait=(("vs", done),), dma=True, store=True)
        ctx.wait("sync", ("st", ctx.cnt["st"]))
    return nc



EPSM = 0.05


def emit_delta_front(ctx, xn_dram, xs_sb, wq_sb, wk_sb, wv_sb, wg_sb, wb_sb,
                     wf_sb, ws_sb, bfs_sb, rep4_sb, o_fm,
                     fs_dram, g_dram, ab_dram, hv_dram, w_dram,
                     ps, sc, w_ready, Ttok=T):
    e = ctx.emit
    nt = Ttok // 128
    NSB = Ttok // 32
    xld = [0, 0]
    xs_user = [0, 0]

    def xload(tt, bi):
        return e("sync", lambda q, tt=tt, bi=bi: q.dma_start(
            out=xs_sb[:, bi * 1024:(bi + 1) * 1024]
            .rearrange("p (c t) -> p c t", c=NCH),
            in_=xn_dram[:, tt * 128:(tt + 1) * 128]
            .rearrange("(c p) t -> p c t", p=128)),
            wait=(("ts", xs_user[bi]),), dma=True)
    xld[0] = xload(0, 0)
    war = {"pq": 0, "pk": 0, "pv": 0, "pb": 0}
    fs_st = 0
    for tt in range(nt):
        bi = tt % 2
        if tt + 1 < nt:
            xld[1 - bi] = xload(tt + 1, 1 - bi)
        xs = xs_sb[:, bi * 1024:(bi + 1) * 1024]
        tvk = tvv = tvb = 0
        for kc in range(NCH):
            tvk = e("tensor", lambda t, kc=kc, xs=xs: t.matmul(
                ps["pk"][:, 0:512], xs[:, kc * 128:(kc + 1) * 128],
                wk_sb[:, kc * 512:(kc + 1) * 512],
                start=(kc == 0), stop=(kc == NCH - 1)),
                wait=(w_ready + (("dS", xld[bi]), ("ss", war["pk"]))) if kc == 0 else (), chain=(kc != 0))
        kv = e("scalar", lambda s2: s2.activation(
            out=sc["ktm"][:, :], in_=ps["pk"][:, 0:512], func=AF.Silu),
            wait=(("ts", tvk), ("vs", war.get("u_rd", 0))))
        war["pk"] = kv
        for kc in range(NCH):
            tvv = e("tensor", lambda t, kc=kc, xs=xs: t.matmul(
                ps["pv"][:, 0:512], xs[:, kc * 128:(kc + 1) * 128],
                wv_sb[:, kc * 512:(kc + 1) * 512],
                start=(kc == 0), stop=(kc == NCH - 1)),
                wait=(("ss", war["pv"]),) if kc == 0 else (), chain=(kc != 0))
        vv = e("scalar", lambda s2: s2.activation(
            out=sc["vtm"][:, :], in_=ps["pv"][:, 0:512], func=AF.Silu),
            wait=(("ts", tvv), ("vs", war.get("u_rd", 0))))
        war["pv"] = vv
        for kc in range(NCH):
            tvb = e("tensor", lambda t, kc=kc, xs=xs: t.matmul(
                ps["pb"][:, 0:24], xs[:, kc * 128:(kc + 1) * 128],
                wb_sb[:, kc * 24:(kc + 1) * 24],
                start=(kc == 0), stop=False),
                wait=(("ss", war["pb"]),) if kc == 0 else (), chain=(kc != 0))
        tvb = e("tensor", lambda t: t.matmul(
            ps["pb"][:, 0:24], sc["onesbf"][0:1, 0:128], ctx.brow_sb[0:1, 0:24],
            start=False, stop=True))
        bv = e("scalar", lambda s2: s2.activation(
            out=sc["bfs"][:, 0:24], in_=ps["pb"][:, 0:24], func=AF.Sigmoid),
            wait=(("ts", tvb),))
        war["pb"] = bv
        uv = e("vector", lambda v: v.tensor_tensor(
            out=sc["utm"][:, :], in0=sc["ktm"][:, :], in1=sc["vtm"][:, :],
            op=OP.mult), wait=(("ss", vv),))
        b0 = sc["bfs"][:, 0:1]
        bbc = bass.AP(tensor=b0.tensor, offset=b0.offset,
                      ap=[list(b0.ap[0]), [1, 8], [0, 64]])
        u3 = ctx.u_tm[:, tt * 512:(tt + 1) * 512].rearrange(
            "p (h d) -> p h d", d=64)
        uv = e("vector", lambda v, u3=u3, bbc=bbc: v.tensor_tensor(
            out=u3, in0=sc["utm"][:, :].rearrange("p (h d) -> p h d", d=64),
            in1=bbc, op=OP.mult), wait=(("ss", bv),))
        war["u_rd"] = uv
        for which, cst in (("f", 8), ("s", 16)):
            tvf = e("tensor", lambda t, cst=cst: t.transpose(
                ps["pt8"][0:8, :], sc["bfs"][:, cst:cst + 8], ctx.ident_f32[:, :]),
                wait=(("ss", bv), ("st", fs_st)))
            ev = e("vector", lambda v: v.tensor_copy(sc["fsrow"][0:8, :],
                                                     ps["pt8"][0:8, :]),
                   wait=(("ts", tvf),))
            fs_st = ctx.emit("sync", lambda q, cst=cst, tt=tt: q.dma_start(
                out=fs_dram[(cst - 8):(cst - 8) + 8, tt * 128:(tt + 1) * 128],
                in_=sc["fsrow"][0:8, :]), wait=(("vs", ev),), dma=True, store=True)
        for name, wsb, fn in (("q", wq_sb, AF.Silu), ("g", wg_sb, AF.Sigmoid)):
            for dc in range(4):
                tvq = 0
                for kc in range(NCH):
                    tvq = e("tensor", lambda t, kc=kc, dc=dc, wsb=wsb, xs=xs: t.matmul(
                        ps["pq"][:, 0:128],
                        wsb[:, kc * 512 + dc * 128: kc * 512 + dc * 128 + 128],
                        xs[:, kc * 128:(kc + 1) * 128],
                        start=(kc == 0), stop=(kc == NCH - 1)),
                        wait=(("ss", war["pq"]),) if kc == 0 else (), chain=(kc != 0))
                dst = ctx.q_fm if name == "q" else ctx.g_fm
                war["pq"] = e("scalar", lambda s2, dst=dst, dc=dc, tt=tt, fn=fn: s2.activation(
                    out=dst[:, dc * Ttok + tt * 128: dc * Ttok + tt * 128 + 128],
                    in_=ps["pq"][:, 0:128], func=fn), wait=(("ts", tvq),))
        xs_user[bi] = ctx.cnt["ts"]
    return fs_st


def emit_delta_scan(ctx, fs_dram, g_dram, ab_dram, hv_dram, w_dram, rep4_sb,
                    o_fm, ps, sc, Ttok=T):
    e = ctx.emit
    NSB = Ttok // 32
    fpl = ctx.emit("sync", lambda q: q.dma_start(
        out=sc["Fpl"][:, :].rearrange("p (h j) -> p h j", j=32),
        in_=bass.AP(tensor=fs_dram.ap().tensor, offset=0,
                    ap=[[32, 128], [Ttok, 8], [1, 32]])),
        wait=(("st", ctx.cnt["st"]),), dma=True)
    fpl = ctx.emit("sync", lambda q: q.dma_start(
        out=sc["Spl"][:, :].rearrange("p (h j) -> p h j", j=32),
        in_=bass.AP(tensor=fs_dram.ap().tensor, offset=8 * Ttok,
                    ap=[[32, 128], [Ttok, 8], [1, 32]])), dma=True)
    gz = e("gpsimd", lambda g: g.memset(sc["H1"][:, :], 0.0))
    gz = e("gpsimd", lambda g: g.memset(sc["H2"][:, :], 0.0))
    gz = e("gpsimd", lambda g: g.memset(sc["Hv"][:, :], 0.0))
    vz = e("vector", lambda v: v.tensor_scalar(
        out=sc["Hv"][:, :].rearrange("p (a h c) -> p a h c", a=2, c=2)[:, :, :, 0:1],
        in0=sc["Hv"][:, :].rearrange("p (a h c) -> p a h c", a=2, c=2)[:, 0:2, :, 0:1],
        scalar1=0.0, scalar2=1.0, op0=OP.mult, op1=OP.add),
        wait=(("gs", gz), ("dS", fpl)))
    vz = e("vector", lambda v: v.tensor_scalar(
        out=sc["Hv"][:, 16:32].rearrange("p (h c) -> p h c", c=2)[:, :, 0:1],
        in0=sc["Hv"][:, 16:32].rearrange("p (h c) -> p h c", c=2)[:, :, 0:1],
        scalar1=0.0, scalar2=0.0, op0=OP.mult, op1=OP.add))
    vz = e("vector", lambda v: v.tensor_scalar(
        out=sc["Hv"][:, 16:32].rearrange("p (h c) -> p h c", c=2)[:, :, 1:2],
        in0=sc["Hv"][:, 16:32].rearrange("p (h c) -> p h c", c=2)[:, :, 1:2],
        scalar1=0.0, scalar2=1.0, op0=OP.mult, op1=OP.add))

    def fsl(j):
        v = sc["Fpl"][:, j:j + 1]
        return bass.AP(tensor=v.tensor, offset=v.offset,
                       ap=[list(v.ap[0]), [32, 8], [0, 32]])

    def ssl(j):
        v = sc["Spl"][:, j:j + 1]
        return bass.AP(tensor=v.tensor, offset=v.offset,
                       ap=[list(v.ap[0]), [32, 8], [0, 32]])

    def fs8(j, src):
        v = src[:, j:j + 1]
        return bass.AP(tensor=v.tensor, offset=v.offset,
                       ap=[list(v.ap[0]), [32, 8], [0, 2]])

    for j in range(32):
        t1 = e("vector", lambda v, j=j: v.tensor_tensor(
            out=sc["T1"][:, :], in0=sc["H1"][:, :], in1=fsl(j), op=OP.mult))
        t2 = e("vector", lambda v, j=j: v.tensor_tensor(
            out=sc["T2"][:, :], in0=sc["H2"][:, :], in1=ssl(j), op=OP.mult))
        gdst = bass.AP(tensor=sc["Gall"][:, 0:1].tensor,
                       offset=sc["Gall"][:, 0:1].offset + j,
                       ap=[list(sc["Gall"][:, 0:1].ap[0]), [34 * 32, 8], [34, 32]])
        e("vector", lambda v, gdst=gdst: v.tensor_tensor(
            out=gdst, in0=sc["T1"][:, :], in1=sc["T2"][:, :], op=OP.add))
        e("vector", lambda v: v.scalar_tensor_tensor(
            out=sc["H1"][:, :], in0=sc["T2"][:, :], scalar=EPSM, in1=sc["T1"][:, :],
            op0=OP.mult, op1=OP.add))
        e("vector", lambda v: v.scalar_tensor_tensor(
            out=sc["H2"][:, :], in0=sc["T1"][:, :], scalar=EPSM, in1=sc["T2"][:, :],
            op0=OP.mult, op1=OP.add))
        hv = sc["Hv"]
        for a_i in range(2):
            hvs = hv[:, a_i * 16:(a_i + 1) * 16]
            ev_ = hvs.rearrange("p (h c) -> p h c", c=2)[:, :, 0:1]
            od_ = hvs.rearrange("p (h c) -> p h c", c=2)[:, :, 1:2]
            e("vector", lambda v, ev_=ev_, j=j: v.tensor_tensor(
                out=sc["tva"][:, 0:8].rearrange("p (h o) -> p h o", o=1),
                in0=ev_, in1=fs8(j, sc["Fpl"])[:, :, 0:1], op=OP.mult))
            e("vector", lambda v, od_=od_, j=j: v.tensor_tensor(
                out=sc["tva"][:, 8:16].rearrange("p (h o) -> p h o", o=1),
                in0=od_, in1=fs8(j, sc["Spl"])[:, :, 0:1], op=OP.mult))
            adst = bass.AP(tensor=sc["abll"][:, 0:1].tensor,
                           offset=sc["abll"][:, 0:1].offset + a_i * 8 * 32 + j,
                           ap=[list(sc["abll"][:, 0:1].ap[0]), [32, 8], [1, 1]])
            e("vector", lambda v, adst=adst: v.tensor_tensor(
                out=adst, in0=sc["tva"][:, 0:8].rearrange("p (h o) -> p h o", o=1),
                in1=sc["tva"][:, 8:16].rearrange("p (h o) -> p h o", o=1), op=OP.add))
            e("vector", lambda v, ev_=ev_: v.scalar_tensor_tensor(
                out=ev_, in0=sc["tva"][:, 8:16].rearrange("p (h o) -> p h o", o=1),
                scalar=EPSM, in1=sc["tva"][:, 0:8].rearrange("p (h o) -> p h o", o=1),
                op0=OP.mult, op1=OP.add))
            e("vector", lambda v, od_=od_: v.scalar_tensor_tensor(
                out=od_, in0=sc["tva"][:, 0:8].rearrange("p (h o) -> p h o", o=1),
                scalar=EPSM, in1=sc["tva"][:, 8:16].rearrange("p (h o) -> p h o", o=1),
                op0=OP.mult, op1=OP.add))
        for Hx in ("H1", "H2"):
            cdst = bass.AP(tensor=sc[Hx][:, 0:1].tensor,
                           offset=sc[Hx][:, 0:1].offset + j,
                           ap=[list(sc[Hx][:, 0:1].ap[0]), [32, 8], [1, 1]])
            e("vector", lambda v, cdst=cdst: v.tensor_scalar(
                out=cdst, in0=cdst, scalar1=0.0, scalar2=1.0 + EPSM,
                op0=OP.mult, op1=OP.add))
    for Hx, so in (("H1", 32), ("H2", 33)):
        gdst = bass.AP(tensor=sc["Gall"][:, 0:1].tensor,
                       offset=sc["Gall"][:, 0:1].offset + so,
                       ap=[list(sc["Gall"][:, 0:1].ap[0]), [34 * 32, 8], [34, 32]])
        e("vector", lambda v, gdst=gdst, Hx=Hx: v.tensor_copy(
            gdst, sc[Hx][:, :].rearrange("p (h s) -> p h s", s=32)))
    gb_done = ctx.cnt["vs"]
    return gb_done


def emit_delta_scan2(ctx, g_dram, ab_dram, hv_dram, w_dram, rep4_sb, o_fm,
                     ps, sc, gb_done, Ttok=T):
    e = ctx.emit
    NSB = Ttok // 32
    NG4 = NSB // 4
    gst = ctx.emit("sync", lambda q: q.dma_start(
        out=g_dram.ap(), in_=sc["Gall"][:, :]), wait=(("vs", gb_done),),
        dma=True, store=True)
    ast = ctx.emit("sync", lambda q: q.dma_start(
        out=ab_dram.ap(), in_=sc["abll"][:, :]), wait=(("vs", gb_done),),
        dma=True, store=True)
    tvh = e("tensor", lambda t: t.transpose(
        ps["pth"][0:32, :], sc["Hv"][:, :], ctx.ident_f32[:, :]),
        wait=(("vs", gb_done),))
    hvv = e("vector", lambda v: v.tensor_copy(sc["hvt"][0:32, :], ps["pth"][0:32, :]),
            wait=(("ts", tvh),))
    hst = ctx.emit("sync", lambda q: q.dma_start(
        out=hv_dram.ap(), in_=sc["hvt"][0:32, :]), wait=(("vs", hvv),),
        dma=True, store=True)
    gzz = e("gpsimd", lambda g: g.memset(sc["Grb"][:, :], 0.0),
            wait=(("vs", ctx.cnt["vs"]), ("ss", ctx.cnt["ss"])))
    grd = 0
    dwev = 0
    oev = 0
    for g4 in range(NG4):
        for sb4 in range(4):
            grd = ctx.emit("sync", lambda q, g4=g4, sb4=sb4: q.dma_start(
                out=bass.AP(
                    tensor=sc["Grb"][sb4 * 32:(sb4 + 1) * 32, 0:1].tensor,
                    offset=sc["Grb"][sb4 * 32:(sb4 + 1) * 32, 0:1].offset
                    + sb4 * 34,
                    ap=[list(sc["Grb"][sb4 * 32:(sb4 + 1) * 32, 0:1].ap[0]),
                        [136, 8], [1, 34]]),
                in_=bass.AP(tensor=g_dram.ap().tensor,
                            offset=(g4 * 4 + sb4) * 8704,
                            ap=[[34, 32], [1088, 8], [1, 34]]),
                ),
                wait=(("st", gst), ("ts", ctx.cnt["ts"]))
                + ((("gs", gzz),) if g4 == 0 else ()), dma=True)
        for h in range(8):
            tvm = e("tensor", lambda t, h=h, g4=g4: t.matmul(
                ps["pdo"][0:64, 0:136],
                ctx.u_tm[:, g4 * 512 + h * 64: g4 * 512 + h * 64 + 64],
                sc["Grb"][:, h * 136:(h + 1) * 136],
                start=True, stop=True),
                wait=(("dS", grd), ("ss", oev), ("vs", dwev)))
            oev = e("scalar", lambda s2, h=h, g4=g4: s2.copy(
                out=o_fm[(h % 2) * 64:(h % 2) * 64 + 64,
                         (h // 2) * Ttok + g4 * 128: (h // 2) * Ttok + g4 * 128 + 128],
                in_=bass.AP(tensor=ps["pdo"][0:64, 0:1].tensor,
                            offset=ps["pdo"][0:64, 0:1].offset,
                            ap=[list(ps["pdo"][0:64, 0:1].ap[0]), [34, 4], [1, 32]])),
                wait=(("ts", tvm),))
            for c_ in range(2):
                dwev = e("vector", lambda v, h=h, g4=g4, c_=c_: v.tensor_copy(
                    bass.AP(tensor=sc["dsA"][0:64, 0:1].tensor,
                            offset=sc["dsA"][0:64, 0:1].offset
                            + c_ * 1536 + h * 192 + 64 + g4 * 4,
                            ap=[list(sc["dsA"][0:64, 0:1].ap[0]), [1, 4]]),
                    bass.AP(tensor=ps["pdo"][0:64, 0:1].tensor,
                            offset=ps["pdo"][0:64, 0:1].offset + 32 + c_,
                            ap=[list(ps["pdo"][0:64, 0:1].ap[0]), [34, 4]])),
                    wait=(("ts", tvm),))
    return dwev, oev, hst, ast


def emit_delta_scan3(ctx, ab_dram, hv_dram, w_dram, rep4_sb, o_fm, qg_fm,
                     ps, sc, dwev, oev, hst, ast, Ttok=T, sub=5):
    e = ctx.emit
    NSB = Ttok // 32
    NG4 = NSB // 4
    mld = 0
    for mi, (a_i, c_) in enumerate(((0, 0), (1, 0), (0, 1), (1, 1))):
        mld = ctx.emit("gpsimd", lambda q, mi=mi, a_i=a_i, c_=c_: q.dma_start(
            out=bass.AP(tensor=sc["mr"][mi][0:1, 0:1].tensor,
                        offset=sc["mr"][mi][0:1, 0:1].offset + 64,
                        ap=[list(sc["mr"][mi][0:1, 0:1].ap[0]), [192, 8], [1, 128]]),
            in_=bass.AP(tensor=hv_dram.ap().tensor,
                        offset=(a_i * 16 + c_) * 128,
                        ap=[[1, 1], [2 * 128, 8], [1, 128]])),
            wait=(("st", hst),), dma=True)

    def padfix(which, lvl):
        gv = 0
        for mi in range(4):
            padv = 1.0 if mi in (0, 3) else 0.0
            gv = e("gpsimd", lambda g, mi=mi, padv=padv, which=which: g.memset(
                bass.AP(tensor=sc[which][mi][0:1, 0:1].tensor,
                        offset=sc[which][mi][0:1, 0:1].offset,
                        ap=[list(sc[which][mi][0:1, 0:1].ap[0]), [192, 8], [1, 64]]),
                padv), wait=(("vs", ctx.cnt["vs"]),))
        return gv

    if sub < 2:
        return ctx.cnt["vs"]
    gz = e("gpsimd", lambda g: g.memset(sc["dsB"][0:64, :], 0.0))
    gz = padfix("mr", -1)
    gz0 = e("gpsimd", lambda g: g.memset(
        bass.AP(tensor=sc["dsA"][0:64, 0:1].tensor,
                offset=sc["dsA"][0:64, 0:1].offset,
                ap=[list(sc["dsA"][0:64, 0:1].ap[0]), [1536, 2], [192, 8],
                    [1, 64]]), 0.0))
    cur_d, nxt_d = "dsA", "dsB"
    cur_m, nxt_m = "mr", "mr2"
    lvl_w = (("vs", dwev), ("dG", mld), ("gs", ctx.cnt["gs"]))

    def shr(ap2d, sh):
        return bass.AP(tensor=ap2d.tensor, offset=ap2d.offset - sh,
                       ap=[list(d) for d in ap2d.ap])

    for k in range(7):
        sh = 1 << k
        mv = 0
        for (o_, a_, bsh, c2_, dsh) in ((0, 0, 0, 1, 1), (1, 0, 0, 1, 1),
                                        (2, 2, 0, 3, 1), (3, 2, 0, 3, 1)):
            pass
        W = 1536 - sh
        for o_, (x1, y1, x2, y2) in enumerate(((0, 0, 1, 2), (0, 1, 1, 3),
                                               (2, 0, 3, 2), (2, 1, 3, 3))):
            for q3 in range(3):
                c0 = q3 * 512
                lo = max(sh, c0) - c0
                if lo >= 512:
                    continue
                e("vector", lambda v, x1=x1, y1=y1, sh=sh, c0=c0, lo=lo: v.tensor_tensor(
                    out=sc["mtf"][0:1, c0 + lo:c0 + 512],
                    in0=sc[cur_m][x1][0:1, c0 + lo:c0 + 512],
                    in1=sc[cur_m][y1][0:1, c0 + lo - sh:c0 + 512 - sh],
                    op=OP.mult),
                    wait=lvl_w if (k == 0 and o_ == 0 and q3 == 0) else ())
                e("vector", lambda v, x2=x2, y2=y2, sh=sh, c0=c0, lo=lo: v.tensor_tensor(
                    out=ps["pB"][0:1, lo:512],
                    in0=sc[cur_m][x2][0:1, c0 + lo:c0 + 512],
                    in1=sc[cur_m][y2][0:1, c0 + lo - sh:c0 + 512 - sh],
                    op=OP.mult))
                mv = e("vector", lambda v, o_=o_, c0=c0, lo=lo: v.tensor_tensor(
                    out=sc[nxt_m][o_][0:1, c0 + lo:c0 + 512],
                    in0=sc["mtf"][0:1, c0 + lo:c0 + 512],
                    in1=ps["pB"][0:1, lo:512], op=OP.add))
        dv = 0
        for ci in range(2):
            for q3 in range(3):
                cs = q3 * 512
                t1 = e("tensor", lambda t, ci=ci, cs=cs: t.matmul(
                    ps["pA"][:, 0:512], ctx.onesrow_bf[0:1, 0:128],
                    sc[cur_m][0 + ci * 2][0:1, cs:cs + 512],
                    start=True, stop=True),
                    wait=(("vs", max(mv, dv)), ("vs", ctx.cnt["vs"])))
                t2 = e("tensor", lambda t, ci=ci, cs=cs: t.matmul(
                    ps["pB"][:, 0:512], ctx.onesrow_bf[0:1, 0:128],
                    sc[cur_m][1 + ci * 2][0:1, cs:cs + 512],
                    start=True, stop=True))
                lo = sh if q3 == 0 else 0
                e("vector", lambda v, ci=ci, cs=cs, lo=lo, sh=sh: v.tensor_tensor(
                    out=sc["dt"][0:64, lo:512],
                    in0=sc[cur_d][0:64, cs + lo - sh: cs + 512 - sh],
                    in1=ps["pA"][0:64, lo:512], op=OP.mult),
                    wait=(("ts", t2),))
                e("vector", lambda v, ci=ci, cs=cs, lo=lo, sh=sh: v.tensor_tensor(
                    out=sc["dt2"][0:64, lo:512],
                    in0=sc[cur_d][0:64, 1536 + cs + lo - sh: 1536 + cs + 512 - sh],
                    in1=ps["pB"][0:64, lo:512], op=OP.mult))
                e("vector", lambda v, ci=ci, cs=cs, lo=lo: v.tensor_tensor(
                    out=sc["dt"][0:64, lo:512], in0=sc["dt"][0:64, lo:512],
                    in1=sc["dt2"][0:64, lo:512], op=OP.add))
                dv = e("vector", lambda v, ci=ci, cs=cs, lo=lo: v.tensor_tensor(
                    out=sc[nxt_d][0:64, ci * 1536 + cs + lo:ci * 1536 + cs + 512],
                    in0=sc["dt"][0:64, lo:512],
                    in1=sc[cur_d][0:64, ci * 1536 + cs + lo:ci * 1536 + cs + 512],
                    op=OP.add))
        e("gpsimd", lambda g: g.memset(
            bass.AP(tensor=sc[nxt_d][0:64, 0:1].tensor,
                    offset=sc[nxt_d][0:64, 0:1].offset,
                    ap=[list(sc[nxt_d][0:64, 0:1].ap[0]), [1536, 2], [192, 8],
                        [1, 64]]), 0.0),
            wait=(("vs", dv),))
        padfix(nxt_m, k)
        cur_d, nxt_d = nxt_d, cur_d
        cur_m, nxt_m = nxt_m, cur_m
        lvl_w = (("gs", ctx.cnt["gs"]),)
    if sub < 3:
        return ctx.cnt["vs"]
    dcp = 0
    wst = 0
    for h in range(8):
        tvw = 0
        for c_ in range(2):
            tvw = e("tensor", lambda t, h=h, c_=c_: t.transpose(
                ps["pg"][:, c_ * 64:(c_ + 1) * 64],
                sc[cur_d][0:64, c_ * 1536 + h * 192 + 64:
                          c_ * 1536 + h * 192 + 64 + 128],
                ctx.ident_f32[0:64, 0:64]),
                wait=((("gs", ctx.cnt["gs"]),) if h == 0 else ())
                + ((("vs", dcp),) if c_ == 0 else ()))
        dcp = e("vector", lambda v: v.tensor_copy(sc["wcp"][:, 0:128],
                                                  ps["pg"][:, 0:128]),
                wait=(("ts", tvw), ("st", wst)))
        wst = ctx.emit("sync", lambda q, h=h: q.dma_start(
            out=w_dram.ap()[h * 128:(h + 1) * 128, :], in_=sc["wcp"][:, 0:128]),
            wait=(("vs", dcp),), dma=True, store=True)
    mtmp = sc["ostm"]
    e("gpsimd", lambda g: g.iota(mtmp[:, 0:32], pattern=[[1, 32]], base=0,
                                 channel_multiplier=0,
                                 allow_small_or_imprecise_dtypes=True),
      wait=(("vs", ctx.cnt["vs"]), ("ss", ctx.cnt["ss"]), ("ts", ctx.cnt["ts"])))
    e("gpsimd", lambda g: g.iota(mtmp[:, 32:33], pattern=[[1, 1]], base=0,
                                 channel_multiplier=1,
                                 allow_small_or_imprecise_dtypes=True))
    ctx.wait("vector", ("gs", ctx.cnt["gs"]))
    for i, thr in enumerate((32.0, 64.0, 96.0)):
        e("vector", lambda v, i=i, thr=thr: v.tensor_scalar(
            out=mtmp[:, 33 + i:34 + i], in0=mtmp[:, 32:33], scalar1=thr,
            scalar2=None, op0=OP.is_ge))
    e("vector", lambda v: v.tensor_tensor(
        out=mtmp[:, 36:37], in0=mtmp[:, 33:34], in1=mtmp[:, 34:35], op=OP.add))
    e("vector", lambda v: v.tensor_tensor(
        out=mtmp[:, 36:37], in0=mtmp[:, 36:37], in1=mtmp[:, 35:36], op=OP.add))
    e("vector", lambda v: v.scalar_tensor_tensor(
        out=mtmp[:, 36:37], in0=mtmp[:, 36:37], scalar=-32.0,
        in1=mtmp[:, 32:33], op0=OP.mult, op1=OP.add))
    e("vector", lambda v: v.tensor_scalar(
        out=sc["msk32"][:, :], in0=mtmp[:, 0:32], scalar1=mtmp[:, 36:37],
        scalar2=None, op0=OP.is_equal))
    if sub < 4:
        return ctx.cnt["vs"]
    abrd = 0
    wrd_c = 0
    war_pab = 0
    for g4 in range(NG4):
        abrd = ctx.emit("sync", lambda q, g4=g4: q.dma_start(
            out=sc["abg"][0:4, :], in_=ab_dram.ap()[g4 * 4:(g4 + 1) * 4, :]),
            wait=(("st", ast), ("ts", war_pab))
            + ((("vs", ctx.cnt["vs"]),) if g4 == 0 else ()), dma=True)
        if g4 > 0:
            wv_ = ctx.emit("sync", lambda q, g4=g4: q.dma_start(
                out=sc["wrd"][0:4, 0:1024].rearrange("p (h x) -> p h x", x=128),
                in_=bass.AP(tensor=w_dram.ap().tensor,
                            offset=(g4 * 4 - 1) * 128,
                            ap=[[128, 4], [128 * 128, 8], [1, 128]])),
                wait=(("st", wst), ("ts", ctx.cnt["ts"])), dma=True)
            wwait = ("dS", wv_)
        else:
            gv_ = e("gpsimd", lambda g: g.memset(sc["wrd"][0:4, 0:1024], 0.0),
                    wait=(("ts", ctx.cnt["ts"]),))
            wv_ = ctx.emit("sync", lambda q: q.dma_start(
                out=sc["wrd"][1:4, 0:1024].rearrange("p (h x) -> p h x", x=128),
                in_=bass.AP(tensor=w_dram.ap().tensor, offset=0,
                            ap=[[128, 3], [128 * 128, 8], [1, 128]])),
                wait=(("st", wst), ("gs", gv_)), dma=True)
            wwait = ("dS", wv_)
        if SUB4 < 3:
            wrd_c = e("vector", lambda v: v.tensor_copy(
                sc["abp"][:, 0:16], sc["abp"][:, 0:16]),
                wait=(("ts", war_pab), ("dS", wv_ if isinstance(wwait, tuple)
                                        and False else wwait[1])))
            continue
        for half in range(2):
            e("tensor", lambda t, half=half: t.matmul(
                (ps["pA"] if half == 0 else ps["pB"])[:, 0:512],
                sc["rep4f"][0:4, 0:128],
                sc["wrd"][0:4, half * 512:(half + 1) * 512],
                start=True, stop=True),
                wait=(wwait, ("vs", ctx.cnt["vs"])))
        if SUB4 < 2:
            wrd_c = e("vector", lambda v: v.tensor_copy(
                sc["abp"][:, 0:16], sc["abg"][0:4, 0:16][0:1, :]
                if False else sc["abp"][:, 0:16]), wait=(("dS", abrd),))
            continue
        war_pab = e("tensor", lambda t: t.matmul(
            ps["pab"][:, 0:512], sc["rep4f"][0:4, 0:128], sc["abg"][0:4, 0:512],
            start=True, stop=True),
            wait=(("dS", abrd), ("vs", wrd_c)))
        e("vector", lambda v: v.tensor_tensor(
            out=sc["abll"][:, :].rearrange("p (c j) -> p c j", j=32),
            in0=ps["pab"][:, 0:512].rearrange("p (c j) -> p c j", j=32),
            in1=bass.AP(tensor=sc["msk32"][:, 0:1].tensor,
                        offset=sc["msk32"][:, 0:1].offset,
                        ap=[list(sc["msk32"][:, 0:1].ap[0]), [0, 16], [1, 32]]),
            op=OP.mult), wait=(("ts", war_pab),))
        abred = e("vector", lambda v: v.tensor_reduce(
            out=sc["abp"][:, 0:16],
            in_=sc["abll"][:, :].rearrange("p (c j) -> p c j", j=32),
            axis=AX.X, op=OP.add))
        if SUB4 < 4:
            wrd_c = abred
            continue
        tvb = ctx.cnt["ts"]
        wA = ps["pA"][:, 0:512].rearrange("p (h c d) -> p h c d", c=2, d=64)
        wB = ps["pB"][:, 0:512].rearrange("p (h c d) -> p h c d", c=2, d=64)
        ot = 0
        for half, wpl in ((0, wA), (1, wB)):
            e("vector", lambda v, wpl=wpl, half=half: v.tensor_tensor(
                out=sc["ostm"][:, half * 256:(half + 1) * 256]
                .rearrange("p (h d) -> p h d", d=64),
                in0=wpl[:, :, 0, :],
                in1=bass.AP(tensor=sc["abp"][:, half * 4:half * 4 + 1].tensor,
                            offset=sc["abp"][:, half * 4:half * 4 + 1].offset,
                            ap=[list(sc["abp"][:, 0:1].ap[0]), [1, 4], [0, 64]]),
                op=OP.mult), wait=(("ts", tvb), ("vs", abred)))
            e("vector", lambda v, wpl=wpl, half=half: v.tensor_tensor(
                out=sc["ostm2"][:, half * 256:(half + 1) * 256]
                .rearrange("p (h d) -> p h d", d=64),
                in0=wpl[:, :, 1, :],
                in1=bass.AP(tensor=sc["abp"][:, 8 + half * 4:8 + half * 4 + 1].tensor,
                            offset=sc["abp"][:, 8 + half * 4:8 + half * 4 + 1].offset,
                            ap=[list(sc["abp"][:, 0:1].ap[0]), [1, 4], [0, 64]]),
                op=OP.mult))
        ot = e("vector", lambda v: v.tensor_tensor(
            out=sc["ostm"][:, :], in0=sc["ostm"][:, :], in1=sc["ostm2"][:, :],
            op=OP.add))
        ot = e("vector", lambda v: v.tensor_copy(sc["ostb"][:, :], sc["ostm"][:, :]))
        wrd_c = ot
        if SUB4 < 5:
            continue
        for ch in range(4):
            tvx = e("tensor", lambda t, ch=ch: t.transpose(
                ps["ptx"][:, 0:128], sc["ostb"][:, ch * 128:(ch + 1) * 128],
                ctx.ident_bf[:, :]),
                wait=(("vs", ot),) if ch == 0 else (("vs", ctx.cnt["vs"]),))
            e("vector", lambda v, ch=ch, g4=g4: v.tensor_tensor(
                out=o_fm[:, ch * Ttok + g4 * 128: ch * Ttok + g4 * 128 + 128],
                in0=o_fm[:, ch * Ttok + g4 * 128: ch * Ttok + g4 * 128 + 128],
                in1=ps["ptx"][:, 0:128], op=OP.add),
                wait=(("ts", tvx), ("ss", oev)))
    if sub < 5:
        return ctx.cnt["vs"]
    fin = 0
    for ch in range(4):
        for gg in range(Ttok // 512):
            fin = e("vector", lambda v, ch=ch, gg=gg: v.tensor_tensor(
                out=o_fm[:, ch * Ttok + gg * 512: ch * Ttok + gg * 512 + 512],
                in0=o_fm[:, ch * Ttok + gg * 512: ch * Ttok + gg * 512 + 512],
                in1=qg_fm[:, ch * Ttok + gg * 512: ch * Ttok + gg * 512 + 512],
                op=OP.mult))
    return fin


def build_delta_front_launch(stage=8):
    nc = bass.Bass("TRN2", target_bir_lowering=False, debug=False,
                   num_devices=NCORES)
    xnin = nc.dram_tensor("xnin", [D, T], BF16, kind="ExternalInput").ap()
    wq = nc.dram_tensor("wq", [D, 512], F32, kind="ExternalInput").ap()
    wk = nc.dram_tensor("wk", [D, 512], F32, kind="ExternalInput").ap()
    wv = nc.dram_tensor("wv", [D, 512], F32, kind="ExternalInput").ap()
    wg = nc.dram_tensor("wg", [D, 512], F32, kind="ExternalInput").ap()
    wb = nc.dram_tensor("wb", [D, 24], F32, kind="ExternalInput").ap()
    brow = nc.dram_tensor("brow", [1, 24], F32, kind="ExternalInput").ap()
    rep4 = nc.dram_tensor("rep4", [4, 128], F32, kind="ExternalInput").ap()
    ofm = nc.dram_tensor("ofm", [512, T], BF16, kind="ExternalOutput").ap()
    fs_dram = nc.dram_tensor("fs_dram", [16, T], F32)
    g_dram = nc.dram_tensor("g_dram", [128, 8704], BF16)
    ab_dram = nc.dram_tensor("ab_dram", [128, 512], F32)
    hv_dram = nc.dram_tensor("hv_dram", [32, 128], F32)
    w_dram = nc.dram_tensor("w_dram", [8 * 128, 128], F32)
    import contextlib
    es = contextlib.ExitStack()

    def sb(name, shape, dt):
        return es.enter_context(nc.sbuf_tensor(name, shape, dt))

    def pst(name, shape, dt=F32):
        return es.enter_context(nc.psum_tensor(name, shape, dt))

    with es:
        xs_sb = sb("xs_sb", [128, 2 * 1024], BF16)
        q_fm = sb("q_fm", [128, 4 * T], BF16)
        g_fm = sb("g_fm", [128, 4 * T], BF16)
        u_tm = sb("u_tm", [128, 32 * 512], BF16)
        wun = sb("wun", [128, 4 * 4096 + 8 * 24], BF16)
        wq_sb = wun[:, 0:4096]
        wk_sb = wun[:, 4096:8192]
        wv_sb = wun[:, 8192:12288]
        wg_sb = wun[:, 12288:16384]
        wb_sb = wun[:, 16384:16384 + 192]
        Gall = wun[:, 0:8704]
        poolf = sb("poolf", [128, 1024], F32)
        dtx = sb("dtx", [128, 512], F32)
        dt2x = sb("dt2x", [128, 512], F32)
        scr3 = sb("scr3", [128, 1536], BF16)
        sc = {
            "ktm": scr3[:, 0:512],
            "vtm": scr3[:, 512:1024],
            "utm": scr3[:, 1024:1536],
            "bfs": sb("bfs", [128, 24], F32),
            "fsrow": sb("fsrow", [8, 128], F32),
            "Fpl": poolf[:, 512:768],
            "Spl": poolf[:, 768:1024],
            "H1": poolf[:, 0:256],
            "H2": poolf[:, 256:512],
            "T1": dtx[:, 0:256],
            "T2": dt2x[:, 0:256],
            "Hv": sb("Hv", [128, 32], F32),
            "tva": sb("tva", [128, 16], F32),
            "abll": sb("abll", [128, 512], F32),
            "dsA": sb("dsA", [64, 3072], F32),
            "dsB": sb("dsB", [64, 3072], F32),
            "dt": dtx,
            "dt2": dt2x,
            "mtf": sb("mtf", [1, 1536], F32),
            "abp": sb("abp", [128, 16], F32),
            "wrd": sb("wrd", [4, 1024], F32),
            "ostm": poolf[:, 0:512],
            "ostm2": poolf[:, 512:1024],
            "ostb": sb("ostb", [128, 512], BF16),
        }
        sc["Grb"] = scr3[:, 0:1088]
        sc.update({
            "wcp": sb("wcp", [128, 128], F32),
            "hvt": sb("hvt", [32, 128], F32),
            "rep4f": sb("rep4f", [4, 128], F32),
        })
        sc["msk32"] = sc["Hv"]
        sc["abg"] = sc["dsA"][0:4, 0:512]
        rep4_sb = sc["rep4f"]
        sc["mr"] = [sb(f"mr{i}", [1, 1536], BF16) for i in range(4)]
        sc["mr2"] = [sb(f"ms{i}", [1, 1536], BF16) for i in range(4)]
        sc["Gall"] = Gall

        brow_sb = sb("brow_sb", [1, 24], BF16)
        ident = sb("ident", [128, 128], BF16)
        identf = sb("identf", [128, 128], F32)
        onesb = sb("onesb", [128, 1], F32)
        onesrow = sc["fsrow"][0:1, 0:128]
        onesrowb = sb("onesrowb", [1, 128], BF16)
        cbank = sb("cbank", [128, 8], F32)
        sc["onesbf"] = onesrowb
        ps = {
            "pk": pst("pk", [128, 512]),
            "pv": pst("pv", [128, 512]),
            "pb": pst("pb", [128, 24]),
            "pq": pst("pq", [128, 128]),
            "pdo": pst("pdo", [64, 136]),
            "ptx": pst("ptx", [128, 128], BF16),
            "pg": pst("pgx", [128, 128]),
        }
        ps["pab"] = pst("pab", [128, 512])
        ps["pA"] = ps["pk"]
        ps["pB"] = ps["pv"]
        ps["pt8"] = ps["pg"][0:8, :]
        ps["pth"] = ps["pg"][0:32, :]
        block = es.enter_context(nc.Block())
        ctx = Ctx(nc, block)
        ctx.cbank = cbank
        ctx.onesrow_bf_t = onesrowb
        ctx.u_tm = u_tm[:, :]
        ctx.q_fm = q_fm[:, :]
        ctx.g_fm = g_fm[:, :]
        ctx.brow_sb = brow_sb
        emit_prelude(ctx, ident, onesb, onesrow, identf)
        wld = 0
        for d_, s_ in ((wq, wq_sb), (wk, wk_sb), (wv, wv_sb), (wg, wg_sb)):
            wld = ctx.emit("gpsimd", lambda g, d_=d_, s_=s_: g.dma_start(
                out=s_.rearrange("p (c f) -> p c f", c=NCH),
                in_=d_.rearrange("(c p) f -> p c f", p=128)), dma=True)
        wld = ctx.emit("gpsimd", lambda g: g.dma_start(
            out=wb_sb.rearrange("p (c f) -> p c f", c=NCH),
            in_=wb.rearrange("(c p) f -> p c f", p=128)), dma=True)
        wld = ctx.emit("gpsimd", lambda g: g.dma_start(
            out=brow_sb[:, :], in_=brow), dma=True)
        wld = ctx.emit("gpsimd", lambda g: g.dma_start(
            out=sc["rep4f"][0:4, :], in_=rep4), dma=True)
        w_ready = (("dG", wld), ("gs", ctx.cnt["gs"]))
        fs_st = emit_delta_front(ctx, xnin, xs_sb, wq_sb, wk_sb, wv_sb, wg_sb,
                                 wb_sb, None, None, None, rep4_sb, None,
                                 fs_dram.ap(), None, None, None, None,
                                 ps, sc, w_ready)
        qgv = 0
        for ch in range(4):
            for gg in range(T // 512):
                qgv = ctx.emit("vector", lambda v, ch=ch, gg=gg: v.scalar_tensor_tensor(
                    out=g_fm[:, ch * T + gg * 512: ch * T + gg * 512 + 512],
                    in0=q_fm[:, ch * T + gg * 512: ch * T + gg * 512 + 512],
                    scalar=0.5, in1=g_fm[:, ch * T + gg * 512: ch * T + gg * 512 + 512],
                    op0=OP.mult, op1=OP.mult),
                    wait=(("ss", ctx.cnt["ss"]),) if (ch == 0 and gg == 0) else ())
        o_fm = q_fm[:, :]
        zf = ctx.emit("gpsimd", lambda g: g.memset(o_fm, 0.0),
                      wait=(("vs", qgv),))
        ctx.wait("vector", ("st", fs_st), ("gs", zf))
        fin = ("gs", zf)
        if stage >= 2:
            gb = emit_delta_scan(ctx, fs_dram, g_dram, ab_dram, hv_dram,
                                 w_dram, rep4_sb, o_fm, ps, sc)
            fin = ("vs", gb)
        if stage >= 3:
            dwev, oev, hst, ast = emit_delta_scan2(
                ctx, g_dram, ab_dram, hv_dram, w_dram, rep4_sb, o_fm, ps, sc, gb)
            fin = ("ss", oev)
        if stage >= 4:
            f4 = emit_delta_scan3(ctx, ab_dram, hv_dram, w_dram, rep4_sb, o_fm,
                                  g_fm, ps, sc, dwev, oev, hst, ast,
                                  sub=stage - 3)
            fin = ("vs", f4)
        ctx.emit("sync", lambda q: q.dma_start(
            out=ofm.rearrange("(c p) t -> p c t", p=128),
            in_=o_fm.rearrange("p (c t) -> p c t", c=4)),
            wait=(fin, ("vs", ctx.cnt["vs"]), ("ss", ctx.cnt["ss"])),
            dma=True, store=True)
        ctx.wait("sync", ("st", ctx.cnt["st"]))
    return nc


def build_epi_launch(kind="plain"):
    nc = bass.Bass("TRN2", target_bir_lowering=False, debug=False,
                   num_devices=NCORES)
    xin = nc.dram_tensor("xin", [D, TL], F32, kind="ExternalInput").ap()
    if kind != "ln":
        ofu = nc.dram_tensor("ofu", [D, TL], BF16, kind="ExternalInput").ap()
        warr = nc.dram_tensor("warr", [D, D], F32, kind="ExternalInput").ap()
        xout = nc.dram_tensor("xout", [D, TL], F32, kind="ExternalOutput").ap()
    if kind == "bias":
        bpc = nc.dram_tensor("bpc", [128, 8], F32, kind="ExternalInput").ap()
    xnout = nc.dram_tensor("xnout", [D, TL], BF16, kind="ExternalOutput").ap()
    es = contextlib.ExitStack()

    def sb(name, shape, dt):
        return es.enter_context(nc.sbuf_tensor(name, shape, dt))

    def pst(name, shape, dt=F32):
        return es.enter_context(nc.psum_tensor(name, shape, dt))

    with es:
        x_sb = sb("x_sb", [128, NCH * TL], F32)
        xn_sb = sb("xn_sb", [128, NCH * TL], BF16)
        if kind != "ln":
            o_sb = sb("o_sb", [128, NCH * TL], BF16)
            w_sb = sb("w_sb", [128, NCH * 1024], BF16)
        if kind == "bias":
            bcc = sb("bcc", [128, 8], F32)
        sqbuf = sb("sqbuf", [128, 1024], F32)
        rows = sb("rows", [1, 3 * TL], F32)
        ident = sb("ident", [128, 128], BF16)
        onesb = sb("onesb", [128, 1], F32)
        onesrow = sb("onesrow", [1, 128], F32)
        cbank = sb("cbank", [128, 8], F32)
        ps_s1 = pst("ps_s1", [1, 512])
        ps_s2 = pst("ps_s2", [1, 512])
        ps_bc = pst("ps_bc", [128, 1024])
        pm = [pst("pm0", [128, 512]), pst("pm1", [128, 512])]
        block = es.enter_context(nc.Block())
        ctx = Ctx(nc, block)
        ctx.cbank = cbank
        emit_prelude(ctx, ident, onesb, onesrow)
        ctx.constcol(1e-5)
        xv = ctx.emit("sync", lambda q: q.dma_start(
            out=x_sb[:, :].rearrange("p (c t) -> p c t", c=NCH),
            in_=xin.rearrange("(c p) t -> p c t", p=128)), dma=True)
        ev = 0
        if kind != "ln":
            ov = ctx.emit("sync", lambda q: q.dma_start(
                out=o_sb[:, :].rearrange("p (c t) -> p c t", c=NCH),
                in_=ofu.rearrange("(c p) t -> p c t", p=128)), dma=True)
            wv = ctx.emit("gpsimd", lambda g: g.dma_start(
                out=w_sb[:, :].rearrange("p (c f) -> p c f", c=NCH),
                in_=warr.rearrange("(c p) f -> p c f", p=128)), dma=True)
            if kind == "bias":
                bv = ctx.emit("sync", lambda q: q.dma_start(
                    out=bcc[:, :], in_=bpc), dma=True)
            rd = [0, 0]
            for tg in range(TL // 512):
                for jc in range(NCH):
                    bi = (tg * NCH + jc) % 2
                    tv = 0
                    for kc in range(NCH):
                        tv = ctx.emit("tensor", lambda t, tg=tg, jc=jc, kc=kc, bi=bi: t.matmul(
                            pm[bi][:, 0:512],
                            w_sb[:, kc * 1024 + jc * 128: kc * 1024 + jc * 128 + 128],
                            o_sb[:, kc * TL + tg * 512: kc * TL + tg * 512 + 512],
                            start=(kc == 0), stop=(kc == NCH - 1)),
                            wait=(("dG", wv), ("dS", ov), ("vs", rd[bi]))
                            if kc == 0 else (), chain=(kc != 0))
                    if kind == "bias":
                        bb = bcc[:, jc:jc + 1]
                        bbc = bass.AP(tensor=bb.tensor, offset=bb.offset,
                                      ap=[list(bb.ap[0]), [0, 512]])
                        ctx.emit("vector", lambda v, tg=tg, jc=jc, bi=bi, bbc=bbc: v.tensor_tensor(
                            out=x_sb[:, jc * TL + tg * 512: jc * TL + tg * 512 + 512],
                            in0=x_sb[:, jc * TL + tg * 512: jc * TL + tg * 512 + 512],
                            in1=bbc, op=OP.add),
                            wait=(("dS", max(xv, bv)),) if (tg == 0 and jc == 0) else ())
                    ev = ctx.emit("vector", lambda v, tg=tg, jc=jc, bi=bi: v.tensor_tensor(
                        out=x_sb[:, jc * TL + tg * 512: jc * TL + tg * 512 + 512],
                        in0=x_sb[:, jc * TL + tg * 512: jc * TL + tg * 512 + 512],
                        in1=pm[bi][:, 0:512], op=OP.add),
                        wait=(("ts", tv),) + ((("dS", xv),) if (tg == 0 and jc == 0 and kind != "bias") else ()))
                    rd[bi] = ev
            lnready = (("vs", ev), ("gs", ctx.cnt["gs"]))
        else:
            lnready = (("dS", xv), ("gs", ctx.cnt["gs"]))
        xn_ready = emit_ln(ctx, x_sb, xn_sb, TL, ps_s1, ps_s2, sqbuf, rows,
                           ps_bc, ready=lnready)
        if kind != "ln":
            ctx.emit("sync", lambda q: q.dma_start(
                out=xout.rearrange("(c p) t -> p c t", p=128),
                in_=x_sb[:, :].rearrange("p (c t) -> p c t", c=NCH)),
                wait=(("vs", ev),), dma=True, store=True)
        ctx.emit("sync", lambda q: q.dma_start(
            out=xnout.rearrange("(c p) t -> p c t", p=128),
            in_=xn_sb[:, :].rearrange("p (c t) -> p c t", c=NCH)),
            wait=(("vs", ctx.cnt["vs"]),), dma=True, store=True)
        ctx.wait("sync", ("st", ctx.cnt["st"]))
    return nc


from concourse.aot_env import allow_ici_all_to_all

PAIRS = [[0, 1], [2, 3], [4, 5], [6, 7]]
ALL8 = [list(range(NCORES))]


def _barrier(ctx):
    snap = tuple((k, ctx.cnt[k]) for k in
                 ("ts", "vs", "ss", "gs", "dS", "dG", "cc", "st"))
    for eng in ("tensor", "vector", "scalar", "gpsimd", "sync"):
        ctx.wait(eng, *snap)


def build_mega():
    nc = bass.Bass("TRN2", target_bir_lowering=False, debug=False,
                   num_devices=NCORES)

    def EI(n, s, d):
        return nc.dram_tensor(n, s, d, kind="ExternalInput").ap()

    xin = EI("xin", [D, TL], F32)
    gla_w = {n: EI("gla_" + n, [D, 512], F32)
             for n in ("wq", "wk", "wv", "wz")}
    gla_bg = EI("gla_bg", [128, 4], F32)
    gla_wo = EI("gla_wo", [D, D], F32)
    win_wq = EI("win_wq", [D, 1024], F32)
    win_wk = EI("win_wk", [D, 1024], F32)
    win_wv = EI("win_wv", [D, 512], F32)
    win_gq = EI("win_gq", [128, 4], F32)
    win_gk = EI("win_gk", [128, 4], F32)
    win_slp = EI("win_slp", [1, 16], F32)
    win_wo = EI("win_wo", [D, D], F32)
    moe_wgt = EI("moe_wgt", [D, 8], F32)
    w1e = EI("w1e", [D, 4096], F32)
    w2e = EI("w2e", [2048, D], F32)
    tblidx = EI("tblidx", [128, CAP // 128], U32)
    mycol = EI("mycol", [8, 1], F32)
    dl_w = {n: EI("dl_" + n, [D, 512], F32)
            for n in ("wq", "wk", "wv", "wg")}
    dl_wb = EI("dl_wb", [D, 24], F32)
    dl_brow = EI("dl_brow", [1, 24], F32)
    dl_rep4 = EI("dl_rep4", [4, 128], F32)
    dl_wo = EI("dl_wo", [D, D], F32)
    at_wq = EI("at_wq", [D, 512], F32)
    at_wk = EI("at_wk", [D, 512], F32)
    at_wv = EI("at_wv", [D, 512], F32)
    at_sincos = EI("at_sincos", [128, 32 * 64], F32)
    at_wp = EI("at_wp", [D, D], F32)
    at_bp = EI("at_bp", [128, 8], F32)
    f_w1 = EI("f_w1", [D, 4096], F32)
    f_w2 = EI("f_w2", [4096, D], F32)
    f_qpv = EI("f_qpv", [128, 160], F32)
    f_b2v = EI("f_b2v", [128, 8], F32)
    xout = nc.dram_tensor("xout", [D, TL], F32, kind="ExternalOutput").ap()
    x_d = nc.dram_tensor("x_d", [D, TL], F32)
    xnmy_d = nc.dram_tensor("xnmy_d", [D, TL], BF16)
    xnp_d = nc.dram_tensor("xnp_d", [2 * D, TL], BF16)
    xnf_d = nc.dram_tensor("xnf_d", [D, T], BF16)
    oa_in_d = nc.dram_tensor("oa_in_d", [512, T], BF16)
    og_d = nc.dram_tensor("og_d", [2 * D, TL], BF16)
    oidx = EI("oidx", [128, 8], U32)
    od2_d = nc.dram_tensor("od2_d", [8 * T, 64], BF16)
    xntm_my_d = nc.dram_tensor("xntm_my_d", [TL, D], BF16)
    xn_all_d = nc.dram_tensor("xn_all_d", [8 * TL, D], BF16)
    rout_my_d = nc.dram_tensor("rout_my_d", [TL, 8], F32)
    rout_all_d = nc.dram_tensor("rout_all_d", [8 * TL, 8], F32)
    totmy_d = nc.dram_tensor("totmy_d", [8, 1], F32)
    totall_d = nc.dram_tensor("totall_d", [64, 1], F32)
    expo_d = nc.dram_tensor("expo_d", [CAP, D], BF16)
    expo_all_d = nc.dram_tensor("expo_all_d", [8 * CAP, D], BF16)
    table_d = nc.dram_tensor("table_d", [8 * CAP, 4], U32)
    fs_dram = nc.dram_tensor("fs_dram", [16, T], F32)
    g_dram = nc.dram_tensor("g_dram", [128, 8704], BF16)
    ab_dram = nc.dram_tensor("ab_dram", [128, 512], F32)
    hv_dram = nc.dram_tensor("hv_dram", [32, 128], F32)
    w_dram = nc.dram_tensor("w_dram", [8 * 128, 128], F32)

    top = contextlib.ExitStack()
    with top:
        ident = top.enter_context(nc.sbuf_tensor("P_ident", [128, 128], BF16))
        identf = top.enter_context(nc.sbuf_tensor("P_identf", [128, 128], F32))
        onesb = top.enter_context(nc.sbuf_tensor("P_onesb", [128, 1], F32))
        onesrow = top.enter_context(nc.sbuf_tensor("P_onesrow", [1, 128], F32))
        onesrowb = top.enter_context(nc.sbuf_tensor("P_onesrowb", [1, 128], BF16))
        cbank = top.enter_context(nc.sbuf_tensor("P_cbank", [128, 4], F32))
        block = top.enter_context(nc.Block())
        ctx = Ctx(nc, block)
        ctx.cbank = cbank
        ctx.onesrow_bf_t = onesrowb
        emit_prelude(ctx, ident, onesb, onesrow, identf)
        ctx.constcol(1e-5)
        ctx.constcol(HALF_PI)
        ctx.constcol(1e-6)
        _barrier(ctx)

        def sget(es, pfx):
            def sb(name, shape, dt):
                return es.enter_context(nc.sbuf_tensor(pfx + name, shape, dt))

            def pst(name, shape, dt=F32):
                return es.enter_context(nc.psum_tensor(pfx + name, shape, dt))
            return sb, pst

        def sec_ln(pfx, x_src):
            es = contextlib.ExitStack()
            with es:
                sb, pst = sget(es, pfx)
                x_sb = sb("x", [128, NCH * TL], F32)
                xn_sb = sb("xn", [128, NCH * TL], BF16)
                sqbuf = sb("sq", [128, 1024], F32)
                rows = sb("rows", [1, 3 * TL], F32)
                ps_s1 = pst("s1", [1, 512])
                ps_s2 = pst("s2", [1, 512])
                ps_bc = pst("bc", [128, 1024])
                xv = ctx.emit("sync", lambda q: q.dma_start(
                    out=x_sb[:, :].rearrange("p (c t) -> p c t", c=NCH),
                    in_=x_src.rearrange("(c p) t -> p c t", p=128)), dma=True)
                done = emit_ln(ctx, x_sb, xn_sb, TL, ps_s1, ps_s2, sqbuf, rows,
                               ps_bc, ready=(("dS", xv), ("gs", ctx.cnt["gs"])))
                ctx.emit("sync", lambda q: q.dma_start(
                    out=xnmy_d.ap().rearrange("(c p) t -> p c t", p=128),
                    in_=xn_sb[:, :].rearrange("p (c t) -> p c t", c=NCH)),
                    wait=(("vs", done),), dma=True, store=True)
                ctx.wait("sync", ("st", ctx.cnt["st"]))
            _barrier(ctx)

        def xn_exchange():
            ctx.emit("gpsimd", lambda g: g.collective_compute(
                "AllGather", mybir.AluOpType.bypass, replica_groups=PAIRS,
                ins=[xnmy_d.ap().opt()], outs=[xnp_d.ap().opt()]), coll=True)
            cc = ctx.cnt["cc"]
            for s_ in range(2):
                ctx.emit("sync", lambda q, s_=s_: q.dma_start(
                    out=bass.AP(tensor=xnf_d.ap().tensor, offset=s_ * TL,
                                ap=[[T, D], [1, TL]]),
                    in_=xnp_d.ap()[s_ * D:(s_ + 1) * D, :]),
                    wait=(("cc", cc),), dma=True)
            ctx.wait("sync", ("dS", ctx.cnt["dS"]))
            _barrier(ctx)

        def store_o_a2a(o_fm):
            ctx.emit("sync", lambda q: q.dma_start(
                out=oa_in_d.ap().rearrange("(c p) t -> p c t", p=128),
                in_=o_fm.rearrange("p (c t) -> p c t", c=4)),
                dma=True, store=True)
            ctx.wait("sync", ("st", ctx.cnt["st"]))
            _barrier(ctx)
            ctx.emit("gpsimd", lambda g: g.collective_compute(
                "AllGather", mybir.AluOpType.bypass, replica_groups=PAIRS,
                ins=[oa_in_d.ap().opt()], outs=[og_d.ap().opt()]),
                coll=True)
            _barrier(ctx)

        def sec_gla(pfx):
            es = contextlib.ExitStack()
            with es:
                sb, pst = sget(es, pfx)
                xn_sb = sb("xn", [128, NCH * T], BF16)
                o_fm = sb("ofm", [128, 4 * T], BF16)
                wq_sb = sb("wq", [128, 8 * 512], BF16)
                wk_sb = sb("wk", [128, 8 * 512], BF16)
                wv_sb = sb("wv", [128, 8 * 512], BF16)
                wz_sb = sb("wz", [128, 8 * 512], BF16)
                bgate = sb("bg", [128, 4], F32)
                S_sb = sb("S", [128, 256], F32)
                sc = {
                    "qg": sb("qg", [128, 4 * 512], BF16),
                    "kg": sb("kg", [128, 4 * 512], BF16),
                    "vg": sb("vg", [128, 4 * 512], BF16),
                    "lag": sb("lag", [128, 4 * 512], F32),
                    "Bt": sb("Bt", [128, 128], F32),
                    "Dex": sb("Dex", [128, 128], F32),
                    "Div": sb("Div", [128, 128], F32),
                    "qt32": sb("qt32", [128, 128], F32),
                    "qtb": sb("qtb", [128, 128], BF16),
                    "khat": sb("khat", [128, 128], BF16),
                    "ktld": sb("ktld", [128, 128], BF16),
                    "ktm": sb("ktm", [128, 128], BF16),
                    "prb": sb("prb", [128, 128], BF16),
                    "asb": sb("asb", [128, 512], F32),
                }
                ps = {
                    "pq": pst("pq", [128, 512]),
                    "pk": pst("pk", [128, 512]),
                    "pv": pst("pv", [128, 512]),
                    "pz": pst("pz", [128, 512]),
                    "pscore": pst("psc", [128, 128]),
                    "po": pst("po", [64, 128]),
                    "pds": pst("pds", [64, 64]),
                    "ptr": pst("ptr", [128, 128], BF16),
                }
                xv = ctx.emit("sync", lambda q: q.dma_start(
                    out=xn_sb[:, :].rearrange("p (c t) -> p c t", c=NCH),
                    in_=xnf_d.ap().rearrange("(c p) t -> p c t", p=128)),
                    dma=True)
                bv = ctx.emit("sync", lambda q: q.dma_start(
                    out=bgate[:, :], in_=gla_bg), dma=True)
                wvv = 0
                for w_d, w_s in ((gla_w["wq"], wq_sb), (gla_w["wk"], wk_sb),
                                 (gla_w["wv"], wv_sb), (gla_w["wz"], wz_sb)):
                    wvv = ctx.emit("gpsimd", lambda g, w_d=w_d, w_s=w_s: g.dma_start(
                        out=w_s[:, :].rearrange("p (c f) -> p c f", c=NCH),
                        in_=w_d.rearrange("(c p) f -> p c f", p=128)), dma=True)
                w_ready = (("dS", xv), ("dS", bv), ("dG", wvv))
                done = emit_gla_front(ctx, xn_sb, wq_sb, wk_sb, wv_sb, wz_sb,
                                      bgate, o_fm, S_sb, ps, sc, w_ready)
                ctx.wait("sync", ("vs", done))
                store_o_a2a(o_fm)

        def sec_epi(pfx, x_src, warr_ap, x_dst, bias_ap=None, store_xn=True,
                    otm=False):
            es = contextlib.ExitStack()
            with es:
                sb, pst = sget(es, pfx)
                x_sb = sb("x", [128, NCH * TL], F32)
                xn_sb = sb("xn", [128, NCH * TL], BF16)
                o_sb = sb("o", [128, NCH * TL], BF16)
                w_sb = sb("w", [128, NCH * 1024], BF16)
                if bias_ap is not None:
                    bcc = sb("bcc", [128, 8], F32)
                sqbuf = sb("sq", [128, 1024], F32)
                rows = sb("rows", [1, 3 * TL], F32)
                ps_s1 = pst("s1", [1, 512])
                ps_s2 = pst("s2", [1, 512])
                ps_bc = pst("bc", [128, 1024])
                pm = [pst("pm0", [128, 512]), pst("pm1", [128, 512])]
                xv = ctx.emit("sync", lambda q: q.dma_start(
                    out=x_sb[:, :].rearrange("p (c t) -> p c t", c=NCH),
                    in_=x_src.rearrange("(c p) t -> p c t", p=128)), dma=True)
                if not otm:
                    oix = sb("oix", [128, 8], U32)
                    iv = ctx.emit("sync", lambda q: q.dma_start(
                        out=oix[:, :], in_=oidx), dma=True)
                    gv = 0
                    for chg in range(NCH):
                        gv = ctx.emit("gpsimd", lambda g, chg=chg: g.indirect_dma_start(
                            out=o_sb[:, chg * TL:(chg + 1) * TL], out_offset=None,
                            in_=og_d.ap(),
                            in_offset=bass.IndirectOffsetOnAxis(
                                ap=oix[:, chg:chg + 1], axis=0)),
                            wait=(("dS", iv),), dma=True)
                    ctx.wait("tensor", ("dG", gv))
                    ov = ctx.cnt["dS"]
                else:
                    otm_sb = sb("otm", [128, 1024], BF16)
                    pstx = pst("ptx", [128, 128], BF16)
                    evs = 0
                    trd = 0
                    for i in range(16):
                        ld = ctx.emit("sync", lambda q, i=i: q.dma_start(
                            out=otm_sb[:, :],
                            in_=bass.AP(tensor=od2_d.ap().tensor,
                                        offset=(i // 2) * (T * 64)
                                        + (i % 2) * 128 * 1024,
                                        ap=[[1024, 128], [1, 1024]])),
                            wait=(("ts", trd),), dma=True)
                        for ch in range(NCH):
                            tv = ctx.emit("tensor", lambda t, ch=ch: t.transpose(
                                pstx[:, :], otm_sb[:, ch * 128:(ch + 1) * 128],
                                ident[:, :]),
                                wait=(("dS", ld), ("ss", evs)))
                            trd = tv
                            evs = ctx.emit("scalar", lambda s2, ch=ch, i=i: s2.copy(
                                out=o_sb[:, ch * TL + i * 128: ch * TL + i * 128 + 128],
                                in_=pstx[:, :]), wait=(("ts", tv),))
                    ov = ctx.cnt["dS"]
                    ctx.wait("tensor", ("ss", evs))
                wv = ctx.emit("gpsimd", lambda g: g.dma_start(
                    out=w_sb[:, :].rearrange("p (c f) -> p c f", c=NCH),
                    in_=warr_ap.rearrange("(c p) f -> p c f", p=128)), dma=True)
                bv = 0
                if bias_ap is not None:
                    bv = ctx.emit("sync", lambda q: q.dma_start(
                        out=bcc[:, :], in_=bias_ap), dma=True)
                rd = [0, 0]
                ev = 0
                for tg in range(TL // 512):
                    for jc in range(NCH):
                        bi = (tg * NCH + jc) % 2
                        tv = 0
                        for kc in range(NCH):
                            tv = ctx.emit("tensor", lambda t, tg=tg, jc=jc, kc=kc, bi=bi: t.matmul(
                                pm[bi][:, 0:512],
                                w_sb[:, kc * 1024 + jc * 128: kc * 1024 + jc * 128 + 128],
                                o_sb[:, kc * TL + tg * 512: kc * TL + tg * 512 + 512],
                                start=(kc == 0), stop=(kc == NCH - 1)),
                                wait=(("dG", wv), ("dS", ov), ("vs", rd[bi]))
                                if kc == 0 else (), chain=(kc != 0))
                        if bias_ap is not None:
                            bb = bcc[:, jc:jc + 1]
                            bbc = bass.AP(tensor=bb.tensor, offset=bb.offset,
                                          ap=[list(bb.ap[0]), [0, 512]])
                            ctx.emit("vector", lambda v, tg=tg, jc=jc, bbc=bbc: v.tensor_tensor(
                                out=x_sb[:, jc * TL + tg * 512: jc * TL + tg * 512 + 512],
                                in0=x_sb[:, jc * TL + tg * 512: jc * TL + tg * 512 + 512],
                                in1=bbc, op=OP.add),
                                wait=(("dS", max(xv, bv)),) if (tg == 0 and jc == 0) else ())
                        ev = ctx.emit("vector", lambda v, tg=tg, jc=jc, bi=bi: v.tensor_tensor(
                            out=x_sb[:, jc * TL + tg * 512: jc * TL + tg * 512 + 512],
                            in0=x_sb[:, jc * TL + tg * 512: jc * TL + tg * 512 + 512],
                            in1=pm[bi][:, 0:512], op=OP.add),
                            wait=(("ts", tv),) + ((("dS", xv),) if (tg == 0 and jc == 0 and bias_ap is None) else ()))
                        rd[bi] = ev
                xn_ready = emit_ln(ctx, x_sb, xn_sb, TL, ps_s1, ps_s2, sqbuf,
                                   rows, ps_bc,
                                   ready=(("vs", ev), ("gs", ctx.cnt["gs"])))
                ctx.emit("sync", lambda q: q.dma_start(
                    out=x_dst.rearrange("(c p) t -> p c t", p=128),
                    in_=x_sb[:, :].rearrange("p (c t) -> p c t", c=NCH)),
                    wait=(("vs", ev),), dma=True, store=True)
                if store_xn:
                    ctx.emit("sync", lambda q: q.dma_start(
                        out=xnmy_d.ap().rearrange("(c p) t -> p c t", p=128),
                        in_=xn_sb[:, :].rearrange("p (c t) -> p c t", c=NCH)),
                        wait=(("vs", ctx.cnt["vs"]),), dma=True, store=True)
                ctx.wait("sync", ("st", ctx.cnt["st"]))
            _barrier(ctx)

        def sec_win(pfx):
            es = contextlib.ExitStack()
            with es:
                sb, pst = sget(es, pfx)
                xs_sb = sb("xs", [128, 2 * 1024], BF16)
                q_fm = sb("qfm", [128, 4 * T], BF16)
                k_fm = sb("kfm", [128, 4 * T], BF16)
                v_tm = sb("vtm", [128, 32 * 516], BF16)
                big2 = sb("big2", [128, 4 * T], BF16)
                o_fm = big2[:, 0:4 * T]
                wq_sb = big2[:, 0:8 * 1024]
                wk_sb = big2[:, 8 * 1024:16 * 1024]
                wv_sb = sb("wv", [128, 8 * 512], BF16)
                gqr = sb("gqr", [128, 4], F32)
                gkr = sb("gkr", [128, 4], F32)
                slps = sb("slps", [1, 16], F32)
                sc = {
                    "qtm": sb("qtm", [128, 512], BF16),
                    "ktm": sb("ktm", [128, 512], BF16),
                    "sq": sb("sqs", [128, 512], F32),
                    "n2": sb("n2", [128, 4], F32),
                    "nb": sb("nb", [128, 8], F32),
                    "mx": sb("mx", [128, 8], F32),
                    "row8": sb("row8", [16, 16], F32),
                    "rstd": sb("rstd", [128, 2], F32),
                    "prob": sb("prob", [128, 1024], BF16),
                    "osc": sb("osc", [128, 512], F32),
                    "trow": sb("trow", [1, 512], F32),
                    "srow": sb("srow", [1, 512], F32),
                    "posT": sb("posT", [1, T], F32),
                    "bcol": sb("bcol", [128, 1], F32),
                    "rstd_dbg": sb("rstd_dbg", [128, 64], F32),
                }
                rrow = sb("rrowt", [1, 512], F32)
                sc["rrow"] = rrow
                ps = {
                    "pq": pst("pq", [128, 512]),
                    "pk": pst("pk", [128, 512]),
                    "pv": pst("pv", [128, 512]),
                    "pt": pst("pt", [128, 512], BF16),
                    "po": pst("po", [128, 512]),
                    "pb2": pst("pb2", [128, 512]),
                    "pr": pst("pr", [1, 512]),
                }
                ctx.emit("gpsimd", lambda g: g.iota(
                    sc["posT"][0:1, :], pattern=[[1, T]], base=0,
                    channel_multiplier=0,
                    allow_small_or_imprecise_dtypes=True))
                v3 = v_tm[:, :].rearrange("p (t h c) -> p (t h) c", c=129,
                                          h=4)[:, :, 128:129]
                ctx.emit("gpsimd", lambda g: g.memset(v3, 1.0))
                ctx.emit("gpsimd", lambda g: g.memset(sc["bcol"][:, :], 0.0))
                dv = 0
                for d_, s_ in ((win_gq, gqr), (win_gk, gkr), (win_slp, slps)):
                    dv = ctx.emit("sync", lambda q, d_=d_, s_=s_: q.dma_start(
                        out=s_[:, :], in_=d_), dma=True)
                wvv = 0
                for w_d, w_s in ((win_wq, wq_sb), (win_wk, wk_sb),
                                 (win_wv, wv_sb)):
                    wvv = ctx.emit("gpsimd", lambda g, w_d=w_d, w_s=w_s: g.dma_start(
                        out=w_s[:, :].rearrange("p (c f) -> p c f", c=NCH),
                        in_=w_d.rearrange("(c p) f -> p c f", p=128)), dma=True)
                w_ready = (("dS", dv), ("dG", wvv), ("gs", ctx.cnt["gs"]))
                done = emit_win_front(ctx, xnf_d.ap(), xs_sb, q_fm, k_fm, v_tm,
                                      o_fm, wq_sb, wk_sb, wv_sb, gqr, gkr,
                                      slps, ps, sc, w_ready)
                ctx.wait("sync", ("vs", done))
                store_o_a2a(o_fm)

        def sec_delta(pfx):
            es = contextlib.ExitStack()
            with es:
                sb, pst = sget(es, pfx)
                xs_sb = sb("xs", [128, 2 * 1024], BF16)
                q_fm = sb("qfm", [128, 4 * T], BF16)
                g_fm = sb("gfm", [128, 4 * T], BF16)
                u_tm = sb("utm", [128, 32 * 512], BF16)
                wun = sb("wun", [128, 4 * 4096 + 8 * 24], BF16)
                wq_sb = wun[:, 0:4096]
                wk_sb = wun[:, 4096:8192]
                wv_sb = wun[:, 8192:12288]
                wg_sb = wun[:, 12288:16384]
                wb_sb = wun[:, 16384:16384 + 192]
                Gall = wun[:, 0:8704]
                poolf = sb("poolf", [128, 1024], F32)
                dtx = sb("dtx", [128, 512], F32)
                dt2x = sb("dt2x", [128, 512], F32)
                scr3 = sb("scr3", [128, 1536], BF16)
                sc = {
                    "ktm": scr3[:, 0:512],
                    "vtm": scr3[:, 512:1024],
                    "utm": scr3[:, 1024:1536],
                    "bfs": sb("bfs", [128, 24], F32),
                    "Fpl": poolf[:, 512:768],
                    "Spl": poolf[:, 768:1024],
                    "H1": poolf[:, 0:256],
                    "H2": poolf[:, 256:512],
                    "T1": dtx[:, 0:256],
                    "T2": dt2x[:, 0:256],
                    "Hv": sb("Hv", [128, 32], F32),
                    "tva": sb("tva", [128, 16], F32),
                    "abll": sb("abll", [128, 512], F32),
                    "dsA": sb("dsA", [64, 3072], F32),
                    "dsB": sb("dsB", [64, 3072], F32),
                    "dt": dtx,
                    "dt2": dt2x,
                    "mtf": sb("mtf", [1, 1536], F32),
                    "abp": sb("abp", [128, 16], F32),
                    "wrd": sb("wrd", [4, 1024], F32),
                    "ostm": poolf[:, 0:512],
                    "ostm2": poolf[:, 512:1024],
                    "ostb": sb("ostb", [128, 512], BF16),
                }
                sc["Grb"] = scr3[:, 0:1088]
                fsh = sb("fsh", [32, 128], F32)
                sc["fsrow"] = fsh
                sc.update({
                    "wcp": sb("wcp", [128, 128], F32),
                    "hvt": fsh,
                    "rep4f": sb("rep4f", [4, 128], F32),
                })
                sc["msk32"] = sc["Hv"]
                sc["abg"] = sc["dsA"][0:4, 0:512]
                rep4_sb = sc["rep4f"]
                sc["mr"] = [sb(f"mr{i}", [1, 1536], BF16) for i in range(4)]
                sc["mr2"] = [sb(f"ms{i}", [1, 1536], BF16) for i in range(4)]
                sc["Gall"] = Gall
                brow_sb = sb("brow", [1, 24], BF16)
                sc["onesbf"] = onesrowb
                ps = {
                    "pk": pst("pk", [128, 512]),
                    "pv": pst("pv", [128, 512]),
                    "pb": pst("pb", [128, 24]),
                    "pq": pst("pq", [128, 128]),
                    "pdo": pst("pdo", [64, 136]),
                    "ptx": pst("ptx", [128, 128], BF16),
                    "pg": pst("pgx", [128, 128]),
                }
                ps["pab"] = pst("pab", [128, 512])
                ps["pA"] = ps["pk"]
                ps["pB"] = ps["pv"]
                ps["pt8"] = ps["pg"][0:8, :]
                ps["pth"] = ps["pg"][0:32, :]
                ctx.u_tm = u_tm[:, :]
                ctx.q_fm = q_fm[:, :]
                ctx.g_fm = g_fm[:, :]
                ctx.brow_sb = brow_sb
                wld = 0
                for d_, s_ in ((dl_w["wq"], wq_sb), (dl_w["wk"], wk_sb),
                               (dl_w["wv"], wv_sb), (dl_w["wg"], wg_sb)):
                    wld = ctx.emit("gpsimd", lambda g, d_=d_, s_=s_: g.dma_start(
                        out=s_.rearrange("p (c f) -> p c f", c=NCH),
                        in_=d_.rearrange("(c p) f -> p c f", p=128)), dma=True)
                wld = ctx.emit("gpsimd", lambda g: g.dma_start(
                    out=wb_sb.rearrange("p (c f) -> p c f", c=NCH),
                    in_=dl_wb.rearrange("(c p) f -> p c f", p=128)), dma=True)
                wld = ctx.emit("gpsimd", lambda g: g.dma_start(
                    out=brow_sb[:, :], in_=dl_brow), dma=True)
                wld = ctx.emit("gpsimd", lambda g: g.dma_start(
                    out=sc["rep4f"][0:4, :], in_=dl_rep4), dma=True)
                w_ready = (("dG", wld), ("gs", ctx.cnt["gs"]))
                fs_st = emit_delta_front(ctx, xnf_d.ap(), xs_sb, wq_sb, wk_sb,
                                         wv_sb, wg_sb, wb_sb, None, None, None,
                                         rep4_sb, None, fs_dram.ap(), None,
                                         None, None, None, ps, sc, w_ready)
                qgv = 0
                for ch in range(4):
                    for gg in range(T // 512):
                        qgv = ctx.emit("vector", lambda v, ch=ch, gg=gg: v.scalar_tensor_tensor(
                            out=g_fm[:, ch * T + gg * 512: ch * T + gg * 512 + 512],
                            in0=q_fm[:, ch * T + gg * 512: ch * T + gg * 512 + 512],
                            scalar=0.5,
                            in1=g_fm[:, ch * T + gg * 512: ch * T + gg * 512 + 512],
                            op0=OP.mult, op1=OP.mult),
                            wait=(("ss", ctx.cnt["ss"]),) if (ch == 0 and gg == 0) else ())
                o_fm = q_fm[:, :]
                zf = ctx.emit("gpsimd", lambda g: g.memset(o_fm, 0.0),
                              wait=(("vs", qgv),))
                ctx.wait("vector", ("st", fs_st), ("gs", zf))
                gb = emit_delta_scan(ctx, fs_dram, g_dram, ab_dram, hv_dram,
                                     w_dram, rep4_sb, o_fm, ps, sc)
                dwev, oev, hst, ast = emit_delta_scan2(
                    ctx, g_dram, ab_dram, hv_dram, w_dram, rep4_sb, o_fm, ps,
                    sc, gb)
                f4 = emit_delta_scan3(ctx, ab_dram, hv_dram, w_dram, rep4_sb,
                                      o_fm, g_fm, ps, sc, dwev, oev, hst, ast,
                                      sub=5)
                ctx.wait("sync", ("vs", f4), ("vs", ctx.cnt["vs"]),
                         ("ss", ctx.cnt["ss"]))
                store_o_a2a(o_fm)

        def sec_moe(pfx):
            es = contextlib.ExitStack()
            with es:
                sb, pst = sget(es, pfx + "r")
                xn_sb = sb("xn", [128, NCH * TL], BF16)
                o_tm = sb("otm", [128, 16 * 1024], BF16)
                wgt_sb = sb("wgt", [128, 8 * 8], BF16)
                rout_sb = sb("rout", [128, 16 * 8], F32)
                oh_all = sb("oh", [128, 16 * 16], F32)
                h_fm = sb("hfm", [8, 2 * TL], F32)
                c_tm_all = sb("ctm", [128, 16 * 8], F32)
                sc = {
                    "lt": sb("lt", [128, 8], F32),
                    "sc8a": sb("sc8a", [128, 8], F32),
                    "sc8b": sb("sc8b", [128, 8], F32),
                    "col": sb("colx", [128, 8], F32),
                    "iota8": sb("iota8", [128, 8], F32),
                }
                tot_sb = sb("tot", [8, 1], F32)
                ps = {
                    "pl": pst("pl", [128, 8]),
                    "pt8": pst("pt8", [8, 128]),
                    "ptb": pst("ptb", [128, 8]),
                    "ptx": pst("ptx", [128, 128], BF16),
                }
                xv = ctx.emit("sync", lambda q: q.dma_start(
                    out=xn_sb[:, :].rearrange("p (c t) -> p c t", c=NCH),
                    in_=xnmy_d.ap().rearrange("(c p) t -> p c t", p=128)),
                    dma=True)
                wv = ctx.emit("gpsimd", lambda g: g.dma_start(
                    out=wgt_sb[:, :].rearrange("p (c f) -> p c f", c=NCH),
                    in_=moe_wgt.rearrange("(c p) f -> p c f", p=128)), dma=True)
                w_ready = (("dS", xv), ("dG", wv), ("gs", ctx.cnt["gs"]))
                rv, hv = emit_moe_route(ctx, xn_sb, wgt_sb, o_tm, rout_sb,
                                        oh_all, h_fm, c_tm_all, ps, sc, w_ready)
                tv = ctx.emit("vector", lambda v: v.tensor_tensor(
                    out=tot_sb[0:8, 0:1], in0=h_fm[0:8, 2 * TL - 1:2 * TL],
                    in1=h_fm[0:8, TL - 1:TL], op=OP.add), wait=(("vs", hv),))
                ctx.emit("sync", lambda q: q.dma_start(
                    out=xntm_my_d.ap().rearrange("(a p) d -> p a d", p=128),
                    in_=o_tm[:, :].rearrange("p (a d) -> p a d", d=D)),
                    wait=(("ss", ctx.cnt["ss"]),), dma=True, store=True)
                ctx.emit("sync", lambda q: q.dma_start(
                    out=rout_my_d.ap().rearrange("(a p) e -> p a e", p=128),
                    in_=rout_sb[:, :].rearrange("p (a e) -> p a e", e=8)),
                    wait=(("vs", rv),), dma=True, store=True)
                ctx.emit("sync", lambda q: q.dma_start(
                    out=totmy_d.ap(), in_=tot_sb[0:8, 0:1]),
                    wait=(("vs", tv),), dma=True, store=True)
                ctx.wait("sync", ("st", ctx.cnt["st"]))
            _barrier(ctx)
            for src, dst in ((xntm_my_d, xn_all_d), (rout_my_d, rout_all_d),
                             (totmy_d, totall_d)):
                ctx.emit("gpsimd", lambda g, src=src, dst=dst: g.collective_compute(
                    "AllGather", mybir.AluOpType.bypass, replica_groups=ALL8,
                    ins=[src.ap().opt()], outs=[dst.ap().opt()]), coll=True)
            _barrier(ctx)
            es = contextlib.ExitStack()
            with es:
                sb, pst = sget(es, pfx + "e")
                rt_sb = sb("rt", [128, 128 * 8], F32)
                w1_sb = sb("w1", [128, 8 * 4096], BF16)
                w2_sb = sb("w2", [128, 16 * 1024], BF16)
                tot_sb = sb("tot", [8, 8], F32)
                offs_cm = sb("offs", [8, 8], F32)
                tbi_sb = sb("tbi", [128, 40], U32)
                sc = {
                    "fill": sb("fill", [128, 1280], U32),
                    "tokall": sb("tokall", [128, 512], U32),
                    "xb": sb("xb", [128, 4096], BF16),
                    "xg": sb("xg", [128, 4096], BF16),
                    "act": sb("act", [128, 16 * 512], BF16),
                    "of": sb("of", [128, 4096], BF16),
                    "gel": sb("gel", [128, 512], F32),
                    "tbl4": sb("tbl4", [128, 4], U32),
                    "tokid": sb("tokid", [128, 4], U32),
                    "slotu": sb("slotu", [128, 2], U32),
                    "m8a": sb("m8a", [128, 8], F32),
                    "m8b": sb("m8b", [128, 8], F32),
                    "mcol": sb("mcol", [128, 8], F32),
                    "iota8": sb("iota8", [128, 8], F32),
                    "orow": sb("orowx", [1, 8], F32),
                    "te8": sb("te8", [8, 16], F32),
                }
                ps = {
                    "ph": pst("ph", [128, 512]),
                    "ph2": pst("ph2", [128, 512]),
                    "ptx": pst("ptx", [128, 128], BF16),
                    "pt8": pst("pt8", [8, 128]),
                    "pseg": pst("pseg", [128, 8]),
                }
                ctx.tblidx_d = tblidx
                ctx.emit("gpsimd", lambda g: g.iota(
                    sc["iota8"][:, :], pattern=[[1, 8]], base=0,
                    channel_multiplier=0,
                    allow_small_or_imprecise_dtypes=True))
                rv = ctx.emit("sync", lambda q: q.dma_start(
                    out=rt_sb[:, :].rearrange("p (a e) -> p a e", e=8),
                    in_=rout_all_d.ap().rearrange("(a p) e -> p a e", p=128)),
                    dma=True)
                tv = ctx.emit("sync", lambda q: q.dma_start(
                    out=tot_sb[0:8, 0:8],
                    in_=totall_d.ap().rearrange("(c e) o -> c (e o)", e=8)),
                    dma=True)
                w1v = ctx.emit("gpsimd", lambda g: g.dma_start(
                    out=w1_sb[:, :].rearrange("p (c f) -> p c f", c=NCH),
                    in_=w1e.rearrange("(c p) f -> p c f", p=128)), dma=True)
                w2v = ctx.emit("gpsimd", lambda g: g.dma_start(
                    out=w2_sb[:, :].rearrange("p (c f) -> p c f", c=16),
                    in_=w2e.rearrange("(c p) f -> p c f", p=128)), dma=True)
                ctx.wait("vector", ("dS", tv), ("dS", rv))
                ctx.wait("tensor", ("dS", tv), ("dS", rv))
                w_ready = (("dG", w2v),)
                emit_moe_expert(ctx, xn_all_d.ap(), table_d, expo_d.ap(),
                                rt_sb, tot_sb, offs_cm, tbi_sb, w1_sb, w2_sb,
                                ps, sc, w_ready)
                ctx.wait("sync", ("st", ctx.cnt["st"]))
            _barrier(ctx)
            ctx.emit("gpsimd", lambda g: g.collective_compute(
                "AllGather", mybir.AluOpType.bypass, replica_groups=ALL8,
                ins=[expo_d.ap().opt()], outs=[expo_all_d.ap().opt()]),
                coll=True)
            _barrier(ctx)
            es = contextlib.ExitStack()
            with es:
                sb, pst = sget(es, pfx + "c")
                x_sb = sb("x", [128, NCH * TL], F32)
                rt_sb = sb("rt", [128, 16 * 8], F32)
                tot_sb = sb("tot", [8, 8], F32)
                offs_cm = sb("offs", [8, 8], F32)
                mycol_sb = sb("mycol", [8, 1], F32)
                sc = {
                    "rowg": sb("rowg", [128, 2048], BF16),
                    "ytm": sb("ytm", [128, 1024], F32),
                    "ybf": sb("ybf", [128, 1024], BF16),
                    "kw": sb("kw", [128, 2], F32),
                    "slotu": sb("slotu", [128, 2], U32),
                    "slotc": sb("slotc", [128, 2], U32),
                    "m8a": sb("m8a", [128, 8], F32),
                    "m8b": sb("m8b", [128, 8], F32),
                    "mcol": sb("mcol", [128, 8], F32),
                    "iota8": sb("iota8", [128, 8], F32),
                    "orow": sb("orowx", [1, 8], F32),
                    "te8": sb("te8", [8, 16], F32),
                }
                ps = {
                    "ptx": pst("ptx", [128, 128], BF16),
                    "pt8": pst("pt8", [8, 128]),
                    "pseg": pst("pseg", [128, 8]),
                }
                ctx.emit("gpsimd", lambda g: g.iota(
                    sc["iota8"][:, :], pattern=[[1, 8]], base=0,
                    channel_multiplier=0,
                    allow_small_or_imprecise_dtypes=True))
                xv = ctx.emit("sync", lambda q: q.dma_start(
                    out=x_sb[:, :].rearrange("p (c t) -> p c t", c=NCH),
                    in_=x_d.ap().rearrange("(c p) t -> p c t", p=128)),
                    dma=True)
                rv = ctx.emit("sync", lambda q: q.dma_start(
                    out=rt_sb[:, :].rearrange("p (a e) -> p a e", e=8),
                    in_=rout_my_d.ap().rearrange("(a p) e -> p a e", p=128)),
                    dma=True)
                tv = ctx.emit("sync", lambda q: q.dma_start(
                    out=tot_sb[0:8, 0:8],
                    in_=totall_d.ap().rearrange("(c e) o -> c (e o)", e=8)),
                    dma=True)
                mv = ctx.emit("sync", lambda q: q.dma_start(
                    out=mycol_sb[0:8, 0:1], in_=mycol), dma=True)
                ctx.wait("vector", ("dS", tv), ("dS", rv), ("dS", xv))
                ctx.wait("tensor", ("dS", tv), ("dS", rv))
                ready = (("dS", mv), ("gs", ctx.cnt["gs"]))
                done = emit_moe_combine(ctx, x_sb, expo_all_d.ap(), rt_sb,
                                        tot_sb, offs_cm, mycol_sb, ps, sc,
                                        ready)
                ctx.emit("sync", lambda q: q.dma_start(
                    out=x_d.ap().rearrange("(c p) t -> p c t", p=128),
                    in_=x_sb[:, :].rearrange("p (c t) -> p c t", c=NCH)),
                    wait=(("vs", done),), dma=True, store=True)
                ctx.wait("sync", ("st", ctx.cnt["st"]))
            _barrier(ctx)

        def sec_attn(pfx):
            es = contextlib.ExitStack()
            with es:
                sb, pst = sget(es, pfx)
                xn_sb = sb("xn", [128, NCH * T], BF16)
                q_fm = sb("qfm", [128, 4 * T], BF16)
                k_fm = sb("kfm", [128, 4 * T], BF16)
                v_tm = sb("vtm", [128, 32 * 520], BF16)
                sincos = sb("sincos", [128, 2048], F32)
                wq_sb = sb("wq", [128, 8 * 512], BF16)
                wk_sb = sb("wk", [128, 8 * 512], BF16)
                wv_sb = sb("wv", [128, 8 * 512], BF16)
                sc = {
                    "qtm": sb("qtm", [128, 512], BF16),
                    "ktm": sb("ktm", [128, 512], BF16),
                    "sq": sb("sqs", [128, 512], F32),
                    "rsc": sb("rsc", [128, 512], F32),
                    "n2": sb("n2", [128, 16], F32),
                    "mx": sb("mx", [128, 16], F32),
                    "row8": sb("row8", [16, 16], F32),
                    "mrowbf": sb("mrowbf", [1, 512], BF16),
                    "onesbf": sb("onesbf", [1, 128], BF16),
                    "prob": sb("prob", [128, 1024], BF16),
                    "osc": sb("osc", [128, 512], F32),
                }
                sc["mrow"] = sc["sq"]
                sc["rrow"] = sc["rsc"]
                stg = sb("stg", [128, 128], BF16)
                ps = {
                    "pq": pst("pq", [128, 512]),
                    "pk": pst("pk", [128, 512]),
                    "pv": pst("pv", [128, 512]),
                    "pt": pst("pt", [128, 512], BF16),
                    "pb2": pst("pb2", [128, 512]),
                }
                ctx.emit("gpsimd", lambda g: g.memset(sc["onesbf"][:, :], 1.0))
                v3 = v_tm[:, :].rearrange("p (t h c) -> p (t h) c", c=65,
                                          h=8)[:, :, 64:65]
                ctx.emit("gpsimd", lambda g: g.memset(v3, 1.0))
                xv = ctx.emit("sync", lambda q: q.dma_start(
                    out=xn_sb[:, :].rearrange("p (c t) -> p c t", c=NCH),
                    in_=xnf_d.ap().rearrange("(c p) t -> p c t", p=128)),
                    dma=True)
                sv = ctx.emit("sync", lambda q: q.dma_start(
                    out=sincos[:, :], in_=at_sincos), dma=True)
                wvv = 0
                for w_d, w_s in ((at_wq, wq_sb), (at_wk, wk_sb), (at_wv, wv_sb)):
                    wvv = ctx.emit("gpsimd", lambda g, w_d=w_d, w_s=w_s: g.dma_start(
                        out=w_s[:, :].rearrange("p (c f) -> p c f", c=NCH),
                        in_=w_d.rearrange("(c p) f -> p c f", p=128)), dma=True)
                w_ready = (("dS", xv), ("dS", sv), ("dG", wvv),
                           ("gs", ctx.cnt["gs"]))
                o_fm = xn_sb[:, 0:4 * T]
                done = emit_attn_front(ctx, xn_sb, q_fm, k_fm, v_tm, o_fm,
                                       sincos, wq_sb, wk_sb, wv_sb, w_ready,
                                       ps, sc)
                stv = 0
                evs = 0
                for hc in range(4):
                    for tt in range(32):
                        k = tt % 4
                        tv = ctx.emit("tensor", lambda t, hc=hc, tt=tt, k=k: t.transpose(
                            ps["pt"][:, k * 128:(k + 1) * 128],
                            o_fm[:, hc * T + tt * 128: hc * T + tt * 128 + 128],
                            ident[:, :]),
                            wait=(("vs", done), ("st", stv)))
                        evs = ctx.emit("scalar", lambda s2, k=k: s2.copy(
                            out=stg[:, :], in_=ps["pt"][:, k * 128:(k + 1) * 128]),
                            wait=(("ts", tv), ("st", stv)))
                        for hh in range(2):
                            h = hc * 2 + hh
                            stv = ctx.emit("sync", lambda q, h=h, tt=tt, hh=hh: q.dma_start(
                                out=bass.AP(tensor=od2_d.ap().tensor,
                                            offset=h * (T * 64) + tt * 128 * 64,
                                            ap=[[64, 128], [1, 64]]),
                                in_=stg[:, hh * 64:(hh + 1) * 64]),
                                wait=(("ss", evs),), dma=True, store=True)
                ctx.wait("sync", ("st", ctx.cnt["st"]))
            _barrier(ctx)

        def sec_l1ffn(pfx, x_src, y_dst):
            es = contextlib.ExitStack()
            with es:
                sb, pst = sget(es, pfx)
                x_sb = sb("x", [128, NCH * TL], F32)
                xn_sb = sb("xn", [128, NCH * TL], BF16)
                w1_sb = sb("w1", [128, 2 * 1024], BF16)
                w2_sb = sb("w2", [128, 2 * 4096], BF16)
                act = sb("act", [128, 32 * 512], BF16)
                sqbuf = sb("sq", [128, 1024], F32)
                rows = sb("rows", [1, 3 * TL], F32)
                ssc = sb("ssc", [128, 1024], F32)
                qp = sb("qp", [128, 160], F32)
                qsc = sb("qsc", [128, 192], F32)
                kI = sb("kI", [128, 32], I32)
                Atil = sb("Atil", [128, 32], F32)
                sinb = sb("sinb", [128, 32], F32)
                b2c = sb("b2c", [128, 8], F32)
                ps_s1 = pst("s1", [1, 512])
                ps_s2 = pst("s2", [1, 512])
                ps_bc = pst("bc", [128, 1024])
                ps_h = pst("ph", [128, 512])
                ps_o = pst("po", [128, 512])
                xv = ctx.emit("sync", lambda q: q.dma_start(
                    out=x_sb[:, :].rearrange("p (c t) -> p c t", c=NCH),
                    in_=x_src.rearrange("(c p) t -> p c t", p=128)), dma=True)
                qv = ctx.emit("sync", lambda q: q.dma_start(
                    out=qp[:, :], in_=f_qpv), dma=True)
                bv = ctx.emit("sync", lambda q: q.dma_start(
                    out=b2c[:, :], in_=f_b2v), dma=True)
                emit_qprep(ctx, qp, kI, qsc, Atil, sinb,
                           (("dS", qv), ("gs", ctx.cnt["gs"])))
                xn_ready = emit_ln(ctx, x_sb, xn_sb, TL, ps_s1, ps_s2, sqbuf,
                                   rows, ps_bc,
                                   ready=(("dS", xv), ("gs", ctx.cnt["gs"])))
                ctx.wait("vector", ("dS", bv))
                ctx.wait("scalar", ("dS", bv))
                done = emit_l1ffn(ctx, x_sb, xn_sb, xn_ready, w1_sb, w2_sb,
                                  f_w1, f_w2, Atil, sinb, qp[:, 96:128], b2c,
                                  act, ps_h, ps_o, ssc)
                ctx.emit("sync", lambda q: q.dma_start(
                    out=y_dst.rearrange("(c p) t -> p c t", p=128),
                    in_=x_sb[:, :].rearrange("p (c t) -> p c t", c=NCH)),
                    wait=(("vs", done), ("dS", bv)), dma=True, store=True)
                ctx.wait("sync", ("st", ctx.cnt["st"]))
            _barrier(ctx)

        sec_ln("a_", xin)
        xn_exchange()
        sec_gla("g_")
        sec_epi("e1_", xin, gla_wo, x_d.ap(), store_xn=True)
        sec_moe("m1")
        sec_ln("b_", x_d.ap())
        xn_exchange()
        sec_win("w_")
        sec_epi("e2_", x_d.ap(), win_wo, x_d.ap(), store_xn=True)
        sec_moe("m2")
        sec_ln("c_", x_d.ap())
        xn_exchange()
        sec_delta("d_")
        sec_epi("e3_", x_d.ap(), dl_wo, x_d.ap(), store_xn=False)
        sec_l1ffn("f1_", x_d.ap(), x_d.ap())
        sec_ln("n4_", x_d.ap())
        xn_exchange()
        sec_attn("t_")
        sec_epi("e4_", x_d.ap(), at_wp, x_d.ap(), bias_ap=at_bp,
                store_xn=False, otm=True)
        sec_l1ffn("f2_", x_d.ap(), xout)
        ctx.wait("sync", ("st", ctx.cnt["st"]))
    return nc


def kernel(**kw):
    f32 = np.float32
    cores = list(range(NCORES))
    tsl = [slice((c % 2) * TL, (c % 2 + 1) * TL) for c in cores]
    csl = [slice((c % 2) * 512, (c % 2 + 1) * 512) for c in cores]
    x0 = kw["L0_attn_a_x"]

    Wq, Wk, Wv = (kw["L0_attn_a_W_q"].astype(f32), kw["L0_attn_a_W_k"].astype(f32),
                  kw["L0_attn_a_W_v"].astype(f32))
    Wz, bgt = kw["L0_attn_a_W_gate"].astype(f32), kw["L0_attn_a_b_gate"].astype(f32)
    wwq, wwk, wwv = (kw["L0_attn_b_wq"].astype(f32), kw["L0_attn_b_wk"].astype(f32),
                     kw["L0_attn_b_wv"].astype(f32))
    gq, gk = (kw["L0_attn_b_q_norm_w"].astype(f32),
              kw["L0_attn_b_k_norm_w"].astype(f32))
    WgT = _asc(kw["L0_ffn_Wg"].T.astype(f32))
    dWq, dWk, dWv, dWg = (kw["L1_attn_a_Wq"].astype(f32), kw["L1_attn_a_Wk"].astype(f32),
                          kw["L1_attn_a_Wv"].astype(f32), kw["L1_attn_a_Wg"].astype(f32))
    Wb, Wfd, Wsd = (kw["L1_attn_a_Wb"].astype(f32), kw["L1_attn_a_Wfd"].astype(f32),
                    kw["L1_attn_a_Wsd"].astype(f32))
    bfd, bsd = kw["L1_attn_a_bfd"].astype(f32), kw["L1_attn_a_bsd"].astype(f32)
    aWq, aWk, aWv = (kw["L1_attn_b_Wq"].astype(f32), kw["L1_attn_b_Wk"].astype(f32),
                     kw["L1_attn_b_Wv"].astype(f32))
    sincos = stage_yarn_sincos()
    qpv = _stage_qpv(kw["L1_ffn_b1"].astype(f32),
                     kw["L1_ffn_quantum_phases"].astype(f32),
                     kw["L1_ffn_quantum_weights_real"].astype(f32),
                     kw["L1_ffn_quantum_weights_imag"].astype(f32))
    b2v = _asc(kw["L1_ffn_b2"].astype(f32).reshape(8, 128).T)
    fw1 = _asc(kw["L1_ffn_W1"].astype(f32))
    fw2 = _asc(kw["L1_ffn_W2"].astype(f32))
    rep4 = np.zeros((4, 128), f32)
    for p in range(4):
        rep4[p, p * 32:(p + 1) * 32] = 1.0
    rr = 2.0 ** (-8.0 / 8)
    SCW = 1.0 / np.sqrt(128.0)
    bpc = _asc(kw["L1_attn_b_bp"].astype(f32).reshape(8, 128).T)

    f_all = np.arange(D)
    orow_of_f = np.where(f_all < 512, 2 * f_all, 1024 + 2 * (f_all - 512))

    in_maps = []
    for c in cores:
        hg = c % 2
        oidx_c = _asc((orow_of_f + hg).astype(np.uint32)
                      .reshape(8, 128).T)
        my, ot = csl[c], slice((1 - hg) * 512, (2 - hg) * 512)
        slpv = np.zeros((1, 16), f32)
        for hl in range(4):
            gh = hg * 4 + hl
            slpv[0, hl] = (rr ** gh) / SCW
            slpv[0, 8 + hl] = -(rr ** gh) / SCW
        hs = slice(hg * 8, (hg + 1) * 8)
        wb = np.concatenate([Wb.T[:, hs], Wfd.T[:, hs], Wsd.T[:, hs]], 1)
        brow = np.concatenate([np.zeros(8, f32), bfd[hs], bsd[hs]])[None, :]
        onehot = np.zeros((8, 1), f32)
        onehot[c] = 1.0
        tbl = (c * CAP + np.arange(CAP)).astype(np.uint32)
        in_maps.append({
            "xin": _asc(x0[c // 2, tsl[c], :].T.astype(f32)),
            "gla_wq": _asc(Wq[:, csl[c]]), "gla_wk": _asc(Wk[:, csl[c]]),
            "gla_wv": _asc(Wv[:, csl[c]]), "gla_wz": _asc(Wz[:, csl[c]]),
            "gla_bg": _asc(bgt[csl[c]].reshape(4, 128).T),
            "gla_wo": _asc(kw["L0_attn_a_W_o"].astype(f32)),
            "win_wq": _asc(np.concatenate([wwq[:, my], wwq[:, ot]], 1)),
            "win_wk": _asc(np.concatenate([wwk[:, my], wwk[:, ot]], 1)),
            "win_wv": _asc(wwv[:, my]),
            "win_gq": _asc(gq[my].reshape(4, 128).T),
            "win_gk": _asc(gk[my].reshape(4, 128).T),
            "win_slp": slpv,
            "win_wo": _asc(kw["L0_attn_b_wo"].astype(f32)),
            "moe_wgt": WgT,
            "w1e": _asc(kw["L0_ffn_W1"][c].astype(f32)),
            "w2e": _asc(kw["L0_ffn_W2"][c].astype(f32)),
            "tblidx": _asc(tbl.reshape(CAP // 128, 128).T),
            "mycol": onehot,
            "dl_wq": _asc(dWq.T[:, csl[c]]), "dl_wk": _asc(dWk.T[:, csl[c]]),
            "dl_wv": _asc(dWv.T[:, csl[c]]), "dl_wg": _asc(dWg.T[:, csl[c]]),
            "dl_wb": _asc(wb), "dl_brow": _asc(brow), "dl_rep4": rep4,
            "dl_wo": _asc(kw["L1_attn_a_Wo"].astype(f32).T),
            "at_wq": _asc(aWq.T[:, csl[c]]), "at_wk": _asc(aWk.T[:, csl[c]]),
            "at_wv": _asc(aWv.T[:, csl[c]]),
            "at_sincos": sincos,
            "at_wp": _asc(kw["L1_attn_b_Wp"].astype(f32).T),
            "at_bp": bpc,
            "f_w1": fw1, "f_w2": fw2, "f_qpv": qpv, "f_b2v": b2v,
            "oidx": oidx_c,
        })
    nc = _prog("mega", build_mega)
    res = run_bass_kernel_spmd(nc, in_maps, core_ids=cores)
    out = np.zeros((NB, T, D), np.float32)
    for c in cores:
        out[c // 2, tsl[c], :] = res.results[c]["xout"].T
    return out
